# version 10
# baseline (speedup 1.0000x reference)
import numpy as np
import ml_dtypes
from contextlib import ExitStack
import concourse.bass as bass
import concourse.mybir as mybir
from concourse.bass_utils import run_bass_kernel_spmd

F32 = mybir.dt.float32
BF16 = mybir.dt.bfloat16
AF = mybir.ActivationFunctionType
ALU = mybir.AluOpType
AX = mybir.AxisListType

NCORES = 8
B_LOC = 4
S = 2048
D = 1024
T = B_LOC * S
NT = T // 128
DCH = D // 128
NH = 16
DH = 64
WIN = 512
NEXP = 8
FF = 4096
NFC = FF // 128
MTT = 512
NMT = T // MTT
JL = S // NCORES
NJT = JL // 128
EPS = 1e-5
NEG = -1e9

ENGINES = ["sync", "gpsimd", "tensor", "vector", "scalar"]
BF = ml_dtypes.bfloat16



class MockIns:
    def then_inc(self, *a, **k):
        raise RuntimeError("use ctx.inc / Chain.step, not .then_inc")


class MockEng:
    def __init__(self, name):
        self.name = name

    def __getattr__(self, op):
        def f(*a, **k):
            return MockIns()
        return f


class Ctx:
    def __init__(self, nc):
        self.nc = nc
        self.mode = "mock"
        self.counts = {}
        self.marks = {}
        self.sems = {}
        self.tensors = {}
        self.consts = {}
        self.sb = {}
        self.ps = {}

    def t(self, name):
        return self.tensors[name].ap()

    def inc(self, ins, name, v=1):
        self.counts[name] = self.counts.get(name, 0) + v
        if self.mode == "real":
            ins.then_inc(self.sems[name], v)

    def wait(self, eng, name, val):
        if self.mode == "real":
            eng.wait_ge(self.sems[name], val)

    def count(self, name):
        return self.counts.get(name, 0)

    def mark(self, key):
        snap = dict(self.counts)
        if self.mode == "mock":
            if key not in self.marks:
                self.marks[key] = snap
        return self.marks[key]

    def at(self, key, sem):
        if key not in self.marks:
            if self.mode == "mock":
                return 0
            raise KeyError(f"mark {key} never recorded in mock pass")
        return self.marks[key].get(sem, 0)

    def bar(self, eng, k):
        self.counts["bar"] = self.counts.get("bar", 0) + 1
        if self.mode == "real":
            eng.drain().then_inc(self.sems["bar"], 1)
            eng.wait_ge(self.sems["bar"], 5 * (k + 1))


class Chain:

    def __init__(self, ctx, en, eng, key):
        self.ctx, self.en, self.eng = ctx, en, eng
        self.sem = "ch_" + key
        self.dsem = {q: f"chd{q[0]}_{key}" for q in ("sync", "gpsimd", "scalar")}
        ctx.counts.setdefault(self.sem, 0)
        for s in self.dsem.values():
            ctx.counts.setdefault(s, 0)
        self.n = 0
        self.d = {q: 0 for q in self.dsem}

    def _wait_cur(self):
        if self.ctx.mode == "real":
            if self.n:
                self.ctx.wait(self.eng, self.sem, self.n)
            for q, v in self.d.items():
                if v:
                    self.ctx.wait(self.eng, self.dsem[q], v)

    def step(self, owner, fn, dv=None):
        dma = dv is not None
        if self.en == owner:
            if self.ctx.mode == "real":
                self._wait_cur()
                ins = fn(self.eng)
                if dma:
                    ins.then_inc(self.ctx.sems[self.dsem[owner]], 16)
                else:
                    ins.then_inc(self.ctx.sems[self.sem], 1)
            self.ctx.counts[self.dsem[owner] if dma else self.sem] += 16 if dma else 1
        if dma:
            self.d[owner] += 16
        else:
            self.n += 1

    def dma_group(self, owner, fns):
        first = True
        for fn in fns:
            if self.en == owner:
                if self.ctx.mode == "real":
                    if first:
                        self._wait_cur()
                    ins = fn(self.eng)
                    ins.then_inc(self.ctx.sems[self.dsem[owner]], 16)
                self.ctx.counts[self.dsem[owner]] += 16
            self.d[owner] += 16
            first = False

    def wait_all(self):
        self._wait_cur()




def mk_ops(ctx, en, eng, key):
    sems = {"tensor": key + "pe", "vector": key + "ve", "scalar": key + "sc",
            "sync": key + "ld", "gpsimd": key + "gp", "sync2": key + "ld2",
            "gpsimd2": key + "gp2", "gpsimdw": key + "gpw", "gpsimdc": key + "gc"}
    engine_of = {"sync2": "sync", "gpsimd2": "gpsimd", "gpsimdw": "gpsimd",
                 "gpsimdc": "gpsimd"}
    for sm in sems.values():
        ctx.counts.setdefault(sm, 0)

    def op(owner, fn, mark=None, deps=()):
        act = engine_of.get(owner, owner)
        if en == act:
            if ctx.mode == "real":
                for (mk, sm) in deps:
                    ctx.wait(eng, sm, ctx.at(mk, sm))
            dv = 1 if act in ("tensor", "vector", "scalar") or owner == "gpsimdc" else 16
            if ctx.mode == "real":
                ins = fn(eng)
                ins.then_inc(ctx.sems[sems[owner]], dv)
            ctx.counts[sems[owner]] += dv
            if mark:
                ctx.mark(mark)
        return None

    def waitall(extra=()):
        if ctx.mode == "real":
            for sm in list(sems.values()) + list(extra):
                v = ctx.count(sm)
                if v:
                    ctx.wait(eng, sm, v)
    return op, sems, waitall


def host_consts():
    c = {}
    c["ident_b"] = np.eye(128, dtype=BF)
    c["ident_f"] = np.eye(128, dtype=np.float32)
    r = np.arange(128)
    c["mask_causal"] = np.where(r[None, :] <= r[:, None], 0.0, NEG).astype(np.float32)
    c["mask_far"] = np.where(r[None, :] >= r[:, None] + 1, 0.0, NEG).astype(np.float32)
    c["ones_row_b"] = np.ones((1, 512), dtype=BF)
    c["ones_row_f"] = np.ones((1, 128), dtype=np.float32)
    oh = np.zeros((8, 8 * 128), dtype=BF)
    for e in range(8):
        oh[e, e * 128:(e + 1) * 128] = 1
    c["onehot8"] = oh
    c["onehot32"] = None
    del c["onehot32"]
    return c


def build_program(phase_list, io_kinds):
    nc = bass.Bass("TRN2", target_bir_lowering=False)
    ctx = Ctx(nc)
    for name, spec in io_kinds.items():
        kind, shape, dtype = spec[:3]
        kw = {"addr_space": spec[3]} if len(spec) > 3 else {}
        ctx.tensors[name] = nc.dram_tensor(name, list(shape), dtype, kind=kind, **kw)
    for cname, arr in host_consts().items():
        ctx.consts[cname] = nc.inline_tensor(arr, cname).ap()

    with ExitStack() as stack:
        en_ = stack.enter_context
        sb = ctx.sb
        sb["A"] = en_(nc.sbuf_tensor("bufA", [128, 32768], BF16))
        sb["B"] = en_(nc.sbuf_tensor("bufB", [128, 32768], BF16))
        sb["C"] = en_(nc.sbuf_tensor("bufC", [128, 16384], BF16))
        sb["D"] = en_(nc.sbuf_tensor("bufD", [128, 3072], F32))
        sb["E"] = en_(nc.sbuf_tensor("bufE", [128, 2048], F32))
        sb["F"] = en_(nc.sbuf_tensor("bufF", [128, 4096], BF16))
        sb["G"] = en_(nc.sbuf_tensor("bufG", [128, 512], F32))
        sb["H"] = en_(nc.sbuf_tensor("bufH", [128, 1024], BF16))
        sb["I"] = en_(nc.sbuf_tensor("bufI", [128, 128], F32))
        sb["X"] = en_(nc.sbuf_tensor("bufX", [128, 4096], BF16))
        ps = ctx.ps
        for i in range(6):
            ps[f"p{i}"] = en_(nc.psum_tensor(f"p{i}", [128, 512], F32))
        ps["tb0"] = en_(nc.psum_tensor("tb0", [128, 512], BF16))
        ps["tb1"] = en_(nc.psum_tensor("tb1", [128, 512], BF16))

        ctx.mode = "mock"
        for en in ENGINES:
            eng = MockEng(en)
            for k, ph in enumerate(phase_list):
                ph(ctx, en, eng)
                ctx.bar(eng, k)

        for name in sorted(ctx.counts):
            ctx.sems[name] = en_(nc.semaphore(name))
        ctx.mode = "real"
        ctx.counts = {}
        block = en_(nc.Block())

        def mk(en):
            def f(eng):
                for k, ph in enumerate(phase_list):
                    ph(ctx, en, eng)
                    ctx.bar(eng, k)
            return f
        block.sync(mk("sync"))
        block.gpsimd(mk("gpsimd"))
        block.tensor(mk("tensor"))
        block.vector(mk("vector"))
        block.scalar(mk("scalar"))
    return nc



def ln_stats_norm(ch, xt, xsq, st, xnf):
    s1, s2, mu, var, rec, rstd, nmr = (st[:, k:k + 1] for k in range(7))
    ch.step("scalar", lambda e: e.activation(xsq, xt, AF.Square))
    ch.step("vector", lambda e: e.reduce_sum(s1, xt, AX.X))
    ch.step("vector", lambda e: e.reduce_sum(s2, xsq, AX.X))
    ch.step("vector", lambda e: e.tensor_scalar(mu, s1, 1.0 / D, None, ALU.mult))
    ch.step("vector", lambda e: e.tensor_mul(var, mu, mu))
    ch.step("vector", lambda e: e.tensor_scalar(var, s2, 1.0 / D, var, ALU.mult, ALU.subtract))
    ch.step("vector", lambda e: e.tensor_scalar_add(var, var, EPS))
    ch.step("vector", lambda e: e.reciprocal(rec, var))
    ch.step("scalar", lambda e: e.activation(rstd, rec, AF.Sqrt))
    ch.step("scalar", lambda e: e.activation(nmr, mu, AF.Copy, scale=rstd))
    ch.step("vector", lambda e: e.tensor_scalar(xnf, xt, rstd, nmr, ALU.mult, ALU.subtract))


def ln_tiles(ctx, en, eng, key, src_ap, n_tiles, outs):
    sb = ctx.sb
    ch = Chain(ctx, en, eng, key)
    for i in range(n_tiles):
        xt = sb["D"].ap()[:, 0:1024]
        xsq = sb["E"].ap()[:, 0:1024]
        xnf = sb["E"].ap()[:, 1024:2048]
        xnb = sb["F"].ap()[:, 0:1024]
        xnl = sb["F"].ap()[:, 1024:2048]
        st = sb["G"].ap()[:, 0:16]
        s1, s2, mu, var, rec, rstd, nmr = (st[:, k:k + 1] for k in range(7))
        r0 = i * 128

        ch.step("sync", lambda e: e.dma_start(out=xt, in_=src_ap[r0:r0 + 128, :]), 16)
        ch.step("scalar", lambda e: e.activation(xsq, xt, AF.Square))
        ch.step("vector", lambda e: e.reduce_sum(s1, xt, AX.X))
        ch.step("vector", lambda e: e.reduce_sum(s2, xsq, AX.X))
        ch.step("vector", lambda e: e.tensor_scalar(mu, s1, 1.0 / D, None, ALU.mult))
        ch.step("vector", lambda e: e.tensor_mul(var, mu, mu))
        ch.step("vector", lambda e: e.tensor_scalar(var, s2, 1.0 / D, var, ALU.mult, ALU.subtract))
        ch.step("vector", lambda e: e.tensor_scalar_add(var, var, EPS))
        ch.step("vector", lambda e: e.reciprocal(rec, var))
        ch.step("scalar", lambda e: e.activation(rstd, rec, AF.Sqrt))
        ch.step("scalar", lambda e: e.activation(nmr, mu, AF.Copy, scale=rstd))
        ch.step("vector", lambda e: e.tensor_scalar(xnf, xt, rstd, nmr, ALU.mult, ALU.subtract))
        if "bf" in outs or "lo" in outs:
            ch.step("vector", lambda e: e.tensor_copy(xnb, xnf))
            ch.step("gpsimd", lambda e: e.dma_start(out=outs["bf"][r0:r0 + 128, :], in_=xnb), 16)
        if "lo" in outs:
            ch.step("vector", lambda e: e.tensor_sub(xnl, xnf, xnb))
            ch.step("gpsimd", lambda e: e.dma_start(out=outs["lo"][r0:r0 + 128, :], in_=xnl), 16)
        if "f32" in outs:
            ch.step("gpsimd", lambda e: e.dma_start(out=outs["f32"][r0:r0 + 128, :], in_=xnf), 16)
        if "x32" in outs:
            ch.step("gpsimd", lambda e: e.dma_start(out=outs["x32"][r0:r0 + 128, :], in_=xt), 16)
    ch.wait_all()


def p0_ln0(ctx, en, eng):
    x = ctx.t("x_in").rearrange("b s d -> (b s) d")
    ln_tiles(ctx, en, eng, "p0", x, NT,
             {"bf": ctx.t("xn0"), "lo": ctx.t("xn0lo")})


def p2_gate(ctx, en, eng):
    sb, ps, cs = ctx.sb, ctx.ps, ctx.consts
    ch = Chain(ctx, en, eng, "p2")
    xn0, xn0lo, comb = ctx.t("xn0"), ctx.t("xn0lo"), ctx.t("comb")
    gw = ctx.t("w_gate")
    E = sb["E"].ap()
    gw_f = E[0:8, 0:1024]
    gwT_f = E[:, 1024:1024 + 64]
    gwT_hi = sb["H"].ap()[:, 0:64]
    gwT_lo = sb["H"].ap()[:, 64:128]
    idf = sb["I"].ap()[:, 0:128]
    idb = sb["H"].ap()[:, 1024 - 128:1024]

    ch.step("sync", lambda e: e.dma_start(out=gw_f, in_=gw), 16)
    ch.step("sync", lambda e: e.dma_start(out=idf, in_=cs["ident_f"]), 16)
    ch.step("sync", lambda e: e.dma_start(out=idb, in_=cs["ident_b"]), 16)
    for c in range(DCH):
        cc = c
        ch.step("tensor", lambda e: e.transpose(
            ps["p0"].ap()[:, 0:8], gw_f[0:8, cc * 128:(cc + 1) * 128], idf[0:8, 0:8]))
        ch.step("vector", lambda e: e.tensor_copy(gwT_f[:, cc * 8:(cc + 1) * 8], ps["p0"].ap()[:, 0:8]))
    ch.step("vector", lambda e: e.tensor_copy(gwT_hi, gwT_f))
    ch.step("vector", lambda e: e.tensor_sub(gwT_lo, gwT_f, gwT_hi))

    A = sb["A"].ap()
    for i in range(T // 512):
        t0 = i * 512
        xhi = A[:, 0:8 * 512].rearrange("p (c t) -> p c t", c=8)
        xlo = A[:, 8 * 512:16 * 512].rearrange("p (c t) -> p c t", c=8)
        fns = []
        for c in range(DCH):
            fns.append(lambda e, cc=c: e.dma_start_transpose(
                out=xhi[:, cc, :], in_=xn0[t0:t0 + 512, cc * 128:(cc + 1) * 128]))
            fns.append(lambda e, cc=c: e.dma_start_transpose(
                out=xlo[:, cc, :], in_=xn0lo[t0:t0 + 512, cc * 128:(cc + 1) * 128]))
        ch.dma_group("sync", fns)
        for q in range(4):
            qq = q
            L = ps["p0"].ap()[:, 0:8]

            def mm_logits(e, q=qq):
                first = True
                for c in range(DCH):
                    for lhs, rhs in ((xhi, gwT_hi), (xhi, gwT_lo), (xlo, gwT_hi)):
                        ins = e.matmul(L, lhs[:, c, q * 128:(q + 1) * 128],
                                       rhs[:, c * 8:(c + 1) * 8],
                                       start=first, stop=(c == DCH - 1 and rhs is gwT_hi and lhs is xlo))
                        first = False
                return ins
            ch.step("tensor", mm_logits)
            G = sb["G"].ap()
            Lf = G[:, 32:40]
            nm1 = G[:, 40:41]
            nm2 = G[:, 41:42]
            eq = G[:, 42:50]
            l2 = G[:, 50:58]
            w = G[:, 58:66]
            ssum = G[:, 66:67]
            rs = G[:, 67:68]
            combf = G[:, 68:76]
            combb = sb["F"].ap()[0:128, 2048:2056]
            ch.step("vector", lambda e: e.tensor_copy(Lf, L))
            ch.step("vector", lambda e: e.reduce_max(nm1, Lf, AX.X, negate=True))
            ch.step("vector", lambda e: e.tensor_scalar(eq, Lf, nm1, 0.0, ALU.add, ALU.is_equal))
            ch.step("vector", lambda e: e.scalar_tensor_tensor(l2, eq, NEG, Lf, ALU.mult, ALU.add))
            ch.step("vector", lambda e: e.reduce_max(nm2, l2, AX.X, negate=True))
            ch.step("vector", lambda e: e.tensor_scalar(eq, Lf, nm2, 0.0, ALU.add, ALU.is_ge))
            ch.step("scalar", lambda e: e.activation(l2, Lf, AF.Exp, bias=nm1))
            ch.step("vector", lambda e: e.tensor_mul(w, l2, eq))
            ch.step("vector", lambda e: e.reduce_sum(ssum, w, AX.X))
            ch.step("vector", lambda e: e.reciprocal(rs, ssum))
            ch.step("vector", lambda e: e.tensor_scalar(combf, w, rs, None, ALU.mult))
            ch.step("vector", lambda e: e.tensor_copy(combb, combf))
            ch.step("tensor", lambda e: e.transpose(ps["tb0"].ap()[0:8, 0:128], combb, idb))
            ch.step("vector", lambda e: e.tensor_copy(sb["F"].ap()[0:8, 2056:2184], ps["tb0"].ap()[0:8, 0:128]))
            ch.step("gpsimd", lambda e: e.dma_start(
                out=comb[0:8, t0 + qq * 128:t0 + (qq + 1) * 128],
                in_=sb["F"].ap()[0:8, 2056:2184]), 16)
    ch.wait_all()


def load_wT(ch, ctx, w_dram, dest, stage, idb, tbp):
    ch.step("gpsimd", lambda e: e.dma_start(
        out=stage, in_=w_dram.rearrange("(oc p) d -> p oc d", p=128)), 16)
    for oc in range(8):
        for dc in range(8):
            o_, d_ = oc, dc
            ch.step("tensor", lambda e: e.transpose(
                tbp[:, 0:128], stage[:, o_, d_ * 128:(d_ + 1) * 128], idb))
            ch.step("vector", lambda e: e.tensor_copy(
                dest[:, d_, o_ * 128:(o_ + 1) * 128], tbp[:, 0:128]))


def attn_phase(ctx, en, eng, key, xn_dram, wq, wk, wv, wo, bq, bk, bv, bo,
               out_dram, window, n_b=B_LOC, n_h=NH):
    sb, ps, cs = ctx.sb, ctx.ps, ctx.consts
    A, B, C, Dd, Ef, F, G, H, I, X = (sb[k].ap() for k in
                                      ("A", "B", "C", "D", "E", "F", "G", "H", "I", "X"))
    op, sems, waitall = mk_ops(ctx, en, eng, key)
    PE, VE, SC, LD, GP = (sems[k] for k in ("tensor", "vector", "scalar", "sync", "gpsimd"))
    Qf = A[:, 0:16384].rearrange("p (c t) -> p c t", c=8)
    Kf = A[:, 16384:32768].rearrange("p (c t) -> p c t", c=8)
    Vt = B[:, 0:16384].rearrange("p (kt o) -> p kt o", kt=16)
    Cx = B[:, 16384:32768].rearrange("p (c t) -> p c t", c=8)
    Xt = X[:, 0:4096].rearrange("p (c t) -> p c t", c=8)
    wT0 = C[:, 0:8192].rearrange("p (c o) -> p c o", c=8)
    wT1 = C[:, 8192:16384].rearrange("p (c o) -> p c o", c=8)
    stage0 = B[:, 0:8192].rearrange("p (oc d) -> p oc d", oc=8)
    stageV = B[:, 16384:24576].rearrange("p (oc d) -> p oc d", oc=8)
    stageO = A[:, 0:8192].rearrange("p (oc d) -> p oc d", oc=8)
    idb = H[:, 896:1024]
    maskc = I[:, 0:128]
    maskf = Dd[:, 2560:2688]
    bcols = G[:, 456:488].rearrange("p (w c) -> p w c", w=4)
    qcol = G[:, 448:456]
    tb = [ps["tb0"].ap(), ps["tb1"].ap()]
    has_bias = bq is not None
    K_ = key
    nW = window // 128 if window is not None else None

    def kt_lo_of(qi):
        return 0 if nW is None else max(0, qi - nW)

    def nkt_of(gi):
        qi = gi % 16
        return qi + 1 - kt_lo_of(qi)

    def pt_eng(j):
        return VE if j % 2 == 0 else SC

    def last_pt_deps(b, gi):
        nk = nkt_of(gi)
        d = [(f"{K_}pt{b}_{gi}_{nk-1}", pt_eng(nk - 1))]
        if nk >= 2:
            d.append((f"{K_}pt{b}_{gi}_{nk-2}", pt_eng(nk - 2)))
        return d

    tbdep = [[], []]

    LD2, GP2, GPW = sems["sync2"], sems["gpsimd2"], sems["gpsimdw"]
    op("sync2", lambda e: e.dma_start(out=idb, in_=cs["ident_b"]))
    op("sync2", lambda e: e.dma_start(out=maskc, in_=cs["mask_causal"]))
    if window is not None:
        op("sync2", lambda e: e.dma_start(out=maskf, in_=cs["mask_far"]))
    if has_bias:
        brows = ctx.t("b5rows")
        for w_i, bt in enumerate((bq, bk, bv, bo)):
            wi, btt = w_i, bt
            op("gpsimd", lambda e, wi=wi, btt=btt: e.dma_start(
                out=brows[wi:wi + 1, :], in_=btt[None, :]), mark=f"{K_}brow{wi}")
        op("sync2", lambda e: e.dma_start_transpose(
            out=H[:, 512:544], in_=brows.rearrange("w (c p) -> (w c) p", p=128)),
            deps=[(f"{K_}brow3", GP)])
        op("vector", lambda e: e.tensor_copy(G[:, 456:488], H[:, 512:544]),
           deps=[(f"{K_}cst", sems["sync2"])], mark=f"{K_}bcf")
    op("sync2", lambda e: e.nop(), mark=f"{K_}cst") if False else None
    if en == "sync":
        ctx.mark(f"{K_}cst")
    if has_bias:
        op("vector", lambda e: e.tensor_scalar(qcol, bcols[:, 0, :], 0.125, None, ALU.mult),
           deps=[(f"{K_}bcf", sems["vector"])], mark=f"{K_}qcol")

    last_stage = [None]

    def load_w(b, wname, w_dram, dest, stage, stage_dep):
        mk = f"{K_}w{b}{wname}"
        sdep = list(stage_dep)
        if last_stage[0]:
            sdep.append((last_stage[0] + "st", GPW))
            sdep.append((last_stage[0] + "tr63", PE))
        op("gpsimdw", lambda e: e.dma_start(
            out=stage, in_=w_dram.rearrange("(oc p) d -> p oc d", p=128)),
            deps=sdep, mark=mk + "st")
        last_stage[0] = mk
        for i in range(64):
            o_, d_ = i // 8, i % 8
            ii = i
            if i >= 2:
                deps = [(mk + f"cp{ii-2}", VE)]
            else:
                deps = [(mk + "st", GPW), (f"{K_}cst", LD2)] + tbdep[i % 2]
            op("tensor", lambda e, o_=o_, d_=d_, ii=ii: e.transpose(
                tb[ii % 2][:, 0:128], stage[:, o_, d_ * 128:(d_ + 1) * 128], idb),
                deps=deps, mark=mk + f"tr{ii}")
            op("vector", lambda e, o_=o_, d_=d_, ii=ii: e.tensor_copy(
                dest[:, d_, o_ * 128:(o_ + 1) * 128], tb[ii % 2][:, 0:128]),
                deps=[(mk + f"tr{ii}", PE)], mark=mk + f"cp{ii}")
        tbdep[0] = [(mk + "cp62", VE)]
        tbdep[1] = [(mk + "cp63", VE)]
        return mk + "cp63"

    for b in range(n_b):
        tbse = b * S
        dep_prevav = [(f"{K_}av{b-1}_{n_h-1}_15", PE)] if b > 0 else []
        wq_done = load_w(b, "q", wq, wT0, stage0, dep_prevav)
        wk_done = load_w(b, "k", wk, wT1, stage0, [])

        for tt in range(4):
            t0 = tt * 512
            if tt > 0:
                xdeps = [(f"{K_}qkm{b}_{tt-1}_15", PE)]
            elif b > 0:
                xdeps = [(f"{K_}av{b-1}_{n_h-1}_15", PE)]
            else:
                xdeps = []
            for c in range(DCH):
                cc = c
                op("sync", lambda e, cc=cc, t0=t0: e.dma_start_transpose(
                    out=Xt[:, cc, :], in_=xn_dram[tbse + t0:tbse + t0 + 512,
                                                  cc * 128:(cc + 1) * 128]),
                    deps=xdeps if cc == 0 else [],
                    mark=f"{K_}qkx{b}_{tt}" if cc == DCH - 1 else None)
            for g in range(16):
                wh, o_ = (0, g) if g < 8 else (1, g - 8)
                wT = wT0 if wh == 0 else wT1
                pp = ps["p0"].ap() if g % 2 == 0 else ps["p1"].ap()
                gg = g
                deps = [(f"{K_}qkx{b}_{tt}", LD)]
                if g == 0 and tt == 0:
                    deps.append((wk_done, VE))
                    if b > 0:
                        deps.append((f"{K_}cx{b-1}_{n_h*16-1}", SC))
                if g >= 2:
                    deps.append((f"{K_}qke{b}_{tt}_{gg-2}", SC))
                elif tt > 0:
                    deps.append((f"{K_}qke{b}_{tt-1}_{gg+14}", SC))

                def mmqk(e, wT=wT, o_=o_, pp=pp):
                    for dc in range(DCH):
                        ins = e.matmul(pp[:], wT[:, dc, o_ * 128:(o_ + 1) * 128],
                                       Xt[:, dc, :], start=(dc == 0), stop=(dc == DCH - 1))
                    return ins
                op("tensor", mmqk, deps=deps, mark=f"{K_}qkm{b}_{tt}_{gg}")
                dst = Qf if wh == 0 else Kf
                sc_ = 0.125 if wh == 0 else 1.0
                if has_bias:
                    bias_ap = qcol[:, o_:o_ + 1] if wh == 0 else bcols[:, 1, o_:o_ + 1]
                else:
                    bias_ap = 0.0
                edeps = [(f"{K_}qkm{b}_{tt}_{gg}", PE)]
                if has_bias and b == 0 and tt == 0 and gg == 0:
                    edeps.append((f"{K_}qcol", VE))
                    edeps.append((f"{K_}cst", LD2))
                op("scalar", lambda e, dst=dst, o_=o_, t0=t0, pp=pp, sc_=sc_, bias_ap=bias_ap:
                    e.activation(dst[:, o_, t0:t0 + 512], pp[:], AF.Identity, bias=bias_ap, scale=sc_),
                    deps=edeps, mark=f"{K_}qke{b}_{tt}_{gg}")

        wv_done = load_w(b, "v", wv, wT0, stageV, [(f"{K_}qkm{b}_3_7", PE)])
        for tt in range(4):
            t0 = tt * 512
            xdeps = [(f"{K_}qkm{b}_3_15" if tt == 0 else f"{K_}vm{b}_{tt-1}_7", PE)]
            for c in range(DCH):
                cc = c
                op("sync", lambda e, cc=cc, t0=t0: e.dma_start_transpose(
                    out=Xt[:, cc, :], in_=xn_dram[tbse + t0:tbse + t0 + 512,
                                                  cc * 128:(cc + 1) * 128]),
                    deps=xdeps if cc == 0 else [],
                    mark=f"{K_}vx{b}_{tt}" if cc == DCH - 1 else None)
            for g in range(8):
                qq, og = g // 2, g % 2
                pp = ps["p0"].ap() if g % 2 == 0 else ps["p1"].ap()
                gg = g
                deps = [(f"{K_}vx{b}_{tt}", LD)]
                if g == 0 and tt == 0:
                    deps.append((wv_done, VE))
                if g < 2 and tt == 0:
                    deps.append((f"{K_}qke{b}_3_{14 + gg}", SC))
                if g >= 2:
                    deps.append((f"{K_}ve{b}_{tt}_{gg-2}", VE))
                elif tt > 0:
                    deps.append((f"{K_}ve{b}_{tt-1}_{gg+6}", VE))

                def mmv(e, qq=qq, og=og, pp=pp):
                    for dc in range(DCH):
                        ins = e.matmul(pp[:], Xt[:, dc, qq * 128:(qq + 1) * 128],
                                       wT0[:, dc, og * 512:(og + 1) * 512],
                                       start=(dc == 0), stop=(dc == DCH - 1))
                    return ins
                op("tensor", mmv, deps=deps, mark=f"{K_}vm{b}_{tt}_{gg}")
                kt = tt * 4 + qq
                op("vector", lambda e, kt=kt, og=og, pp=pp: e.tensor_copy(
                    Vt[:, kt, og * 512:(og + 1) * 512], pp[:]),
                    deps=[(f"{K_}vm{b}_{tt}_{gg}", PE)], mark=f"{K_}ve{b}_{tt}_{gg}")

        rowbs = [Dd[:, 0:2048], Ef[:, 0:2048], C[:, 8192:16384].bitcast(F32)]
        Pbs = [F[:, 0:2048], X[:, 0:2048], C[:, 0:2048]]
        Pts = [F[:, 2048:4096], X[:, 2048:4096], C[:, 2048:4096]]

        def row_env(gi):
            h, qi = gi // 16, gi % 16
            sl = gi % 3
            psA = ps[f"p{2*sl}"].ap()
            psB = ps[f"p{2*sl+1}"].ap()
            st8 = G[:, sl * 8:sl * 8 + 8]
            return dict(h=h, qi=qi, sl=sl, hc=h // 2, hr=(h % 2) * 64,
                        rowb=rowbs[sl], Pb=Pbs[sl], Pt=Pts[sl], psA=psA, psB=psB,
                        tbA=psA.bitcast(BF16), tbB=psB.bitcast(BF16),
                        pav=psA[0:64, 0:128],
                        negm=st8[:, 0:1], rsum=st8[:, 1:2], rinv=st8[:, 2:3],
                        kt_lo=kt_lo_of(qi), L0=kt_lo_of(qi) * 128,
                        L=(qi + 1) * 128 - kt_lo_of(qi) * 128)

        def emit_scores(gi):
            v = row_env(gi)
            h, qi, sl = v["h"], v["qi"], v["sl"]
            qs = Qf[v["hr"]:v["hr"] + 64, v["hc"], :]
            ks = Kf[v["hr"]:v["hr"] + 64, v["hc"], :]
            far = nW is not None and qi - v["kt_lo"] == nW
            L0, L = v["L0"], v["L"]
            pgi = gi - 3
            ng = (L + 511) // 512
            for kg in range(ng):
                k0 = L0 + kg * 512
                n = min(512, (qi + 1) * 128 - k0)
                pp = v["psA"] if kg % 2 == 0 else v["psB"]
                deps = []
                if kg >= 2:
                    pe_kg = kg - 2
                    ev_eng = SC if (pe_kg % 2 == 1 and pe_kg != ng - 1) else VE
                    deps.append((f"{K_}sev{b}_{gi}_{pe_kg}", ev_eng))
                elif pgi >= 0:
                    deps += last_pt_deps(b, pgi)
                    deps.append((f"{K_}cx{b}_{pgi}", SC))
                if gi == 0:
                    deps.append((f"{K_}qke{b}_3_15", SC))
                    deps.append((f"{K_}ve{b}_3_7", VE))
                    if b == 0:
                        deps.append((f"{K_}cst", LD2))
                if gi == 1 and b > 0:
                    deps.append((f"{K_}ops{b-1}_31", GP2))
                op("tensor", lambda e, k0=k0, n=n, pp=pp, qs=qs, ks=ks, qi=qi:
                    e.matmul(pp[:, 0:n], qs[:, qi * 128:(qi + 1) * 128],
                             ks[:, k0:k0 + n], start=True, stop=True),
                    deps=deps, mark=f"{K_}ss{b}_{gi}_{kg}")
                off0 = k0 - L0
                segs = [[0, n, None]]
                if kg == ng - 1:
                    segs = ([[0, n - 128, None]] if n > 128 else []) + [[n - 128, 128, "c"]]
                if far and kg == 0:
                    if segs[0][2] is None:
                        hn = segs[0][1]
                        rest = ([[128, hn - 128, None]] if hn > 128 else []) + segs[1:]
                    else:
                        rest = []
                    segs = [[0, 128, "f"]] + rest
                segs = [sg for sg in segs if sg[1] > 0]
                use_sc = (kg % 2 == 1) and (kg != ng - 1)
                for si, (o, nn, mk) in enumerate(segs):
                    oo, nn_, mk_ = o, nn, mk
                    deps = [(f"{K_}ss{b}_{gi}_{kg}", PE)]
                    if kg == 0 and si == 0 and pgi >= 0:
                        deps.append((f"{K_}xp{b}_{pgi}", SC))
                    mkname = f"{K_}sev{b}_{gi}_{kg}" if si == len(segs) - 1 else None
                    rowb = v["rowb"]
                    if mk_ is None and use_sc:
                        op("scalar", lambda e, pp=pp, oo=oo, nn_=nn_, off0=off0, rowb=rowb:
                            e.copy(rowb[:, off0 + oo:off0 + oo + nn_], pp[:, oo:oo + nn_]),
                            deps=deps, mark=mkname)
                    elif mk_ is None:
                        op("vector", lambda e, pp=pp, oo=oo, nn_=nn_, off0=off0, rowb=rowb:
                            e.tensor_copy(rowb[:, off0 + oo:off0 + oo + nn_],
                                          pp[:, oo:oo + nn_]),
                            deps=deps, mark=mkname)
                    else:
                        msk = maskc if mk_ == "c" else maskf
                        op("vector", lambda e, pp=pp, oo=oo, nn_=nn_, off0=off0, msk=msk, rowb=rowb:
                            e.tensor_add(rowb[:, off0 + oo:off0 + oo + nn_],
                                         pp[:, oo:oo + nn_], msk),
                            deps=deps, mark=mkname)
            return ng

        def emit_softmax(gi, ng):
            v = row_env(gi)
            pgi = gi - 3
            rowb, Pb, L = v["rowb"], v["Pb"], v["L"]
            negm, rsum, rinv = v["negm"], v["rsum"], v["rinv"]
            mdeps = [(f"{K_}sev{b}_{gi}_{ng-1}", VE)]
            sc_kgs = [kg for kg in range(ng) if kg % 2 == 1 and kg != ng - 1]
            if sc_kgs:
                mdeps.append((f"{K_}sev{b}_{gi}_{sc_kgs[-1]}", SC))
            op("vector", lambda e, rowb=rowb, L=L, negm=negm:
                e.reduce_max(negm, rowb[:, 0:L], AX.X, negate=True),
                deps=mdeps, mark=f"{K_}rw{b}_{gi}")
            deps = [(f"{K_}rw{b}_{gi}", VE)]
            if pgi >= 0:
                deps.append((f"{K_}tp{b}_{pgi}", PE))
            op("scalar", lambda e, Pb=Pb, rowb=rowb, L=L, negm=negm, rsum=rsum:
                e.activation(Pb[:, 0:L], rowb[:, 0:L], AF.Exp, bias=negm, accum_out=rsum),
                deps=deps, mark=f"{K_}xp{b}_{gi}")
            op("vector", lambda e, rinv=rinv, rsum=rsum: e.reciprocal(rinv, rsum),
                deps=[(f"{K_}xp{b}_{gi}", SC)], mark=f"{K_}rc{b}_{gi}")
            op("gpsimdc", lambda e, Pb=Pb, L=L, rinv=rinv: e.tensor_scalar(
                Pb[:, 0:L], Pb[:, 0:L], rinv, None, ALU.mult),
                deps=[(f"{K_}rc{b}_{gi}", VE)], mark=f"{K_}pn{b}_{gi}")

        def emit_transposes(gi):
            v = row_env(gi)
            Pb, Pt = v["Pb"], v["Pt"]
            nkt = nkt_of(gi)
            for j in range(nkt):
                jj = j
                tpp = (v["tbA"] if j % 2 == 0 else v["tbB"])[:, 0:128]
                if j >= 2:
                    deps = [(f"{K_}pt{b}_{gi}_{jj-2}", pt_eng(jj - 2))]
                else:
                    deps = [(f"{K_}pn{b}_{gi}", sems["gpsimdc"])]
                op("tensor", lambda e, jj=jj, tpp=tpp, Pb=Pb: e.transpose(
                    tpp, Pb[:, jj * 128:(jj + 1) * 128], idb),
                    deps=deps, mark=f"{K_}tr{b}_{gi}_{jj}")
                if en == "tensor" and jj == nkt - 1:
                    ctx.mark(f"{K_}tp{b}_{gi}")
                if jj % 2 == 0:
                    op("vector", lambda e, tpp=tpp, jj=jj, Pt=Pt: e.tensor_copy(
                        Pt[:, jj * 128:(jj + 1) * 128], tpp),
                        deps=[(f"{K_}tr{b}_{gi}_{jj}", PE)], mark=f"{K_}pt{b}_{gi}_{jj}")
                else:
                    op("scalar", lambda e, tpp=tpp, jj=jj, Pt=Pt: e.copy(
                        Pt[:, jj * 128:(jj + 1) * 128], tpp),
                        deps=[(f"{K_}tr{b}_{gi}_{jj}", PE)], mark=f"{K_}pt{b}_{gi}_{jj}")

        def emit_av(gi):
            v = row_env(gi)
            h, qi = v["h"], v["qi"]
            Pt, pav, kt_lo = v["Pt"], v["pav"], v["kt_lo"]
            nkt = nkt_of(gi)
            deps = last_pt_deps(b, gi)

            def mm_av(e, nkt=nkt, kt_lo=kt_lo, h=h, pav=pav, Pt=Pt):
                for j in range(nkt):
                    ins = e.matmul(pav, Vt[:, kt_lo + j, h * 64:(h + 1) * 64],
                                   Pt[:, j * 128:(j + 1) * 128],
                                   start=(j == 0), stop=(j == nkt - 1))
                return ins
            op("tensor", mm_av, deps=deps, mark=f"{K_}av{b}_{h}_{qi}")

        def emit_cx(gi):
            v = row_env(gi)
            h, qi = v["h"], v["qi"]
            pav = v["pav"]
            bias_ap = (bcols[(h % 2) * 64:(h % 2) * 64 + 64, 2, v["hc"]:v["hc"] + 1]
                       if has_bias else 0.0)
            op("scalar", lambda e, hc=v["hc"], hr=v["hr"], qi=qi, pav=pav, bias_ap=bias_ap:
                e.activation(Cx[hr:hr + 64, hc, qi * 128:(qi + 1) * 128], pav,
                             AF.Identity, bias=bias_ap),
                deps=[(f"{K_}av{b}_{h}_{qi}", PE)], mark=f"{K_}cx{b}_{gi}")

        nrows = n_h * 16
        for i2 in range(nrows + 3):
            if i2 >= 3 and i2 - 3 < nrows:
                emit_cx(i2 - 3)
            if i2 < nrows:
                ng = emit_scores(i2)
            if i2 >= 2 and i2 - 2 < nrows:
                emit_transposes(i2 - 2)
            if i2 < nrows:
                emit_softmax(i2, ng)
            if i2 >= 2 and i2 - 2 < nrows:
                emit_av(i2 - 2)

        gl = n_h * 16 - 1
        wo_done = load_w(b, "o", wo, wT0, stageO, [(f"{K_}av{b}_{n_h-1}_15", PE)])
        for g in range(32):
            tt, o_ = g // 8, g % 8
            t0 = tt * 512
            pp = ps["p0"].ap() if g % 2 == 0 else ps["p1"].ap()
            gg = g
            deps = []
            if g == 0:
                deps.append((wo_done, VE))
                deps.append((f"{K_}cx{b}_{gl}", SC))
            if g >= 2:
                deps.append((f"{K_}ope{b}_{gg-2}", VE))

            def mmo(e, o_=o_, pp=pp, t0=t0):
                for dc in range(DCH):
                    ins = e.matmul(pp[:], wT0[:, dc, o_ * 128:(o_ + 1) * 128],
                                   Cx[:, dc, t0:t0 + 512], start=(dc == 0), stop=(dc == DCH - 1))
                return ins
            op("tensor", mmo, deps=deps, mark=f"{K_}opm{b}_{gg}")
            ev = Dd[:, 2048:2560] if g % 2 == 0 else Ef[:, 0:512]
            deps = [(f"{K_}opm{b}_{gg}", PE)]
            if g >= 2:
                deps.append((f"{K_}ops{b}_{gg-2}", GP if (gg - 2) % 2 == 0 else GP2))
            elif b > 0:
                deps.append((f"{K_}ops{b-1}_{30 + gg}", GP if gg % 2 == 0 else GP2))
            if has_bias:
                op("vector", lambda e, ev=ev, pp=pp, o_=o_: e.tensor_scalar(
                    ev, pp[:], bcols[:, 3, o_:o_ + 1], None, ALU.add),
                    deps=deps, mark=f"{K_}ope{b}_{gg}")
            else:
                op("vector", lambda e, ev=ev, pp=pp: e.tensor_copy(ev, pp[:]),
                    deps=deps, mark=f"{K_}ope{b}_{gg}")
            kwargs = {"accum_op": ALU.add} if window is None else {}
            op("gpsimd" if g % 2 == 0 else "gpsimd2",
               lambda e, ev=ev, o_=o_, t0=t0, kwargs=kwargs: e.dma_start(
                out=out_dram[o_ * 128:(o_ + 1) * 128, tbse + t0:tbse + t0 + 512],
                in_=ev, **kwargs),
                deps=[(f"{K_}ope{b}_{gg}", VE)], mark=f"{K_}ops{b}_{gg}")
    waitall()


def p3z_zero_acc(ctx, en, eng):
    sb = ctx.sb
    ch = Chain(ctx, en, eng, "p3z")
    accf = ctx.t("accf")
    z = sb["D"].ap()[:, 0:2048]
    ch.step("vector", lambda e: e.memset(z, 0.0))
    for dc in range(DCH):
        for half in range(T // 2048):
            d0, h0 = dc * 128, half * 2048
            ch.step("sync", lambda e: e.dma_start(
                out=accf[d0:d0 + 128, h0:h0 + 2048], in_=z), 16)
    ch.wait_all()


def p3_moe(ctx, en, eng, n_tt=NMT):
    sb, ps, cs = ctx.sb, ctx.ps, ctx.consts
    A, B, C, Dd, F, X = (sb[k].ap() for k in ("A", "B", "C", "D", "F", "X"))
    xn0, comb, accf = ctx.t("xn0"), ctx.t("comb"), ctx.t("accf")
    w1 = ctx.t("w_moe1")
    w2 = ctx.t("w_moe2")
    sw1 = ctx.t("w_sw1")
    sw2 = ctx.t("w_sw2")
    b2 = ctx.t("b_moe2")
    sb2 = ctx.t("b_ssb2")
    b1b = ctx.t("b1b")
    b1 = ctx.t("b_moe1")
    sb1 = ctx.t("b_ssb1")

    onehot = F[0:8, 0:1024]
    Be = F[:, 1024:1536]
    combst = F[0:8, 1536:2048]
    b2r = F[0:8, 2048:3072]
    sb2r = F[0:1, 3072:4096]
    onesr = sb["H"].ap()[0:1, 0:512]
    b1col = F[:, 3584:3616]

    ch = Chain(ctx, en, eng, "p3p")
    ch.step("gpsimd", lambda e: e.dma_start(out=b1b[0:8, :], in_=b1), 16)
    ch.step("gpsimd", lambda e: e.dma_start(out=b1b[8:9, :], in_=sb1[None, :]), 16)
    ch.step("gpsimd", lambda e: e.dma_start(out=b2r, in_=b2), 16)
    ch.step("gpsimd", lambda e: e.dma_start(out=sb2r, in_=sb2[None, :]), 16)
    ch.step("sync", lambda e: e.dma_start(out=onehot, in_=cs["onehot8"]), 16)
    ch.step("sync", lambda e: e.dma_start(out=onesr, in_=cs["ones_row_b"]), 16)
    ch.wait_all()

    Aw = A[:, 0:8 * 4096].rearrange("p (c f) -> p c f", c=8)
    Bw = B[:, 0:32 * 1024].rearrange("p (c o) -> p c o", c=32)
    Hb = C[:, 0:32 * 512].rearrange("p (c t) -> p c t", c=32)
    Xt = X[:, 0:8 * 512].rearrange("p (c t) -> p c t", c=8)

    for m in range(9):
        shared = (m == 0)
        e_id = m - 1
        if en == "gpsimd":
            if m > 0:
                ctx.wait(eng, "mpe", ctx.at(f"m_e{m-1}_end", "mpe"))
            src1 = sw1 if shared else w1[e_id]
            src2 = sw2 if shared else w2[e_id]
            ctx.inc(eng.dma_start(out=Aw, in_=src1.rearrange("(c p) f -> p c f", p=128)), "mgw", 16)
            ctx.inc(eng.dma_start(out=Bw, in_=src2.rearrange("(c p) o -> p c o", p=128)), "mgw", 16)
            ctx.mark(f"m_w_{m}")
        if en == "sync":
            if m > 0:
                ctx.wait(eng, "msc", ctx.at(f"m_gelu_end_{m-1}", "msc"))
            ctx.inc(eng.dma_start_transpose(
                out=b1col, in_=b1b[8 if shared else e_id].rearrange("(c p) -> c p", p=128)), "mlb", 16)
            ctx.mark(f"m_b1_{m}")

        for it in range(n_tt):
            t0 = it * MTT
            p1a = ps["p0"].ap()
            p1b = ps["p1"].ap()
            p2a = ps["p2"].ap()
            p2b = ps["p3"].ap()
            pbc = ps["p4"].ap()

            if en == "sync":
                if it > 0:
                    ctx.wait(eng, "mpe", ctx.at(f"m_s1_{m}_{it-1}_31", "mpe"))
                elif m > 0:
                    ctx.wait(eng, "mpe", ctx.at(f"m_e{m-1}_end", "mpe"))
                for c in range(DCH):
                    ctx.inc(eng.dma_start_transpose(
                        out=Xt[:, c, :], in_=xn0[t0:t0 + MTT, c * 128:(c + 1) * 128]), "mlx", 16)
                ctx.mark(f"m_x_{m}_{it}")

            if not shared:
                if en == "gpsimd":
                    if m == 8 and it > 0:
                        ctx.wait(eng, "mpe", ctx.at(f"m_s2_{8}_{it-1}_{7}", "mpe"))
                    elif not (m == 1 and it == 0):
                        ctx.wait(eng, "mpe", ctx.at(f"m_bc_prev_{m}_{it}", "mpe"))
                    ctx.inc(eng.dma_start(out=combst, in_=comb[0:8, t0:t0 + MTT]), "mgc", 16)
                    ctx.mark(f"m_compst_{m}_{it}")
                if en == "tensor":
                    ctx.wait(eng, "mgc", ctx.at(f"m_compst_{m}_{it}", "mgc"))
                    if not (m == 1 and it == 0):
                        ctx.wait(eng, "mev", ctx.at(f"m_bee_prev_{m}_{it}", "mev"))
                    ctx.inc(eng.matmul(pbc[:], onehot[:, e_id * 128:(e_id + 1) * 128],
                                       combst, start=True, stop=True), "mpe", 1)
                    ctx.mark(f"m_bc_{m}_{it}")
                    ctx.mark(f"m_bc_prev_{m}_{it + 1}" if it + 1 < n_tt else f"m_bc_prev_{m + 1}_0")
                if en == "vector":
                    ctx.wait(eng, "mpe", ctx.at(f"m_bc_{m}_{it}", "mpe"))
                    ctx.inc(eng.tensor_copy(Be, pbc[:]), "mev", 1)
                    ctx.mark(f"m_bee_{m}_{it}")
                    ctx.mark(f"m_bee_prev_{m}_{it + 1}" if it + 1 < n_tt else f"m_bee_prev_{m + 1}_0")

            for fc in range(NFC):
                pp = p1a if fc % 2 == 0 else p1b
                if en == "tensor":
                    ctx.wait(eng, "mlx", ctx.at(f"m_x_{m}_{it}", "mlx"))
                    if fc == 0:
                        ctx.wait(eng, "mgw", ctx.at(f"m_w_{m}", "mgw"))
                    if fc >= 2:
                        ctx.wait(eng, "msc", ctx.at(f"m_g_{m}_{it}_{fc-2}", "msc"))
                    elif it > 0:
                        ctx.wait(eng, "msc", ctx.at(f"m_g_{m}_{it-1}_{31 if fc == 0 else 31}", "msc") - (1 - fc))
                    elif m > 0:
                        ctx.wait(eng, "msc", ctx.at(f"m_gelu_end_{m-1}", "msc"))
                    for dc in range(DCH):
                        ins = eng.matmul(pp[:], Aw[:, dc, fc * 128:(fc + 1) * 128],
                                         Xt[:, dc, :], start=(dc == 0), stop=(dc == DCH - 1))
                    ctx.inc(ins, "mpe", 1)
                    ctx.mark(f"m_s1_{m}_{it}_{fc}")
                if en == "scalar":
                    ctx.wait(eng, "mpe", ctx.at(f"m_s1_{m}_{it}_{fc}", "mpe"))
                    ctx.wait(eng, "mlb", ctx.at(f"m_b1_{m}", "mlb"))
                    ctx.inc(eng.activation(Hb[:, fc, :], (p1a if fc % 2 == 0 else p1b)[:],
                                           AF.Gelu, bias=b1col[:, fc:fc + 1]), "msc", 1)
                    ctx.mark(f"m_g_{m}_{it}_{fc}")
                if not shared and en == "vector":
                    ctx.wait(eng, "msc", ctx.at(f"m_g_{m}_{it}_{fc}", "msc"))
                    ctx.wait(eng, "mev", ctx.at(f"m_bee_{m}_{it}", "mev"))
                    ctx.inc(eng.tensor_mul(Hb[:, fc, :], Hb[:, fc, :], Be), "mev", 1)
                    ctx.mark(f"m_sc_{m}_{it}_{fc}")
            if en in ("scalar",) and it == n_tt - 1:
                ctx.mark(f"m_gelu_end_{m}")

            for oc in range(8):
                pp = p2a if oc % 2 == 0 else p2b
                if en == "tensor":
                    if not shared:
                        ctx.wait(eng, "mev", ctx.at(f"m_sc_{m}_{it}_{31}", "mev"))
                    else:
                        ctx.wait(eng, "msc", ctx.at(f"m_g_{m}_{it}_{31}", "msc"))
                    if oc >= 2:
                        ctx.wait(eng, "mev", ctx.at(f"m_ev_{m}_{it}_{oc-2}", "mev"))
                    elif it > 0:
                        ctx.wait(eng, "mev", ctx.at(f"m_ev_{m}_{it-1}_{6 + oc}", "mev"))
                    elif m > 0:
                        ctx.wait(eng, "mev", ctx.at(f"m_ev_{m-1}_{n_tt-1}_{6 + oc}", "mev"))
                    bias_here = (m == 8)
                    for fc in range(NFC):
                        last = (fc == NFC - 1) and not bias_here
                        ins = eng.matmul(pp[:], Bw[:, fc, oc * 128:(oc + 1) * 128],
                                         Hb[:, fc, :], start=(fc == 0), stop=last)
                    if bias_here:
                        ins = eng.matmul(pp[:], b2r[:, oc * 128:(oc + 1) * 128], combst,
                                         start=False, stop=False)
                        ins = eng.matmul(pp[:], sb2r[:, oc * 128:(oc + 1) * 128], onesr,
                                         start=False, stop=True)
                    ctx.inc(ins, "mpe", 1)
                    ctx.mark(f"m_s2_{m}_{it}_{oc}")
                if en == "vector":
                    ctx.wait(eng, "mpe", ctx.at(f"m_s2_{m}_{it}_{oc}", "mpe"))
                    dsl = Dd[:, (oc % 2) * 512:(oc % 2) * 512 + 512]
                    par = f"mst{oc % 2}"
                    if not (m == 0 and it == 0 and oc < 2):
                        key = (f"m_st_{m}_{it}_{oc-2}" if oc >= 2 else
                               (f"m_st_{m}_{it-1}_{6+oc}" if it > 0 else
                                f"m_st_{m-1}_{n_tt-1}_{6+oc}"))
                        ctx.wait(eng, par, ctx.at(key, par))
                    ctx.inc(eng.tensor_copy(dsl, pp[:]), "mev", 1)
                    ctx.mark(f"m_ev_{m}_{it}_{oc}")
                if en == "gpsimd":
                    ctx.wait(eng, "mev", ctx.at(f"m_ev_{m}_{it}_{oc}", "mev"))
                    dsl = Dd[:, (oc % 2) * 512:(oc % 2) * 512 + 512]
                    ctx.inc(eng.dma_start(out=accf[oc * 128:(oc + 1) * 128, t0:t0 + MTT],
                                          in_=dsl, accum_op=ALU.add), f"mst{oc % 2}", 16)
                    ctx.mark(f"m_st_{m}_{it}_{oc}")
        if en == "tensor":
            ctx.mark(f"m_e{m}_end")
    ctx.wait(eng, "mst0", ctx.at(f"m_st_{8}_{n_tt-1}_{6}", "mst0"))
    ctx.wait(eng, "mst1", ctx.at(f"m_st_{8}_{n_tt-1}_{7}", "mst1"))


def p4_ln1(ctx, en, eng):
    sb, ps, cs = ctx.sb, ctx.ps, ctx.consts
    Dd, Ef, F, G, I = (sb[k].ap() for k in ("D", "E", "F", "G", "I"))
    ch = Chain(ctx, en, eng, "p4")
    x = ctx.t("x_in").rearrange("b s d -> (b s) d")
    accf, x1ag, xn1 = ctx.t("accf"), ctx.t("x1ag"), ctx.t("xn1")
    idf = I[:, 0:128]
    ch.step("sync", lambda e: e.dma_start(out=idf, in_=cs["ident_f"]), 16)
    for i in range(NT):
        r0 = i * 128
        xt = Dd[:, 0:1024]
        x1t = Dd[:, 1024:2048]
        xsq = Ef[:, 1024:2048]
        xnb = F[:, 0:1024]
        st = G[:, 0:16]
        ch.dma_group("sync", [lambda e: e.dma_start(out=xt, in_=x[r0:r0 + 128, :])] + [
            (lambda e, d_=dc: e.dma_start(
                out=Ef[:, d_ * 128:(d_ + 1) * 128],
                in_=accf[d_ * 128:(d_ + 1) * 128, r0:r0 + 128]))
            for dc in range(DCH)])
        for dc in range(DCH):
            d_ = dc
            pb2 = (ps["p2"] if dc % 2 == 0 else ps["p3"]).ap()
            ch.step("tensor", lambda e, d_=d_, pb2=pb2: e.transpose(
                pb2[:, 0:128], Ef[:, d_ * 128:(d_ + 1) * 128], idf))
            ch.step("vector", lambda e, d_=d_, pb2=pb2: e.tensor_add(
                x1t[:, d_ * 128:(d_ + 1) * 128], pb2[:, 0:128],
                xt[:, d_ * 128:(d_ + 1) * 128]))
        ch.step("gpsimd", lambda e: e.dma_start(out=x1ag[r0:r0 + 128, :], in_=x1t), 16)
        ln_stats_norm(ch, x1t, xsq, st, xsq)
        ch.step("vector", lambda e: e.tensor_copy(xnb, xsq))
        ch.step("gpsimd", lambda e: e.dma_start(out=xn1[r0:r0 + 128, :], in_=xnb), 16)
    ch.wait_all()


def p6_ag1(ctx, en, eng, cc_group):
    if en == "gpsimd":
        ctx.inc(eng.collective_compute(
            "AllGather", ALU.bypass, replica_groups=[cc_group],
            ins=[ctx.t("x1ag").opt()], outs=[ctx.t("x1full").opt()]), "cc1", 1)
        ctx.wait(eng, "cc1", ctx.count("cc1"))


def rank0(ctx, eng):
    return 0 if ctx.mode == "mock" else eng.partition_id()


def dyn(ap, off_expr):
    return bass.AP(tensor=ap.tensor, offset=ap.offset + off_expr, ap=ap.ap)


def p7_ffn1(ctx, en, eng, ncc, cc_group):
    sb, ps, cs = ctx.sb, ctx.ps, ctx.consts
    A, B, C, Dd, Ef, F, G, H, I, X = (sb[k].ap() for k in
                                      ("A", "B", "C", "D", "E", "F", "G", "H", "I", "X"))
    jl = S // ncc
    njt = jl // 128
    cells = 32 * jl
    nct = cells // 128
    ch = Chain(ctx, en, eng, "p7")
    x1full = ctx.t("x1full")
    xn1s, xn1slo, t32 = ctx.t("xn1s"), ctx.t("xn1slo"), ctx.t("t32")
    probs_d = ctx.t("probs")
    qall_d, kvall_d, su_d = ctx.t("qall"), ctx.t("kvall"), ctx.t("su")
    v1d, v2d, m1d, m2d, cnt_d = (ctx.t(k) for k in ("v1d", "v2d", "m1d", "m2d", "cntd"))
    agsm_in, agsm_out = ctx.t("agsm_in"), ctx.t("agsm_out")
    ffn1ag = ctx.t("ffn1ag")
    wg, wqh, wkvh, woh = ctx.t("w1_wg"), ctx.t("w1_wq"), ctx.t("w1_wkv"), ctx.t("w1_wo")
    bqh, bkvh, boh = ctx.t("b1_bq"), ctx.t("b1_bkv"), ctx.t("b1_bo")
    lng, lnb = ctx.t("b1_lng"), ctx.t("b1_lnb")

    idf = I[:, 0:128]
    onesb = H[0:1, 0:512]
    r = [None]

    def R(e):
        if r[0] is None:
            r[0] = rank0(ctx, e)
        return r[0]

    wgHI = H[:, 512:640].rearrange("p (c h) -> p c h", c=8)
    wgLO = H[:, 640:768].rearrange("p (c h) -> p c h", c=8)
    wgF = Ef[:, 0:128].rearrange("p (c h) -> p c h", c=8)
    ch.step("sync", lambda e: e.dma_start(out=idf, in_=cs["ident_f"]), 16)
    ch.step("sync", lambda e: e.dma_start(out=onesb, in_=cs["ones_row_b"]), 16)
    ch.step("sync", lambda e: e.dma_start(
        out=wgF, in_=wg.rearrange("(c p) h -> p c h", p=128)), 16)
    ch.step("vector", lambda e: e.tensor_copy(wgHI, wgF))
    ch.step("vector", lambda e: e.tensor_sub(wgLO, wgF, wgHI))
    WqA = C[:, 0:8192].rearrange("p (c n) -> p c n", c=8)
    WoS = C[:, 8192:16384].rearrange("p (c n) -> p c n", c=8)
    WkvA = B[:, 0:16384].rearrange("p (c n) -> p c n", c=8)
    for c_ in range(DCH):
        cc = c_
        ch.step("gpsimd", lambda e, cc=cc: e.dma_start(
            out=C[:, cc * 1024:(cc + 1) * 1024].rearrange("p (h n) -> p h n", h=16),
            in_=wqh.rearrange("h (c p) n -> c p h n", p=128)[cc]), 16)
        ch.step("gpsimd", lambda e, cc=cc: e.dma_start(
            out=B[:, cc * 2048:(cc + 1) * 2048].rearrange("p (h n) -> p h n", h=16),
            in_=wkvh.rearrange("h (c p) n -> c p h n", p=128)[cc]), 16)
    ch.step("gpsimd", lambda e: e.dma_start(
        out=WoS, in_=woh.rearrange("h e o -> (h e) o").rearrange("(c p) o -> p c o", p=128)), 16)
    bqA = F[0:1, 0:1024]
    bkvA = F[0:1, 1024:3072]
    ch.step("gpsimd", lambda e: e.dma_start(out=bqA, in_=bqh.rearrange("h n -> (h n)")[None, :]), 16)
    ch.step("gpsimd", lambda e: e.dma_start(out=bkvA, in_=bkvh.rearrange("h n -> (h n)")[None, :]), 16)

    x1seq = ctx.t("x1seq")
    ch.step("gpsimd", lambda e: e.dma_start(
        out=x1seq.rearrange("(b j) d -> b j d", b=32),
        in_=dyn(x1full.rearrange("(b s) d -> b s d", b=32)[:, 0:jl, :], R(e) * (jl * D))), 16)

    for ct in range(nct):
        b, j0 = ct // njt, (ct % njt) * 128
        xt = Dd[:, 0:1024]
        xsq = Ef[:, 0:1024]
        xnf = Ef[:, 1024:2048]
        xnb = F[:, 2048 + 0:3072]
        xnl = F[:, 3072:4096]
        st = G[:, 0:16]
        r0_ = ct * 128
        ch.step("sync", lambda e, r0_=r0_: e.dma_start(
            out=xt, in_=x1seq[r0_:r0_ + 128, :]), 16)
        ln_stats_norm(ch, xt, xsq, st, xnf)
        r0 = ct * 128
        ch.step("vector", lambda e: e.tensor_copy(xnb, xnf))
        ch.step("vector", lambda e: e.tensor_sub(xnl, xnf, xnb))
        ch.dma_group("gpsimd", [
            lambda e, r0=r0: e.dma_start(out=xn1s[r0:r0 + 128, :], in_=xnb),
            lambda e, r0=r0: e.dma_start(out=xn1slo[r0:r0 + 128, :], in_=xnl),
            lambda e, r0=r0: e.dma_start(out=t32[r0:r0 + 128, :], in_=xnf)])

    for ct in range(nct):
        r0 = ct * 128
        xhi = X[:, 0:1024].rearrange("p (c t) -> p c t", c=8)
        xlo = X[:, 1024:2048].rearrange("p (c t) -> p c t", c=8)
        fns = []
        for c in range(DCH):
            fns.append(lambda e, cc=c, r0=r0: e.dma_start_transpose(
                out=xhi[:, cc, :], in_=xn1s[r0:r0 + 128, cc * 128:(cc + 1) * 128]))
            fns.append(lambda e, cc=c, r0=r0: e.dma_start_transpose(
                out=xlo[:, cc, :], in_=xn1slo[r0:r0 + 128, cc * 128:(cc + 1) * 128]))
        ch.dma_group("sync", fns)
        L = ps["p0"].ap()[:, 0:16]

        def mmg(e):
            first = True
            for c in range(DCH):
                for lhs, rhs in ((xhi, wgHI), (xhi, wgLO), (xlo, wgHI)):
                    ins = e.matmul(L, lhs[:, c, :], rhs[:, c, :], start=first,
                                   stop=(c == DCH - 1 and lhs is xlo))
                    first = False
            return ins
        ch.step("tensor", mmg)
        Lf = G[:, 32:48]
        nm1 = G[:, 48:49]
        ssum = G[:, 49:50]
        rs = G[:, 50:51]
        pr = G[:, 52:68]
        ch.step("vector", lambda e: e.tensor_copy(Lf, L))
        ch.step("vector", lambda e: e.reduce_max(nm1, Lf, AX.X, negate=True))
        ch.step("scalar", lambda e: e.activation(pr, Lf, AF.Exp, bias=nm1, accum_out=ssum))
        ch.step("vector", lambda e: e.reciprocal(rs, ssum))
        ch.step("vector", lambda e: e.tensor_scalar(pr, pr, rs, None, ALU.mult))
        ch.step("gpsimd", lambda e, r0=r0: e.dma_start(out=probs_d[r0:r0 + 128, :], in_=pr), 16)

    for ct in range(nct):
        r0 = ct * 128
        xhi = X[:, 1024:2048].rearrange("p (c t) -> p c t", c=8)
        ch.dma_group("sync", [
            (lambda e, cc=c, r0=r0: e.dma_start_transpose(
                out=xhi[:, cc, :], in_=xn1s[r0:r0 + 128, cc * 128:(cc + 1) * 128]))
            for c in range(DCH)])
        for ng in range(6):
            is_q = ng < 2
            W = WqA if is_q else WkvA
            brow = bqA if is_q else bkvA
            o0 = (ng if is_q else ng - 2) * 512
            pp = ps["p0"].ap() if ng % 2 == 0 else ps["p1"].ap()

            def mmq(e, W=W, brow=brow, o0=o0, pp=pp):
                for dc in range(DCH):
                    ins = e.matmul(pp[:], xhi[:, dc, :], W[:, dc, o0:o0 + 512],
                                   start=(dc == 0), stop=False)
                ins = e.matmul(pp[:], onesb[0:1, 0:128], brow[:, o0:o0 + 512],
                               start=False, stop=True)
                return ins
            ch.step("tensor", mmq)
            dst = X[:, 2048 + (ng % 2) * 512: 2048 + (ng % 2) * 512 + 512]
            ch.step("vector", lambda e, pp=pp, dst=dst: e.tensor_copy(dst, pp[:]))
            dd = qall_d if is_q else kvall_d
            ch.step("gpsimd", lambda e, dd=dd, r0=r0, o0=o0, dst=dst: e.dma_start(
                out=dd[r0:r0 + 128, o0:o0 + 512], in_=dst), 16)

    for jt in range(njt):
        j0 = jt * 128
        pj = G[:, 0:512]
        ch.step("sync", lambda e, j0=j0: e.dma_start(
            out=pj.rearrange("p (b h) -> p b h", b=32),
            in_=probs_d.rearrange("(b j) h -> j b h", b=32)[j0:j0 + 128],
            allow_slow_non_contiguous=True), 16)
        m1v = Dd[:, 0:16]
        m2v = Dd[:, 16:32]
        eqv = Ef[:, 0:512]
        l2v = Ef[:, 1024:1536]
        tmpv = Dd[:, 512:1024]
        c0 = Ef[:, 0:512]
        c1 = Ef[:, 512:1024]
        M1 = X[:, 0:512]
        M2 = X[:, 512:1024]

        def treemax(dst, src0):
            ch.step("vector", lambda e: e.tensor_max(tmpv[:, 0:256], src0[:, 0:256], src0[:, 256:512]))
            ch.step("vector", lambda e: e.tensor_max(tmpv[:, 0:128], tmpv[:, 0:128], tmpv[:, 128:256]))
            ch.step("vector", lambda e: e.tensor_max(tmpv[:, 0:64], tmpv[:, 0:64], tmpv[:, 64:128]))
            ch.step("vector", lambda e: e.tensor_max(tmpv[:, 0:32], tmpv[:, 0:32], tmpv[:, 32:64]))
            ch.step("vector", lambda e: e.tensor_max(dst, tmpv[:, 0:16], tmpv[:, 16:32]))
        treemax(m1v, pj)
        pj3 = pj.rearrange("p (b h) -> p b h", b=32)
        ch.step("vector", lambda e: e.tensor_tensor(
            eqv.rearrange("p (b h) -> p b h", b=32), pj3,
            m1v[:, None, :].broadcast_to([128, 32, 16]), ALU.is_ge))
        ch.step("vector", lambda e: e.scalar_tensor_tensor(l2v, eqv, NEG, pj, ALU.mult, ALU.add))
        treemax(m2v, l2v)
        ch.step("vector", lambda e: e.tensor_tensor(
            c0.rearrange("p (b h) -> p b h", b=32), pj3,
            m2v[:, None, :].broadcast_to([128, 32, 16]), ALU.is_ge))
        ch.step("vector", lambda e: e.tensor_copy(M2, c0))
        cur, nxt = c0, c1
        for sh in (1, 2, 4, 8, 16):
            s_ = sh * 16
            ch.step("vector", lambda e, cur=cur, nxt=nxt, s_=s_: e.tensor_copy(nxt[:, 0:s_], cur[:, 0:s_]))
            ch.step("vector", lambda e, cur=cur, nxt=nxt, s_=s_: e.tensor_add(
                nxt[:, s_:512], cur[:, s_:512], cur[:, 0:512 - s_]))
            cur, nxt = nxt, cur
        ch.step("vector", lambda e, cur=cur: e.tensor_single_scalar(tmpv, cur, 1.0, ALU.is_equal))
        ch.step("vector", lambda e: e.tensor_mul(M1, tmpv, M2))
        ch.step("vector", lambda e, cur=cur: e.tensor_single_scalar(tmpv, cur, 2.0, ALU.is_equal))
        ch.step("vector", lambda e: e.tensor_mul(M2, tmpv, M2))
        ch.step("gpsimd", lambda e, j0=j0: e.dma_start(out=m1d[j0:j0 + 128, :], in_=M1), 16)
        ch.step("gpsimd", lambda e, j0=j0: e.dma_start(out=m2d[j0:j0 + 128, :], in_=M2), 16)
        ch.step("vector", lambda e: e.tensor_add(tmpv, M1, M2))
        cnt = Dd[:, 32:64]
        ch.step("vector", lambda e: e.tensor_reduce(
            cnt, tmpv.rearrange("p (b h) -> p b h", b=32), AX.X, ALU.add))
        ch.step("gpsimd", lambda e, j0=j0: e.dma_start(out=cnt_d[j0:j0 + 128, :], in_=cnt), 16)

        Q1, Q2 = F[:, 0:1024], F[:, 1024:2048]
        V1, V2 = F[:, 2048:3072], F[:, 3072:4096]
        K1, K2 = H[:, 0:1024], X[:, 1024:2048]
        for bufz in (Q1, Q2, V1, V2, K1, K2):
            bz = bufz
            ch.step("vector", lambda e, bz=bz: e.memset(bz, 0.0))
        qb = C[:, 0:1024]
        kvb = C[:, 1024:3072]
        tq = C[:, 3072:4096]
        tkv = C[:, 4096:6144]
        for b in range(32):
            bb = b
            ch.dma_group("sync", [
                lambda e, bb=bb, j0=j0: e.dma_start(
                    out=qb, in_=qall_d[bb * jl + j0: bb * jl + j0 + 128, :]),
                lambda e, bb=bb, j0=j0: e.dma_start(
                    out=kvb, in_=kvall_d[bb * jl + j0: bb * jl + j0 + 128, :])])
            m1s = M1[:, bb * 16:(bb + 1) * 16]
            m2s = M2[:, bb * 16:(bb + 1) * 16]
            qb3 = qb.rearrange("p (h e) -> p h e", h=16)
            kv4 = kvb.rearrange("p (h w e) -> p h w e", h=16, w=2)
            for (msk, qd, kd, vd) in ((m1s, Q1, K1, V1), (m2s, Q2, K2, V2)):
                mq, qd_, kd_, vd_ = msk, qd, kd, vd
                ch.step("vector", lambda e, mq=mq: e.tensor_tensor(
                    tq.rearrange("p (h e) -> p h e", h=16), qb3,
                    mq[:, :, None].broadcast_to([128, 16, 64]), ALU.mult))
                ch.step("vector", lambda e, qd_=qd_: e.tensor_add(qd_, qd_, tq))
                ch.step("vector", lambda e, mq=mq: e.tensor_tensor(
                    tkv.rearrange("p (h w e) -> p h w e", h=16, w=2), kv4,
                    mq[:, :, None, None].broadcast_to([128, 16, 2, 64]), ALU.mult))
                tkv4 = tkv.rearrange("p (h w e) -> p h w e", h=16, w=2)
                ch.step("vector", lambda e, kd_=kd_, tkv4=tkv4: e.tensor_add(
                    kd_.rearrange("p (h e) -> p h e", h=16), kd_.rearrange("p (h e) -> p h e", h=16),
                    tkv4[:, :, 0, :]))
                ch.step("vector", lambda e, vd_=vd_, tkv4=tkv4: e.tensor_add(
                    vd_.rearrange("p (h e) -> p h e", h=16), vd_.rearrange("p (h e) -> p h e", h=16),
                    tkv4[:, :, 1, :]))
        ch.step("gpsimd", lambda e, j0=j0: e.dma_start(out=v1d[j0:j0 + 128, :], in_=V1), 16)
        ch.step("gpsimd", lambda e, j0=j0: e.dma_start(out=v2d[j0:j0 + 128, :], in_=V2), 16)
        attj = G[:, 32 + jt * 64: 96 + jt * 64]
        tmpf = Dd[:, 0:1024]
        for ci, (qq, kk) in enumerate(((Q1, K1), (Q1, K2), (Q2, K1), (Q2, K2))):
            cc_, qq_, kk_ = ci, qq, kk
            ch.step("vector", lambda e, qq_=qq_, kk_=kk_: e.tensor_mul(tmpf, qq_, kk_))
            ch.step("vector", lambda e, cc_=cc_: e.tensor_reduce(
                attj[:, cc_ * 16:(cc_ + 1) * 16],
                tmpf.rearrange("p (h e) -> p h e", h=16), AX.X, ALU.add))
        ch.step("vector", lambda e, attj=attj: e.tensor_scalar(attj, attj, 0.125, None, ALU.mult))
        ch.step("tensor", lambda e, attj=attj: e.transpose(ps["p2"].ap()[0:64, 0:128], attj, idf))
        ch.step("vector", lambda e, jt=jt: e.tensor_copy(
            Ef[0:64, 1536 + jt * 128:1536 + jt * 128 + 128], ps["p2"].ap()[0:64, 0:128]))

    attT = Ef[0:64, 1536:1536 + 256]
    mloc = G[0:64, 480:481]
    ch.step("vector", lambda e: e.reduce_max(mloc, attT[:, 0:njt * 128], AX.X))
    ch.step("tensor", lambda e: e.transpose(ps["p3"].ap()[0:1, 0:64], mloc, idf[0:64, 0:64]))
    mrow = Ef[0:1, 1800:1864]
    ch.step("vector", lambda e: e.tensor_copy(mrow, ps["p3"].ap()[0:1, 0:64]))
    onesf = Dd[0:1, 0:128]
    ch.step("sync", lambda e: e.dma_start(out=onesf, in_=cs["ones_row_f"]), 16)
    ch.step("tensor", lambda e: e.matmul(ps["p4"].ap()[:, 0:64], onesf, mrow, start=True, stop=True))
    mb = Dd[:, 1024:1088]
    ch.step("vector", lambda e: e.tensor_scalar(mb, ps["p4"].ap()[:, 0:64], -1.0, None, ALU.mult))
    onescol = Dd[0:128, 1088:1089]
    ch.step("vector", lambda e: e.memset(onescol, 1.0))
    for jt in range(njt):
        attj = G[:, 32 + jt * 64: 96 + jt * 64]
        elj = Dd[:, 1152 + jt * 64: 1216 + jt * 64]
        ch.step("vector", lambda e, attj=attj, elj=elj: e.tensor_add(elj, attj, mb))
        ch.step("scalar", lambda e, elj=elj: e.activation(elj, elj, AF.Exp))
        ch.step("tensor", lambda e, elj=elj, jt=jt: e.matmul(
            ps["p5"].ap()[0:1, 0:64], onescol, elj, start=(jt == 0), stop=(jt == njt - 1)))
    srow = Ef[0:1, 1864:1928]
    ch.step("vector", lambda e: e.tensor_copy(srow, ps["p5"].ap()[0:1, 0:64]))
    ch.step("gpsimd", lambda e: e.dma_start(out=agsm_in[0:1, 0:64], in_=mrow), 16)
    ch.step("gpsimd", lambda e: e.dma_start(out=agsm_in[0:1, 64:128], in_=srow), 16)
    nrk = len(cc_group)
    if en == "gpsimd":
        if ctx.mode == "real":
            ch._wait_cur()
        ctx.inc(eng.collective_compute(
            "AllGather", ALU.bypass, replica_groups=[cc_group],
            ins=[agsm_in.opt()], outs=[agsm_out.opt()]), "cc2", 1)
        ctx.wait(eng, "cc2", ctx.count("cc2"))
    ch.step("gpsimd", lambda e: e.nop())
    ch.step("sync", lambda e, nrk=nrk: e.dma_start(out=Ef[0:nrk, 0:128], in_=agsm_out[0:nrk, :]), 16)
    GtT = Dd[:, 1280:1280 + 8]
    ch.step("tensor", lambda e, nrk=nrk: e.transpose(ps["p2"].ap()[:, 0:nrk], Ef[0:nrk, 0:128], idf[0:nrk, 0:nrk]))
    ch.step("vector", lambda e, nrk=nrk: e.tensor_copy(GtT[:, 0:nrk], ps["p2"].ap()[:, 0:nrk]))
    negM = G[0:64, 481:482]
    Sv = G[0:64, 482:483]
    rS = G[0:64, 483:484]
    Fv = G[0:64, 484:485]
    ch.step("vector", lambda e, nrk=nrk: e.reduce_max(negM, GtT[0:64, 0:nrk], AX.X, negate=True))
    wv_ = Dd[0:64, 1300:1300 + 8]
    ch.step("scalar", lambda e, nrk=nrk: e.activation(wv_[:, 0:nrk], GtT[0:64, 0:nrk], AF.Exp, bias=negM))
    pv_ = Dd[0:64, 1320:1320 + 8]
    scop = Dd[0:64, 1340:1340 + 8]
    ch.step("vector", lambda e, nrk=nrk: e.tensor_copy(scop[:, 0:nrk], GtT[64:128, 0:nrk]))
    ch.step("vector", lambda e, nrk=nrk: e.tensor_mul(pv_[:, 0:nrk], wv_[:, 0:nrk], scop[:, 0:nrk]))
    ch.step("vector", lambda e, nrk=nrk: e.reduce_sum(Sv, pv_[:, 0:nrk], AX.X))
    ch.step("vector", lambda e: e.reciprocal(rS, Sv))
    ch.step("scalar", lambda e: e.activation(Fv, mloc, AF.Exp, bias=negM))
    ch.step("vector", lambda e: e.tensor_mul(Fv, Fv, rS))
    ch.step("tensor", lambda e: e.transpose(ps["p3"].ap()[0:1, 0:64], Fv, idf[0:64, 0:64]))
    frow = Ef[0:1, 1800:1864]
    ch.step("vector", lambda e: e.tensor_copy(frow, ps["p3"].ap()[0:1, 0:64]))
    ch.step("tensor", lambda e: e.matmul(ps["p4"].ap()[:, 0:64], onesf, frow, start=True, stop=True))
    Fb = Dd[:, 2112:2176]
    ch.step("vector", lambda e: e.tensor_copy(Fb, ps["p4"].ap()[:, 0:64]))

    for jt in range(njt):
        j0 = jt * 128
        elj = Dd[:, 1152 + jt * 64: 1216 + jt * 64]
        pfin = Dd[:, 1152 + jt * 64: 1216 + jt * 64]
        ch.step("vector", lambda e, elj=elj: e.tensor_mul(pfin, elj, Fb))
        V1, V2 = F[:, 2048:3072], F[:, 3072:4096]
        M1, M2 = X[:, 0:512], X[:, 512:1024]
        ch.dma_group("sync", [
            lambda e, j0=j0: e.dma_start(out=V1, in_=v1d[j0:j0 + 128, :]),
            lambda e, j0=j0: e.dma_start(out=V2, in_=v2d[j0:j0 + 128, :]),
            lambda e, j0=j0: e.dma_start(out=M1, in_=m1d[j0:j0 + 128, :]),
            lambda e, j0=j0: e.dma_start(out=M2, in_=m2d[j0:j0 + 128, :])])
        av1, av2 = C[:, 0:1024], C[:, 1024:2048]
        tmpb = C[:, 2048:3072]
        for i_, avd in ((0, av1), (1, av2)):
            ii, avd_ = i_, avd
            p1s = pfin[:, (ii * 2 + 0) * 16:(ii * 2 + 0) * 16 + 16]
            p2s = pfin[:, (ii * 2 + 1) * 16:(ii * 2 + 1) * 16 + 16]
            ch.step("vector", lambda e, avd_=avd_, p1s=p1s: e.tensor_tensor(
                avd_.rearrange("p (h e) -> p h e", h=16),
                V1.rearrange("p (h e) -> p h e", h=16),
                p1s[:, :, None].broadcast_to([128, 16, 64]), ALU.mult))
            ch.step("vector", lambda e, p2s=p2s: e.tensor_tensor(
                tmpb.rearrange("p (h e) -> p h e", h=16),
                V2.rearrange("p (h e) -> p h e", h=16),
                p2s[:, :, None].broadcast_to([128, 16, 64]), ALU.mult))
            ch.step("vector", lambda e, avd_=avd_: e.tensor_add(avd_, avd_, tmpb))
        sut = C[:, 3072:4096]
        for b in range(32):
            bb = b
            m1s = M1[:, bb * 16:(bb + 1) * 16]
            m2s = M2[:, bb * 16:(bb + 1) * 16]
            ch.step("vector", lambda e, m1s=m1s: e.tensor_tensor(
                sut.rearrange("p (h e) -> p h e", h=16),
                av1.rearrange("p (h e) -> p h e", h=16),
                m1s[:, :, None].broadcast_to([128, 16, 64]), ALU.mult))
            ch.step("vector", lambda e, m2s=m2s: e.tensor_tensor(
                tmpb.rearrange("p (h e) -> p h e", h=16),
                av2.rearrange("p (h e) -> p h e", h=16),
                m2s[:, :, None].broadcast_to([128, 16, 64]), ALU.mult))
            ch.step("vector", lambda e: e.tensor_add(sut, sut, tmpb))
            ch.step("gpsimd", lambda e, bb=bb, j0=j0: e.dma_start(
                out=su_d[bb * jl + j0: bb * jl + j0 + 128, :], in_=sut), 16)

    gb = Ef[:, 0:1024]
    bb_ = Ef[:, 1024:2048]
    bob = Dd[:, 2048:3072]
    for half in range(2):
        hh = half
        for src_row, dstb in ((lng, gb), (lnb, bb_), (boh, bob)):
            sr, db = src_row, dstb
            ch.step("sync", lambda e, sr=sr, hh=hh: e.dma_start(
                out=Ef[0:1, 2048 - 512:2048].bitcast(F32) if False else G[0:1, 0:512],
                in_=sr[None, hh * 512:(hh + 1) * 512]), 16)
            ch.step("tensor", lambda e: e.matmul(ps["p2"].ap()[:, 0:512], onesf,
                                                 G[0:1, 0:512], start=True, stop=True))
            ch.step("vector", lambda e, db=db, hh=hh: e.tensor_copy(
                db[:, hh * 512:(hh + 1) * 512], ps["p2"].ap()[:, 0:512]))
    for ct in range(nct):
        b, jt = ct // njt, ct % njt
        j0 = jt * 128
        r0 = ct * 128
        cntt = G[:, 0:32]
        suf = X[:, 0:1024].rearrange("p (c t) -> p c t", c=8)
        ch.dma_group("sync", [lambda e, j0=j0: e.dma_start(out=cntt, in_=cnt_d[j0:j0 + 128, :])] + [
            (lambda e, cc=c, r0=r0: e.dma_start_transpose(
                out=suf[:, cc, :], in_=su_d[r0:r0 + 128, cc * 128:(cc + 1) * 128]))
            for c in range(DCH)])
        y = Dd[:, 0:1024]
        for og in range(2):
            oo = og
            pp = ps["p0"].ap() if og == 0 else ps["p1"].ap()

            def mmo(e, oo=oo, pp=pp):
                for dc in range(DCH):
                    ins = e.matmul(pp[:], suf[:, dc, :], WoS[:, dc, oo * 512:(oo + 1) * 512],
                                   start=(dc == 0), stop=(dc == DCH - 1))
                return ins
            ch.step("tensor", mmo)
            ccol = cntt[:, b:b + 1]
            ch.step("vector", lambda e, oo=oo, pp=pp, ccol=ccol: e.scalar_tensor_tensor(
                y[:, oo * 512:(oo + 1) * 512], bob[:, oo * 512:(oo + 1) * 512],
                ccol, pp[:], ALU.mult, ALU.add))
        tt_ = Dd[:, 1024:2048]
        ch.step("sync", lambda e, r0=r0: e.dma_start(out=tt_, in_=t32[r0:r0 + 128, :]), 16)
        ch.step("vector", lambda e: e.tensor_add(y, y, tt_))
        xsq = Dd[:, 1024:2048]
        st = G[:, 100:116]
        ch.step("scalar", lambda e: e.activation(xsq, y, AF.Square))
        s1, s2, mu, var, rec, rstd, nmr = (st[:, k:k + 1] for k in range(7))
        ch.step("vector", lambda e: e.reduce_sum(s1, y, AX.X))
        ch.step("vector", lambda e: e.reduce_sum(s2, xsq, AX.X))
        ch.step("vector", lambda e: e.tensor_scalar(mu, s1, 1.0 / D, None, ALU.mult))
        ch.step("vector", lambda e: e.tensor_mul(var, mu, mu))
        ch.step("vector", lambda e: e.tensor_scalar(var, s2, 1.0 / D, var, ALU.mult, ALU.subtract))
        ch.step("vector", lambda e: e.tensor_scalar_add(var, var, EPS))
        ch.step("vector", lambda e: e.reciprocal(rec, var))
        ch.step("scalar", lambda e: e.activation(rstd, rec, AF.Sqrt))
        ch.step("scalar", lambda e: e.activation(nmr, mu, AF.Copy, scale=rstd))
        yn = Dd[:, 1024:2048]
        ch.step("vector", lambda e: e.tensor_scalar(yn, y, rstd, nmr, ALU.mult, ALU.subtract))
        ch.step("vector", lambda e: e.tensor_mul(yn, yn, gb))
        ch.step("vector", lambda e: e.tensor_add(yn, yn, bb_))
        ch.step("gpsimd", lambda e, r0=r0: e.dma_start(out=ffn1ag[r0:r0 + 128, :], in_=yn), 16)
    ch.wait_all()


def p7h_ag2(ctx, en, eng, cc_group):
    if en == "gpsimd":
        ctx.inc(eng.collective_compute(
            "AllGather", ALU.bypass, replica_groups=[cc_group],
            ins=[ctx.t("ffn1ag").opt()], outs=[ctx.t("ffn1full").opt()]), "cc3", 1)
        ctx.wait(eng, "cc3", ctx.count("cc3"))


def p8_final(ctx, en, eng, ncc):
    sb, ps, cs = ctx.sb, ctx.ps, ctx.consts
    Dd, Ef, I = sb["D"].ap(), sb["E"].ap(), sb["I"].ap()
    ch = Chain(ctx, en, eng, "p8")
    x1ag, attn1f, ffn1full = ctx.t("x1ag"), ctx.t("accf"), ctx.t("ffn1full")
    ffn1loc = ctx.t("ffn1loc")
    out = ctx.t("out").rearrange("b s d -> (b s) d")
    jl = S // ncc
    cells_ = 32 * jl
    idf = I[:, 0:128]
    r = [None]

    def R(e):
        if r[0] is None:
            r[0] = rank0(ctx, e)
        return r[0]
    ch.step("sync", lambda e: e.dma_start(out=idf, in_=cs["ident_f"]), 16)
    for b_ in range(B_LOC):
        bb = b_
        src3 = ffn1full.rearrange("(r2 c) d -> r2 c d", c=cells_)[:, bb * jl:(bb + 1) * jl, :]
        ch.step("gpsimd", lambda e, bb=bb, src3=src3: e.dma_start(
            out=ffn1loc.rearrange("(b s) d -> b s d", b=B_LOC)[bb].rearrange(
                "(r2 j) d -> r2 j d", j=jl),
            in_=dyn(src3, R(e) * (4 * jl * D))), 16)
    for i in range(NT):
        r0 = i * 128
        x1t = Dd[:, 0:1024]
        ft = Dd[:, 1024:2048]
        ch.dma_group("sync", [
            lambda e: e.dma_start(out=x1t, in_=x1ag[r0:r0 + 128, :]),
            lambda e, r0=r0: e.dma_start(out=ft, in_=ffn1loc[r0:r0 + 128, :])] + [
            (lambda e, d_=dc, r0=r0: e.dma_start(
                out=Ef[:, d_ * 128:(d_ + 1) * 128],
                in_=attn1f[d_ * 128:(d_ + 1) * 128, r0:r0 + 128]))
            for dc in range(DCH)])
        ch.step("vector", lambda e: e.tensor_add(x1t, x1t, ft))
        for dc in range(DCH):
            d_ = dc
            pb2 = (ps["p2"] if dc % 2 == 0 else ps["p3"]).ap()
            ch.step("tensor", lambda e, d_=d_, pb2=pb2: e.transpose(
                pb2[:, 0:128], Ef[:, d_ * 128:(d_ + 1) * 128], idf))
            ch.step("vector", lambda e, d_=d_, pb2=pb2: e.tensor_add(
                x1t[:, d_ * 128:(d_ + 1) * 128], x1t[:, d_ * 128:(d_ + 1) * 128],
                pb2[:, 0:128]))
        ch.step("gpsimd", lambda e, r0=r0: e.dma_start(out=out[r0:r0 + 128, :], in_=x1t), 16)
    ch.wait_all()



CELLS8 = 32 * JL

SHAPES = {
    "x_in": ([B_LOC, S, D], F32),
    "w0q": ([D, D], F32), "w0k": ([D, D], F32), "w0v": ([D, D], F32), "w0o": ([D, D], F32),
    "w_gate": ([8, D], F32),
    "w_moe1": ([8, D, FF], F32), "w_moe2": ([8, FF, D], F32),
    "w_sw1": ([D, FF], F32), "w_sw2": ([FF, D], F32),
    "b_moe1": ([8, FF], F32), "b_moe2": ([8, D], F32),
    "b_ssb1": ([FF], F32), "b_ssb2": ([D], F32),
    "w5q": ([D, D], F32), "w5k": ([D, D], F32), "w5v": ([D, D], F32), "w5o": ([D, D], F32),
    "b5q": ([D], F32), "b5k": ([D], F32), "b5v": ([D], F32), "b5o": ([D], F32),
    "w1_wg": ([D, 16], F32), "w1_wq": ([16, D, 64], F32), "w1_wkv": ([16, D, 128], F32),
    "w1_wo": ([16, 64, D], F32),
    "b1_bq": ([16, 64], F32), "b1_bkv": ([16, 128], F32), "b1_bo": ([D], F32),
    "b1_lng": ([D], F32), "b1_lnb": ([D], F32),
    "xn0": ([T, D], BF16), "xn0lo": ([T, D], BF16),
    "comb": ([8, T], BF16), "b1b": ([9, FF], BF16),
    "accf": ([D, T], F32),
    "b5rows": ([4, D], BF16),
    "x1ag": ([T, D], F32), "xn1": ([T, D], BF16),
    "x1full": ([NCORES * T, D], F32),
    "x1seq": ([CELLS8, D], F32),
    "xn1s": ([CELLS8, D], BF16), "xn1slo": ([CELLS8, D], BF16), "t32": ([CELLS8, D], F32),
    "probs": ([CELLS8, 16], F32),
    "qall": ([CELLS8, 1024], BF16), "kvall": ([CELLS8, 2048], BF16), "su": ([CELLS8, 1024], BF16),
    "v1d": ([JL, 1024], BF16), "v2d": ([JL, 1024], BF16),
    "m1d": ([JL, 512], BF16), "m2d": ([JL, 512], BF16), "cntd": ([JL, 32], F32),
    "agsm_in": ([1, 128], F32), "agsm_out": ([NCORES, 128], F32),
    "ffn1ag": ([CELLS8, D], F32), "ffn1full": ([NCORES * CELLS8, D], F32),
    "ffn1loc": ([T, D], F32),
    "out": ([B_LOC, S, D], F32),
}

WEIGHT_MAP = {
    "w0q": "L0_attn_Wq", "w0k": "L0_attn_Wk", "w0v": "L0_attn_Wv", "w0o": "L0_attn_Wo",
    "w_gate": "L0_ffn_gate_w",
    "w_moe1": "L0_ffn_w1", "w_moe2": "L0_ffn_w2",
    "w_sw1": "L0_ffn_sw1", "w_sw2": "L0_ffn_sw2",
    "b_moe1": "L0_ffn_b1", "b_moe2": "L0_ffn_b2",
    "b_ssb1": "L0_ffn_sb1", "b_ssb2": "L0_ffn_sb2",
    "w5q": "L1_attn_Wq", "w5k": "L1_attn_Wk", "w5v": "L1_attn_Wv", "w5o": "L1_attn_Wo",
    "b5q": "L1_attn_bq", "b5k": "L1_attn_bk", "b5v": "L1_attn_bv", "b5o": "L1_attn_bo",
    "w1_wg": "L1_ffn_Wg", "w1_wq": "L1_ffn_Wq", "w1_wkv": "L1_ffn_Wkv", "w1_wo": "L1_ffn_Wo",
    "b1_bq": "L1_ffn_bq", "b1_bkv": "L1_ffn_bkv", "b1_bo": "L1_ffn_bo",
    "b1_lng": "L1_ffn_ln_g", "b1_lnb": "L1_ffn_ln_b",
}


def ph_p1(ctx, en, eng):
    attn_phase(ctx, en, eng, "p1", ctx.t("xn0"), ctx.t("w0q"), ctx.t("w0k"),
               ctx.t("w0v"), ctx.t("w0o"), None, None, None, None,
               ctx.t("accf"), None)


def ph_p5(ctx, en, eng):
    attn_phase(ctx, en, eng, "p5", ctx.t("xn1"), ctx.t("w5q"), ctx.t("w5k"),
               ctx.t("w5v"), ctx.t("w5o"), ctx.t("b5q"), ctx.t("b5k"),
               ctx.t("b5v"), ctx.t("b5o"), ctx.t("accf"), WIN)


def ph_p7(ctx, en, eng):
    p7_ffn1(ctx, en, eng, NCORES, list(range(NCORES)))


def ph_p8(ctx, en, eng):
    p8_final(ctx, en, eng, NCORES)


def mk_io(ins, outs, shared=()):
    io = {}
    for n in ins:
        io[n] = ("ExternalInput", *SHAPES[n])
    for n in outs:
        io[n] = ("ExternalOutput", *SHAPES[n])
    for n, (shape, dt) in SHAPES.items():
        if n not in io:
            if n in shared:
                io[n] = ("Internal", shape, dt, "Shared")
            else:
                io[n] = ("Internal", shape, dt)
    return io


L0_W = ["w0q", "w0k", "w0v", "w0o", "w_gate", "w_moe1", "w_moe2", "w_sw1",
        "w_sw2", "b_moe1", "b_moe2", "b_ssb1", "b_ssb2"]
L5_W = ["w5q", "w5k", "w5v", "w5o", "b5q", "b5k", "b5v", "b5o"]
L7_W = ["w1_wg", "w1_wq", "w1_wkv", "w1_wo", "b1_bq", "b1_bkv", "b1_bo",
        "b1_lng", "b1_lnb"]

FUSED = True
_nc_cache = {}


def _get_nc(key, phases, io):
    if key not in _nc_cache:
        _nc_cache[key] = build_program(phases, io)
    return _nc_cache[key]


def kernel(**inputs):
    x = np.ascontiguousarray(np.asarray(inputs["L0_attn_x"], dtype=np.float32))
    w = {k: np.ascontiguousarray(np.asarray(inputs[v], dtype=np.float32))
         for k, v in WEIGHT_MAP.items()}
    xs = [x[4 * c:4 * c + 4] for c in range(NCORES)]

    if FUSED:
        phases = [p0_ln0, p2_gate, p3z_zero_acc, ph_p1, p3_moe, p4_ln1,
                  lambda c, en, e: p6_ag1(c, en, e, list(range(NCORES))),
                  ph_p5, ph_p7,
                  lambda c, en, e: p7h_ag2(c, en, e, list(range(NCORES))),
                  ph_p8]
        io = mk_io(["x_in"] + L0_W + L5_W + L7_W, ["out"],
                   shared=("x1full", "ffn1full", "agsm_out"))
        nc = _get_nc("fused", phases, io)
        in_maps = [{"x_in": xs[c], **w} for c in range(NCORES)]
        res = run_bass_kernel_spmd(nc, in_maps, core_ids=list(range(NCORES)))
        return np.concatenate([r["out"] for r in res.results], axis=0)

    ioA = mk_io(["x_in"] + L0_W, ["x1ag", "xn1"])
    ncA = _get_nc("A", [p0_ln0, p2_gate, p3z_zero_acc, ph_p1, p3_moe, p4_ln1], ioA)
    resA = run_bass_kernel_spmd(ncA, [{"x_in": xs[c], **{k: w[k] for k in L0_W}}
                                      for c in range(NCORES)],
                                core_ids=list(range(NCORES)))
    x1s = [r["x1ag"] for r in resA.results]
    xn1s_ = [r["xn1"] for r in resA.results]

    ioB = mk_io(["xn1"] + L5_W, ["accf"])
    ncB = _get_nc("B", [ph_p5], ioB)
    resB = run_bass_kernel_spmd(ncB, [{"xn1": xn1s_[c], **{k: w[k] for k in L5_W}}
                                      for c in range(NCORES)],
                                core_ids=list(range(NCORES)))
    attn1s = [r["accf"] for r in resB.results]

    x1full = np.concatenate(x1s, axis=0)
    ioC = mk_io(["x1full"] + L7_W, ["ffn1ag"], shared=("agsm_out",))
    ncC = _get_nc("C", [ph_p7], ioC)
    resC = run_bass_kernel_spmd(ncC, [{"x1full": x1full, **{k: w[k] for k in L7_W}}
                                      for c in range(NCORES)],
                                core_ids=list(range(NCORES)))
    ffn1full = np.concatenate([r["ffn1ag"] for r in resC.results], axis=0)

    ioD = mk_io(["x1ag", "accf", "ffn1full"], ["out"])
    ncD = _get_nc("D", [ph_p8], ioD)
    resD = run_bass_kernel_spmd(ncD, [{"x1ag": x1s[c], "accf": attn1s[c],
                                       "ffn1full": ffn1full}
                                      for c in range(NCORES)],
                                core_ids=list(range(NCORES)))
    return np.concatenate([r["out"] for r in resD.results], axis=0)


# version 11
# speedup vs baseline: 1.1691x; 1.1691x over previous
import numpy as np
import ml_dtypes
from contextlib import ExitStack
import concourse.bass as bass
import concourse.mybir as mybir
from concourse.bass_utils import run_bass_kernel_spmd

F32 = mybir.dt.float32
BF16 = mybir.dt.bfloat16
AF = mybir.ActivationFunctionType
ALU = mybir.AluOpType
AX = mybir.AxisListType

NCORES = 8
B_LOC = 4
S = 2048
D = 1024
T = B_LOC * S
NT = T // 128
DCH = D // 128
NH = 16
DH = 64
WIN = 512
NEXP = 8
FF = 4096
NFC = FF // 128
MTT = 512
NMT = T // MTT
JL = S // NCORES
NJT = JL // 128
EPS = 1e-5
NEG = -1e9

ENGINES = ["sync", "gpsimd", "tensor", "vector", "scalar"]
BF = ml_dtypes.bfloat16



class MockIns:
    def then_inc(self, *a, **k):
        raise RuntimeError("use ctx.inc / Chain.step, not .then_inc")


class MockEng:
    def __init__(self, name):
        self.name = name

    def __getattr__(self, op):
        def f(*a, **k):
            return MockIns()
        return f


class Ctx:
    def __init__(self, nc):
        self.nc = nc
        self.mode = "mock"
        self.counts = {}
        self.marks = {}
        self.sems = {}
        self.tensors = {}
        self.consts = {}
        self.sb = {}
        self.ps = {}

    def t(self, name):
        return self.tensors[name].ap()

    def inc(self, ins, name, v=1):
        self.counts[name] = self.counts.get(name, 0) + v
        if self.mode == "real":
            ins.then_inc(self.sems[name], v)

    def wait(self, eng, name, val):
        if self.mode == "real":
            eng.wait_ge(self.sems[name], val)

    def count(self, name):
        return self.counts.get(name, 0)

    def mark(self, key):
        snap = dict(self.counts)
        if self.mode == "mock":
            if key not in self.marks:
                self.marks[key] = snap
        return self.marks[key]

    def at(self, key, sem):
        if key not in self.marks:
            if self.mode == "mock":
                return 0
            raise KeyError(f"mark {key} never recorded in mock pass")
        return self.marks[key].get(sem, 0)

    def bar(self, eng, k):
        self.counts["bar"] = self.counts.get("bar", 0) + 1
        if self.mode == "real":
            eng.drain().then_inc(self.sems["bar"], 1)
            eng.wait_ge(self.sems["bar"], 5 * (k + 1))


class Chain:

    def __init__(self, ctx, en, eng, key):
        self.ctx, self.en, self.eng = ctx, en, eng
        self.sem = "ch_" + key
        self.dsem = {q: f"chd{q[0]}_{key}" for q in ("sync", "gpsimd", "scalar")}
        ctx.counts.setdefault(self.sem, 0)
        for s in self.dsem.values():
            ctx.counts.setdefault(s, 0)
        self.n = 0
        self.d = {q: 0 for q in self.dsem}

    def _wait_cur(self):
        if self.ctx.mode == "real":
            if self.n:
                self.ctx.wait(self.eng, self.sem, self.n)
            for q, v in self.d.items():
                if v:
                    self.ctx.wait(self.eng, self.dsem[q], v)

    def step(self, owner, fn, dv=None):
        dma = dv is not None
        if self.en == owner:
            if self.ctx.mode == "real":
                self._wait_cur()
                ins = fn(self.eng)
                if dma:
                    ins.then_inc(self.ctx.sems[self.dsem[owner]], 16)
                else:
                    ins.then_inc(self.ctx.sems[self.sem], 1)
            self.ctx.counts[self.dsem[owner] if dma else self.sem] += 16 if dma else 1
        if dma:
            self.d[owner] += 16
        else:
            self.n += 1

    def dma_group(self, owner, fns):
        first = True
        for fn in fns:
            if self.en == owner:
                if self.ctx.mode == "real":
                    if first:
                        self._wait_cur()
                    ins = fn(self.eng)
                    ins.then_inc(self.ctx.sems[self.dsem[owner]], 16)
                self.ctx.counts[self.dsem[owner]] += 16
            self.d[owner] += 16
            first = False

    def wait_all(self):
        self._wait_cur()




def mk_ops(ctx, en, eng, key):
    sems = {"tensor": key + "pe", "vector": key + "ve", "scalar": key + "sc",
            "sync": key + "ld", "gpsimd": key + "gp", "sync2": key + "ld2",
            "gpsimd2": key + "gp2", "gpsimdw": key + "gpw", "gpsimdc": key + "gc"}
    engine_of = {"sync2": "sync", "gpsimd2": "gpsimd", "gpsimdw": "gpsimd",
                 "gpsimdc": "gpsimd"}
    for sm in sems.values():
        ctx.counts.setdefault(sm, 0)

    def op(owner, fn, mark=None, deps=()):
        act = engine_of.get(owner, owner)
        if en == act:
            if ctx.mode == "real":
                for (mk, sm) in deps:
                    ctx.wait(eng, sm, ctx.at(mk, sm))
            dv = 1 if act in ("tensor", "vector", "scalar") or owner == "gpsimdc" else 16
            if ctx.mode == "real":
                ins = fn(eng)
                ins.then_inc(ctx.sems[sems[owner]], dv)
            ctx.counts[sems[owner]] += dv
            if mark:
                ctx.mark(mark)
        return None

    def waitall(extra=()):
        if ctx.mode == "real":
            for sm in list(sems.values()) + list(extra):
                v = ctx.count(sm)
                if v:
                    ctx.wait(eng, sm, v)
    return op, sems, waitall


def host_consts():
    c = {}
    c["ident_b"] = np.eye(128, dtype=BF)
    c["ident_f"] = np.eye(128, dtype=np.float32)
    r = np.arange(128)
    c["mask_causal"] = np.where(r[None, :] <= r[:, None], 0.0, NEG).astype(np.float32)
    c["mask_far"] = np.where(r[None, :] >= r[:, None] + 1, 0.0, NEG).astype(np.float32)
    c["ones_row_b"] = np.ones((1, 512), dtype=BF)
    c["ones_row_f"] = np.ones((1, 128), dtype=np.float32)
    oh = np.zeros((8, 8 * 128), dtype=BF)
    for e in range(8):
        oh[e, e * 128:(e + 1) * 128] = 1
    c["onehot8"] = oh
    c["onehot32"] = None
    del c["onehot32"]
    return c


def build_program(phase_list, io_kinds):
    nc = bass.Bass("TRN2", target_bir_lowering=False)
    ctx = Ctx(nc)
    for name, spec in io_kinds.items():
        kind, shape, dtype = spec[:3]
        kw = {"addr_space": spec[3]} if len(spec) > 3 else {}
        ctx.tensors[name] = nc.dram_tensor(name, list(shape), dtype, kind=kind, **kw)
    for cname, arr in host_consts().items():
        ctx.consts[cname] = nc.inline_tensor(arr, cname).ap()

    with ExitStack() as stack:
        en_ = stack.enter_context
        sb = ctx.sb
        sb["A"] = en_(nc.sbuf_tensor("bufA", [128, 32768], BF16))
        sb["B"] = en_(nc.sbuf_tensor("bufB", [128, 32768], BF16))
        sb["C"] = en_(nc.sbuf_tensor("bufC", [128, 16384], BF16))
        sb["D"] = en_(nc.sbuf_tensor("bufD", [128, 3072], F32))
        sb["E"] = en_(nc.sbuf_tensor("bufE", [128, 2048], F32))
        sb["F"] = en_(nc.sbuf_tensor("bufF", [128, 4096], BF16))
        sb["G"] = en_(nc.sbuf_tensor("bufG", [128, 512], F32))
        sb["H"] = en_(nc.sbuf_tensor("bufH", [128, 1024], BF16))
        sb["I"] = en_(nc.sbuf_tensor("bufI", [128, 128], F32))
        sb["X"] = en_(nc.sbuf_tensor("bufX", [128, 4096], BF16))
        ps = ctx.ps
        for i in range(6):
            ps[f"p{i}"] = en_(nc.psum_tensor(f"p{i}", [128, 512], F32))
        ps["tb0"] = en_(nc.psum_tensor("tb0", [128, 512], BF16))
        ps["tb1"] = en_(nc.psum_tensor("tb1", [128, 512], BF16))

        ctx.mode = "mock"
        for en in ENGINES:
            eng = MockEng(en)
            for k, ph in enumerate(phase_list):
                ph(ctx, en, eng)
                ctx.bar(eng, k)

        for name in sorted(ctx.counts):
            ctx.sems[name] = en_(nc.semaphore(name))
        ctx.mode = "real"
        ctx.counts = {}
        block = en_(nc.Block())

        def mk(en):
            def f(eng):
                for k, ph in enumerate(phase_list):
                    ph(ctx, en, eng)
                    ctx.bar(eng, k)
            return f
        block.sync(mk("sync"))
        block.gpsimd(mk("gpsimd"))
        block.tensor(mk("tensor"))
        block.vector(mk("vector"))
        block.scalar(mk("scalar"))
    return nc



def ln_stats_norm(ch, xt, xsq, st, xnf):
    s1, s2, mu, var, rec, rstd, nmr = (st[:, k:k + 1] for k in range(7))
    ch.step("scalar", lambda e: e.activation(xsq, xt, AF.Square))
    ch.step("vector", lambda e: e.reduce_sum(s1, xt, AX.X))
    ch.step("vector", lambda e: e.reduce_sum(s2, xsq, AX.X))
    ch.step("vector", lambda e: e.tensor_scalar(mu, s1, 1.0 / D, None, ALU.mult))
    ch.step("vector", lambda e: e.tensor_mul(var, mu, mu))
    ch.step("vector", lambda e: e.tensor_scalar(var, s2, 1.0 / D, var, ALU.mult, ALU.subtract))
    ch.step("vector", lambda e: e.tensor_scalar_add(var, var, EPS))
    ch.step("vector", lambda e: e.reciprocal(rec, var))
    ch.step("scalar", lambda e: e.activation(rstd, rec, AF.Sqrt))
    ch.step("scalar", lambda e: e.activation(nmr, mu, AF.Copy, scale=rstd))
    ch.step("vector", lambda e: e.tensor_scalar(xnf, xt, rstd, nmr, ALU.mult, ALU.subtract))


def ln_tiles(ctx, en, eng, key, src_ap, n_tiles, outs):
    sb = ctx.sb
    ch = Chain(ctx, en, eng, key)
    for i in range(n_tiles):
        xt = sb["D"].ap()[:, 0:1024]
        xsq = sb["E"].ap()[:, 0:1024]
        xnf = sb["E"].ap()[:, 1024:2048]
        xnb = sb["F"].ap()[:, 0:1024]
        xnl = sb["F"].ap()[:, 1024:2048]
        st = sb["G"].ap()[:, 0:16]
        s1, s2, mu, var, rec, rstd, nmr = (st[:, k:k + 1] for k in range(7))
        r0 = i * 128

        ch.step("sync", lambda e: e.dma_start(out=xt, in_=src_ap[r0:r0 + 128, :]), 16)
        ch.step("scalar", lambda e: e.activation(xsq, xt, AF.Square))
        ch.step("vector", lambda e: e.reduce_sum(s1, xt, AX.X))
        ch.step("vector", lambda e: e.reduce_sum(s2, xsq, AX.X))
        ch.step("vector", lambda e: e.tensor_scalar(mu, s1, 1.0 / D, None, ALU.mult))
        ch.step("vector", lambda e: e.tensor_mul(var, mu, mu))
        ch.step("vector", lambda e: e.tensor_scalar(var, s2, 1.0 / D, var, ALU.mult, ALU.subtract))
        ch.step("vector", lambda e: e.tensor_scalar_add(var, var, EPS))
        ch.step("vector", lambda e: e.reciprocal(rec, var))
        ch.step("scalar", lambda e: e.activation(rstd, rec, AF.Sqrt))
        ch.step("scalar", lambda e: e.activation(nmr, mu, AF.Copy, scale=rstd))
        ch.step("vector", lambda e: e.tensor_scalar(xnf, xt, rstd, nmr, ALU.mult, ALU.subtract))
        if "bf" in outs or "lo" in outs:
            ch.step("vector", lambda e: e.tensor_copy(xnb, xnf))
            ch.step("gpsimd", lambda e: e.dma_start(out=outs["bf"][r0:r0 + 128, :], in_=xnb), 16)
        if "lo" in outs:
            ch.step("vector", lambda e: e.tensor_sub(xnl, xnf, xnb))
            ch.step("gpsimd", lambda e: e.dma_start(out=outs["lo"][r0:r0 + 128, :], in_=xnl), 16)
        if "f32" in outs:
            ch.step("gpsimd", lambda e: e.dma_start(out=outs["f32"][r0:r0 + 128, :], in_=xnf), 16)
        if "x32" in outs:
            ch.step("gpsimd", lambda e: e.dma_start(out=outs["x32"][r0:r0 + 128, :], in_=xt), 16)
    ch.wait_all()


def p0_ln0(ctx, en, eng):
    x = ctx.t("x_in").rearrange("b s d -> (b s) d")
    ln_tiles(ctx, en, eng, "p0", x, NT,
             {"bf": ctx.t("xn0"), "lo": ctx.t("xn0lo")})


def p2_gate(ctx, en, eng):
    sb, ps, cs = ctx.sb, ctx.ps, ctx.consts
    ch = Chain(ctx, en, eng, "p2")
    xn0, xn0lo, comb = ctx.t("xn0"), ctx.t("xn0lo"), ctx.t("comb")
    gw = ctx.t("w_gate")
    E = sb["E"].ap()
    gw_f = E[0:8, 0:1024]
    gwT_f = E[:, 1024:1024 + 64]
    gwT_hi = sb["H"].ap()[:, 0:64]
    gwT_lo = sb["H"].ap()[:, 64:128]
    idf = sb["I"].ap()[:, 0:128]
    idb = sb["H"].ap()[:, 1024 - 128:1024]

    ch.step("sync", lambda e: e.dma_start(out=gw_f, in_=gw), 16)
    ch.step("sync", lambda e: e.dma_start(out=idf, in_=cs["ident_f"]), 16)
    ch.step("sync", lambda e: e.dma_start(out=idb, in_=cs["ident_b"]), 16)
    for c in range(DCH):
        cc = c
        ch.step("tensor", lambda e: e.transpose(
            ps["p0"].ap()[:, 0:8], gw_f[0:8, cc * 128:(cc + 1) * 128], idf[0:8, 0:8]))
        ch.step("vector", lambda e: e.tensor_copy(gwT_f[:, cc * 8:(cc + 1) * 8], ps["p0"].ap()[:, 0:8]))
    ch.step("vector", lambda e: e.tensor_copy(gwT_hi, gwT_f))
    ch.step("vector", lambda e: e.tensor_sub(gwT_lo, gwT_f, gwT_hi))

    A = sb["A"].ap()
    for i in range(T // 512):
        t0 = i * 512
        xhi = A[:, 0:8 * 512].rearrange("p (c t) -> p c t", c=8)
        xlo = A[:, 8 * 512:16 * 512].rearrange("p (c t) -> p c t", c=8)
        fns = []
        for c in range(DCH):
            fns.append(lambda e, cc=c: e.dma_start_transpose(
                out=xhi[:, cc, :], in_=xn0[t0:t0 + 512, cc * 128:(cc + 1) * 128]))
            fns.append(lambda e, cc=c: e.dma_start_transpose(
                out=xlo[:, cc, :], in_=xn0lo[t0:t0 + 512, cc * 128:(cc + 1) * 128]))
        ch.dma_group("sync", fns)
        for q in range(4):
            qq = q
            L = ps["p0"].ap()[:, 0:8]

            def mm_logits(e, q=qq):
                first = True
                for c in range(DCH):
                    for lhs, rhs in ((xhi, gwT_hi), (xhi, gwT_lo), (xlo, gwT_hi)):
                        ins = e.matmul(L, lhs[:, c, q * 128:(q + 1) * 128],
                                       rhs[:, c * 8:(c + 1) * 8],
                                       start=first, stop=(c == DCH - 1 and rhs is gwT_hi and lhs is xlo))
                        first = False
                return ins
            ch.step("tensor", mm_logits)
            G = sb["G"].ap()
            Lf = G[:, 32:40]
            nm1 = G[:, 40:41]
            nm2 = G[:, 41:42]
            eq = G[:, 42:50]
            l2 = G[:, 50:58]
            w = G[:, 58:66]
            ssum = G[:, 66:67]
            rs = G[:, 67:68]
            combf = G[:, 68:76]
            combb = sb["F"].ap()[0:128, 2048:2056]
            ch.step("vector", lambda e: e.tensor_copy(Lf, L))
            ch.step("vector", lambda e: e.reduce_max(nm1, Lf, AX.X, negate=True))
            ch.step("vector", lambda e: e.tensor_scalar(eq, Lf, nm1, 0.0, ALU.add, ALU.is_equal))
            ch.step("vector", lambda e: e.scalar_tensor_tensor(l2, eq, NEG, Lf, ALU.mult, ALU.add))
            ch.step("vector", lambda e: e.reduce_max(nm2, l2, AX.X, negate=True))
            ch.step("vector", lambda e: e.tensor_scalar(eq, Lf, nm2, 0.0, ALU.add, ALU.is_ge))
            ch.step("scalar", lambda e: e.activation(l2, Lf, AF.Exp, bias=nm1))
            ch.step("vector", lambda e: e.tensor_mul(w, l2, eq))
            ch.step("vector", lambda e: e.reduce_sum(ssum, w, AX.X))
            ch.step("vector", lambda e: e.reciprocal(rs, ssum))
            ch.step("vector", lambda e: e.tensor_scalar(combf, w, rs, None, ALU.mult))
            ch.step("vector", lambda e: e.tensor_copy(combb, combf))
            ch.step("tensor", lambda e: e.transpose(ps["tb0"].ap()[0:8, 0:128], combb, idb))
            ch.step("vector", lambda e: e.tensor_copy(sb["F"].ap()[0:8, 2056:2184], ps["tb0"].ap()[0:8, 0:128]))
            ch.step("gpsimd", lambda e: e.dma_start(
                out=comb[0:8, t0 + qq * 128:t0 + (qq + 1) * 128],
                in_=sb["F"].ap()[0:8, 2056:2184]), 16)
    ch.wait_all()


def load_wT(ch, ctx, w_dram, dest, stage, idb, tbp):
    ch.step("gpsimd", lambda e: e.dma_start(
        out=stage, in_=w_dram.rearrange("(oc p) d -> p oc d", p=128)), 16)
    for oc in range(8):
        for dc in range(8):
            o_, d_ = oc, dc
            ch.step("tensor", lambda e: e.transpose(
                tbp[:, 0:128], stage[:, o_, d_ * 128:(d_ + 1) * 128], idb))
            ch.step("vector", lambda e: e.tensor_copy(
                dest[:, d_, o_ * 128:(o_ + 1) * 128], tbp[:, 0:128]))


def attn_phase(ctx, en, eng, key, xn_dram, wq, wk, wv, wo, bq, bk, bv, bo,
               out_dram, window, n_b=B_LOC, n_h=NH):
    sb, ps, cs = ctx.sb, ctx.ps, ctx.consts
    A, B, C, Dd, Ef, F, G, H, I, X = (sb[k].ap() for k in
                                      ("A", "B", "C", "D", "E", "F", "G", "H", "I", "X"))
    op, sems, waitall = mk_ops(ctx, en, eng, key)
    PE, VE, SC, LD, GP = (sems[k] for k in ("tensor", "vector", "scalar", "sync", "gpsimd"))
    Qf = A[:, 0:16384].rearrange("p (c t) -> p c t", c=8)
    Kf = A[:, 16384:32768].rearrange("p (c t) -> p c t", c=8)
    Vt = B[:, 0:16384].rearrange("p (kt o) -> p kt o", kt=16)
    Cx = B[:, 16384:32768].rearrange("p (c t) -> p c t", c=8)
    Xt = X[:, 0:4096].rearrange("p (c t) -> p c t", c=8)
    wT0 = C[:, 0:8192].rearrange("p (c o) -> p c o", c=8)
    wT1 = C[:, 8192:16384].rearrange("p (c o) -> p c o", c=8)
    stage0 = B[:, 0:8192].rearrange("p (oc d) -> p oc d", oc=8)
    stageV = B[:, 16384:24576].rearrange("p (oc d) -> p oc d", oc=8)
    stageO = A[:, 0:8192].rearrange("p (oc d) -> p oc d", oc=8)
    idb = H[:, 896:1024]
    maskc = I[:, 0:128]
    maskf = Dd[:, 2560:2688]
    bcols = G[:, 456:488].rearrange("p (w c) -> p w c", w=4)
    qcol = G[:, 448:456]
    tb = [ps["tb0"].ap(), ps["tb1"].ap()]
    has_bias = bq is not None
    K_ = key
    nW = window // 128 if window is not None else None

    def kt_lo_of(qi):
        return 0 if nW is None else max(0, qi - nW)

    def nkt_of(gi):
        qi = gi % 16
        return qi + 1 - kt_lo_of(qi)

    def pt_eng(j):
        return VE if j % 2 == 0 else SC

    def last_pt_deps(b, gi):
        nk = nkt_of(gi)
        d = [(f"{K_}pt{b}_{gi}_{nk-1}", pt_eng(nk - 1))]
        if nk >= 2:
            d.append((f"{K_}pt{b}_{gi}_{nk-2}", pt_eng(nk - 2)))
        return d

    tbdep = [[], []]

    LD2, GP2, GPW = sems["sync2"], sems["gpsimd2"], sems["gpsimdw"]
    op("sync2", lambda e: e.dma_start(out=idb, in_=cs["ident_b"]))
    op("sync2", lambda e: e.dma_start(out=maskc, in_=cs["mask_causal"]))
    if window is not None:
        op("sync2", lambda e: e.dma_start(out=maskf, in_=cs["mask_far"]))
    if has_bias:
        brows = ctx.t("b5rows")
        for w_i, bt in enumerate((bq, bk, bv, bo)):
            wi, btt = w_i, bt
            op("gpsimd", lambda e, wi=wi, btt=btt: e.dma_start(
                out=brows[wi:wi + 1, :], in_=btt[None, :]), mark=f"{K_}brow{wi}")
        op("sync2", lambda e: e.dma_start_transpose(
            out=H[:, 512:544], in_=brows.rearrange("w (c p) -> (w c) p", p=128)),
            deps=[(f"{K_}brow3", GP)])
        op("vector", lambda e: e.tensor_copy(G[:, 456:488], H[:, 512:544]),
           deps=[(f"{K_}cst", sems["sync2"])], mark=f"{K_}bcf")
    op("sync2", lambda e: e.nop(), mark=f"{K_}cst") if False else None
    if en == "sync":
        ctx.mark(f"{K_}cst")
    if has_bias:
        op("vector", lambda e: e.tensor_scalar(qcol, bcols[:, 0, :], 0.125, None, ALU.mult),
           deps=[(f"{K_}bcf", sems["vector"])], mark=f"{K_}qcol")

    last_stage = [None]

    def load_w(b, wname, w_dram, dest, stage, stage_dep):
        mk = f"{K_}w{b}{wname}"
        sdep = list(stage_dep)
        if last_stage[0]:
            sdep.append((last_stage[0] + "st", GPW))
            sdep.append((last_stage[0] + "tr63", PE))
        op("gpsimdw", lambda e: e.dma_start(
            out=stage, in_=w_dram.rearrange("(oc p) d -> p oc d", p=128)),
            deps=sdep, mark=mk + "st")
        last_stage[0] = mk
        for i in range(64):
            o_, d_ = i // 8, i % 8
            ii = i
            if i >= 2:
                deps = [(mk + f"cp{ii-2}", VE)]
            else:
                deps = [(mk + "st", GPW), (f"{K_}cst", LD2)] + tbdep[i % 2]
            op("tensor", lambda e, o_=o_, d_=d_, ii=ii: e.transpose(
                tb[ii % 2][:, 0:128], stage[:, o_, d_ * 128:(d_ + 1) * 128], idb),
                deps=deps, mark=mk + f"tr{ii}")
            op("vector", lambda e, o_=o_, d_=d_, ii=ii: e.tensor_copy(
                dest[:, d_, o_ * 128:(o_ + 1) * 128], tb[ii % 2][:, 0:128]),
                deps=[(mk + f"tr{ii}", PE)], mark=mk + f"cp{ii}")
        tbdep[0] = [(mk + "cp62", VE)]
        tbdep[1] = [(mk + "cp63", VE)]
        return mk + "cp63"

    for b in range(n_b):
        tbse = b * S
        dep_prevav = [(f"{K_}av{b-1}_{n_h-1}_15", PE)] if b > 0 else []
        wq_done = load_w(b, "q", wq, wT0, stage0, dep_prevav)
        wk_done = load_w(b, "k", wk, wT1, stage0, [])

        for tt in range(4):
            t0 = tt * 512
            if tt > 0:
                xdeps = [(f"{K_}qkm{b}_{tt-1}_15", PE)]
            elif b > 0:
                xdeps = [(f"{K_}av{b-1}_{n_h-1}_15", PE)]
            else:
                xdeps = []
            for c in range(DCH):
                cc = c
                op("sync", lambda e, cc=cc, t0=t0: e.dma_start_transpose(
                    out=Xt[:, cc, :], in_=xn_dram[tbse + t0:tbse + t0 + 512,
                                                  cc * 128:(cc + 1) * 128]),
                    deps=xdeps if cc == 0 else [],
                    mark=f"{K_}qkx{b}_{tt}" if cc == DCH - 1 else None)
            for g in range(16):
                wh, o_ = (0, g) if g < 8 else (1, g - 8)
                wT = wT0 if wh == 0 else wT1
                pp = ps["p0"].ap() if g % 2 == 0 else ps["p1"].ap()
                gg = g
                deps = [(f"{K_}qkx{b}_{tt}", LD)]
                if g == 0 and tt == 0:
                    deps.append((wk_done, VE))
                    if b > 0:
                        deps.append((f"{K_}cx{b-1}_{n_h*16-1}", SC))
                if g >= 2:
                    deps.append((f"{K_}qke{b}_{tt}_{gg-2}", SC))
                elif tt > 0:
                    deps.append((f"{K_}qke{b}_{tt-1}_{gg+14}", SC))

                def mmqk(e, wT=wT, o_=o_, pp=pp):
                    for dc in range(DCH):
                        ins = e.matmul(pp[:], wT[:, dc, o_ * 128:(o_ + 1) * 128],
                                       Xt[:, dc, :], start=(dc == 0), stop=(dc == DCH - 1))
                    return ins
                op("tensor", mmqk, deps=deps, mark=f"{K_}qkm{b}_{tt}_{gg}")
                dst = Qf if wh == 0 else Kf
                sc_ = 0.125 if wh == 0 else 1.0
                if has_bias:
                    bias_ap = qcol[:, o_:o_ + 1] if wh == 0 else bcols[:, 1, o_:o_ + 1]
                else:
                    bias_ap = 0.0
                edeps = [(f"{K_}qkm{b}_{tt}_{gg}", PE)]
                if has_bias and b == 0 and tt == 0 and gg == 0:
                    edeps.append((f"{K_}qcol", VE))
                    edeps.append((f"{K_}cst", LD2))
                op("scalar", lambda e, dst=dst, o_=o_, t0=t0, pp=pp, sc_=sc_, bias_ap=bias_ap:
                    e.activation(dst[:, o_, t0:t0 + 512], pp[:], AF.Identity, bias=bias_ap, scale=sc_),
                    deps=edeps, mark=f"{K_}qke{b}_{tt}_{gg}")

        wv_done = load_w(b, "v", wv, wT0, stageV, [(f"{K_}qkm{b}_3_7", PE)])
        for tt in range(4):
            t0 = tt * 512
            xdeps = [(f"{K_}qkm{b}_3_15" if tt == 0 else f"{K_}vm{b}_{tt-1}_7", PE)]
            for c in range(DCH):
                cc = c
                op("sync", lambda e, cc=cc, t0=t0: e.dma_start_transpose(
                    out=Xt[:, cc, :], in_=xn_dram[tbse + t0:tbse + t0 + 512,
                                                  cc * 128:(cc + 1) * 128]),
                    deps=xdeps if cc == 0 else [],
                    mark=f"{K_}vx{b}_{tt}" if cc == DCH - 1 else None)
            for g in range(8):
                qq, og = g // 2, g % 2
                pp = ps["p0"].ap() if g % 2 == 0 else ps["p1"].ap()
                gg = g
                deps = [(f"{K_}vx{b}_{tt}", LD)]
                if g == 0 and tt == 0:
                    deps.append((wv_done, VE))
                if g < 2 and tt == 0:
                    deps.append((f"{K_}qke{b}_3_{14 + gg}", SC))
                if g >= 2:
                    deps.append((f"{K_}ve{b}_{tt}_{gg-2}", VE))
                elif tt > 0:
                    deps.append((f"{K_}ve{b}_{tt-1}_{gg+6}", VE))

                def mmv(e, qq=qq, og=og, pp=pp):
                    for dc in range(DCH):
                        ins = e.matmul(pp[:], Xt[:, dc, qq * 128:(qq + 1) * 128],
                                       wT0[:, dc, og * 512:(og + 1) * 512],
                                       start=(dc == 0), stop=(dc == DCH - 1))
                    return ins
                op("tensor", mmv, deps=deps, mark=f"{K_}vm{b}_{tt}_{gg}")
                kt = tt * 4 + qq
                op("vector", lambda e, kt=kt, og=og, pp=pp: e.tensor_copy(
                    Vt[:, kt, og * 512:(og + 1) * 512], pp[:]),
                    deps=[(f"{K_}vm{b}_{tt}_{gg}", PE)], mark=f"{K_}ve{b}_{tt}_{gg}")

        rowbs = [Dd[:, 0:2048], Ef[:, 0:2048], C[:, 8192:16384].bitcast(F32)]
        Pbs = [F[:, 0:2048], X[:, 0:2048], C[:, 0:2048]]
        Pts = [F[:, 2048:4096], X[:, 2048:4096], C[:, 2048:4096]]

        def row_env(gi):
            h, qi = gi // 16, gi % 16
            sl = gi % 3
            psA = ps[f"p{2*sl}"].ap()
            psB = ps[f"p{2*sl+1}"].ap()
            st8 = G[:, sl * 8:sl * 8 + 8]
            return dict(h=h, qi=qi, sl=sl, hc=h // 2, hr=(h % 2) * 64,
                        rowb=rowbs[sl], Pb=Pbs[sl], Pt=Pts[sl], psA=psA, psB=psB,
                        tbA=psA.bitcast(BF16), tbB=psB.bitcast(BF16),
                        pav=psA[0:64, 0:128],
                        negm=st8[:, 0:1], rsum=st8[:, 1:2], rinv=st8[:, 2:3],
                        kt_lo=kt_lo_of(qi), L0=kt_lo_of(qi) * 128,
                        L=(qi + 1) * 128 - kt_lo_of(qi) * 128)

        def emit_scores(gi):
            v = row_env(gi)
            h, qi, sl = v["h"], v["qi"], v["sl"]
            qs = Qf[v["hr"]:v["hr"] + 64, v["hc"], :]
            ks = Kf[v["hr"]:v["hr"] + 64, v["hc"], :]
            far = nW is not None and qi - v["kt_lo"] == nW
            L0, L = v["L0"], v["L"]
            pgi = gi - 3
            ng = (L + 511) // 512
            for kg in range(ng):
                k0 = L0 + kg * 512
                n = min(512, (qi + 1) * 128 - k0)
                pp = v["psA"] if kg % 2 == 0 else v["psB"]
                deps = []
                if kg >= 2:
                    deps.append((f"{K_}sev{b}_{gi}_{kg-2}", VE))
                elif pgi >= 0:
                    deps += last_pt_deps(b, pgi)
                    deps.append((f"{K_}cx{b}_{pgi}", SC))
                if gi == 0:
                    deps.append((f"{K_}qke{b}_3_15", SC))
                    deps.append((f"{K_}ve{b}_3_7", VE))
                    if b == 0:
                        deps.append((f"{K_}cst", LD2))
                if gi == 1 and b > 0:
                    deps.append((f"{K_}ops{b-1}_31", GP2))
                op("tensor", lambda e, k0=k0, n=n, pp=pp, qs=qs, ks=ks, qi=qi:
                    e.matmul(pp[:, 0:n], qs[:, qi * 128:(qi + 1) * 128],
                             ks[:, k0:k0 + n], start=True, stop=True),
                    deps=deps, mark=f"{K_}ss{b}_{gi}_{kg}")
                off0 = k0 - L0
                segs = [[0, n, None]]
                if kg == ng - 1:
                    segs = ([[0, n - 128, None]] if n > 128 else []) + [[n - 128, 128, "c"]]
                if far and kg == 0:
                    if segs[0][2] is None:
                        hn = segs[0][1]
                        rest = ([[128, hn - 128, None]] if hn > 128 else []) + segs[1:]
                    else:
                        rest = []
                    segs = [[0, 128, "f"]] + rest
                segs = [sg for sg in segs if sg[1] > 0]
                use_sc = False
                for si, (o, nn, mk) in enumerate(segs):
                    oo, nn_, mk_ = o, nn, mk
                    deps = [(f"{K_}ss{b}_{gi}_{kg}", PE)]
                    if kg == 0 and si == 0 and pgi >= 0:
                        deps.append((f"{K_}xp{b}_{pgi}", SC))
                    mkname = f"{K_}sev{b}_{gi}_{kg}" if si == len(segs) - 1 else None
                    rowb = v["rowb"]
                    if mk_ is None and use_sc:
                        op("scalar", lambda e, pp=pp, oo=oo, nn_=nn_, off0=off0, rowb=rowb:
                            e.copy(rowb[:, off0 + oo:off0 + oo + nn_], pp[:, oo:oo + nn_]),
                            deps=deps, mark=mkname)
                    elif mk_ is None:
                        op("vector", lambda e, pp=pp, oo=oo, nn_=nn_, off0=off0, rowb=rowb:
                            e.tensor_copy(rowb[:, off0 + oo:off0 + oo + nn_],
                                          pp[:, oo:oo + nn_]),
                            deps=deps, mark=mkname)
                    else:
                        msk = maskc if mk_ == "c" else maskf
                        op("vector", lambda e, pp=pp, oo=oo, nn_=nn_, off0=off0, msk=msk, rowb=rowb:
                            e.tensor_add(rowb[:, off0 + oo:off0 + oo + nn_],
                                         pp[:, oo:oo + nn_], msk),
                            deps=deps, mark=mkname)
            return ng

        def emit_softmax(gi, ng):
            v = row_env(gi)
            pgi = gi - 3
            rowb, Pb, L = v["rowb"], v["Pb"], v["L"]
            negm, rsum, rinv = v["negm"], v["rsum"], v["rinv"]
            mdeps = [(f"{K_}sev{b}_{gi}_{ng-1}", VE)]
            op("vector", lambda e, rowb=rowb, L=L, negm=negm:
                e.reduce_max(negm, rowb[:, 0:L], AX.X, negate=True),
                deps=mdeps, mark=f"{K_}rw{b}_{gi}")
            deps = [(f"{K_}rw{b}_{gi}", VE)]
            if pgi >= 0:
                deps.append((f"{K_}tp{b}_{pgi}", PE))
            op("scalar", lambda e, Pb=Pb, rowb=rowb, L=L, negm=negm, rsum=rsum:
                e.activation(Pb[:, 0:L], rowb[:, 0:L], AF.Exp, bias=negm, accum_out=rsum),
                deps=deps, mark=f"{K_}xp{b}_{gi}")
            op("vector", lambda e, rinv=rinv, rsum=rsum: e.reciprocal(rinv, rsum),
                deps=[(f"{K_}xp{b}_{gi}", SC)], mark=f"{K_}rc{b}_{gi}")
            op("vector", lambda e, Pb=Pb, L=L, rinv=rinv: e.tensor_scalar(
                Pb[:, 0:L], Pb[:, 0:L], rinv, None, ALU.mult),
                deps=[(f"{K_}rc{b}_{gi}", VE)], mark=f"{K_}pn{b}_{gi}")

        def emit_transposes(gi):
            v = row_env(gi)
            Pb, Pt = v["Pb"], v["Pt"]
            nkt = nkt_of(gi)
            for j in range(nkt):
                jj = j
                tpp = (v["tbA"] if j % 2 == 0 else v["tbB"])[:, 0:128]
                if j >= 2:
                    deps = [(f"{K_}pt{b}_{gi}_{jj-2}", pt_eng(jj - 2))]
                else:
                    deps = [(f"{K_}pn{b}_{gi}", VE)]
                op("tensor", lambda e, jj=jj, tpp=tpp, Pb=Pb: e.transpose(
                    tpp, Pb[:, jj * 128:(jj + 1) * 128], idb),
                    deps=deps, mark=f"{K_}tr{b}_{gi}_{jj}")
                if en == "tensor" and jj == nkt - 1:
                    ctx.mark(f"{K_}tp{b}_{gi}")
                if jj % 2 == 0:
                    op("vector", lambda e, tpp=tpp, jj=jj, Pt=Pt: e.tensor_copy(
                        Pt[:, jj * 128:(jj + 1) * 128], tpp),
                        deps=[(f"{K_}tr{b}_{gi}_{jj}", PE)], mark=f"{K_}pt{b}_{gi}_{jj}")
                else:
                    op("scalar", lambda e, tpp=tpp, jj=jj, Pt=Pt: e.copy(
                        Pt[:, jj * 128:(jj + 1) * 128], tpp),
                        deps=[(f"{K_}tr{b}_{gi}_{jj}", PE)], mark=f"{K_}pt{b}_{gi}_{jj}")

        def emit_av(gi):
            v = row_env(gi)
            h, qi = v["h"], v["qi"]
            Pt, pav, kt_lo = v["Pt"], v["pav"], v["kt_lo"]
            nkt = nkt_of(gi)
            deps = last_pt_deps(b, gi)

            def mm_av(e, nkt=nkt, kt_lo=kt_lo, h=h, pav=pav, Pt=Pt):
                for j in range(nkt):
                    ins = e.matmul(pav, Vt[:, kt_lo + j, h * 64:(h + 1) * 64],
                                   Pt[:, j * 128:(j + 1) * 128],
                                   start=(j == 0), stop=(j == nkt - 1))
                return ins
            op("tensor", mm_av, deps=deps, mark=f"{K_}av{b}_{h}_{qi}")

        def emit_cx(gi):
            v = row_env(gi)
            h, qi = v["h"], v["qi"]
            pav = v["pav"]
            bias_ap = (bcols[(h % 2) * 64:(h % 2) * 64 + 64, 2, v["hc"]:v["hc"] + 1]
                       if has_bias else 0.0)
            op("scalar", lambda e, hc=v["hc"], hr=v["hr"], qi=qi, pav=pav, bias_ap=bias_ap:
                e.activation(Cx[hr:hr + 64, hc, qi * 128:(qi + 1) * 128], pav,
                             AF.Identity, bias=bias_ap),
                deps=[(f"{K_}av{b}_{h}_{qi}", PE)], mark=f"{K_}cx{b}_{gi}")

        nrows = n_h * 16
        for i2 in range(nrows + 3):
            if i2 >= 3 and i2 - 3 < nrows:
                emit_cx(i2 - 3)
            if i2 < nrows:
                ng = emit_scores(i2)
            if i2 >= 2 and i2 - 2 < nrows:
                emit_transposes(i2 - 2)
            if i2 < nrows:
                emit_softmax(i2, ng)
            if i2 >= 2 and i2 - 2 < nrows:
                emit_av(i2 - 2)

        gl = n_h * 16 - 1
        wo_done = load_w(b, "o", wo, wT0, stageO, [(f"{K_}av{b}_{n_h-1}_15", PE)])
        for g in range(32):
            tt, o_ = g // 8, g % 8
            t0 = tt * 512
            pp = ps["p0"].ap() if g % 2 == 0 else ps["p1"].ap()
            gg = g
            deps = []
            if g == 0:
                deps.append((wo_done, VE))
                deps.append((f"{K_}cx{b}_{gl}", SC))
            if g >= 2:
                deps.append((f"{K_}ope{b}_{gg-2}", VE))

            def mmo(e, o_=o_, pp=pp, t0=t0):
                for dc in range(DCH):
                    ins = e.matmul(pp[:], wT0[:, dc, o_ * 128:(o_ + 1) * 128],
                                   Cx[:, dc, t0:t0 + 512], start=(dc == 0), stop=(dc == DCH - 1))
                return ins
            op("tensor", mmo, deps=deps, mark=f"{K_}opm{b}_{gg}")
            ev = Dd[:, 2048:2560] if g % 2 == 0 else Ef[:, 0:512]
            deps = [(f"{K_}opm{b}_{gg}", PE)]
            if g >= 2:
                deps.append((f"{K_}ops{b}_{gg-2}", GP if (gg - 2) % 2 == 0 else GP2))
            elif b > 0:
                deps.append((f"{K_}ops{b-1}_{30 + gg}", GP if gg % 2 == 0 else GP2))
            if has_bias:
                op("vector", lambda e, ev=ev, pp=pp, o_=o_: e.tensor_scalar(
                    ev, pp[:], bcols[:, 3, o_:o_ + 1], None, ALU.add),
                    deps=deps, mark=f"{K_}ope{b}_{gg}")
            else:
                op("vector", lambda e, ev=ev, pp=pp: e.tensor_copy(ev, pp[:]),
                    deps=deps, mark=f"{K_}ope{b}_{gg}")
            kwargs = {"accum_op": ALU.add} if window is None else {}
            op("gpsimd" if g % 2 == 0 else "gpsimd2",
               lambda e, ev=ev, o_=o_, t0=t0, kwargs=kwargs: e.dma_start(
                out=out_dram[o_ * 128:(o_ + 1) * 128, tbse + t0:tbse + t0 + 512],
                in_=ev, **kwargs),
                deps=[(f"{K_}ope{b}_{gg}", VE)], mark=f"{K_}ops{b}_{gg}")
    waitall()


def p3z_zero_acc(ctx, en, eng):
    sb = ctx.sb
    ch = Chain(ctx, en, eng, "p3z")
    accf = ctx.t("accf")
    z = sb["D"].ap()[:, 0:2048]
    ch.step("vector", lambda e: e.memset(z, 0.0))
    for dc in range(DCH):
        for half in range(T // 2048):
            d0, h0 = dc * 128, half * 2048
            ch.step("sync", lambda e: e.dma_start(
                out=accf[d0:d0 + 128, h0:h0 + 2048], in_=z), 16)
    ch.wait_all()


def p3_moe(ctx, en, eng, n_tt=NMT):
    sb, ps, cs = ctx.sb, ctx.ps, ctx.consts
    A, B, C, Dd, F, X = (sb[k].ap() for k in ("A", "B", "C", "D", "F", "X"))
    xn0, comb, accf = ctx.t("xn0"), ctx.t("comb"), ctx.t("accf")
    w1 = ctx.t("w_moe1")
    w2 = ctx.t("w_moe2")
    sw1 = ctx.t("w_sw1")
    sw2 = ctx.t("w_sw2")
    b2 = ctx.t("b_moe2")
    sb2 = ctx.t("b_ssb2")
    b1b = ctx.t("b1b")
    b1 = ctx.t("b_moe1")
    sb1 = ctx.t("b_ssb1")

    onehot = F[0:8, 0:1024]
    Be = F[:, 1024:1536]
    combst = F[0:8, 1536:2048]
    b2r = F[0:8, 2048:3072]
    sb2r = F[0:1, 3072:4096]
    onesr = sb["H"].ap()[0:1, 0:512]
    b1col = F[:, 3584:3616]

    ch = Chain(ctx, en, eng, "p3p")
    ch.step("gpsimd", lambda e: e.dma_start(out=b1b[0:8, :], in_=b1), 16)
    ch.step("gpsimd", lambda e: e.dma_start(out=b1b[8:9, :], in_=sb1[None, :]), 16)
    ch.step("gpsimd", lambda e: e.dma_start(out=b2r, in_=b2), 16)
    ch.step("gpsimd", lambda e: e.dma_start(out=sb2r, in_=sb2[None, :]), 16)
    ch.step("sync", lambda e: e.dma_start(out=onehot, in_=cs["onehot8"]), 16)
    ch.step("sync", lambda e: e.dma_start(out=onesr, in_=cs["ones_row_b"]), 16)
    ch.wait_all()

    Aw = A[:, 0:8 * 4096].rearrange("p (c f) -> p c f", c=8)
    Bw = B[:, 0:32 * 1024].rearrange("p (c o) -> p c o", c=32)
    Hb = C[:, 0:32 * 512].rearrange("p (c t) -> p c t", c=32)
    Xt = X[:, 0:8 * 512].rearrange("p (c t) -> p c t", c=8)

    for m in range(9):
        shared = (m == 0)
        e_id = m - 1
        if en == "gpsimd":
            if m > 0:
                ctx.wait(eng, "mpe", ctx.at(f"m_e{m-1}_end", "mpe"))
            src1 = sw1 if shared else w1[e_id]
            src2 = sw2 if shared else w2[e_id]
            ctx.inc(eng.dma_start(out=Aw, in_=src1.rearrange("(c p) f -> p c f", p=128)), "mgw", 16)
            ctx.inc(eng.dma_start(out=Bw, in_=src2.rearrange("(c p) o -> p c o", p=128)), "mgw", 16)
            ctx.mark(f"m_w_{m}")
        if en == "sync":
            if m > 0:
                ctx.wait(eng, "msc", ctx.at(f"m_gelu_end_{m-1}", "msc"))
            ctx.inc(eng.dma_start_transpose(
                out=b1col, in_=b1b[8 if shared else e_id].rearrange("(c p) -> c p", p=128)), "mlb", 16)
            ctx.mark(f"m_b1_{m}")

        for it in range(n_tt):
            t0 = it * MTT
            p1a = ps["p0"].ap()
            p1b = ps["p1"].ap()
            p2a = ps["p2"].ap()
            p2b = ps["p3"].ap()
            pbc = ps["p4"].ap()

            if en == "sync":
                if it > 0:
                    ctx.wait(eng, "mpe", ctx.at(f"m_s1_{m}_{it-1}_31", "mpe"))
                elif m > 0:
                    ctx.wait(eng, "mpe", ctx.at(f"m_e{m-1}_end", "mpe"))
                for c in range(DCH):
                    ctx.inc(eng.dma_start_transpose(
                        out=Xt[:, c, :], in_=xn0[t0:t0 + MTT, c * 128:(c + 1) * 128]), "mlx", 16)
                ctx.mark(f"m_x_{m}_{it}")

            if not shared:
                if en == "gpsimd":
                    if m == 8 and it > 0:
                        ctx.wait(eng, "mpe", ctx.at(f"m_s2_{8}_{it-1}_{7}", "mpe"))
                    elif not (m == 1 and it == 0):
                        ctx.wait(eng, "mpe", ctx.at(f"m_bc_prev_{m}_{it}", "mpe"))
                    ctx.inc(eng.dma_start(out=combst, in_=comb[0:8, t0:t0 + MTT]), "mgc", 16)
                    ctx.mark(f"m_compst_{m}_{it}")
                if en == "tensor":
                    ctx.wait(eng, "mgc", ctx.at(f"m_compst_{m}_{it}", "mgc"))
                    if not (m == 1 and it == 0):
                        ctx.wait(eng, "mev", ctx.at(f"m_bee_prev_{m}_{it}", "mev"))
                    ctx.inc(eng.matmul(pbc[:], onehot[:, e_id * 128:(e_id + 1) * 128],
                                       combst, start=True, stop=True), "mpe", 1)
                    ctx.mark(f"m_bc_{m}_{it}")
                    ctx.mark(f"m_bc_prev_{m}_{it + 1}" if it + 1 < n_tt else f"m_bc_prev_{m + 1}_0")
                if en == "vector":
                    ctx.wait(eng, "mpe", ctx.at(f"m_bc_{m}_{it}", "mpe"))
                    ctx.inc(eng.tensor_copy(Be, pbc[:]), "mev", 1)
                    ctx.mark(f"m_bee_{m}_{it}")
                    ctx.mark(f"m_bee_prev_{m}_{it + 1}" if it + 1 < n_tt else f"m_bee_prev_{m + 1}_0")

            for fc in range(NFC):
                pp = p1a if fc % 2 == 0 else p1b
                if en == "tensor":
                    ctx.wait(eng, "mlx", ctx.at(f"m_x_{m}_{it}", "mlx"))
                    if fc == 0:
                        ctx.wait(eng, "mgw", ctx.at(f"m_w_{m}", "mgw"))
                    if fc >= 2:
                        ctx.wait(eng, "msc", ctx.at(f"m_g_{m}_{it}_{fc-2}", "msc"))
                    elif it > 0:
                        ctx.wait(eng, "msc", ctx.at(f"m_g_{m}_{it-1}_{31 if fc == 0 else 31}", "msc") - (1 - fc))
                    elif m > 0:
                        ctx.wait(eng, "msc", ctx.at(f"m_gelu_end_{m-1}", "msc"))
                    for dc in range(DCH):
                        ins = eng.matmul(pp[:], Aw[:, dc, fc * 128:(fc + 1) * 128],
                                         Xt[:, dc, :], start=(dc == 0), stop=(dc == DCH - 1))
                    ctx.inc(ins, "mpe", 1)
                    ctx.mark(f"m_s1_{m}_{it}_{fc}")
                if en == "scalar":
                    ctx.wait(eng, "mpe", ctx.at(f"m_s1_{m}_{it}_{fc}", "mpe"))
                    ctx.wait(eng, "mlb", ctx.at(f"m_b1_{m}", "mlb"))
                    ctx.inc(eng.activation(Hb[:, fc, :], (p1a if fc % 2 == 0 else p1b)[:],
                                           AF.Gelu, bias=b1col[:, fc:fc + 1]), "msc", 1)
                    ctx.mark(f"m_g_{m}_{it}_{fc}")
                if not shared and en == "vector":
                    ctx.wait(eng, "msc", ctx.at(f"m_g_{m}_{it}_{fc}", "msc"))
                    ctx.wait(eng, "mev", ctx.at(f"m_bee_{m}_{it}", "mev"))
                    ctx.inc(eng.tensor_mul(Hb[:, fc, :], Hb[:, fc, :], Be), "mev", 1)
                    ctx.mark(f"m_sc_{m}_{it}_{fc}")
            if en in ("scalar",) and it == n_tt - 1:
                ctx.mark(f"m_gelu_end_{m}")

            for oc in range(8):
                pp = p2a if oc % 2 == 0 else p2b
                if en == "tensor":
                    if not shared:
                        ctx.wait(eng, "mev", ctx.at(f"m_sc_{m}_{it}_{31}", "mev"))
                    else:
                        ctx.wait(eng, "msc", ctx.at(f"m_g_{m}_{it}_{31}", "msc"))
                    if oc >= 2:
                        ctx.wait(eng, "mev", ctx.at(f"m_ev_{m}_{it}_{oc-2}", "mev"))
                    elif it > 0:
                        ctx.wait(eng, "mev", ctx.at(f"m_ev_{m}_{it-1}_{6 + oc}", "mev"))
                    elif m > 0:
                        ctx.wait(eng, "mev", ctx.at(f"m_ev_{m-1}_{n_tt-1}_{6 + oc}", "mev"))
                    bias_here = (m == 8)
                    for fc in range(NFC):
                        last = (fc == NFC - 1) and not bias_here
                        ins = eng.matmul(pp[:], Bw[:, fc, oc * 128:(oc + 1) * 128],
                                         Hb[:, fc, :], start=(fc == 0), stop=last)
                    if bias_here:
                        ins = eng.matmul(pp[:], b2r[:, oc * 128:(oc + 1) * 128], combst,
                                         start=False, stop=False)
                        ins = eng.matmul(pp[:], sb2r[:, oc * 128:(oc + 1) * 128], onesr,
                                         start=False, stop=True)
                    ctx.inc(ins, "mpe", 1)
                    ctx.mark(f"m_s2_{m}_{it}_{oc}")
                if en == "vector":
                    ctx.wait(eng, "mpe", ctx.at(f"m_s2_{m}_{it}_{oc}", "mpe"))
                    dsl = Dd[:, (oc % 2) * 512:(oc % 2) * 512 + 512]
                    par = f"mst{oc % 2}"
                    if not (m == 0 and it == 0 and oc < 2):
                        key = (f"m_st_{m}_{it}_{oc-2}" if oc >= 2 else
                               (f"m_st_{m}_{it-1}_{6+oc}" if it > 0 else
                                f"m_st_{m-1}_{n_tt-1}_{6+oc}"))
                        ctx.wait(eng, par, ctx.at(key, par))
                    ctx.inc(eng.tensor_copy(dsl, pp[:]), "mev", 1)
                    ctx.mark(f"m_ev_{m}_{it}_{oc}")
                if en == "gpsimd":
                    ctx.wait(eng, "mev", ctx.at(f"m_ev_{m}_{it}_{oc}", "mev"))
                    dsl = Dd[:, (oc % 2) * 512:(oc % 2) * 512 + 512]
                    ctx.inc(eng.dma_start(out=accf[oc * 128:(oc + 1) * 128, t0:t0 + MTT],
                                          in_=dsl, accum_op=ALU.add), f"mst{oc % 2}", 16)
                    ctx.mark(f"m_st_{m}_{it}_{oc}")
        if en == "tensor":
            ctx.mark(f"m_e{m}_end")
    ctx.wait(eng, "mst0", ctx.at(f"m_st_{8}_{n_tt-1}_{6}", "mst0"))
    ctx.wait(eng, "mst1", ctx.at(f"m_st_{8}_{n_tt-1}_{7}", "mst1"))


def p4_ln1(ctx, en, eng):
    sb, ps, cs = ctx.sb, ctx.ps, ctx.consts
    Dd, Ef, F, G, I = (sb[k].ap() for k in ("D", "E", "F", "G", "I"))
    ch = Chain(ctx, en, eng, "p4")
    x = ctx.t("x_in").rearrange("b s d -> (b s) d")
    accf, x1ag, xn1 = ctx.t("accf"), ctx.t("x1ag"), ctx.t("xn1")
    idf = I[:, 0:128]
    ch.step("sync", lambda e: e.dma_start(out=idf, in_=cs["ident_f"]), 16)
    for i in range(NT):
        r0 = i * 128
        xt = Dd[:, 0:1024]
        x1t = Dd[:, 1024:2048]
        xsq = Ef[:, 1024:2048]
        xnb = F[:, 0:1024]
        st = G[:, 0:16]
        ch.dma_group("sync", [lambda e: e.dma_start(out=xt, in_=x[r0:r0 + 128, :])] + [
            (lambda e, d_=dc: e.dma_start(
                out=Ef[:, d_ * 128:(d_ + 1) * 128],
                in_=accf[d_ * 128:(d_ + 1) * 128, r0:r0 + 128]))
            for dc in range(DCH)])
        for dc in range(DCH):
            d_ = dc
            pb2 = (ps["p2"] if dc % 2 == 0 else ps["p3"]).ap()
            ch.step("tensor", lambda e, d_=d_, pb2=pb2: e.transpose(
                pb2[:, 0:128], Ef[:, d_ * 128:(d_ + 1) * 128], idf))
            ch.step("vector", lambda e, d_=d_, pb2=pb2: e.tensor_add(
                x1t[:, d_ * 128:(d_ + 1) * 128], pb2[:, 0:128],
                xt[:, d_ * 128:(d_ + 1) * 128]))
        ch.step("gpsimd", lambda e: e.dma_start(out=x1ag[r0:r0 + 128, :], in_=x1t), 16)
        ln_stats_norm(ch, x1t, xsq, st, xsq)
        ch.step("vector", lambda e: e.tensor_copy(xnb, xsq))
        ch.step("gpsimd", lambda e: e.dma_start(out=xn1[r0:r0 + 128, :], in_=xnb), 16)
    ch.wait_all()


def p6_ag1(ctx, en, eng, cc_group):
    if en == "gpsimd":
        ctx.inc(eng.collective_compute(
            "AllGather", ALU.bypass, replica_groups=[cc_group],
            ins=[ctx.t("x1ag").opt()], outs=[ctx.t("x1full").opt()]), "cc1", 1)
        ctx.wait(eng, "cc1", ctx.count("cc1"))


def rank0(ctx, eng):
    return 0 if ctx.mode == "mock" else eng.partition_id()


def dyn(ap, off_expr):
    return bass.AP(tensor=ap.tensor, offset=ap.offset + off_expr, ap=ap.ap)


def p7_ffn1(ctx, en, eng, ncc, cc_group):
    sb, ps, cs = ctx.sb, ctx.ps, ctx.consts
    A, B, C, Dd, Ef, F, G, H, I, X = (sb[k].ap() for k in
                                      ("A", "B", "C", "D", "E", "F", "G", "H", "I", "X"))
    jl = S // ncc
    njt = jl // 128
    cells = 32 * jl
    nct = cells // 128
    ch = Chain(ctx, en, eng, "p7")
    x1full = ctx.t("x1full")
    xn1s, xn1slo, t32 = ctx.t("xn1s"), ctx.t("xn1slo"), ctx.t("t32")
    probs_d = ctx.t("probs")
    qall_d, kvall_d, su_d = ctx.t("qall"), ctx.t("kvall"), ctx.t("su")
    v1d, v2d, m1d, m2d, cnt_d = (ctx.t(k) for k in ("v1d", "v2d", "m1d", "m2d", "cntd"))
    agsm_in, agsm_out = ctx.t("agsm_in"), ctx.t("agsm_out")
    ffn1ag = ctx.t("ffn1ag")
    wg, wqh, wkvh, woh = ctx.t("w1_wg"), ctx.t("w1_wq"), ctx.t("w1_wkv"), ctx.t("w1_wo")
    bqh, bkvh, boh = ctx.t("b1_bq"), ctx.t("b1_bkv"), ctx.t("b1_bo")
    lng, lnb = ctx.t("b1_lng"), ctx.t("b1_lnb")

    idf = I[:, 0:128]
    onesb = H[0:1, 0:512]
    r = [None]

    def R(e):
        if r[0] is None:
            r[0] = rank0(ctx, e)
        return r[0]

    wgHI = H[:, 512:640].rearrange("p (c h) -> p c h", c=8)
    wgLO = H[:, 640:768].rearrange("p (c h) -> p c h", c=8)
    wgF = Ef[:, 0:128].rearrange("p (c h) -> p c h", c=8)
    ch.step("sync", lambda e: e.dma_start(out=idf, in_=cs["ident_f"]), 16)
    ch.step("sync", lambda e: e.dma_start(out=onesb, in_=cs["ones_row_b"]), 16)
    ch.step("sync", lambda e: e.dma_start(
        out=wgF, in_=wg.rearrange("(c p) h -> p c h", p=128)), 16)
    ch.step("vector", lambda e: e.tensor_copy(wgHI, wgF))
    ch.step("vector", lambda e: e.tensor_sub(wgLO, wgF, wgHI))
    WqA = C[:, 0:8192].rearrange("p (c n) -> p c n", c=8)
    WoS = C[:, 8192:16384].rearrange("p (c n) -> p c n", c=8)
    WkvA = B[:, 0:16384].rearrange("p (c n) -> p c n", c=8)
    for c_ in range(DCH):
        cc = c_
        ch.step("gpsimd", lambda e, cc=cc: e.dma_start(
            out=C[:, cc * 1024:(cc + 1) * 1024].rearrange("p (h n) -> p h n", h=16),
            in_=wqh.rearrange("h (c p) n -> c p h n", p=128)[cc]), 16)
        ch.step("gpsimd", lambda e, cc=cc: e.dma_start(
            out=B[:, cc * 2048:(cc + 1) * 2048].rearrange("p (h n) -> p h n", h=16),
            in_=wkvh.rearrange("h (c p) n -> c p h n", p=128)[cc]), 16)
    ch.step("gpsimd", lambda e: e.dma_start(
        out=WoS, in_=woh.rearrange("h e o -> (h e) o").rearrange("(c p) o -> p c o", p=128)), 16)
    bqA = F[0:1, 0:1024]
    bkvA = F[0:1, 1024:3072]
    ch.step("gpsimd", lambda e: e.dma_start(out=bqA, in_=bqh.rearrange("h n -> (h n)")[None, :]), 16)
    ch.step("gpsimd", lambda e: e.dma_start(out=bkvA, in_=bkvh.rearrange("h n -> (h n)")[None, :]), 16)

    x1seq = ctx.t("x1seq")
    ch.step("gpsimd", lambda e: e.dma_start(
        out=x1seq.rearrange("(b j) d -> b j d", b=32),
        in_=dyn(x1full.rearrange("(b s) d -> b s d", b=32)[:, 0:jl, :], R(e) * (jl * D))), 16)

    for ct in range(nct):
        b, j0 = ct // njt, (ct % njt) * 128
        xt = Dd[:, 0:1024]
        xsq = Ef[:, 0:1024]
        xnf = Ef[:, 1024:2048]
        xnb = F[:, 2048 + 0:3072]
        xnl = F[:, 3072:4096]
        st = G[:, 0:16]
        r0_ = ct * 128
        ch.step("sync", lambda e, r0_=r0_: e.dma_start(
            out=xt, in_=x1seq[r0_:r0_ + 128, :]), 16)
        ln_stats_norm(ch, xt, xsq, st, xnf)
        r0 = ct * 128
        ch.step("vector", lambda e: e.tensor_copy(xnb, xnf))
        ch.step("vector", lambda e: e.tensor_sub(xnl, xnf, xnb))
        ch.dma_group("gpsimd", [
            lambda e, r0=r0: e.dma_start(out=xn1s[r0:r0 + 128, :], in_=xnb),
            lambda e, r0=r0: e.dma_start(out=xn1slo[r0:r0 + 128, :], in_=xnl),
            lambda e, r0=r0: e.dma_start(out=t32[r0:r0 + 128, :], in_=xnf)])

    for ct in range(nct):
        r0 = ct * 128
        xhi = X[:, 0:1024].rearrange("p (c t) -> p c t", c=8)
        xlo = X[:, 1024:2048].rearrange("p (c t) -> p c t", c=8)
        fns = []
        for c in range(DCH):
            fns.append(lambda e, cc=c, r0=r0: e.dma_start_transpose(
                out=xhi[:, cc, :], in_=xn1s[r0:r0 + 128, cc * 128:(cc + 1) * 128]))
            fns.append(lambda e, cc=c, r0=r0: e.dma_start_transpose(
                out=xlo[:, cc, :], in_=xn1slo[r0:r0 + 128, cc * 128:(cc + 1) * 128]))
        ch.dma_group("sync", fns)
        L = ps["p0"].ap()[:, 0:16]

        def mmg(e):
            first = True
            for c in range(DCH):
                for lhs, rhs in ((xhi, wgHI), (xhi, wgLO), (xlo, wgHI)):
                    ins = e.matmul(L, lhs[:, c, :], rhs[:, c, :], start=first,
                                   stop=(c == DCH - 1 and lhs is xlo))
                    first = False
            return ins
        ch.step("tensor", mmg)
        Lf = G[:, 32:48]
        nm1 = G[:, 48:49]
        ssum = G[:, 49:50]
        rs = G[:, 50:51]
        pr = G[:, 52:68]
        ch.step("vector", lambda e: e.tensor_copy(Lf, L))
        ch.step("vector", lambda e: e.reduce_max(nm1, Lf, AX.X, negate=True))
        ch.step("scalar", lambda e: e.activation(pr, Lf, AF.Exp, bias=nm1, accum_out=ssum))
        ch.step("vector", lambda e: e.reciprocal(rs, ssum))
        ch.step("vector", lambda e: e.tensor_scalar(pr, pr, rs, None, ALU.mult))
        ch.step("gpsimd", lambda e, r0=r0: e.dma_start(out=probs_d[r0:r0 + 128, :], in_=pr), 16)

    for ct in range(nct):
        r0 = ct * 128
        xhi = X[:, 1024:2048].rearrange("p (c t) -> p c t", c=8)
        ch.dma_group("sync", [
            (lambda e, cc=c, r0=r0: e.dma_start_transpose(
                out=xhi[:, cc, :], in_=xn1s[r0:r0 + 128, cc * 128:(cc + 1) * 128]))
            for c in range(DCH)])
        for ng in range(6):
            is_q = ng < 2
            W = WqA if is_q else WkvA
            brow = bqA if is_q else bkvA
            o0 = (ng if is_q else ng - 2) * 512
            pp = ps["p0"].ap() if ng % 2 == 0 else ps["p1"].ap()

            def mmq(e, W=W, brow=brow, o0=o0, pp=pp):
                for dc in range(DCH):
                    ins = e.matmul(pp[:], xhi[:, dc, :], W[:, dc, o0:o0 + 512],
                                   start=(dc == 0), stop=False)
                ins = e.matmul(pp[:], onesb[0:1, 0:128], brow[:, o0:o0 + 512],
                               start=False, stop=True)
                return ins
            ch.step("tensor", mmq)
            dst = X[:, 2048 + (ng % 2) * 512: 2048 + (ng % 2) * 512 + 512]
            ch.step("vector", lambda e, pp=pp, dst=dst: e.tensor_copy(dst, pp[:]))
            dd = qall_d if is_q else kvall_d
            ch.step("gpsimd", lambda e, dd=dd, r0=r0, o0=o0, dst=dst: e.dma_start(
                out=dd[r0:r0 + 128, o0:o0 + 512], in_=dst), 16)

    for jt in range(njt):
        j0 = jt * 128
        pj = G[:, 0:512]
        ch.step("sync", lambda e, j0=j0: e.dma_start(
            out=pj.rearrange("p (b h) -> p b h", b=32),
            in_=probs_d.rearrange("(b j) h -> j b h", b=32)[j0:j0 + 128],
            allow_slow_non_contiguous=True), 16)
        m1v = Dd[:, 0:16]
        m2v = Dd[:, 16:32]
        eqv = Ef[:, 0:512]
        l2v = Ef[:, 1024:1536]
        tmpv = Dd[:, 512:1024]
        c0 = Ef[:, 0:512]
        c1 = Ef[:, 512:1024]
        M1 = X[:, 0:512]
        M2 = X[:, 512:1024]

        def treemax(dst, src0):
            ch.step("vector", lambda e: e.tensor_max(tmpv[:, 0:256], src0[:, 0:256], src0[:, 256:512]))
            ch.step("vector", lambda e: e.tensor_max(tmpv[:, 0:128], tmpv[:, 0:128], tmpv[:, 128:256]))
            ch.step("vector", lambda e: e.tensor_max(tmpv[:, 0:64], tmpv[:, 0:64], tmpv[:, 64:128]))
            ch.step("vector", lambda e: e.tensor_max(tmpv[:, 0:32], tmpv[:, 0:32], tmpv[:, 32:64]))
            ch.step("vector", lambda e: e.tensor_max(dst, tmpv[:, 0:16], tmpv[:, 16:32]))
        treemax(m1v, pj)
        pj3 = pj.rearrange("p (b h) -> p b h", b=32)
        ch.step("vector", lambda e: e.tensor_tensor(
            eqv.rearrange("p (b h) -> p b h", b=32), pj3,
            m1v[:, None, :].broadcast_to([128, 32, 16]), ALU.is_ge))
        ch.step("vector", lambda e: e.scalar_tensor_tensor(l2v, eqv, NEG, pj, ALU.mult, ALU.add))
        treemax(m2v, l2v)
        ch.step("vector", lambda e: e.tensor_tensor(
            c0.rearrange("p (b h) -> p b h", b=32), pj3,
            m2v[:, None, :].broadcast_to([128, 32, 16]), ALU.is_ge))
        ch.step("vector", lambda e: e.tensor_copy(M2, c0))
        cur, nxt = c0, c1
        for sh in (1, 2, 4, 8, 16):
            s_ = sh * 16
            ch.step("vector", lambda e, cur=cur, nxt=nxt, s_=s_: e.tensor_copy(nxt[:, 0:s_], cur[:, 0:s_]))
            ch.step("vector", lambda e, cur=cur, nxt=nxt, s_=s_: e.tensor_add(
                nxt[:, s_:512], cur[:, s_:512], cur[:, 0:512 - s_]))
            cur, nxt = nxt, cur
        ch.step("vector", lambda e, cur=cur: e.tensor_single_scalar(tmpv, cur, 1.0, ALU.is_equal))
        ch.step("vector", lambda e: e.tensor_mul(M1, tmpv, M2))
        ch.step("vector", lambda e, cur=cur: e.tensor_single_scalar(tmpv, cur, 2.0, ALU.is_equal))
        ch.step("vector", lambda e: e.tensor_mul(M2, tmpv, M2))
        ch.step("gpsimd", lambda e, j0=j0: e.dma_start(out=m1d[j0:j0 + 128, :], in_=M1), 16)
        ch.step("gpsimd", lambda e, j0=j0: e.dma_start(out=m2d[j0:j0 + 128, :], in_=M2), 16)
        ch.step("vector", lambda e: e.tensor_add(tmpv, M1, M2))
        cnt = Dd[:, 32:64]
        ch.step("vector", lambda e: e.tensor_reduce(
            cnt, tmpv.rearrange("p (b h) -> p b h", b=32), AX.X, ALU.add))
        ch.step("gpsimd", lambda e, j0=j0: e.dma_start(out=cnt_d[j0:j0 + 128, :], in_=cnt), 16)

        Q1, Q2 = F[:, 0:1024], F[:, 1024:2048]
        V1, V2 = F[:, 2048:3072], F[:, 3072:4096]
        K1, K2 = H[:, 0:1024], X[:, 1024:2048]
        for bufz in (Q1, Q2, V1, V2, K1, K2):
            bz = bufz
            ch.step("vector", lambda e, bz=bz: e.memset(bz, 0.0))
        qb = C[:, 0:1024]
        kvb = C[:, 1024:3072]
        tq = C[:, 3072:4096]
        tkv = C[:, 4096:6144]
        for b in range(32):
            bb = b
            ch.dma_group("sync", [
                lambda e, bb=bb, j0=j0: e.dma_start(
                    out=qb, in_=qall_d[bb * jl + j0: bb * jl + j0 + 128, :]),
                lambda e, bb=bb, j0=j0: e.dma_start(
                    out=kvb, in_=kvall_d[bb * jl + j0: bb * jl + j0 + 128, :])])
            m1s = M1[:, bb * 16:(bb + 1) * 16]
            m2s = M2[:, bb * 16:(bb + 1) * 16]
            qb3 = qb.rearrange("p (h e) -> p h e", h=16)
            kv4 = kvb.rearrange("p (h w e) -> p h w e", h=16, w=2)
            for (msk, qd, kd, vd) in ((m1s, Q1, K1, V1), (m2s, Q2, K2, V2)):
                mq, qd_, kd_, vd_ = msk, qd, kd, vd
                ch.step("vector", lambda e, mq=mq: e.tensor_tensor(
                    tq.rearrange("p (h e) -> p h e", h=16), qb3,
                    mq[:, :, None].broadcast_to([128, 16, 64]), ALU.mult))
                ch.step("vector", lambda e, qd_=qd_: e.tensor_add(qd_, qd_, tq))
                ch.step("vector", lambda e, mq=mq: e.tensor_tensor(
                    tkv.rearrange("p (h w e) -> p h w e", h=16, w=2), kv4,
                    mq[:, :, None, None].broadcast_to([128, 16, 2, 64]), ALU.mult))
                tkv4 = tkv.rearrange("p (h w e) -> p h w e", h=16, w=2)
                ch.step("vector", lambda e, kd_=kd_, tkv4=tkv4: e.tensor_add(
                    kd_.rearrange("p (h e) -> p h e", h=16), kd_.rearrange("p (h e) -> p h e", h=16),
                    tkv4[:, :, 0, :]))
                ch.step("vector", lambda e, vd_=vd_, tkv4=tkv4: e.tensor_add(
                    vd_.rearrange("p (h e) -> p h e", h=16), vd_.rearrange("p (h e) -> p h e", h=16),
                    tkv4[:, :, 1, :]))
        ch.step("gpsimd", lambda e, j0=j0: e.dma_start(out=v1d[j0:j0 + 128, :], in_=V1), 16)
        ch.step("gpsimd", lambda e, j0=j0: e.dma_start(out=v2d[j0:j0 + 128, :], in_=V2), 16)
        attj = G[:, 32 + jt * 64: 96 + jt * 64]
        tmpf = Dd[:, 0:1024]
        for ci, (qq, kk) in enumerate(((Q1, K1), (Q1, K2), (Q2, K1), (Q2, K2))):
            cc_, qq_, kk_ = ci, qq, kk
            ch.step("vector", lambda e, qq_=qq_, kk_=kk_: e.tensor_mul(tmpf, qq_, kk_))
            ch.step("vector", lambda e, cc_=cc_: e.tensor_reduce(
                attj[:, cc_ * 16:(cc_ + 1) * 16],
                tmpf.rearrange("p (h e) -> p h e", h=16), AX.X, ALU.add))
        ch.step("vector", lambda e, attj=attj: e.tensor_scalar(attj, attj, 0.125, None, ALU.mult))
        ch.step("tensor", lambda e, attj=attj: e.transpose(ps["p2"].ap()[0:64, 0:128], attj, idf))
        ch.step("vector", lambda e, jt=jt: e.tensor_copy(
            Ef[0:64, 1536 + jt * 128:1536 + jt * 128 + 128], ps["p2"].ap()[0:64, 0:128]))

    attT = Ef[0:64, 1536:1536 + 256]
    mloc = G[0:64, 480:481]
    ch.step("vector", lambda e: e.reduce_max(mloc, attT[:, 0:njt * 128], AX.X))
    ch.step("tensor", lambda e: e.transpose(ps["p3"].ap()[0:1, 0:64], mloc, idf[0:64, 0:64]))
    mrow = Ef[0:1, 1800:1864]
    ch.step("vector", lambda e: e.tensor_copy(mrow, ps["p3"].ap()[0:1, 0:64]))
    onesf = Dd[0:1, 0:128]
    ch.step("sync", lambda e: e.dma_start(out=onesf, in_=cs["ones_row_f"]), 16)
    ch.step("tensor", lambda e: e.matmul(ps["p4"].ap()[:, 0:64], onesf, mrow, start=True, stop=True))
    mb = Dd[:, 1024:1088]
    ch.step("vector", lambda e: e.tensor_scalar(mb, ps["p4"].ap()[:, 0:64], -1.0, None, ALU.mult))
    onescol = Dd[0:128, 1088:1089]
    ch.step("vector", lambda e: e.memset(onescol, 1.0))
    for jt in range(njt):
        attj = G[:, 32 + jt * 64: 96 + jt * 64]
        elj = Dd[:, 1152 + jt * 64: 1216 + jt * 64]
        ch.step("vector", lambda e, attj=attj, elj=elj: e.tensor_add(elj, attj, mb))
        ch.step("scalar", lambda e, elj=elj: e.activation(elj, elj, AF.Exp))
        ch.step("tensor", lambda e, elj=elj, jt=jt: e.matmul(
            ps["p5"].ap()[0:1, 0:64], onescol, elj, start=(jt == 0), stop=(jt == njt - 1)))
    srow = Ef[0:1, 1864:1928]
    ch.step("vector", lambda e: e.tensor_copy(srow, ps["p5"].ap()[0:1, 0:64]))
    ch.step("gpsimd", lambda e: e.dma_start(out=agsm_in[0:1, 0:64], in_=mrow), 16)
    ch.step("gpsimd", lambda e: e.dma_start(out=agsm_in[0:1, 64:128], in_=srow), 16)
    nrk = len(cc_group)
    if en == "gpsimd":
        if ctx.mode == "real":
            ch._wait_cur()
        ctx.inc(eng.collective_compute(
            "AllGather", ALU.bypass, replica_groups=[cc_group],
            ins=[agsm_in.opt()], outs=[agsm_out.opt()]), "cc2", 1)
        ctx.wait(eng, "cc2", ctx.count("cc2"))
    ch.step("gpsimd", lambda e: e.nop())
    ch.step("sync", lambda e, nrk=nrk: e.dma_start(out=Ef[0:nrk, 0:128], in_=agsm_out[0:nrk, :]), 16)
    GtT = Dd[:, 1280:1280 + 8]
    ch.step("tensor", lambda e, nrk=nrk: e.transpose(ps["p2"].ap()[:, 0:nrk], Ef[0:nrk, 0:128], idf[0:nrk, 0:nrk]))
    ch.step("vector", lambda e, nrk=nrk: e.tensor_copy(GtT[:, 0:nrk], ps["p2"].ap()[:, 0:nrk]))
    negM = G[0:64, 481:482]
    Sv = G[0:64, 482:483]
    rS = G[0:64, 483:484]
    Fv = G[0:64, 484:485]
    ch.step("vector", lambda e, nrk=nrk: e.reduce_max(negM, GtT[0:64, 0:nrk], AX.X, negate=True))
    wv_ = Dd[0:64, 1300:1300 + 8]
    ch.step("scalar", lambda e, nrk=nrk: e.activation(wv_[:, 0:nrk], GtT[0:64, 0:nrk], AF.Exp, bias=negM))
    pv_ = Dd[0:64, 1320:1320 + 8]
    scop = Dd[0:64, 1340:1340 + 8]
    ch.step("vector", lambda e, nrk=nrk: e.tensor_copy(scop[:, 0:nrk], GtT[64:128, 0:nrk]))
    ch.step("vector", lambda e, nrk=nrk: e.tensor_mul(pv_[:, 0:nrk], wv_[:, 0:nrk], scop[:, 0:nrk]))
    ch.step("vector", lambda e, nrk=nrk: e.reduce_sum(Sv, pv_[:, 0:nrk], AX.X))
    ch.step("vector", lambda e: e.reciprocal(rS, Sv))
    ch.step("scalar", lambda e: e.activation(Fv, mloc, AF.Exp, bias=negM))
    ch.step("vector", lambda e: e.tensor_mul(Fv, Fv, rS))
    ch.step("tensor", lambda e: e.transpose(ps["p3"].ap()[0:1, 0:64], Fv, idf[0:64, 0:64]))
    frow = Ef[0:1, 1800:1864]
    ch.step("vector", lambda e: e.tensor_copy(frow, ps["p3"].ap()[0:1, 0:64]))
    ch.step("tensor", lambda e: e.matmul(ps["p4"].ap()[:, 0:64], onesf, frow, start=True, stop=True))
    Fb = Dd[:, 2112:2176]
    ch.step("vector", lambda e: e.tensor_copy(Fb, ps["p4"].ap()[:, 0:64]))

    for jt in range(njt):
        j0 = jt * 128
        elj = Dd[:, 1152 + jt * 64: 1216 + jt * 64]
        pfin = Dd[:, 1152 + jt * 64: 1216 + jt * 64]
        ch.step("vector", lambda e, elj=elj: e.tensor_mul(pfin, elj, Fb))
        V1, V2 = F[:, 2048:3072], F[:, 3072:4096]
        M1, M2 = X[:, 0:512], X[:, 512:1024]
        ch.dma_group("sync", [
            lambda e, j0=j0: e.dma_start(out=V1, in_=v1d[j0:j0 + 128, :]),
            lambda e, j0=j0: e.dma_start(out=V2, in_=v2d[j0:j0 + 128, :]),
            lambda e, j0=j0: e.dma_start(out=M1, in_=m1d[j0:j0 + 128, :]),
            lambda e, j0=j0: e.dma_start(out=M2, in_=m2d[j0:j0 + 128, :])])
        av1, av2 = C[:, 0:1024], C[:, 1024:2048]
        tmpb = C[:, 2048:3072]
        for i_, avd in ((0, av1), (1, av2)):
            ii, avd_ = i_, avd
            p1s = pfin[:, (ii * 2 + 0) * 16:(ii * 2 + 0) * 16 + 16]
            p2s = pfin[:, (ii * 2 + 1) * 16:(ii * 2 + 1) * 16 + 16]
            ch.step("vector", lambda e, avd_=avd_, p1s=p1s: e.tensor_tensor(
                avd_.rearrange("p (h e) -> p h e", h=16),
                V1.rearrange("p (h e) -> p h e", h=16),
                p1s[:, :, None].broadcast_to([128, 16, 64]), ALU.mult))
            ch.step("vector", lambda e, p2s=p2s: e.tensor_tensor(
                tmpb.rearrange("p (h e) -> p h e", h=16),
                V2.rearrange("p (h e) -> p h e", h=16),
                p2s[:, :, None].broadcast_to([128, 16, 64]), ALU.mult))
            ch.step("vector", lambda e, avd_=avd_: e.tensor_add(avd_, avd_, tmpb))
        sut = C[:, 3072:4096]
        for b in range(32):
            bb = b
            m1s = M1[:, bb * 16:(bb + 1) * 16]
            m2s = M2[:, bb * 16:(bb + 1) * 16]
            ch.step("vector", lambda e, m1s=m1s: e.tensor_tensor(
                sut.rearrange("p (h e) -> p h e", h=16),
                av1.rearrange("p (h e) -> p h e", h=16),
                m1s[:, :, None].broadcast_to([128, 16, 64]), ALU.mult))
            ch.step("vector", lambda e, m2s=m2s: e.tensor_tensor(
                tmpb.rearrange("p (h e) -> p h e", h=16),
                av2.rearrange("p (h e) -> p h e", h=16),
                m2s[:, :, None].broadcast_to([128, 16, 64]), ALU.mult))
            ch.step("vector", lambda e: e.tensor_add(sut, sut, tmpb))
            ch.step("gpsimd", lambda e, bb=bb, j0=j0: e.dma_start(
                out=su_d[bb * jl + j0: bb * jl + j0 + 128, :], in_=sut), 16)

    gb = Ef[:, 0:1024]
    bb_ = Ef[:, 1024:2048]
    bob = Dd[:, 2048:3072]
    for half in range(2):
        hh = half
        for src_row, dstb in ((lng, gb), (lnb, bb_), (boh, bob)):
            sr, db = src_row, dstb
            ch.step("sync", lambda e, sr=sr, hh=hh: e.dma_start(
                out=Ef[0:1, 2048 - 512:2048].bitcast(F32) if False else G[0:1, 0:512],
                in_=sr[None, hh * 512:(hh + 1) * 512]), 16)
            ch.step("tensor", lambda e: e.matmul(ps["p2"].ap()[:, 0:512], onesf,
                                                 G[0:1, 0:512], start=True, stop=True))
            ch.step("vector", lambda e, db=db, hh=hh: e.tensor_copy(
                db[:, hh * 512:(hh + 1) * 512], ps["p2"].ap()[:, 0:512]))
    for ct in range(nct):
        b, jt = ct // njt, ct % njt
        j0 = jt * 128
        r0 = ct * 128
        cntt = G[:, 0:32]
        suf = X[:, 0:1024].rearrange("p (c t) -> p c t", c=8)
        ch.dma_group("sync", [lambda e, j0=j0: e.dma_start(out=cntt, in_=cnt_d[j0:j0 + 128, :])] + [
            (lambda e, cc=c, r0=r0: e.dma_start_transpose(
                out=suf[:, cc, :], in_=su_d[r0:r0 + 128, cc * 128:(cc + 1) * 128]))
            for c in range(DCH)])
        y = Dd[:, 0:1024]
        for og in range(2):
            oo = og
            pp = ps["p0"].ap() if og == 0 else ps["p1"].ap()

            def mmo(e, oo=oo, pp=pp):
                for dc in range(DCH):
                    ins = e.matmul(pp[:], suf[:, dc, :], WoS[:, dc, oo * 512:(oo + 1) * 512],
                                   start=(dc == 0), stop=(dc == DCH - 1))
                return ins
            ch.step("tensor", mmo)
            ccol = cntt[:, b:b + 1]
            ch.step("vector", lambda e, oo=oo, pp=pp, ccol=ccol: e.scalar_tensor_tensor(
                y[:, oo * 512:(oo + 1) * 512], bob[:, oo * 512:(oo + 1) * 512],
                ccol, pp[:], ALU.mult, ALU.add))
        tt_ = Dd[:, 1024:2048]
        ch.step("sync", lambda e, r0=r0: e.dma_start(out=tt_, in_=t32[r0:r0 + 128, :]), 16)
        ch.step("vector", lambda e: e.tensor_add(y, y, tt_))
        xsq = Dd[:, 1024:2048]
        st = G[:, 100:116]
        ch.step("scalar", lambda e: e.activation(xsq, y, AF.Square))
        s1, s2, mu, var, rec, rstd, nmr = (st[:, k:k + 1] for k in range(7))
        ch.step("vector", lambda e: e.reduce_sum(s1, y, AX.X))
        ch.step("vector", lambda e: e.reduce_sum(s2, xsq, AX.X))
        ch.step("vector", lambda e: e.tensor_scalar(mu, s1, 1.0 / D, None, ALU.mult))
        ch.step("vector", lambda e: e.tensor_mul(var, mu, mu))
        ch.step("vector", lambda e: e.tensor_scalar(var, s2, 1.0 / D, var, ALU.mult, ALU.subtract))
        ch.step("vector", lambda e: e.tensor_scalar_add(var, var, EPS))
        ch.step("vector", lambda e: e.reciprocal(rec, var))
        ch.step("scalar", lambda e: e.activation(rstd, rec, AF.Sqrt))
        ch.step("scalar", lambda e: e.activation(nmr, mu, AF.Copy, scale=rstd))
        yn = Dd[:, 1024:2048]
        ch.step("vector", lambda e: e.tensor_scalar(yn, y, rstd, nmr, ALU.mult, ALU.subtract))
        ch.step("vector", lambda e: e.tensor_mul(yn, yn, gb))
        ch.step("vector", lambda e: e.tensor_add(yn, yn, bb_))
        ch.step("gpsimd", lambda e, r0=r0: e.dma_start(out=ffn1ag[r0:r0 + 128, :], in_=yn), 16)
    ch.wait_all()


def p7h_ag2(ctx, en, eng, cc_group):
    if en == "gpsimd":
        ctx.inc(eng.collective_compute(
            "AllGather", ALU.bypass, replica_groups=[cc_group],
            ins=[ctx.t("ffn1ag").opt()], outs=[ctx.t("ffn1full").opt()]), "cc3", 1)
        ctx.wait(eng, "cc3", ctx.count("cc3"))


def p8_final(ctx, en, eng, ncc):
    sb, ps, cs = ctx.sb, ctx.ps, ctx.consts
    Dd, Ef, I = sb["D"].ap(), sb["E"].ap(), sb["I"].ap()
    ch = Chain(ctx, en, eng, "p8")
    x1ag, attn1f, ffn1full = ctx.t("x1ag"), ctx.t("accf"), ctx.t("ffn1full")
    ffn1loc = ctx.t("ffn1loc")
    out = ctx.t("out").rearrange("b s d -> (b s) d")
    jl = S // ncc
    cells_ = 32 * jl
    idf = I[:, 0:128]
    r = [None]

    def R(e):
        if r[0] is None:
            r[0] = rank0(ctx, e)
        return r[0]
    ch.step("sync", lambda e: e.dma_start(out=idf, in_=cs["ident_f"]), 16)
    for b_ in range(B_LOC):
        bb = b_
        src3 = ffn1full.rearrange("(r2 c) d -> r2 c d", c=cells_)[:, bb * jl:(bb + 1) * jl, :]
        ch.step("gpsimd", lambda e, bb=bb, src3=src3: e.dma_start(
            out=ffn1loc.rearrange("(b s) d -> b s d", b=B_LOC)[bb].rearrange(
                "(r2 j) d -> r2 j d", j=jl),
            in_=dyn(src3, R(e) * (4 * jl * D))), 16)
    for i in range(NT):
        r0 = i * 128
        x1t = Dd[:, 0:1024]
        ft = Dd[:, 1024:2048]
        ch.dma_group("sync", [
            lambda e: e.dma_start(out=x1t, in_=x1ag[r0:r0 + 128, :]),
            lambda e, r0=r0: e.dma_start(out=ft, in_=ffn1loc[r0:r0 + 128, :])] + [
            (lambda e, d_=dc, r0=r0: e.dma_start(
                out=Ef[:, d_ * 128:(d_ + 1) * 128],
                in_=attn1f[d_ * 128:(d_ + 1) * 128, r0:r0 + 128]))
            for dc in range(DCH)])
        ch.step("vector", lambda e: e.tensor_add(x1t, x1t, ft))
        for dc in range(DCH):
            d_ = dc
            pb2 = (ps["p2"] if dc % 2 == 0 else ps["p3"]).ap()
            ch.step("tensor", lambda e, d_=d_, pb2=pb2: e.transpose(
                pb2[:, 0:128], Ef[:, d_ * 128:(d_ + 1) * 128], idf))
            ch.step("vector", lambda e, d_=d_, pb2=pb2: e.tensor_add(
                x1t[:, d_ * 128:(d_ + 1) * 128], x1t[:, d_ * 128:(d_ + 1) * 128],
                pb2[:, 0:128]))
        ch.step("gpsimd", lambda e, r0=r0: e.dma_start(out=out[r0:r0 + 128, :], in_=x1t), 16)
    ch.wait_all()



CELLS8 = 32 * JL

SHAPES = {
    "x_in": ([B_LOC, S, D], F32),
    "w0q": ([D, D], F32), "w0k": ([D, D], F32), "w0v": ([D, D], F32), "w0o": ([D, D], F32),
    "w_gate": ([8, D], F32),
    "w_moe1": ([8, D, FF], F32), "w_moe2": ([8, FF, D], F32),
    "w_sw1": ([D, FF], F32), "w_sw2": ([FF, D], F32),
    "b_moe1": ([8, FF], F32), "b_moe2": ([8, D], F32),
    "b_ssb1": ([FF], F32), "b_ssb2": ([D], F32),
    "w5q": ([D, D], F32), "w5k": ([D, D], F32), "w5v": ([D, D], F32), "w5o": ([D, D], F32),
    "b5q": ([D], F32), "b5k": ([D], F32), "b5v": ([D], F32), "b5o": ([D], F32),
    "w1_wg": ([D, 16], F32), "w1_wq": ([16, D, 64], F32), "w1_wkv": ([16, D, 128], F32),
    "w1_wo": ([16, 64, D], F32),
    "b1_bq": ([16, 64], F32), "b1_bkv": ([16, 128], F32), "b1_bo": ([D], F32),
    "b1_lng": ([D], F32), "b1_lnb": ([D], F32),
    "xn0": ([T, D], BF16), "xn0lo": ([T, D], BF16),
    "comb": ([8, T], BF16), "b1b": ([9, FF], BF16),
    "accf": ([D, T], F32),
    "b5rows": ([4, D], BF16),
    "x1ag": ([T, D], F32), "xn1": ([T, D], BF16),
    "x1full": ([NCORES * T, D], F32),
    "x1seq": ([CELLS8, D], F32),
    "xn1s": ([CELLS8, D], BF16), "xn1slo": ([CELLS8, D], BF16), "t32": ([CELLS8, D], F32),
    "probs": ([CELLS8, 16], F32),
    "qall": ([CELLS8, 1024], BF16), "kvall": ([CELLS8, 2048], BF16), "su": ([CELLS8, 1024], BF16),
    "v1d": ([JL, 1024], BF16), "v2d": ([JL, 1024], BF16),
    "m1d": ([JL, 512], BF16), "m2d": ([JL, 512], BF16), "cntd": ([JL, 32], F32),
    "agsm_in": ([1, 128], F32), "agsm_out": ([NCORES, 128], F32),
    "ffn1ag": ([CELLS8, D], F32), "ffn1full": ([NCORES * CELLS8, D], F32),
    "ffn1loc": ([T, D], F32),
    "out": ([B_LOC, S, D], F32),
}

WEIGHT_MAP = {
    "w0q": "L0_attn_Wq", "w0k": "L0_attn_Wk", "w0v": "L0_attn_Wv", "w0o": "L0_attn_Wo",
    "w_gate": "L0_ffn_gate_w",
    "w_moe1": "L0_ffn_w1", "w_moe2": "L0_ffn_w2",
    "w_sw1": "L0_ffn_sw1", "w_sw2": "L0_ffn_sw2",
    "b_moe1": "L0_ffn_b1", "b_moe2": "L0_ffn_b2",
    "b_ssb1": "L0_ffn_sb1", "b_ssb2": "L0_ffn_sb2",
    "w5q": "L1_attn_Wq", "w5k": "L1_attn_Wk", "w5v": "L1_attn_Wv", "w5o": "L1_attn_Wo",
    "b5q": "L1_attn_bq", "b5k": "L1_attn_bk", "b5v": "L1_attn_bv", "b5o": "L1_attn_bo",
    "w1_wg": "L1_ffn_Wg", "w1_wq": "L1_ffn_Wq", "w1_wkv": "L1_ffn_Wkv", "w1_wo": "L1_ffn_Wo",
    "b1_bq": "L1_ffn_bq", "b1_bkv": "L1_ffn_bkv", "b1_bo": "L1_ffn_bo",
    "b1_lng": "L1_ffn_ln_g", "b1_lnb": "L1_ffn_ln_b",
}


def ph_p1(ctx, en, eng):
    attn_phase(ctx, en, eng, "p1", ctx.t("xn0"), ctx.t("w0q"), ctx.t("w0k"),
               ctx.t("w0v"), ctx.t("w0o"), None, None, None, None,
               ctx.t("accf"), None)


def ph_p5(ctx, en, eng):
    attn_phase(ctx, en, eng, "p5", ctx.t("xn1"), ctx.t("w5q"), ctx.t("w5k"),
               ctx.t("w5v"), ctx.t("w5o"), ctx.t("b5q"), ctx.t("b5k"),
               ctx.t("b5v"), ctx.t("b5o"), ctx.t("accf"), WIN)


def ph_p7(ctx, en, eng):
    p7_ffn1(ctx, en, eng, NCORES, list(range(NCORES)))


def ph_p8(ctx, en, eng):
    p8_final(ctx, en, eng, NCORES)


def mk_io(ins, outs, shared=()):
    io = {}
    for n in ins:
        io[n] = ("ExternalInput", *SHAPES[n])
    for n in outs:
        io[n] = ("ExternalOutput", *SHAPES[n])
    for n, (shape, dt) in SHAPES.items():
        if n not in io:
            if n in shared:
                io[n] = ("Internal", shape, dt, "Shared")
            else:
                io[n] = ("Internal", shape, dt)
    return io


L0_W = ["w0q", "w0k", "w0v", "w0o", "w_gate", "w_moe1", "w_moe2", "w_sw1",
        "w_sw2", "b_moe1", "b_moe2", "b_ssb1", "b_ssb2"]
L5_W = ["w5q", "w5k", "w5v", "w5o", "b5q", "b5k", "b5v", "b5o"]
L7_W = ["w1_wg", "w1_wq", "w1_wkv", "w1_wo", "b1_bq", "b1_bkv", "b1_bo",
        "b1_lng", "b1_lnb"]

FUSED = True
_nc_cache = {}


def _get_nc(key, phases, io):
    if key not in _nc_cache:
        _nc_cache[key] = build_program(phases, io)
    return _nc_cache[key]


def kernel(**inputs):
    x = np.ascontiguousarray(np.asarray(inputs["L0_attn_x"], dtype=np.float32))
    w = {k: np.ascontiguousarray(np.asarray(inputs[v], dtype=np.float32))
         for k, v in WEIGHT_MAP.items()}
    xs = [x[4 * c:4 * c + 4] for c in range(NCORES)]

    if FUSED:
        phases = [p0_ln0, p2_gate, p3z_zero_acc, ph_p1, p3_moe, p4_ln1,
                  lambda c, en, e: p6_ag1(c, en, e, list(range(NCORES))),
                  ph_p5, ph_p7,
                  lambda c, en, e: p7h_ag2(c, en, e, list(range(NCORES))),
                  ph_p8]
        io = mk_io(["x_in"] + L0_W + L5_W + L7_W, ["out"],
                   shared=("x1full", "ffn1full", "agsm_out"))
        nc = _get_nc("fused", phases, io)
        in_maps = [{"x_in": xs[c], **w} for c in range(NCORES)]
        res = run_bass_kernel_spmd(nc, in_maps, core_ids=list(range(NCORES)))
        return np.concatenate([r["out"] for r in res.results], axis=0)

    ioA = mk_io(["x_in"] + L0_W, ["x1ag", "xn1"])
    ncA = _get_nc("A", [p0_ln0, p2_gate, p3z_zero_acc, ph_p1, p3_moe, p4_ln1], ioA)
    resA = run_bass_kernel_spmd(ncA, [{"x_in": xs[c], **{k: w[k] for k in L0_W}}
                                      for c in range(NCORES)],
                                core_ids=list(range(NCORES)))
    x1s = [r["x1ag"] for r in resA.results]
    xn1s_ = [r["xn1"] for r in resA.results]

    ioB = mk_io(["xn1"] + L5_W, ["accf"])
    ncB = _get_nc("B", [ph_p5], ioB)
    resB = run_bass_kernel_spmd(ncB, [{"xn1": xn1s_[c], **{k: w[k] for k in L5_W}}
                                      for c in range(NCORES)],
                                core_ids=list(range(NCORES)))
    attn1s = [r["accf"] for r in resB.results]

    x1full = np.concatenate(x1s, axis=0)
    ioC = mk_io(["x1full"] + L7_W, ["ffn1ag"], shared=("agsm_out",))
    ncC = _get_nc("C", [ph_p7], ioC)
    resC = run_bass_kernel_spmd(ncC, [{"x1full": x1full, **{k: w[k] for k in L7_W}}
                                      for c in range(NCORES)],
                                core_ids=list(range(NCORES)))
    ffn1full = np.concatenate([r["ffn1ag"] for r in resC.results], axis=0)

    ioD = mk_io(["x1ag", "accf", "ffn1full"], ["out"])
    ncD = _get_nc("D", [ph_p8], ioD)
    resD = run_bass_kernel_spmd(ncD, [{"x1ag": x1s[c], "accf": attn1s[c],
                                       "ffn1full": ffn1full}
                                      for c in range(NCORES)],
                                core_ids=list(range(NCORES)))
    return np.concatenate([r["out"] for r in resD.results], axis=0)


# version 13
# speedup vs baseline: 1.2065x; 1.0320x over previous
import numpy as np
import ml_dtypes
from contextlib import ExitStack
import concourse.bass as bass
import concourse.mybir as mybir
from concourse.bass_utils import run_bass_kernel_spmd

F32 = mybir.dt.float32
BF16 = mybir.dt.bfloat16
AF = mybir.ActivationFunctionType
ALU = mybir.AluOpType
AX = mybir.AxisListType

NCORES = 8
B_LOC = 4
S = 2048
D = 1024
T = B_LOC * S
NT = T // 128
DCH = D // 128
NH = 16
DH = 64
WIN = 512
NEXP = 8
FF = 4096
NFC = FF // 128
MTT = 512
NMT = T // MTT
JL = S // NCORES
NJT = JL // 128
EPS = 1e-5
NEG = -1e9

ENGINES = ["sync", "gpsimd", "tensor", "vector", "scalar"]
BF = ml_dtypes.bfloat16



class MockIns:
    def then_inc(self, *a, **k):
        raise RuntimeError("use ctx.inc / Chain.step, not .then_inc")


class MockEng:
    def __init__(self, name):
        self.name = name

    def __getattr__(self, op):
        def f(*a, **k):
            return MockIns()
        return f


class Ctx:
    def __init__(self, nc):
        self.nc = nc
        self.mode = "mock"
        self.counts = {}
        self.marks = {}
        self.sems = {}
        self.tensors = {}
        self.consts = {}
        self.sb = {}
        self.ps = {}

    def t(self, name):
        return self.tensors[name].ap()

    def inc(self, ins, name, v=1):
        self.counts[name] = self.counts.get(name, 0) + v
        if self.mode == "real":
            ins.then_inc(self.sems[name], v)

    def wait(self, eng, name, val):
        if self.mode == "real":
            eng.wait_ge(self.sems[name], val)

    def count(self, name):
        return self.counts.get(name, 0)

    def mark(self, key):
        snap = dict(self.counts)
        if self.mode == "mock":
            if key not in self.marks:
                self.marks[key] = snap
        return self.marks[key]

    def at(self, key, sem):
        if key not in self.marks:
            if self.mode == "mock":
                return 0
            raise KeyError(f"mark {key} never recorded in mock pass")
        return self.marks[key].get(sem, 0)

    def bar(self, eng, k):
        self.counts["bar"] = self.counts.get("bar", 0) + 1
        if self.mode == "real":
            eng.drain().then_inc(self.sems["bar"], 1)
            eng.wait_ge(self.sems["bar"], 5 * (k + 1))


class Chain:

    def __init__(self, ctx, en, eng, key):
        self.ctx, self.en, self.eng = ctx, en, eng
        self.sem = "ch_" + key
        self.dsem = {q: f"chd{q[0]}_{key}" for q in ("sync", "gpsimd", "scalar")}
        ctx.counts.setdefault(self.sem, 0)
        for s in self.dsem.values():
            ctx.counts.setdefault(s, 0)
        self.n = 0
        self.d = {q: 0 for q in self.dsem}
        self.after = []

    def join(self, others):
        for o in others:
            if o is self:
                continue
            if o.n:
                self.after.append((o.sem, o.n))
            for q, v in o.d.items():
                if v:
                    self.after.append((o.dsem[q], v))

    def _wait_cur(self):
        if self.ctx.mode == "real":
            if self.n:
                self.ctx.wait(self.eng, self.sem, self.n)
            for q, v in self.d.items():
                if v:
                    self.ctx.wait(self.eng, self.dsem[q], v)
            for (sm, v) in self.after:
                self.ctx.wait(self.eng, sm, v)

    def step(self, owner, fn, dv=None):
        dma = dv is not None
        if self.en == owner:
            if self.ctx.mode == "real":
                self._wait_cur()
                ins = fn(self.eng)
                if dma:
                    ins.then_inc(self.ctx.sems[self.dsem[owner]], 16)
                else:
                    ins.then_inc(self.ctx.sems[self.sem], 1)
            self.ctx.counts[self.dsem[owner] if dma else self.sem] += 16 if dma else 1
        if dma:
            self.d[owner] += 16
        else:
            self.n += 1

    def dma_group(self, owner, fns):
        first = True
        for fn in fns:
            if self.en == owner:
                if self.ctx.mode == "real":
                    if first:
                        self._wait_cur()
                    ins = fn(self.eng)
                    ins.then_inc(self.ctx.sems[self.dsem[owner]], 16)
                self.ctx.counts[self.dsem[owner]] += 16
            self.d[owner] += 16
            first = False

    def wait_all(self):
        self._wait_cur()




def mk_ops(ctx, en, eng, key):
    sems = {"tensor": key + "pe", "vector": key + "ve", "scalar": key + "sc",
            "sync": key + "ld", "gpsimd": key + "gp", "sync2": key + "ld2",
            "gpsimd2": key + "gp2", "gpsimdw": key + "gpw", "gpsimdc": key + "gc"}
    engine_of = {"sync2": "sync", "gpsimd2": "gpsimd", "gpsimdw": "gpsimd",
                 "gpsimdc": "gpsimd"}
    for sm in sems.values():
        ctx.counts.setdefault(sm, 0)

    def op(owner, fn, mark=None, deps=()):
        act = engine_of.get(owner, owner)
        if en == act:
            if ctx.mode == "real":
                for (mk, sm) in deps:
                    ctx.wait(eng, sm, ctx.at(mk, sm))
            dv = 1 if act in ("tensor", "vector", "scalar") or owner == "gpsimdc" else 16
            if ctx.mode == "real":
                ins = fn(eng)
                ins.then_inc(ctx.sems[sems[owner]], dv)
            ctx.counts[sems[owner]] += dv
            if mark:
                ctx.mark(mark)
        return None

    def waitall(extra=()):
        if ctx.mode == "real":
            for sm in list(sems.values()) + list(extra):
                v = ctx.count(sm)
                if v:
                    ctx.wait(eng, sm, v)
    return op, sems, waitall


def host_consts():
    c = {}
    c["ident_b"] = np.eye(128, dtype=BF)
    c["ident_f"] = np.eye(128, dtype=np.float32)
    r = np.arange(128)
    c["mask_causal"] = np.where(r[None, :] <= r[:, None], 0.0, NEG).astype(np.float32)
    c["mask_far"] = np.where(r[None, :] >= r[:, None] + 1, 0.0, NEG).astype(np.float32)
    c["ones_row_b"] = np.ones((1, 512), dtype=BF)
    c["ones_row_f"] = np.ones((1, 128), dtype=np.float32)
    oh = np.zeros((8, 8 * 128), dtype=BF)
    for e in range(8):
        oh[e, e * 128:(e + 1) * 128] = 1
    c["onehot8"] = oh
    c["onehot32"] = None
    del c["onehot32"]
    return c


def build_program(phase_list, io_kinds):
    nc = bass.Bass("TRN2", target_bir_lowering=False)
    ctx = Ctx(nc)
    for name, spec in io_kinds.items():
        kind, shape, dtype = spec[:3]
        kw = {"addr_space": spec[3]} if len(spec) > 3 else {}
        ctx.tensors[name] = nc.dram_tensor(name, list(shape), dtype, kind=kind, **kw)
    for cname, arr in host_consts().items():
        ctx.consts[cname] = nc.inline_tensor(arr, cname).ap()

    with ExitStack() as stack:
        en_ = stack.enter_context
        sb = ctx.sb
        sb["A"] = en_(nc.sbuf_tensor("bufA", [128, 32768], BF16))
        sb["B"] = en_(nc.sbuf_tensor("bufB", [128, 32768], BF16))
        sb["C"] = en_(nc.sbuf_tensor("bufC", [128, 16384], BF16))
        sb["D"] = en_(nc.sbuf_tensor("bufD", [128, 3072], F32))
        sb["E"] = en_(nc.sbuf_tensor("bufE", [128, 2048], F32))
        sb["F"] = en_(nc.sbuf_tensor("bufF", [128, 4096], BF16))
        sb["G"] = en_(nc.sbuf_tensor("bufG", [128, 512], F32))
        sb["H"] = en_(nc.sbuf_tensor("bufH", [128, 1024], BF16))
        sb["I"] = en_(nc.sbuf_tensor("bufI", [128, 128], F32))
        sb["X"] = en_(nc.sbuf_tensor("bufX", [128, 4096], BF16))
        ps = ctx.ps
        for i in range(6):
            ps[f"p{i}"] = en_(nc.psum_tensor(f"p{i}", [128, 512], F32))
        ps["tb0"] = en_(nc.psum_tensor("tb0", [128, 512], BF16))
        ps["tb1"] = en_(nc.psum_tensor("tb1", [128, 512], BF16))

        ctx.mode = "mock"
        for en in ENGINES:
            eng = MockEng(en)
            for k, ph in enumerate(phase_list):
                ph(ctx, en, eng)
                ctx.bar(eng, k)

        for name in sorted(ctx.counts):
            ctx.sems[name] = en_(nc.semaphore(name))
        ctx.mode = "real"
        ctx.counts = {}
        block = en_(nc.Block())

        def mk(en):
            def f(eng):
                for k, ph in enumerate(phase_list):
                    ph(ctx, en, eng)
                    ctx.bar(eng, k)
            return f
        block.sync(mk("sync"))
        block.gpsimd(mk("gpsimd"))
        block.tensor(mk("tensor"))
        block.vector(mk("vector"))
        block.scalar(mk("scalar"))
    return nc



def ln_stats_norm(ch, xt, xsq, st, xnf):
    s1, s2, mu, var, rec, rstd, nmr = (st[:, k:k + 1] for k in range(7))
    ch.step("scalar", lambda e: e.activation(xsq, xt, AF.Square))
    ch.step("vector", lambda e: e.reduce_sum(s1, xt, AX.X))
    ch.step("vector", lambda e: e.reduce_sum(s2, xsq, AX.X))
    ch.step("vector", lambda e: e.tensor_scalar(mu, s1, 1.0 / D, None, ALU.mult))
    ch.step("vector", lambda e: e.tensor_mul(var, mu, mu))
    ch.step("vector", lambda e: e.tensor_scalar(var, s2, 1.0 / D, var, ALU.mult, ALU.subtract))
    ch.step("vector", lambda e: e.tensor_scalar_add(var, var, EPS))
    ch.step("vector", lambda e: e.reciprocal(rec, var))
    ch.step("scalar", lambda e: e.activation(rstd, rec, AF.Sqrt))
    ch.step("scalar", lambda e: e.activation(nmr, mu, AF.Copy, scale=rstd))
    ch.step("vector", lambda e: e.tensor_scalar(xnf, xt, rstd, nmr, ALU.mult, ALU.subtract))


def ln_tiles(ctx, en, eng, key, src_ap, n_tiles, outs):
    sb = ctx.sb
    ch = Chain(ctx, en, eng, key)
    for i in range(n_tiles):
        xt = sb["D"].ap()[:, 0:1024]
        xsq = sb["E"].ap()[:, 0:1024]
        xnf = sb["E"].ap()[:, 1024:2048]
        xnb = sb["F"].ap()[:, 0:1024]
        xnl = sb["F"].ap()[:, 1024:2048]
        st = sb["G"].ap()[:, 0:16]
        s1, s2, mu, var, rec, rstd, nmr = (st[:, k:k + 1] for k in range(7))
        r0 = i * 128

        ch.step("sync", lambda e: e.dma_start(out=xt, in_=src_ap[r0:r0 + 128, :]), 16)
        ch.step("scalar", lambda e: e.activation(xsq, xt, AF.Square))
        ch.step("vector", lambda e: e.reduce_sum(s1, xt, AX.X))
        ch.step("vector", lambda e: e.reduce_sum(s2, xsq, AX.X))
        ch.step("vector", lambda e: e.tensor_scalar(mu, s1, 1.0 / D, None, ALU.mult))
        ch.step("vector", lambda e: e.tensor_mul(var, mu, mu))
        ch.step("vector", lambda e: e.tensor_scalar(var, s2, 1.0 / D, var, ALU.mult, ALU.subtract))
        ch.step("vector", lambda e: e.tensor_scalar_add(var, var, EPS))
        ch.step("vector", lambda e: e.reciprocal(rec, var))
        ch.step("scalar", lambda e: e.activation(rstd, rec, AF.Sqrt))
        ch.step("scalar", lambda e: e.activation(nmr, mu, AF.Copy, scale=rstd))
        ch.step("vector", lambda e: e.tensor_scalar(xnf, xt, rstd, nmr, ALU.mult, ALU.subtract))
        if "bf" in outs or "lo" in outs:
            ch.step("vector", lambda e: e.tensor_copy(xnb, xnf))
            ch.step("gpsimd", lambda e: e.dma_start(out=outs["bf"][r0:r0 + 128, :], in_=xnb), 16)
        if "lo" in outs:
            ch.step("vector", lambda e: e.tensor_sub(xnl, xnf, xnb))
            ch.step("gpsimd", lambda e: e.dma_start(out=outs["lo"][r0:r0 + 128, :], in_=xnl), 16)
        if "f32" in outs:
            ch.step("gpsimd", lambda e: e.dma_start(out=outs["f32"][r0:r0 + 128, :], in_=xnf), 16)
        if "x32" in outs:
            ch.step("gpsimd", lambda e: e.dma_start(out=outs["x32"][r0:r0 + 128, :], in_=xt), 16)
    ch.wait_all()


def p0_ln0(ctx, en, eng):
    x = ctx.t("x_in").rearrange("b s d -> (b s) d")
    ln_tiles(ctx, en, eng, "p0", x, NT,
             {"bf": ctx.t("xn0"), "lo": ctx.t("xn0lo")})


def p2_gate(ctx, en, eng):
    sb, ps, cs = ctx.sb, ctx.ps, ctx.consts
    ch = Chain(ctx, en, eng, "p2")
    xn0, xn0lo, comb = ctx.t("xn0"), ctx.t("xn0lo"), ctx.t("comb")
    gw = ctx.t("w_gate")
    E = sb["E"].ap()
    gw_f = E[0:8, 0:1024]
    gwT_f = E[:, 1024:1024 + 64]
    gwT_hi = sb["H"].ap()[:, 0:64]
    gwT_lo = sb["H"].ap()[:, 64:128]
    idf = sb["I"].ap()[:, 0:128]
    idb = sb["H"].ap()[:, 1024 - 128:1024]

    ch.step("sync", lambda e: e.dma_start(out=gw_f, in_=gw), 16)
    ch.step("sync", lambda e: e.dma_start(out=idf, in_=cs["ident_f"]), 16)
    ch.step("sync", lambda e: e.dma_start(out=idb, in_=cs["ident_b"]), 16)
    for c in range(DCH):
        cc = c
        ch.step("tensor", lambda e: e.transpose(
            ps["p0"].ap()[:, 0:8], gw_f[0:8, cc * 128:(cc + 1) * 128], idf[0:8, 0:8]))
        ch.step("vector", lambda e: e.tensor_copy(gwT_f[:, cc * 8:(cc + 1) * 8], ps["p0"].ap()[:, 0:8]))
    ch.step("vector", lambda e: e.tensor_copy(gwT_hi, gwT_f))
    ch.step("vector", lambda e: e.tensor_sub(gwT_lo, gwT_f, gwT_hi))

    A = sb["A"].ap()
    for i in range(T // 512):
        t0 = i * 512
        xhi = A[:, 0:8 * 512].rearrange("p (c t) -> p c t", c=8)
        xlo = A[:, 8 * 512:16 * 512].rearrange("p (c t) -> p c t", c=8)
        fns = []
        for c in range(DCH):
            fns.append(lambda e, cc=c: e.dma_start_transpose(
                out=xhi[:, cc, :], in_=xn0[t0:t0 + 512, cc * 128:(cc + 1) * 128]))
            fns.append(lambda e, cc=c: e.dma_start_transpose(
                out=xlo[:, cc, :], in_=xn0lo[t0:t0 + 512, cc * 128:(cc + 1) * 128]))
        ch.dma_group("sync", fns)
        for q in range(4):
            qq = q
            L = ps["p0"].ap()[:, 0:8]

            def mm_logits(e, q=qq):
                first = True
                for c in range(DCH):
                    for lhs, rhs in ((xhi, gwT_hi), (xhi, gwT_lo), (xlo, gwT_hi)):
                        ins = e.matmul(L, lhs[:, c, q * 128:(q + 1) * 128],
                                       rhs[:, c * 8:(c + 1) * 8],
                                       start=first, stop=(c == DCH - 1 and rhs is gwT_hi and lhs is xlo))
                        first = False
                return ins
            ch.step("tensor", mm_logits)
            G = sb["G"].ap()
            Lf = G[:, 32:40]
            nm1 = G[:, 40:41]
            nm2 = G[:, 41:42]
            eq = G[:, 42:50]
            l2 = G[:, 50:58]
            w = G[:, 58:66]
            ssum = G[:, 66:67]
            rs = G[:, 67:68]
            combf = G[:, 68:76]
            combb = sb["F"].ap()[0:128, 2048:2056]
            ch.step("vector", lambda e: e.tensor_copy(Lf, L))
            ch.step("vector", lambda e: e.reduce_max(nm1, Lf, AX.X, negate=True))
            ch.step("vector", lambda e: e.tensor_scalar(eq, Lf, nm1, 0.0, ALU.add, ALU.is_equal))
            ch.step("vector", lambda e: e.scalar_tensor_tensor(l2, eq, NEG, Lf, ALU.mult, ALU.add))
            ch.step("vector", lambda e: e.reduce_max(nm2, l2, AX.X, negate=True))
            ch.step("vector", lambda e: e.tensor_scalar(eq, Lf, nm2, 0.0, ALU.add, ALU.is_ge))
            ch.step("scalar", lambda e: e.activation(l2, Lf, AF.Exp, bias=nm1))
            ch.step("vector", lambda e: e.tensor_mul(w, l2, eq))
            ch.step("vector", lambda e: e.reduce_sum(ssum, w, AX.X))
            ch.step("vector", lambda e: e.reciprocal(rs, ssum))
            ch.step("vector", lambda e: e.tensor_scalar(combf, w, rs, None, ALU.mult))
            ch.step("vector", lambda e: e.tensor_copy(combb, combf))
            ch.step("tensor", lambda e: e.transpose(ps["tb0"].ap()[0:8, 0:128], combb, idb))
            ch.step("vector", lambda e: e.tensor_copy(sb["F"].ap()[0:8, 2056:2184], ps["tb0"].ap()[0:8, 0:128]))
            ch.step("gpsimd", lambda e: e.dma_start(
                out=comb[0:8, t0 + qq * 128:t0 + (qq + 1) * 128],
                in_=sb["F"].ap()[0:8, 2056:2184]), 16)
    ch.wait_all()


def load_wT(ch, ctx, w_dram, dest, stage, idb, tbp):
    ch.step("gpsimd", lambda e: e.dma_start(
        out=stage, in_=w_dram.rearrange("(oc p) d -> p oc d", p=128)), 16)
    for oc in range(8):
        for dc in range(8):
            o_, d_ = oc, dc
            ch.step("tensor", lambda e: e.transpose(
                tbp[:, 0:128], stage[:, o_, d_ * 128:(d_ + 1) * 128], idb))
            ch.step("vector", lambda e: e.tensor_copy(
                dest[:, d_, o_ * 128:(o_ + 1) * 128], tbp[:, 0:128]))


def attn_phase(ctx, en, eng, key, xn_dram, wq, wk, wv, wo, bq, bk, bv, bo,
               out_dram, window, n_b=B_LOC, n_h=NH):
    sb, ps, cs = ctx.sb, ctx.ps, ctx.consts
    A, B, C, Dd, Ef, F, G, H, I, X = (sb[k].ap() for k in
                                      ("A", "B", "C", "D", "E", "F", "G", "H", "I", "X"))
    op, sems, waitall = mk_ops(ctx, en, eng, key)
    PE, VE, SC, LD, GP = (sems[k] for k in ("tensor", "vector", "scalar", "sync", "gpsimd"))
    Qf = A[:, 0:16384].rearrange("p (c t) -> p c t", c=8)
    Kf = A[:, 16384:32768].rearrange("p (c t) -> p c t", c=8)
    Vt = B[:, 0:16384].rearrange("p (kt o) -> p kt o", kt=16)
    Cx = B[:, 16384:32768].rearrange("p (c t) -> p c t", c=8)
    Xt = X[:, 0:4096].rearrange("p (c t) -> p c t", c=8)
    wT0 = C[:, 0:8192].rearrange("p (c o) -> p c o", c=8)
    wT1 = C[:, 8192:16384].rearrange("p (c o) -> p c o", c=8)
    stage0 = B[:, 0:8192].rearrange("p (oc d) -> p oc d", oc=8)
    stageV = B[:, 16384:24576].rearrange("p (oc d) -> p oc d", oc=8)
    stageO = A[:, 0:8192].rearrange("p (oc d) -> p oc d", oc=8)
    idb = H[:, 896:1024]
    maskc = I[:, 0:128]
    maskf = Dd[:, 2560:2688]
    bcols = G[:, 456:488].rearrange("p (w c) -> p w c", w=4)
    qcol = G[:, 448:456]
    tb = [ps["tb0"].ap(), ps["tb1"].ap()]
    has_bias = bq is not None
    K_ = key
    nW = window // 128 if window is not None else None

    def kt_lo_of(qi):
        return 0 if nW is None else max(0, qi - nW)

    def nkt_of(gi):
        qi = gi % 16
        return qi + 1 - kt_lo_of(qi)

    def pt_eng(j):
        return VE if j % 2 == 0 else SC

    def last_pt_deps(b, gi):
        nk = nkt_of(gi)
        d = [(f"{K_}pt{b}_{gi}_{nk-1}", pt_eng(nk - 1))]
        if nk >= 2:
            d.append((f"{K_}pt{b}_{gi}_{nk-2}", pt_eng(nk - 2)))
        return d

    tbdep = [[], []]

    LD2, GP2, GPW = sems["sync2"], sems["gpsimd2"], sems["gpsimdw"]
    op("sync2", lambda e: e.dma_start(out=idb, in_=cs["ident_b"]))
    op("sync2", lambda e: e.dma_start(out=maskc, in_=cs["mask_causal"]))
    if window is not None:
        op("sync2", lambda e: e.dma_start(out=maskf, in_=cs["mask_far"]))
    if has_bias:
        brows = ctx.t("b5rows")
        for w_i, bt in enumerate((bq, bk, bv, bo)):
            wi, btt = w_i, bt
            op("gpsimd", lambda e, wi=wi, btt=btt: e.dma_start(
                out=brows[wi:wi + 1, :], in_=btt[None, :]), mark=f"{K_}brow{wi}")
        op("sync2", lambda e: e.dma_start_transpose(
            out=H[:, 512:544], in_=brows.rearrange("w (c p) -> (w c) p", p=128)),
            deps=[(f"{K_}brow3", GP)])
        op("vector", lambda e: e.tensor_copy(G[:, 456:488], H[:, 512:544]),
           deps=[(f"{K_}cst", sems["sync2"])], mark=f"{K_}bcf")
    op("sync2", lambda e: e.nop(), mark=f"{K_}cst") if False else None
    if en == "sync":
        ctx.mark(f"{K_}cst")
    if has_bias:
        op("vector", lambda e: e.tensor_scalar(qcol, bcols[:, 0, :], 0.125, None, ALU.mult),
           deps=[(f"{K_}bcf", sems["vector"])], mark=f"{K_}qcol")

    last_stage = [None]

    def load_w(b, wname, w_dram, dest, stage, stage_dep):
        mk = f"{K_}w{b}{wname}"
        sdep = list(stage_dep)
        if last_stage[0]:
            sdep.append((last_stage[0] + "st", GPW))
            sdep.append((last_stage[0] + "tr63", PE))
        op("gpsimdw", lambda e: e.dma_start(
            out=stage, in_=w_dram.rearrange("(oc p) d -> p oc d", p=128)),
            deps=sdep, mark=mk + "st")
        last_stage[0] = mk
        for i in range(64):
            o_, d_ = i // 8, i % 8
            ii = i
            if i >= 2:
                deps = [(mk + f"cp{ii-2}", VE)]
            else:
                deps = [(mk + "st", GPW), (f"{K_}cst", LD2)] + tbdep[i % 2]
            op("tensor", lambda e, o_=o_, d_=d_, ii=ii: e.transpose(
                tb[ii % 2][:, 0:128], stage[:, o_, d_ * 128:(d_ + 1) * 128], idb),
                deps=deps, mark=mk + f"tr{ii}")
            op("vector", lambda e, o_=o_, d_=d_, ii=ii: e.tensor_copy(
                dest[:, d_, o_ * 128:(o_ + 1) * 128], tb[ii % 2][:, 0:128]),
                deps=[(mk + f"tr{ii}", PE)], mark=mk + f"cp{ii}")
        tbdep[0] = [(mk + "cp62", VE)]
        tbdep[1] = [(mk + "cp63", VE)]
        return mk + "cp63"

    for b in range(n_b):
        tbse = b * S
        dep_prevav = [(f"{K_}av{b-1}_{n_h-1}_15", PE)] if b > 0 else []
        wq_done = load_w(b, "q", wq, wT0, stage0, dep_prevav)
        wk_done = load_w(b, "k", wk, wT1, stage0, [])

        for tt in range(4):
            t0 = tt * 512
            if tt > 0:
                xdeps = [(f"{K_}qkm{b}_{tt-1}_15", PE)]
            elif b > 0:
                xdeps = [(f"{K_}av{b-1}_{n_h-1}_15", PE)]
            else:
                xdeps = []
            for c in range(DCH):
                cc = c
                op("sync", lambda e, cc=cc, t0=t0: e.dma_start_transpose(
                    out=Xt[:, cc, :], in_=xn_dram[tbse + t0:tbse + t0 + 512,
                                                  cc * 128:(cc + 1) * 128]),
                    deps=xdeps if cc == 0 else [],
                    mark=f"{K_}qkx{b}_{tt}" if cc == DCH - 1 else None)
            for g in range(16):
                wh, o_ = (0, g) if g < 8 else (1, g - 8)
                wT = wT0 if wh == 0 else wT1
                pp = ps["p0"].ap() if g % 2 == 0 else ps["p1"].ap()
                gg = g
                deps = [(f"{K_}qkx{b}_{tt}", LD)]
                if g == 0 and tt == 0:
                    deps.append((wk_done, VE))
                    if b > 0:
                        deps.append((f"{K_}cx{b-1}_{n_h*16-1}", SC))
                if g >= 2:
                    deps.append((f"{K_}qke{b}_{tt}_{gg-2}", SC))
                elif tt > 0:
                    deps.append((f"{K_}qke{b}_{tt-1}_{gg+14}", SC))

                def mmqk(e, wT=wT, o_=o_, pp=pp):
                    for dc in range(DCH):
                        ins = e.matmul(pp[:], wT[:, dc, o_ * 128:(o_ + 1) * 128],
                                       Xt[:, dc, :], start=(dc == 0), stop=(dc == DCH - 1))
                    return ins
                op("tensor", mmqk, deps=deps, mark=f"{K_}qkm{b}_{tt}_{gg}")
                dst = Qf if wh == 0 else Kf
                sc_ = 0.125 if wh == 0 else 1.0
                if has_bias:
                    bias_ap = qcol[:, o_:o_ + 1] if wh == 0 else bcols[:, 1, o_:o_ + 1]
                else:
                    bias_ap = 0.0
                edeps = [(f"{K_}qkm{b}_{tt}_{gg}", PE)]
                if has_bias and b == 0 and tt == 0 and gg == 0:
                    edeps.append((f"{K_}qcol", VE))
                    edeps.append((f"{K_}cst", LD2))
                op("scalar", lambda e, dst=dst, o_=o_, t0=t0, pp=pp, sc_=sc_, bias_ap=bias_ap:
                    e.activation(dst[:, o_, t0:t0 + 512], pp[:], AF.Identity, bias=bias_ap, scale=sc_),
                    deps=edeps, mark=f"{K_}qke{b}_{tt}_{gg}")

        wv_done = load_w(b, "v", wv, wT0, stageV, [(f"{K_}qkm{b}_3_7", PE)])
        for tt in range(4):
            t0 = tt * 512
            xdeps = [(f"{K_}qkm{b}_3_15" if tt == 0 else f"{K_}vm{b}_{tt-1}_7", PE)]
            for c in range(DCH):
                cc = c
                op("sync", lambda e, cc=cc, t0=t0: e.dma_start_transpose(
                    out=Xt[:, cc, :], in_=xn_dram[tbse + t0:tbse + t0 + 512,
                                                  cc * 128:(cc + 1) * 128]),
                    deps=xdeps if cc == 0 else [],
                    mark=f"{K_}vx{b}_{tt}" if cc == DCH - 1 else None)
            for g in range(8):
                qq, og = g // 2, g % 2
                pp = ps["p0"].ap() if g % 2 == 0 else ps["p1"].ap()
                gg = g
                deps = [(f"{K_}vx{b}_{tt}", LD)]
                if g == 0 and tt == 0:
                    deps.append((wv_done, VE))
                if g < 2 and tt == 0:
                    deps.append((f"{K_}qke{b}_3_{14 + gg}", SC))
                if g >= 2:
                    deps.append((f"{K_}ve{b}_{tt}_{gg-2}", VE))
                elif tt > 0:
                    deps.append((f"{K_}ve{b}_{tt-1}_{gg+6}", VE))

                def mmv(e, qq=qq, og=og, pp=pp):
                    for dc in range(DCH):
                        ins = e.matmul(pp[:], Xt[:, dc, qq * 128:(qq + 1) * 128],
                                       wT0[:, dc, og * 512:(og + 1) * 512],
                                       start=(dc == 0), stop=(dc == DCH - 1))
                    return ins
                op("tensor", mmv, deps=deps, mark=f"{K_}vm{b}_{tt}_{gg}")
                kt = tt * 4 + qq
                op("vector", lambda e, kt=kt, og=og, pp=pp: e.tensor_copy(
                    Vt[:, kt, og * 512:(og + 1) * 512], pp[:]),
                    deps=[(f"{K_}vm{b}_{tt}_{gg}", PE)], mark=f"{K_}ve{b}_{tt}_{gg}")

        rowbs = [Dd[:, 0:2048], Ef[:, 0:2048], C[:, 8192:16384].bitcast(F32)]
        Pbs = [F[:, 0:2048], X[:, 0:2048], C[:, 0:2048]]
        Pts = [F[:, 2048:4096], X[:, 2048:4096], C[:, 2048:4096]]

        def row_env(gi):
            h, qi = gi // 16, gi % 16
            sl = gi % 3
            psA = ps[f"p{2*sl}"].ap()
            psB = ps[f"p{2*sl+1}"].ap()
            st8 = G[:, sl * 8:sl * 8 + 8]
            return dict(h=h, qi=qi, sl=sl, hc=h // 2, hr=(h % 2) * 64,
                        rowb=rowbs[sl], Pb=Pbs[sl], Pt=Pts[sl], psA=psA, psB=psB,
                        tbA=psA.bitcast(BF16), tbB=psB.bitcast(BF16),
                        pav=psA[0:64, 0:128],
                        negm=st8[:, 0:1], rsum=st8[:, 1:2], rinv=st8[:, 2:3],
                        kt_lo=kt_lo_of(qi), L0=kt_lo_of(qi) * 128,
                        L=(qi + 1) * 128 - kt_lo_of(qi) * 128)

        def emit_scores(gi):
            v = row_env(gi)
            h, qi, sl = v["h"], v["qi"], v["sl"]
            qs = Qf[v["hr"]:v["hr"] + 64, v["hc"], :]
            ks = Kf[v["hr"]:v["hr"] + 64, v["hc"], :]
            far = nW is not None and qi - v["kt_lo"] == nW
            L0, L = v["L0"], v["L"]
            pgi = gi - 3
            ng = (L + 511) // 512
            for kg in range(ng):
                k0 = L0 + kg * 512
                n = min(512, (qi + 1) * 128 - k0)
                pp = v["psA"] if kg % 2 == 0 else v["psB"]
                deps = []
                if kg >= 2:
                    deps.append((f"{K_}sev{b}_{gi}_{kg-2}", VE))
                elif pgi >= 0:
                    deps += last_pt_deps(b, pgi)
                    deps.append((f"{K_}cx{b}_{pgi}", SC))
                if gi == 0:
                    deps.append((f"{K_}qke{b}_3_15", SC))
                    deps.append((f"{K_}ve{b}_3_7", VE))
                    if b == 0:
                        deps.append((f"{K_}cst", LD2))
                if gi == 1 and b > 0:
                    deps.append((f"{K_}ops{b-1}_31", GP2))
                op("tensor", lambda e, k0=k0, n=n, pp=pp, qs=qs, ks=ks, qi=qi:
                    e.matmul(pp[:, 0:n], qs[:, qi * 128:(qi + 1) * 128],
                             ks[:, k0:k0 + n], start=True, stop=True),
                    deps=deps, mark=f"{K_}ss{b}_{gi}_{kg}")
                off0 = k0 - L0
                segs = [[0, n, None]]
                if kg == ng - 1:
                    segs = ([[0, n - 128, None]] if n > 128 else []) + [[n - 128, 128, "c"]]
                if far and kg == 0:
                    if segs[0][2] is None:
                        hn = segs[0][1]
                        rest = ([[128, hn - 128, None]] if hn > 128 else []) + segs[1:]
                    else:
                        rest = []
                    segs = [[0, 128, "f"]] + rest
                segs = [sg for sg in segs if sg[1] > 0]
                use_sc = False
                for si, (o, nn, mk) in enumerate(segs):
                    oo, nn_, mk_ = o, nn, mk
                    deps = [(f"{K_}ss{b}_{gi}_{kg}", PE)]
                    if kg == 0 and si == 0 and pgi >= 0:
                        deps.append((f"{K_}xp{b}_{pgi}", SC))
                    mkname = f"{K_}sev{b}_{gi}_{kg}" if si == len(segs) - 1 else None
                    rowb = v["rowb"]
                    if mk_ is None and use_sc:
                        op("scalar", lambda e, pp=pp, oo=oo, nn_=nn_, off0=off0, rowb=rowb:
                            e.copy(rowb[:, off0 + oo:off0 + oo + nn_], pp[:, oo:oo + nn_]),
                            deps=deps, mark=mkname)
                    elif mk_ is None:
                        op("vector", lambda e, pp=pp, oo=oo, nn_=nn_, off0=off0, rowb=rowb:
                            e.tensor_copy(rowb[:, off0 + oo:off0 + oo + nn_],
                                          pp[:, oo:oo + nn_]),
                            deps=deps, mark=mkname)
                    else:
                        msk = maskc if mk_ == "c" else maskf
                        op("vector", lambda e, pp=pp, oo=oo, nn_=nn_, off0=off0, msk=msk, rowb=rowb:
                            e.tensor_add(rowb[:, off0 + oo:off0 + oo + nn_],
                                         pp[:, oo:oo + nn_], msk),
                            deps=deps, mark=mkname)
            return ng

        def emit_softmax(gi, ng):
            v = row_env(gi)
            pgi = gi - 3
            rowb, Pb, L = v["rowb"], v["Pb"], v["L"]
            negm, rsum, rinv = v["negm"], v["rsum"], v["rinv"]
            mdeps = [(f"{K_}sev{b}_{gi}_{ng-1}", VE)]
            op("vector", lambda e, rowb=rowb, L=L, negm=negm:
                e.reduce_max(negm, rowb[:, 0:L], AX.X, negate=True),
                deps=mdeps, mark=f"{K_}rw{b}_{gi}")
            deps = [(f"{K_}rw{b}_{gi}", VE)]
            if pgi >= 0:
                deps.append((f"{K_}tp{b}_{pgi}", PE))
            op("scalar", lambda e, Pb=Pb, rowb=rowb, L=L, negm=negm, rsum=rsum:
                e.activation(Pb[:, 0:L], rowb[:, 0:L], AF.Exp, bias=negm, accum_out=rsum),
                deps=deps, mark=f"{K_}xp{b}_{gi}")
            op("vector", lambda e, rinv=rinv, rsum=rsum: e.reciprocal(rinv, rsum),
                deps=[(f"{K_}xp{b}_{gi}", SC)], mark=f"{K_}rc{b}_{gi}")
            op("vector", lambda e, Pb=Pb, L=L, rinv=rinv: e.tensor_scalar(
                Pb[:, 0:L], Pb[:, 0:L], rinv, None, ALU.mult),
                deps=[(f"{K_}rc{b}_{gi}", VE)], mark=f"{K_}pn{b}_{gi}")

        def emit_transposes(gi):
            v = row_env(gi)
            Pb, Pt = v["Pb"], v["Pt"]
            nkt = nkt_of(gi)
            for j in range(nkt):
                jj = j
                tpp = (v["tbA"] if j % 2 == 0 else v["tbB"])[:, 0:128]
                if j >= 2:
                    deps = [(f"{K_}pt{b}_{gi}_{jj-2}", pt_eng(jj - 2))]
                else:
                    deps = [(f"{K_}pn{b}_{gi}", VE)]
                op("tensor", lambda e, jj=jj, tpp=tpp, Pb=Pb: e.transpose(
                    tpp, Pb[:, jj * 128:(jj + 1) * 128], idb),
                    deps=deps, mark=f"{K_}tr{b}_{gi}_{jj}")
                if en == "tensor" and jj == nkt - 1:
                    ctx.mark(f"{K_}tp{b}_{gi}")
                if jj % 2 == 0:
                    op("vector", lambda e, tpp=tpp, jj=jj, Pt=Pt: e.tensor_copy(
                        Pt[:, jj * 128:(jj + 1) * 128], tpp),
                        deps=[(f"{K_}tr{b}_{gi}_{jj}", PE)], mark=f"{K_}pt{b}_{gi}_{jj}")
                else:
                    op("scalar", lambda e, tpp=tpp, jj=jj, Pt=Pt: e.copy(
                        Pt[:, jj * 128:(jj + 1) * 128], tpp),
                        deps=[(f"{K_}tr{b}_{gi}_{jj}", PE)], mark=f"{K_}pt{b}_{gi}_{jj}")

        def emit_av(gi):
            v = row_env(gi)
            h, qi = v["h"], v["qi"]
            Pt, pav, kt_lo = v["Pt"], v["pav"], v["kt_lo"]
            nkt = nkt_of(gi)
            deps = last_pt_deps(b, gi)

            def mm_av(e, nkt=nkt, kt_lo=kt_lo, h=h, pav=pav, Pt=Pt):
                for j in range(nkt):
                    ins = e.matmul(pav, Vt[:, kt_lo + j, h * 64:(h + 1) * 64],
                                   Pt[:, j * 128:(j + 1) * 128],
                                   start=(j == 0), stop=(j == nkt - 1))
                return ins
            op("tensor", mm_av, deps=deps, mark=f"{K_}av{b}_{h}_{qi}")

        def emit_cx(gi):
            v = row_env(gi)
            h, qi = v["h"], v["qi"]
            pav = v["pav"]
            bias_ap = (bcols[(h % 2) * 64:(h % 2) * 64 + 64, 2, v["hc"]:v["hc"] + 1]
                       if has_bias else 0.0)
            op("scalar", lambda e, hc=v["hc"], hr=v["hr"], qi=qi, pav=pav, bias_ap=bias_ap:
                e.activation(Cx[hr:hr + 64, hc, qi * 128:(qi + 1) * 128], pav,
                             AF.Identity, bias=bias_ap),
                deps=[(f"{K_}av{b}_{h}_{qi}", PE)], mark=f"{K_}cx{b}_{gi}")

        nrows = n_h * 16
        for i2 in range(nrows + 3):
            if i2 >= 3 and i2 - 3 < nrows:
                emit_cx(i2 - 3)
            if i2 < nrows:
                ng = emit_scores(i2)
            if i2 >= 2 and i2 - 2 < nrows:
                emit_transposes(i2 - 2)
            if i2 < nrows:
                emit_softmax(i2, ng)
            if i2 >= 2 and i2 - 2 < nrows:
                emit_av(i2 - 2)

        gl = n_h * 16 - 1
        wo_done = load_w(b, "o", wo, wT0, stageO, [(f"{K_}av{b}_{n_h-1}_15", PE)])
        for g in range(32):
            tt, o_ = g // 8, g % 8
            t0 = tt * 512
            pp = ps["p0"].ap() if g % 2 == 0 else ps["p1"].ap()
            gg = g
            deps = []
            if g == 0:
                deps.append((wo_done, VE))
                deps.append((f"{K_}cx{b}_{gl}", SC))
            if g >= 2:
                deps.append((f"{K_}ope{b}_{gg-2}", VE))

            def mmo(e, o_=o_, pp=pp, t0=t0):
                for dc in range(DCH):
                    ins = e.matmul(pp[:], wT0[:, dc, o_ * 128:(o_ + 1) * 128],
                                   Cx[:, dc, t0:t0 + 512], start=(dc == 0), stop=(dc == DCH - 1))
                return ins
            op("tensor", mmo, deps=deps, mark=f"{K_}opm{b}_{gg}")
            ev = Dd[:, 2048:2560] if g % 2 == 0 else Ef[:, 0:512]
            deps = [(f"{K_}opm{b}_{gg}", PE)]
            if g >= 2:
                deps.append((f"{K_}ops{b}_{gg-2}", GP if (gg - 2) % 2 == 0 else GP2))
            elif b > 0:
                deps.append((f"{K_}ops{b-1}_{30 + gg}", GP if gg % 2 == 0 else GP2))
            if has_bias:
                op("vector", lambda e, ev=ev, pp=pp, o_=o_: e.tensor_scalar(
                    ev, pp[:], bcols[:, 3, o_:o_ + 1], None, ALU.add),
                    deps=deps, mark=f"{K_}ope{b}_{gg}")
            else:
                op("vector", lambda e, ev=ev, pp=pp: e.tensor_copy(ev, pp[:]),
                    deps=deps, mark=f"{K_}ope{b}_{gg}")
            kwargs = {"accum_op": ALU.add} if window is None else {}
            op("gpsimd" if g % 2 == 0 else "gpsimd2",
               lambda e, ev=ev, o_=o_, t0=t0, kwargs=kwargs: e.dma_start(
                out=out_dram[o_ * 128:(o_ + 1) * 128, tbse + t0:tbse + t0 + 512],
                in_=ev, **kwargs),
                deps=[(f"{K_}ope{b}_{gg}", VE)], mark=f"{K_}ops{b}_{gg}")
    waitall()


def p3z_zero_acc(ctx, en, eng):
    sb = ctx.sb
    ch = Chain(ctx, en, eng, "p3z")
    accf = ctx.t("accf")
    z = sb["D"].ap()[:, 0:2048]
    ch.step("vector", lambda e: e.memset(z, 0.0))
    for dc in range(DCH):
        for half in range(T // 2048):
            d0, h0 = dc * 128, half * 2048
            ch.step("sync", lambda e: e.dma_start(
                out=accf[d0:d0 + 128, h0:h0 + 2048], in_=z), 16)
    ch.wait_all()


def p3_moe(ctx, en, eng, n_tt=NMT):
    sb, ps, cs = ctx.sb, ctx.ps, ctx.consts
    A, B, C, Dd, F, X = (sb[k].ap() for k in ("A", "B", "C", "D", "F", "X"))
    xn0, comb, accf = ctx.t("xn0"), ctx.t("comb"), ctx.t("accf")
    w1 = ctx.t("w_moe1")
    w2 = ctx.t("w_moe2")
    sw1 = ctx.t("w_sw1")
    sw2 = ctx.t("w_sw2")
    b2 = ctx.t("b_moe2")
    sb2 = ctx.t("b_ssb2")
    b1b = ctx.t("b1b")
    b1 = ctx.t("b_moe1")
    sb1 = ctx.t("b_ssb1")

    onehot = F[0:8, 0:1024]
    Be = F[:, 1024:1536]
    combst = F[0:8, 1536:2048]
    b2r = F[0:8, 2048:3072]
    sb2r = F[0:1, 3072:4096]
    onesr = sb["H"].ap()[0:1, 0:512]
    b1col = F[:, 3584:3616]

    ch = Chain(ctx, en, eng, "p3p")
    ch.step("gpsimd", lambda e: e.dma_start(out=b1b[0:8, :], in_=b1), 16)
    ch.step("gpsimd", lambda e: e.dma_start(out=b1b[8:9, :], in_=sb1[None, :]), 16)
    ch.step("gpsimd", lambda e: e.dma_start(out=b2r, in_=b2), 16)
    ch.step("gpsimd", lambda e: e.dma_start(out=sb2r, in_=sb2[None, :]), 16)
    ch.step("sync", lambda e: e.dma_start(out=onehot, in_=cs["onehot8"]), 16)
    ch.step("sync", lambda e: e.dma_start(out=onesr, in_=cs["ones_row_b"]), 16)
    ch.wait_all()

    Aw = A[:, 0:8 * 4096].rearrange("p (c f) -> p c f", c=8)
    Bw = B[:, 0:32 * 1024].rearrange("p (c o) -> p c o", c=32)
    Hb = C[:, 0:32 * 512].rearrange("p (c t) -> p c t", c=32)
    Xt = X[:, 0:8 * 512].rearrange("p (c t) -> p c t", c=8)

    for m in range(9):
        shared = (m == 0)
        e_id = m - 1
        if en == "gpsimd":
            if m > 0:
                ctx.wait(eng, "mpe", ctx.at(f"m_e{m-1}_end", "mpe"))
            src1 = sw1 if shared else w1[e_id]
            src2 = sw2 if shared else w2[e_id]
            ctx.inc(eng.dma_start(out=Aw, in_=src1.rearrange("(c p) f -> p c f", p=128)), "mgw", 16)
            ctx.inc(eng.dma_start(out=Bw, in_=src2.rearrange("(c p) o -> p c o", p=128)), "mgw", 16)
            ctx.mark(f"m_w_{m}")
        if en == "sync":
            if m > 0:
                ctx.wait(eng, "msc", ctx.at(f"m_gelu_end_{m-1}", "msc"))
            ctx.inc(eng.dma_start_transpose(
                out=b1col, in_=b1b[8 if shared else e_id].rearrange("(c p) -> c p", p=128)), "mlb", 16)
            ctx.mark(f"m_b1_{m}")

        for it in range(n_tt):
            t0 = it * MTT
            p1a = ps["p0"].ap()
            p1b = ps["p1"].ap()
            p2a = ps["p2"].ap()
            p2b = ps["p3"].ap()
            pbc = ps["p4"].ap()

            if en == "sync":
                if it > 0:
                    ctx.wait(eng, "mpe", ctx.at(f"m_s1_{m}_{it-1}_31", "mpe"))
                elif m > 0:
                    ctx.wait(eng, "mpe", ctx.at(f"m_e{m-1}_end", "mpe"))
                for c in range(DCH):
                    ctx.inc(eng.dma_start_transpose(
                        out=Xt[:, c, :], in_=xn0[t0:t0 + MTT, c * 128:(c + 1) * 128]), "mlx", 16)
                ctx.mark(f"m_x_{m}_{it}")

            if not shared:
                if en == "gpsimd":
                    if m == 8 and it > 0:
                        ctx.wait(eng, "mpe", ctx.at(f"m_s2_{8}_{it-1}_{7}", "mpe"))
                    elif not (m == 1 and it == 0):
                        ctx.wait(eng, "mpe", ctx.at(f"m_bc_prev_{m}_{it}", "mpe"))
                    ctx.inc(eng.dma_start(out=combst, in_=comb[0:8, t0:t0 + MTT]), "mgc", 16)
                    ctx.mark(f"m_compst_{m}_{it}")
                if en == "tensor":
                    ctx.wait(eng, "mgc", ctx.at(f"m_compst_{m}_{it}", "mgc"))
                    if not (m == 1 and it == 0):
                        ctx.wait(eng, "mev", ctx.at(f"m_bee_prev_{m}_{it}", "mev"))
                    ctx.inc(eng.matmul(pbc[:], onehot[:, e_id * 128:(e_id + 1) * 128],
                                       combst, start=True, stop=True), "mpe", 1)
                    ctx.mark(f"m_bc_{m}_{it}")
                    ctx.mark(f"m_bc_prev_{m}_{it + 1}" if it + 1 < n_tt else f"m_bc_prev_{m + 1}_0")
                if en == "vector":
                    ctx.wait(eng, "mpe", ctx.at(f"m_bc_{m}_{it}", "mpe"))
                    ctx.inc(eng.tensor_copy(Be, pbc[:]), "mev", 1)
                    ctx.mark(f"m_bee_{m}_{it}")
                    ctx.mark(f"m_bee_prev_{m}_{it + 1}" if it + 1 < n_tt else f"m_bee_prev_{m + 1}_0")

            for fc in range(NFC):
                pp = p1a if fc % 2 == 0 else p1b
                if en == "tensor":
                    ctx.wait(eng, "mlx", ctx.at(f"m_x_{m}_{it}", "mlx"))
                    if fc == 0:
                        ctx.wait(eng, "mgw", ctx.at(f"m_w_{m}", "mgw"))
                    if fc >= 2:
                        ctx.wait(eng, "msc", ctx.at(f"m_g_{m}_{it}_{fc-2}", "msc"))
                    elif it > 0:
                        ctx.wait(eng, "msc", ctx.at(f"m_g_{m}_{it-1}_{31 if fc == 0 else 31}", "msc") - (1 - fc))
                    elif m > 0:
                        ctx.wait(eng, "msc", ctx.at(f"m_gelu_end_{m-1}", "msc"))
                    for dc in range(DCH):
                        ins = eng.matmul(pp[:], Aw[:, dc, fc * 128:(fc + 1) * 128],
                                         Xt[:, dc, :], start=(dc == 0), stop=(dc == DCH - 1))
                    ctx.inc(ins, "mpe", 1)
                    ctx.mark(f"m_s1_{m}_{it}_{fc}")
                if en == "scalar":
                    ctx.wait(eng, "mpe", ctx.at(f"m_s1_{m}_{it}_{fc}", "mpe"))
                    ctx.wait(eng, "mlb", ctx.at(f"m_b1_{m}", "mlb"))
                    ctx.inc(eng.activation(Hb[:, fc, :], (p1a if fc % 2 == 0 else p1b)[:],
                                           AF.Gelu, bias=b1col[:, fc:fc + 1]), "msc", 1)
                    ctx.mark(f"m_g_{m}_{it}_{fc}")
                if not shared and en == "vector":
                    ctx.wait(eng, "msc", ctx.at(f"m_g_{m}_{it}_{fc}", "msc"))
                    ctx.wait(eng, "mev", ctx.at(f"m_bee_{m}_{it}", "mev"))
                    ctx.inc(eng.tensor_mul(Hb[:, fc, :], Hb[:, fc, :], Be), "mev", 1)
                    ctx.mark(f"m_sc_{m}_{it}_{fc}")
            if en in ("scalar",) and it == n_tt - 1:
                ctx.mark(f"m_gelu_end_{m}")

            for oc in range(8):
                pp = p2a if oc % 2 == 0 else p2b
                if en == "tensor":
                    if not shared:
                        ctx.wait(eng, "mev", ctx.at(f"m_sc_{m}_{it}_{31}", "mev"))
                    else:
                        ctx.wait(eng, "msc", ctx.at(f"m_g_{m}_{it}_{31}", "msc"))
                    if oc >= 2:
                        ctx.wait(eng, "mev", ctx.at(f"m_ev_{m}_{it}_{oc-2}", "mev"))
                    elif it > 0:
                        ctx.wait(eng, "mev", ctx.at(f"m_ev_{m}_{it-1}_{6 + oc}", "mev"))
                    elif m > 0:
                        ctx.wait(eng, "mev", ctx.at(f"m_ev_{m-1}_{n_tt-1}_{6 + oc}", "mev"))
                    bias_here = (m == 8)
                    for fc in range(NFC):
                        last = (fc == NFC - 1) and not bias_here
                        ins = eng.matmul(pp[:], Bw[:, fc, oc * 128:(oc + 1) * 128],
                                         Hb[:, fc, :], start=(fc == 0), stop=last)
                    if bias_here:
                        ins = eng.matmul(pp[:], b2r[:, oc * 128:(oc + 1) * 128], combst,
                                         start=False, stop=False)
                        ins = eng.matmul(pp[:], sb2r[:, oc * 128:(oc + 1) * 128], onesr,
                                         start=False, stop=True)
                    ctx.inc(ins, "mpe", 1)
                    ctx.mark(f"m_s2_{m}_{it}_{oc}")
                if en == "vector":
                    ctx.wait(eng, "mpe", ctx.at(f"m_s2_{m}_{it}_{oc}", "mpe"))
                    dsl = Dd[:, (oc % 2) * 512:(oc % 2) * 512 + 512]
                    par = f"mst{oc % 2}"
                    if not (m == 0 and it == 0 and oc < 2):
                        key = (f"m_st_{m}_{it}_{oc-2}" if oc >= 2 else
                               (f"m_st_{m}_{it-1}_{6+oc}" if it > 0 else
                                f"m_st_{m-1}_{n_tt-1}_{6+oc}"))
                        ctx.wait(eng, par, ctx.at(key, par))
                    ctx.inc(eng.tensor_copy(dsl, pp[:]), "mev", 1)
                    ctx.mark(f"m_ev_{m}_{it}_{oc}")
                if en == "gpsimd":
                    ctx.wait(eng, "mev", ctx.at(f"m_ev_{m}_{it}_{oc}", "mev"))
                    dsl = Dd[:, (oc % 2) * 512:(oc % 2) * 512 + 512]
                    ctx.inc(eng.dma_start(out=accf[oc * 128:(oc + 1) * 128, t0:t0 + MTT],
                                          in_=dsl, accum_op=ALU.add), f"mst{oc % 2}", 16)
                    ctx.mark(f"m_st_{m}_{it}_{oc}")
        if en == "tensor":
            ctx.mark(f"m_e{m}_end")
    ctx.wait(eng, "mst0", ctx.at(f"m_st_{8}_{n_tt-1}_{6}", "mst0"))
    ctx.wait(eng, "mst1", ctx.at(f"m_st_{8}_{n_tt-1}_{7}", "mst1"))


def p4_ln1(ctx, en, eng):
    sb, ps, cs = ctx.sb, ctx.ps, ctx.consts
    C, Dd, Ef, F, G, I, X = (sb[k].ap() for k in ("C", "D", "E", "F", "G", "I", "X"))
    Xf32 = X.bitcast(F32)
    Cf32 = C.bitcast(F32)
    ch0 = Chain(ctx, en, eng, "p4A")
    ch1 = Chain(ctx, en, eng, "p4B")
    chs = [ch0, ch1]
    x = ctx.t("x_in").rearrange("b s d -> (b s) d")
    accf, x1ag, xn1 = ctx.t("accf"), ctx.t("x1ag"), ctx.t("xn1")
    idf = I[:, 0:128]
    ch0.step("sync", lambda e: e.dma_start(out=idf, in_=cs["ident_f"]), 16)
    ch1.join([ch0])
    for i in range(NT):
        sl = i % 2
        cc_ = chs[sl]
        r0 = i * 128
        xt = Dd[:, 0:1024] if sl == 0 else Dd[:, 1024:2048]
        chunks = Ef[:, 0:1024] if sl == 0 else Ef[:, 1024:2048]
        xsq = Xf32[:, 0:1024] if sl == 0 else Xf32[:, 1024:2048]
        x1t = Dd[:, 2048:3072] if sl == 0 else Cf32[:, 0:1024]
        xnb = F[:, 0:1024] if sl == 0 else F[:, 1024:2048]
        st = G[:, sl * 16:sl * 16 + 16]
        cc_.dma_group("sync", [lambda e, xt=xt, r0=r0: e.dma_start(out=xt, in_=x[r0:r0 + 128, :])] + [
            (lambda e, d_=dc, r0=r0, chunks=chunks: e.dma_start(
                out=chunks[:, d_ * 128:(d_ + 1) * 128],
                in_=accf[d_ * 128:(d_ + 1) * 128, r0:r0 + 128]))
            for dc in range(DCH)])
        for dc in range(DCH):
            d_ = dc
            pb2 = (ps[f"p{2 + sl*2}"] if dc % 2 == 0 else ps[f"p{3 + sl*2}"]).ap()
            cc_.step("tensor", lambda e, d_=d_, pb2=pb2, chunks=chunks: e.transpose(
                pb2[:, 0:128], chunks[:, d_ * 128:(d_ + 1) * 128], idf))
            cc_.step("vector", lambda e, d_=d_, pb2=pb2, x1t=x1t, xt=xt: e.tensor_add(
                x1t[:, d_ * 128:(d_ + 1) * 128], pb2[:, 0:128],
                xt[:, d_ * 128:(d_ + 1) * 128]))
        cc_.step("gpsimd", lambda e, r0=r0, x1t=x1t: e.dma_start(out=x1ag[r0:r0 + 128, :], in_=x1t), 16)
        ln_stats_norm(cc_, x1t, xsq, st, xsq)
        cc_.step("vector", lambda e, xnb=xnb, xsq=xsq: e.tensor_copy(xnb, xsq))
        cc_.step("gpsimd", lambda e, r0=r0, xnb=xnb: e.dma_start(out=xn1[r0:r0 + 128, :], in_=xnb), 16)
    for c_ in chs:
        c_.wait_all()


def p6_ag1(ctx, en, eng, cc_group):
    if en == "gpsimd":
        ctx.inc(eng.collective_compute(
            "AllGather", ALU.bypass, replica_groups=[cc_group],
            ins=[ctx.t("x1ag").opt()], outs=[ctx.t("x1full").opt()]), "cc1", 1)
        ctx.wait(eng, "cc1", ctx.count("cc1"))


def rank0(ctx, eng):
    return 0 if ctx.mode == "mock" else eng.partition_id()


def dyn(ap, off_expr):
    return bass.AP(tensor=ap.tensor, offset=ap.offset + off_expr, ap=ap.ap)


def p7_ffn1(ctx, en, eng, ncc, cc_group):
    sb, ps, cs = ctx.sb, ctx.ps, ctx.consts
    A, B, C, Dd, Ef, F, G, H, I, X = (sb[k].ap() for k in
                                      ("A", "B", "C", "D", "E", "F", "G", "H", "I", "X"))
    jl = S // ncc
    njt = jl // 128
    cells = 32 * jl
    nct = cells // 128
    ch = Chain(ctx, en, eng, "p7")
    x1full = ctx.t("x1full")
    xn1s, xn1slo, t32 = ctx.t("xn1s"), ctx.t("xn1slo"), ctx.t("t32")
    probs_d = ctx.t("probs")
    qall_d, kvall_d, su_d = ctx.t("qall"), ctx.t("kvall"), ctx.t("su")
    v1d, v2d, m1d, m2d, cnt_d = (ctx.t(k) for k in ("v1d", "v2d", "m1d", "m2d", "cntd"))
    agsm_in, agsm_out = ctx.t("agsm_in"), ctx.t("agsm_out")
    ffn1ag = ctx.t("ffn1ag")
    wg, wqh, wkvh, woh = ctx.t("w1_wg"), ctx.t("w1_wq"), ctx.t("w1_wkv"), ctx.t("w1_wo")
    bqh, bkvh, boh = ctx.t("b1_bq"), ctx.t("b1_bkv"), ctx.t("b1_bo")
    lng, lnb = ctx.t("b1_lng"), ctx.t("b1_lnb")

    idf = I[:, 0:128]
    onesb = H[0:1, 0:512]
    r = [None]

    def R(e):
        if r[0] is None:
            r[0] = rank0(ctx, e)
        return r[0]

    wgHI = H[:, 512:640].rearrange("p (c h) -> p c h", c=8)
    wgLO = H[:, 640:768].rearrange("p (c h) -> p c h", c=8)
    wgF = Ef[:, 0:128].rearrange("p (c h) -> p c h", c=8)
    ch.step("sync", lambda e: e.dma_start(out=idf, in_=cs["ident_f"]), 16)
    ch.step("sync", lambda e: e.dma_start(out=onesb, in_=cs["ones_row_b"]), 16)
    ch.step("sync", lambda e: e.dma_start(
        out=wgF, in_=wg.rearrange("(c p) h -> p c h", p=128)), 16)
    ch.step("vector", lambda e: e.tensor_copy(wgHI, wgF))
    ch.step("vector", lambda e: e.tensor_sub(wgLO, wgF, wgHI))
    WqA = C[:, 0:8192].rearrange("p (c n) -> p c n", c=8)
    WoS = C[:, 8192:16384].rearrange("p (c n) -> p c n", c=8)
    WkvA = B[:, 0:16384].rearrange("p (c n) -> p c n", c=8)
    for c_ in range(DCH):
        cc = c_
        ch.step("gpsimd", lambda e, cc=cc: e.dma_start(
            out=C[:, cc * 1024:(cc + 1) * 1024].rearrange("p (h n) -> p h n", h=16),
            in_=wqh.rearrange("h (c p) n -> c p h n", p=128)[cc]), 16)
        ch.step("gpsimd", lambda e, cc=cc: e.dma_start(
            out=B[:, cc * 2048:(cc + 1) * 2048].rearrange("p (h n) -> p h n", h=16),
            in_=wkvh.rearrange("h (c p) n -> c p h n", p=128)[cc]), 16)
    ch.step("gpsimd", lambda e: e.dma_start(
        out=WoS, in_=woh.rearrange("h e o -> (h e) o").rearrange("(c p) o -> p c o", p=128)), 16)
    bqA = F[0:1, 0:1024]
    bkvA = F[0:1, 1024:3072]
    ch.step("gpsimd", lambda e: e.dma_start(out=bqA, in_=bqh.rearrange("h n -> (h n)")[None, :]), 16)
    ch.step("gpsimd", lambda e: e.dma_start(out=bkvA, in_=bkvh.rearrange("h n -> (h n)")[None, :]), 16)

    x1seq = ctx.t("x1seq")
    ch.step("gpsimd", lambda e: e.dma_start(
        out=x1seq.rearrange("(b j) d -> b j d", b=32),
        in_=dyn(x1full.rearrange("(b s) d -> b s d", b=32)[:, 0:jl, :], R(e) * (jl * D))), 16)

    ch7a = [ch, Chain(ctx, en, eng, "p7a2")]
    Xf32 = X.bitcast(F32)
    for ct in range(nct):
        sl = ct % 2
        cc_ = ch7a[sl]
        xt = Dd[:, 0:1024] if sl == 0 else Dd[:, 1024:2048]
        xsq = Ef[:, 0:1024] if sl == 0 else Ef[:, 1024:2048]
        xnf = Dd[:, 2048:3072] if sl == 0 else Xf32[:, 0:1024]
        xnb = F[:, sl * 2048:sl * 2048 + 1024]
        xnl = F[:, sl * 2048 + 1024:sl * 2048 + 2048]
        st = G[:, sl * 16:sl * 16 + 16]
        r0 = ct * 128
        cc_.step("sync", lambda e, r0=r0: e.dma_start(
            out=xt, in_=x1seq[r0:r0 + 128, :]), 16)
        ln_stats_norm(cc_, xt, xsq, st, xnf)
        cc_.step("vector", lambda e: e.tensor_copy(xnb, xnf))
        cc_.step("vector", lambda e: e.tensor_sub(xnl, xnf, xnb))
        cc_.dma_group("gpsimd", [
            lambda e, r0=r0: e.dma_start(out=xn1s[r0:r0 + 128, :], in_=xnb),
            lambda e, r0=r0: e.dma_start(out=xn1slo[r0:r0 + 128, :], in_=xnl),
            lambda e, r0=r0: e.dma_start(out=t32[r0:r0 + 128, :], in_=xnf)])


    ch7b = [Chain(ctx, en, eng, "p7bA"), Chain(ctx, en, eng, "p7bB")]
    for c_ in ch7b:
        c_.join(ch7a)
    for ct in range(nct):
        sl = ct % 2
        cc_ = ch7b[sl]
        r0 = ct * 128
        xhi = X[:, sl * 2048:sl * 2048 + 1024].rearrange("p (c t) -> p c t", c=8)
        xlo = X[:, sl * 2048 + 1024:sl * 2048 + 2048].rearrange("p (c t) -> p c t", c=8)
        fns = []
        for c in range(DCH):
            fns.append(lambda e, cc=c, r0=r0, xhi=xhi: e.dma_start_transpose(
                out=xhi[:, cc, :], in_=xn1s[r0:r0 + 128, cc * 128:(cc + 1) * 128]))
            fns.append(lambda e, cc=c, r0=r0, xlo=xlo: e.dma_start_transpose(
                out=xlo[:, cc, :], in_=xn1slo[r0:r0 + 128, cc * 128:(cc + 1) * 128]))
        cc_.dma_group("sync", fns)
        L = (ps["p0"] if sl == 0 else ps["p1"]).ap()[:, 0:16]

        def mmg(e, xhi=xhi, xlo=xlo, L=L):
            first = True
            for c in range(DCH):
                for lhs, rhs in ((xhi, wgHI), (xhi, wgLO), (xlo, wgHI)):
                    ins = e.matmul(L, lhs[:, c, :], rhs[:, c, :], start=first,
                                   stop=(c == DCH - 1 and lhs is xlo))
                    first = False
            return ins
        cc_.step("tensor", mmg)
        gof = 32 + sl * 40
        Lf = G[:, gof:gof + 16]
        nm1 = G[:, gof + 16:gof + 17]
        ssum = G[:, gof + 17:gof + 18]
        rs = G[:, gof + 18:gof + 19]
        pr = G[:, gof + 20:gof + 36]
        cc_.step("vector", lambda e, Lf=Lf, L=L: e.tensor_copy(Lf, L))
        cc_.step("vector", lambda e, Lf=Lf, nm1=nm1: e.reduce_max(nm1, Lf, AX.X, negate=True))
        cc_.step("scalar", lambda e, pr=pr, Lf=Lf, nm1=nm1, ssum=ssum: e.activation(
            pr, Lf, AF.Exp, bias=nm1, accum_out=ssum))
        cc_.step("vector", lambda e, rs=rs, ssum=ssum: e.reciprocal(rs, ssum))
        cc_.step("vector", lambda e, pr=pr, rs=rs: e.tensor_scalar(pr, pr, rs, None, ALU.mult))
        cc_.step("gpsimd", lambda e, r0=r0, pr=pr: e.dma_start(out=probs_d[r0:r0 + 128, :], in_=pr), 16)


    ch7d = [Chain(ctx, en, eng, "p7dA"), Chain(ctx, en, eng, "p7dB")]
    for c_ in ch7d:
        c_.join(ch7a)
        c_.join(ch7b)
    for ct in range(nct):
        sl = ct % 2
        cc_ = ch7d[sl]
        r0 = ct * 128
        xhi = X[:, sl * 1024:sl * 1024 + 1024].rearrange("p (c t) -> p c t", c=8)
        cc_.dma_group("sync", [
            (lambda e, cc=c, r0=r0, xhi=xhi: e.dma_start_transpose(
                out=xhi[:, cc, :], in_=xn1s[r0:r0 + 128, cc * 128:(cc + 1) * 128]))
            for c in range(DCH)])
        for ng in range(6):
            is_q = ng < 2
            W = WqA if is_q else WkvA
            brow = bqA if is_q else bkvA
            o0 = (ng if is_q else ng - 2) * 512
            pp = (ps[f"p{sl*2}"] if ng % 2 == 0 else ps[f"p{sl*2+1}"]).ap()

            def mmq(e, W=W, brow=brow, o0=o0, pp=pp, xhi=xhi):
                for dc in range(DCH):
                    ins = e.matmul(pp[:], xhi[:, dc, :], W[:, dc, o0:o0 + 512],
                                   start=(dc == 0), stop=False)
                ins = e.matmul(pp[:], onesb[0:1, 0:128], brow[:, o0:o0 + 512],
                               start=False, stop=True)
                return ins
            cc_.step("tensor", mmq)
            dst = (X[:, 2048 + (ng % 2) * 512: 2048 + (ng % 2) * 512 + 512] if sl == 0
                   else X[:, 3072 + (ng % 2) * 512: 3072 + (ng % 2) * 512 + 512])
            cc_.step("vector", lambda e, pp=pp, dst=dst: e.tensor_copy(dst, pp[:]))
            dd = qall_d if is_q else kvall_d
            cc_.step("gpsimd", lambda e, dd=dd, r0=r0, o0=o0, dst=dst: e.dma_start(
                out=dd[r0:r0 + 128, o0:o0 + 512], in_=dst), 16)

    ch.join(ch7a)
    ch.join(ch7b)
    ch.join(ch7d)
    for jt in range(njt):
        j0 = jt * 128
        pj = G[:, 0:512]
        ch.step("sync", lambda e, j0=j0: e.dma_start(
            out=pj.rearrange("p (b h) -> p b h", b=32),
            in_=probs_d.rearrange("(b j) h -> j b h", b=32)[j0:j0 + 128],
            allow_slow_non_contiguous=True), 16)
        m1v = Dd[:, 0:16]
        m2v = Dd[:, 16:32]
        eqv = Ef[:, 0:512]
        l2v = Ef[:, 1024:1536]
        tmpv = Dd[:, 512:1024]
        c0 = Ef[:, 0:512]
        c1 = Ef[:, 512:1024]
        M1 = X[:, 0:512]
        M2 = X[:, 512:1024]

        def treemax(dst, src0):
            ch.step("vector", lambda e: e.tensor_max(tmpv[:, 0:256], src0[:, 0:256], src0[:, 256:512]))
            ch.step("vector", lambda e: e.tensor_max(tmpv[:, 0:128], tmpv[:, 0:128], tmpv[:, 128:256]))
            ch.step("vector", lambda e: e.tensor_max(tmpv[:, 0:64], tmpv[:, 0:64], tmpv[:, 64:128]))
            ch.step("vector", lambda e: e.tensor_max(tmpv[:, 0:32], tmpv[:, 0:32], tmpv[:, 32:64]))
            ch.step("vector", lambda e: e.tensor_max(dst, tmpv[:, 0:16], tmpv[:, 16:32]))
        treemax(m1v, pj)
        pj3 = pj.rearrange("p (b h) -> p b h", b=32)
        ch.step("vector", lambda e: e.tensor_tensor(
            eqv.rearrange("p (b h) -> p b h", b=32), pj3,
            m1v[:, None, :].broadcast_to([128, 32, 16]), ALU.is_ge))
        ch.step("vector", lambda e: e.scalar_tensor_tensor(l2v, eqv, NEG, pj, ALU.mult, ALU.add))
        treemax(m2v, l2v)
        ch.step("vector", lambda e: e.tensor_tensor(
            c0.rearrange("p (b h) -> p b h", b=32), pj3,
            m2v[:, None, :].broadcast_to([128, 32, 16]), ALU.is_ge))
        ch.step("vector", lambda e: e.tensor_copy(M2, c0))
        cur, nxt = c0, c1
        for sh in (1, 2, 4, 8, 16):
            s_ = sh * 16
            ch.step("vector", lambda e, cur=cur, nxt=nxt, s_=s_: e.tensor_copy(nxt[:, 0:s_], cur[:, 0:s_]))
            ch.step("vector", lambda e, cur=cur, nxt=nxt, s_=s_: e.tensor_add(
                nxt[:, s_:512], cur[:, s_:512], cur[:, 0:512 - s_]))
            cur, nxt = nxt, cur
        ch.step("vector", lambda e, cur=cur: e.tensor_single_scalar(tmpv, cur, 1.0, ALU.is_equal))
        ch.step("vector", lambda e: e.tensor_mul(M1, tmpv, M2))
        ch.step("vector", lambda e, cur=cur: e.tensor_single_scalar(tmpv, cur, 2.0, ALU.is_equal))
        ch.step("vector", lambda e: e.tensor_mul(M2, tmpv, M2))
        ch.step("gpsimd", lambda e, j0=j0: e.dma_start(out=m1d[j0:j0 + 128, :], in_=M1), 16)
        ch.step("gpsimd", lambda e, j0=j0: e.dma_start(out=m2d[j0:j0 + 128, :], in_=M2), 16)
        ch.step("vector", lambda e: e.tensor_add(tmpv, M1, M2))
        cnt = Dd[:, 32:64]
        ch.step("vector", lambda e: e.tensor_reduce(
            cnt, tmpv.rearrange("p (b h) -> p b h", b=32), AX.X, ALU.add))
        ch.step("gpsimd", lambda e, j0=j0: e.dma_start(out=cnt_d[j0:j0 + 128, :], in_=cnt), 16)

        Q1, Q2 = F[:, 0:1024], F[:, 1024:2048]
        V1, V2 = F[:, 2048:3072], F[:, 3072:4096]
        K1, K2 = H[:, 0:1024], X[:, 1024:2048]
        for bufz in (Q1, Q2, V1, V2, K1, K2):
            bz = bufz
            ch.step("vector", lambda e, bz=bz: e.memset(bz, 0.0))
        qb = C[:, 0:1024]
        kvb = C[:, 1024:3072]
        tq = C[:, 3072:4096]
        tkv = C[:, 4096:6144]
        for b in range(32):
            bb = b
            ch.dma_group("sync", [
                lambda e, bb=bb, j0=j0: e.dma_start(
                    out=qb, in_=qall_d[bb * jl + j0: bb * jl + j0 + 128, :]),
                lambda e, bb=bb, j0=j0: e.dma_start(
                    out=kvb, in_=kvall_d[bb * jl + j0: bb * jl + j0 + 128, :])])
            m1s = M1[:, bb * 16:(bb + 1) * 16]
            m2s = M2[:, bb * 16:(bb + 1) * 16]
            qb3 = qb.rearrange("p (h e) -> p h e", h=16)
            kv4 = kvb.rearrange("p (h w e) -> p h w e", h=16, w=2)
            for (msk, qd, kd, vd) in ((m1s, Q1, K1, V1), (m2s, Q2, K2, V2)):
                mq, qd_, kd_, vd_ = msk, qd, kd, vd
                ch.step("vector", lambda e, mq=mq: e.tensor_tensor(
                    tq.rearrange("p (h e) -> p h e", h=16), qb3,
                    mq[:, :, None].broadcast_to([128, 16, 64]), ALU.mult))
                ch.step("vector", lambda e, qd_=qd_: e.tensor_add(qd_, qd_, tq))
                ch.step("vector", lambda e, mq=mq: e.tensor_tensor(
                    tkv.rearrange("p (h w e) -> p h w e", h=16, w=2), kv4,
                    mq[:, :, None, None].broadcast_to([128, 16, 2, 64]), ALU.mult))
                tkv4 = tkv.rearrange("p (h w e) -> p h w e", h=16, w=2)
                ch.step("vector", lambda e, kd_=kd_, tkv4=tkv4: e.tensor_add(
                    kd_.rearrange("p (h e) -> p h e", h=16), kd_.rearrange("p (h e) -> p h e", h=16),
                    tkv4[:, :, 0, :]))
                ch.step("vector", lambda e, vd_=vd_, tkv4=tkv4: e.tensor_add(
                    vd_.rearrange("p (h e) -> p h e", h=16), vd_.rearrange("p (h e) -> p h e", h=16),
                    tkv4[:, :, 1, :]))
        ch.step("gpsimd", lambda e, j0=j0: e.dma_start(out=v1d[j0:j0 + 128, :], in_=V1), 16)
        ch.step("gpsimd", lambda e, j0=j0: e.dma_start(out=v2d[j0:j0 + 128, :], in_=V2), 16)
        attj = G[:, 32 + jt * 64: 96 + jt * 64]
        tmpf = Dd[:, 0:1024]
        for ci, (qq, kk) in enumerate(((Q1, K1), (Q1, K2), (Q2, K1), (Q2, K2))):
            cc_, qq_, kk_ = ci, qq, kk
            ch.step("vector", lambda e, qq_=qq_, kk_=kk_: e.tensor_mul(tmpf, qq_, kk_))
            ch.step("vector", lambda e, cc_=cc_: e.tensor_reduce(
                attj[:, cc_ * 16:(cc_ + 1) * 16],
                tmpf.rearrange("p (h e) -> p h e", h=16), AX.X, ALU.add))
        ch.step("vector", lambda e, attj=attj: e.tensor_scalar(attj, attj, 0.125, None, ALU.mult))
        ch.step("tensor", lambda e, attj=attj: e.transpose(ps["p2"].ap()[0:64, 0:128], attj, idf))
        ch.step("vector", lambda e, jt=jt: e.tensor_copy(
            Ef[0:64, 1536 + jt * 128:1536 + jt * 128 + 128], ps["p2"].ap()[0:64, 0:128]))

    attT = Ef[0:64, 1536:1536 + 256]
    mloc = G[0:64, 480:481]
    ch.step("vector", lambda e: e.reduce_max(mloc, attT[:, 0:njt * 128], AX.X))
    ch.step("tensor", lambda e: e.transpose(ps["p3"].ap()[0:1, 0:64], mloc, idf[0:64, 0:64]))
    mrow = Ef[0:1, 1800:1864]
    ch.step("vector", lambda e: e.tensor_copy(mrow, ps["p3"].ap()[0:1, 0:64]))
    onesf = Dd[0:1, 0:128]
    ch.step("sync", lambda e: e.dma_start(out=onesf, in_=cs["ones_row_f"]), 16)
    ch.step("tensor", lambda e: e.matmul(ps["p4"].ap()[:, 0:64], onesf, mrow, start=True, stop=True))
    mb = Dd[:, 1024:1088]
    ch.step("vector", lambda e: e.tensor_scalar(mb, ps["p4"].ap()[:, 0:64], -1.0, None, ALU.mult))
    onescol = Dd[0:128, 1088:1089]
    ch.step("vector", lambda e: e.memset(onescol, 1.0))
    for jt in range(njt):
        attj = G[:, 32 + jt * 64: 96 + jt * 64]
        elj = Dd[:, 1152 + jt * 64: 1216 + jt * 64]
        ch.step("vector", lambda e, attj=attj, elj=elj: e.tensor_add(elj, attj, mb))
        ch.step("scalar", lambda e, elj=elj: e.activation(elj, elj, AF.Exp))
        ch.step("tensor", lambda e, elj=elj, jt=jt: e.matmul(
            ps["p5"].ap()[0:1, 0:64], onescol, elj, start=(jt == 0), stop=(jt == njt - 1)))
    srow = Ef[0:1, 1864:1928]
    ch.step("vector", lambda e: e.tensor_copy(srow, ps["p5"].ap()[0:1, 0:64]))
    ch.step("gpsimd", lambda e: e.dma_start(out=agsm_in[0:1, 0:64], in_=mrow), 16)
    ch.step("gpsimd", lambda e: e.dma_start(out=agsm_in[0:1, 64:128], in_=srow), 16)
    nrk = len(cc_group)
    if en == "gpsimd":
        if ctx.mode == "real":
            ch._wait_cur()
        ctx.inc(eng.collective_compute(
            "AllGather", ALU.bypass, replica_groups=[cc_group],
            ins=[agsm_in.opt()], outs=[agsm_out.opt()]), "cc2", 1)
        ctx.wait(eng, "cc2", ctx.count("cc2"))
    ch.step("gpsimd", lambda e: e.nop())
    ch.step("sync", lambda e, nrk=nrk: e.dma_start(out=Ef[0:nrk, 0:128], in_=agsm_out[0:nrk, :]), 16)
    GtT = Dd[:, 1280:1280 + 8]
    ch.step("tensor", lambda e, nrk=nrk: e.transpose(ps["p2"].ap()[:, 0:nrk], Ef[0:nrk, 0:128], idf[0:nrk, 0:nrk]))
    ch.step("vector", lambda e, nrk=nrk: e.tensor_copy(GtT[:, 0:nrk], ps["p2"].ap()[:, 0:nrk]))
    negM = G[0:64, 481:482]
    Sv = G[0:64, 482:483]
    rS = G[0:64, 483:484]
    Fv = G[0:64, 484:485]
    ch.step("vector", lambda e, nrk=nrk: e.reduce_max(negM, GtT[0:64, 0:nrk], AX.X, negate=True))
    wv_ = Dd[0:64, 1300:1300 + 8]
    ch.step("scalar", lambda e, nrk=nrk: e.activation(wv_[:, 0:nrk], GtT[0:64, 0:nrk], AF.Exp, bias=negM))
    pv_ = Dd[0:64, 1320:1320 + 8]
    scop = Dd[0:64, 1340:1340 + 8]
    ch.step("vector", lambda e, nrk=nrk: e.tensor_copy(scop[:, 0:nrk], GtT[64:128, 0:nrk]))
    ch.step("vector", lambda e, nrk=nrk: e.tensor_mul(pv_[:, 0:nrk], wv_[:, 0:nrk], scop[:, 0:nrk]))
    ch.step("vector", lambda e, nrk=nrk: e.reduce_sum(Sv, pv_[:, 0:nrk], AX.X))
    ch.step("vector", lambda e: e.reciprocal(rS, Sv))
    ch.step("scalar", lambda e: e.activation(Fv, mloc, AF.Exp, bias=negM))
    ch.step("vector", lambda e: e.tensor_mul(Fv, Fv, rS))
    ch.step("tensor", lambda e: e.transpose(ps["p3"].ap()[0:1, 0:64], Fv, idf[0:64, 0:64]))
    frow = Ef[0:1, 1800:1864]
    ch.step("vector", lambda e: e.tensor_copy(frow, ps["p3"].ap()[0:1, 0:64]))
    ch.step("tensor", lambda e: e.matmul(ps["p4"].ap()[:, 0:64], onesf, frow, start=True, stop=True))
    Fb = Dd[:, 2112:2176]
    ch.step("vector", lambda e: e.tensor_copy(Fb, ps["p4"].ap()[:, 0:64]))

    for jt in range(njt):
        j0 = jt * 128
        elj = Dd[:, 1152 + jt * 64: 1216 + jt * 64]
        pfin = Dd[:, 1152 + jt * 64: 1216 + jt * 64]
        ch.step("vector", lambda e, elj=elj: e.tensor_mul(pfin, elj, Fb))
        V1, V2 = F[:, 2048:3072], F[:, 3072:4096]
        M1, M2 = X[:, 0:512], X[:, 512:1024]
        ch.dma_group("sync", [
            lambda e, j0=j0: e.dma_start(out=V1, in_=v1d[j0:j0 + 128, :]),
            lambda e, j0=j0: e.dma_start(out=V2, in_=v2d[j0:j0 + 128, :]),
            lambda e, j0=j0: e.dma_start(out=M1, in_=m1d[j0:j0 + 128, :]),
            lambda e, j0=j0: e.dma_start(out=M2, in_=m2d[j0:j0 + 128, :])])
        av1, av2 = C[:, 0:1024], C[:, 1024:2048]
        tmpb = C[:, 2048:3072]
        for i_, avd in ((0, av1), (1, av2)):
            ii, avd_ = i_, avd
            p1s = pfin[:, (ii * 2 + 0) * 16:(ii * 2 + 0) * 16 + 16]
            p2s = pfin[:, (ii * 2 + 1) * 16:(ii * 2 + 1) * 16 + 16]
            ch.step("vector", lambda e, avd_=avd_, p1s=p1s: e.tensor_tensor(
                avd_.rearrange("p (h e) -> p h e", h=16),
                V1.rearrange("p (h e) -> p h e", h=16),
                p1s[:, :, None].broadcast_to([128, 16, 64]), ALU.mult))
            ch.step("vector", lambda e, p2s=p2s: e.tensor_tensor(
                tmpb.rearrange("p (h e) -> p h e", h=16),
                V2.rearrange("p (h e) -> p h e", h=16),
                p2s[:, :, None].broadcast_to([128, 16, 64]), ALU.mult))
            ch.step("vector", lambda e, avd_=avd_: e.tensor_add(avd_, avd_, tmpb))
        sut = C[:, 3072:4096]
        for b in range(32):
            bb = b
            m1s = M1[:, bb * 16:(bb + 1) * 16]
            m2s = M2[:, bb * 16:(bb + 1) * 16]
            ch.step("vector", lambda e, m1s=m1s: e.tensor_tensor(
                sut.rearrange("p (h e) -> p h e", h=16),
                av1.rearrange("p (h e) -> p h e", h=16),
                m1s[:, :, None].broadcast_to([128, 16, 64]), ALU.mult))
            ch.step("vector", lambda e, m2s=m2s: e.tensor_tensor(
                tmpb.rearrange("p (h e) -> p h e", h=16),
                av2.rearrange("p (h e) -> p h e", h=16),
                m2s[:, :, None].broadcast_to([128, 16, 64]), ALU.mult))
            ch.step("vector", lambda e: e.tensor_add(sut, sut, tmpb))
            ch.step("gpsimd", lambda e, bb=bb, j0=j0: e.dma_start(
                out=su_d[bb * jl + j0: bb * jl + j0 + 128, :], in_=sut), 16)

    gb = Ef[:, 0:1024]
    bb_ = Ef[:, 1024:2048]
    bob = Dd[:, 2048:3072]
    for half in range(2):
        hh = half
        for src_row, dstb in ((lng, gb), (lnb, bb_), (boh, bob)):
            sr, db = src_row, dstb
            ch.step("sync", lambda e, sr=sr, hh=hh: e.dma_start(
                out=Ef[0:1, 2048 - 512:2048].bitcast(F32) if False else G[0:1, 0:512],
                in_=sr[None, hh * 512:(hh + 1) * 512]), 16)
            ch.step("tensor", lambda e: e.matmul(ps["p2"].ap()[:, 0:512], onesf,
                                                 G[0:1, 0:512], start=True, stop=True))
            ch.step("vector", lambda e, db=db, hh=hh: e.tensor_copy(
                db[:, hh * 512:(hh + 1) * 512], ps["p2"].ap()[:, 0:512]))
    ch7g = [Chain(ctx, en, eng, "p7gA"), Chain(ctx, en, eng, "p7gB")]
    for c_ in ch7g:
        c_.join([ch])
    Cf32 = C.bitcast(F32)
    Xf32b = X.bitcast(F32)
    for ct in range(nct):
        b, jt = ct // njt, ct % njt
        sl = ct % 2
        cc_ = ch7g[sl]
        j0 = jt * 128
        r0 = ct * 128
        cntt = G[:, sl * 32:sl * 32 + 32]
        suf = X[:, sl * 1024:sl * 1024 + 1024].rearrange("p (c t) -> p c t", c=8)
        cc_.dma_group("sync", [lambda e, j0=j0, cntt=cntt: e.dma_start(out=cntt, in_=cnt_d[j0:j0 + 128, :])] + [
            (lambda e, cc=c, r0=r0, suf=suf: e.dma_start_transpose(
                out=suf[:, cc, :], in_=su_d[r0:r0 + 128, cc * 128:(cc + 1) * 128]))
            for c in range(DCH)])
        y = Dd[:, 0:1024] if sl == 0 else Xf32b[:, 1024:2048]
        scr = Dd[:, 1024:2048] if sl == 0 else Cf32[:, 0:1024]
        for og in range(2):
            oo = og
            pp = (ps[f"p{sl*2}"] if og == 0 else ps[f"p{sl*2+1}"]).ap()

            def mmo(e, oo=oo, pp=pp, suf=suf):
                for dc in range(DCH):
                    ins = e.matmul(pp[:], suf[:, dc, :], WoS[:, dc, oo * 512:(oo + 1) * 512],
                                   start=(dc == 0), stop=(dc == DCH - 1))
                return ins
            cc_.step("tensor", mmo)
            ccol = cntt[:, b:b + 1]
            cc_.step("vector", lambda e, oo=oo, pp=pp, ccol=ccol, y=y: e.scalar_tensor_tensor(
                y[:, oo * 512:(oo + 1) * 512], bob[:, oo * 512:(oo + 1) * 512],
                ccol, pp[:], ALU.mult, ALU.add))
        cc_.step("sync", lambda e, r0=r0, scr=scr: e.dma_start(out=scr, in_=t32[r0:r0 + 128, :]), 16)
        cc_.step("vector", lambda e, y=y, scr=scr: e.tensor_add(y, y, scr))
        st = G[:, 100 + sl * 16:116 + sl * 16]
        cc_.step("scalar", lambda e, scr=scr, y=y: e.activation(scr, y, AF.Square))
        s1, s2, mu, var, rec, rstd, nmr = (st[:, k:k + 1] for k in range(7))
        cc_.step("vector", lambda e, s1=s1, y=y: e.reduce_sum(s1, y, AX.X))
        cc_.step("vector", lambda e, s2=s2, scr=scr: e.reduce_sum(s2, scr, AX.X))
        cc_.step("vector", lambda e, mu=mu, s1=s1: e.tensor_scalar(mu, s1, 1.0 / D, None, ALU.mult))
        cc_.step("vector", lambda e, var=var, mu=mu: e.tensor_mul(var, mu, mu))
        cc_.step("vector", lambda e, var=var, s2=s2: e.tensor_scalar(var, s2, 1.0 / D, var, ALU.mult, ALU.subtract))
        cc_.step("vector", lambda e, var=var: e.tensor_scalar_add(var, var, EPS))
        cc_.step("vector", lambda e, rec=rec, var=var: e.reciprocal(rec, var))
        cc_.step("scalar", lambda e, rstd=rstd, rec=rec: e.activation(rstd, rec, AF.Sqrt))
        cc_.step("scalar", lambda e, nmr=nmr, mu=mu, rstd=rstd: e.activation(nmr, mu, AF.Copy, scale=rstd))
        cc_.step("vector", lambda e, scr=scr, y=y, rstd=rstd, nmr=nmr: e.tensor_scalar(
            scr, y, rstd, nmr, ALU.mult, ALU.subtract))
        cc_.step("vector", lambda e, scr=scr: e.tensor_mul(scr, scr, gb))
        cc_.step("vector", lambda e, scr=scr: e.tensor_add(scr, scr, bb_))
        cc_.step("gpsimd", lambda e, r0=r0, scr=scr: e.dma_start(out=ffn1ag[r0:r0 + 128, :], in_=scr), 16)
    ch.wait_all()
    for c_ in ch7a + ch7b + ch7d + ch7g:
        c_.wait_all()


def p7h_ag2(ctx, en, eng, cc_group):
    if en == "gpsimd":
        ctx.inc(eng.collective_compute(
            "AllGather", ALU.bypass, replica_groups=[cc_group],
            ins=[ctx.t("ffn1ag").opt()], outs=[ctx.t("ffn1full").opt()]), "cc3", 1)
        ctx.wait(eng, "cc3", ctx.count("cc3"))


def p8_final(ctx, en, eng, ncc):
    sb, ps, cs = ctx.sb, ctx.ps, ctx.consts
    C, Dd, Ef, I, X = (sb[k].ap() for k in ("C", "D", "E", "I", "X"))
    Xf32 = X.bitcast(F32)
    Cf32 = C.bitcast(F32)
    ch0 = Chain(ctx, en, eng, "p8A")
    ch1 = Chain(ctx, en, eng, "p8B")
    chs = [ch0, ch1]
    x1ag, attn1f, ffn1full = ctx.t("x1ag"), ctx.t("accf"), ctx.t("ffn1full")
    ffn1loc = ctx.t("ffn1loc")
    out = ctx.t("out").rearrange("b s d -> (b s) d")
    jl = S // ncc
    cells_ = 32 * jl
    idf = I[:, 0:128]
    r = [None]

    def R(e):
        if r[0] is None:
            r[0] = rank0(ctx, e)
        return r[0]
    ch0.step("sync", lambda e: e.dma_start(out=idf, in_=cs["ident_f"]), 16)
    for b_ in range(B_LOC):
        bb = b_
        src3 = ffn1full.rearrange("(r2 c) d -> r2 c d", c=cells_)[:, bb * jl:(bb + 1) * jl, :]
        ch0.step("gpsimd", lambda e, bb=bb, src3=src3: e.dma_start(
            out=ffn1loc.rearrange("(b s) d -> b s d", b=B_LOC)[bb].rearrange(
                "(r2 j) d -> r2 j d", j=jl),
            in_=dyn(src3, R(e) * (4 * jl * D))), 16)
    ch1.join([ch0])
    for i in range(NT):
        sl = i % 2
        cc_ = chs[sl]
        r0 = i * 128
        x1t = Dd[:, 0:1024] if sl == 0 else Dd[:, 1024:2048]
        ft = Dd[:, 2048:3072] if sl == 0 else Cf32[:, 0:1024]
        chunks = Ef[:, 0:1024] if sl == 0 else Ef[:, 1024:2048]
        cc_.dma_group("sync", [
            lambda e, x1t=x1t, r0=r0: e.dma_start(out=x1t, in_=x1ag[r0:r0 + 128, :]),
            lambda e, ft=ft, r0=r0: e.dma_start(out=ft, in_=ffn1loc[r0:r0 + 128, :])] + [
            (lambda e, d_=dc, r0=r0, chunks=chunks: e.dma_start(
                out=chunks[:, d_ * 128:(d_ + 1) * 128],
                in_=attn1f[d_ * 128:(d_ + 1) * 128, r0:r0 + 128]))
            for dc in range(DCH)])
        cc_.step("vector", lambda e, x1t=x1t, ft=ft: e.tensor_add(x1t, x1t, ft))
        for dc in range(DCH):
            d_ = dc
            pb2 = (ps[f"p{2 + sl*2}"] if dc % 2 == 0 else ps[f"p{3 + sl*2}"]).ap()
            cc_.step("tensor", lambda e, d_=d_, pb2=pb2, chunks=chunks: e.transpose(
                pb2[:, 0:128], chunks[:, d_ * 128:(d_ + 1) * 128], idf))
            cc_.step("vector", lambda e, d_=d_, pb2=pb2, x1t=x1t: e.tensor_add(
                x1t[:, d_ * 128:(d_ + 1) * 128], x1t[:, d_ * 128:(d_ + 1) * 128],
                pb2[:, 0:128]))
        cc_.step("gpsimd", lambda e, r0=r0, x1t=x1t: e.dma_start(out=out[r0:r0 + 128, :], in_=x1t), 16)
    for c_ in chs:
        c_.wait_all()



CELLS8 = 32 * JL

SHAPES = {
    "x_in": ([B_LOC, S, D], F32),
    "w0q": ([D, D], F32), "w0k": ([D, D], F32), "w0v": ([D, D], F32), "w0o": ([D, D], F32),
    "w_gate": ([8, D], F32),
    "w_moe1": ([8, D, FF], F32), "w_moe2": ([8, FF, D], F32),
    "w_sw1": ([D, FF], F32), "w_sw2": ([FF, D], F32),
    "b_moe1": ([8, FF], F32), "b_moe2": ([8, D], F32),
    "b_ssb1": ([FF], F32), "b_ssb2": ([D], F32),
    "w5q": ([D, D], F32), "w5k": ([D, D], F32), "w5v": ([D, D], F32), "w5o": ([D, D], F32),
    "b5q": ([D], F32), "b5k": ([D], F32), "b5v": ([D], F32), "b5o": ([D], F32),
    "w1_wg": ([D, 16], F32), "w1_wq": ([16, D, 64], F32), "w1_wkv": ([16, D, 128], F32),
    "w1_wo": ([16, 64, D], F32),
    "b1_bq": ([16, 64], F32), "b1_bkv": ([16, 128], F32), "b1_bo": ([D], F32),
    "b1_lng": ([D], F32), "b1_lnb": ([D], F32),
    "xn0": ([T, D], BF16), "xn0lo": ([T, D], BF16),
    "comb": ([8, T], BF16), "b1b": ([9, FF], BF16),
    "accf": ([D, T], F32),
    "b5rows": ([4, D], BF16),
    "x1ag": ([T, D], F32), "xn1": ([T, D], BF16),
    "x1full": ([NCORES * T, D], F32),
    "x1seq": ([CELLS8, D], F32),
    "xn1s": ([CELLS8, D], BF16), "xn1slo": ([CELLS8, D], BF16), "t32": ([CELLS8, D], F32),
    "probs": ([CELLS8, 16], F32),
    "qall": ([CELLS8, 1024], BF16), "kvall": ([CELLS8, 2048], BF16), "su": ([CELLS8, 1024], BF16),
    "v1d": ([JL, 1024], BF16), "v2d": ([JL, 1024], BF16),
    "m1d": ([JL, 512], BF16), "m2d": ([JL, 512], BF16), "cntd": ([JL, 32], F32),
    "agsm_in": ([1, 128], F32), "agsm_out": ([NCORES, 128], F32),
    "ffn1ag": ([CELLS8, D], F32), "ffn1full": ([NCORES * CELLS8, D], F32),
    "ffn1loc": ([T, D], F32),
    "out": ([B_LOC, S, D], F32),
}

WEIGHT_MAP = {
    "w0q": "L0_attn_Wq", "w0k": "L0_attn_Wk", "w0v": "L0_attn_Wv", "w0o": "L0_attn_Wo",
    "w_gate": "L0_ffn_gate_w",
    "w_moe1": "L0_ffn_w1", "w_moe2": "L0_ffn_w2",
    "w_sw1": "L0_ffn_sw1", "w_sw2": "L0_ffn_sw2",
    "b_moe1": "L0_ffn_b1", "b_moe2": "L0_ffn_b2",
    "b_ssb1": "L0_ffn_sb1", "b_ssb2": "L0_ffn_sb2",
    "w5q": "L1_attn_Wq", "w5k": "L1_attn_Wk", "w5v": "L1_attn_Wv", "w5o": "L1_attn_Wo",
    "b5q": "L1_attn_bq", "b5k": "L1_attn_bk", "b5v": "L1_attn_bv", "b5o": "L1_attn_bo",
    "w1_wg": "L1_ffn_Wg", "w1_wq": "L1_ffn_Wq", "w1_wkv": "L1_ffn_Wkv", "w1_wo": "L1_ffn_Wo",
    "b1_bq": "L1_ffn_bq", "b1_bkv": "L1_ffn_bkv", "b1_bo": "L1_ffn_bo",
    "b1_lng": "L1_ffn_ln_g", "b1_lnb": "L1_ffn_ln_b",
}


def ph_p1(ctx, en, eng):
    attn_phase(ctx, en, eng, "p1", ctx.t("xn0"), ctx.t("w0q"), ctx.t("w0k"),
               ctx.t("w0v"), ctx.t("w0o"), None, None, None, None,
               ctx.t("accf"), None)


def ph_p5(ctx, en, eng):
    attn_phase(ctx, en, eng, "p5", ctx.t("xn1"), ctx.t("w5q"), ctx.t("w5k"),
               ctx.t("w5v"), ctx.t("w5o"), ctx.t("b5q"), ctx.t("b5k"),
               ctx.t("b5v"), ctx.t("b5o"), ctx.t("accf"), WIN)


def ph_p7(ctx, en, eng):
    p7_ffn1(ctx, en, eng, NCORES, list(range(NCORES)))


def ph_p8(ctx, en, eng):
    p8_final(ctx, en, eng, NCORES)


def mk_io(ins, outs, shared=()):
    io = {}
    for n in ins:
        io[n] = ("ExternalInput", *SHAPES[n])
    for n in outs:
        io[n] = ("ExternalOutput", *SHAPES[n])
    for n, (shape, dt) in SHAPES.items():
        if n not in io:
            if n in shared:
                io[n] = ("Internal", shape, dt, "Shared")
            else:
                io[n] = ("Internal", shape, dt)
    return io


L0_W = ["w0q", "w0k", "w0v", "w0o", "w_gate", "w_moe1", "w_moe2", "w_sw1",
        "w_sw2", "b_moe1", "b_moe2", "b_ssb1", "b_ssb2"]
L5_W = ["w5q", "w5k", "w5v", "w5o", "b5q", "b5k", "b5v", "b5o"]
L7_W = ["w1_wg", "w1_wq", "w1_wkv", "w1_wo", "b1_bq", "b1_bkv", "b1_bo",
        "b1_lng", "b1_lnb"]

FUSED = True
_nc_cache = {}


def _get_nc(key, phases, io):
    if key not in _nc_cache:
        _nc_cache[key] = build_program(phases, io)
    return _nc_cache[key]


def kernel(**inputs):
    x = np.ascontiguousarray(np.asarray(inputs["L0_attn_x"], dtype=np.float32))
    w = {k: np.ascontiguousarray(np.asarray(inputs[v], dtype=np.float32))
         for k, v in WEIGHT_MAP.items()}
    xs = [x[4 * c:4 * c + 4] for c in range(NCORES)]

    if FUSED:
        phases = [p0_ln0, p2_gate, p3z_zero_acc, ph_p1, p3_moe, p4_ln1,
                  lambda c, en, e: p6_ag1(c, en, e, list(range(NCORES))),
                  ph_p5, ph_p7,
                  lambda c, en, e: p7h_ag2(c, en, e, list(range(NCORES))),
                  ph_p8]
        io = mk_io(["x_in"] + L0_W + L5_W + L7_W, ["out"],
                   shared=("x1full", "ffn1full", "agsm_out"))
        nc = _get_nc("fused", phases, io)
        in_maps = [{"x_in": xs[c], **w} for c in range(NCORES)]
        res = run_bass_kernel_spmd(nc, in_maps, core_ids=list(range(NCORES)))
        return np.concatenate([r["out"] for r in res.results], axis=0)

    ioA = mk_io(["x_in"] + L0_W, ["x1ag", "xn1"])
    ncA = _get_nc("A", [p0_ln0, p2_gate, p3z_zero_acc, ph_p1, p3_moe, p4_ln1], ioA)
    resA = run_bass_kernel_spmd(ncA, [{"x_in": xs[c], **{k: w[k] for k in L0_W}}
                                      for c in range(NCORES)],
                                core_ids=list(range(NCORES)))
    x1s = [r["x1ag"] for r in resA.results]
    xn1s_ = [r["xn1"] for r in resA.results]

    ioB = mk_io(["xn1"] + L5_W, ["accf"])
    ncB = _get_nc("B", [ph_p5], ioB)
    resB = run_bass_kernel_spmd(ncB, [{"xn1": xn1s_[c], **{k: w[k] for k in L5_W}}
                                      for c in range(NCORES)],
                                core_ids=list(range(NCORES)))
    attn1s = [r["accf"] for r in resB.results]

    x1full = np.concatenate(x1s, axis=0)
    ioC = mk_io(["x1full"] + L7_W, ["ffn1ag"], shared=("agsm_out",))
    ncC = _get_nc("C", [ph_p7], ioC)
    resC = run_bass_kernel_spmd(ncC, [{"x1full": x1full, **{k: w[k] for k in L7_W}}
                                      for c in range(NCORES)],
                                core_ids=list(range(NCORES)))
    ffn1full = np.concatenate([r["ffn1ag"] for r in resC.results], axis=0)

    ioD = mk_io(["x1ag", "accf", "ffn1full"], ["out"])
    ncD = _get_nc("D", [ph_p8], ioD)
    resD = run_bass_kernel_spmd(ncD, [{"x1ag": x1s[c], "accf": attn1s[c],
                                       "ffn1full": ffn1full}
                                      for c in range(NCORES)],
                                core_ids=list(range(NCORES)))
    return np.concatenate([r["out"] for r in resD.results], axis=0)


# version 14
# speedup vs baseline: 1.2394x; 1.0272x over previous
import numpy as np
import ml_dtypes
from contextlib import ExitStack
import concourse.bass as bass
import concourse.mybir as mybir
from concourse.bass_utils import run_bass_kernel_spmd

F32 = mybir.dt.float32
BF16 = mybir.dt.bfloat16
AF = mybir.ActivationFunctionType
ALU = mybir.AluOpType
AX = mybir.AxisListType

NCORES = 8
B_LOC = 4
S = 2048
D = 1024
T = B_LOC * S
NT = T // 128
DCH = D // 128
NH = 16
DH = 64
WIN = 512
NEXP = 8
FF = 4096
NFC = FF // 128
MTT = 512
NMT = T // MTT
JL = S // NCORES
NJT = JL // 128
EPS = 1e-5
NEG = -1e9

ENGINES = ["sync", "gpsimd", "tensor", "vector", "scalar"]
BF = ml_dtypes.bfloat16



class MockIns:
    def then_inc(self, *a, **k):
        raise RuntimeError("use ctx.inc / Chain.step, not .then_inc")


class MockEng:
    def __init__(self, name):
        self.name = name

    def __getattr__(self, op):
        def f(*a, **k):
            return MockIns()
        return f


class Ctx:
    def __init__(self, nc):
        self.nc = nc
        self.mode = "mock"
        self.counts = {}
        self.marks = {}
        self.sems = {}
        self.tensors = {}
        self.consts = {}
        self.sb = {}
        self.ps = {}

    def t(self, name):
        return self.tensors[name].ap()

    def inc(self, ins, name, v=1):
        self.counts[name] = self.counts.get(name, 0) + v
        if self.mode == "real":
            ins.then_inc(self.sems[name], v)

    def wait(self, eng, name, val):
        if self.mode == "real":
            eng.wait_ge(self.sems[name], val)

    def count(self, name):
        return self.counts.get(name, 0)

    def mark(self, key):
        snap = dict(self.counts)
        if self.mode == "mock":
            if key not in self.marks:
                self.marks[key] = snap
        return self.marks[key]

    def at(self, key, sem):
        if key not in self.marks:
            if self.mode == "mock":
                return 0
            raise KeyError(f"mark {key} never recorded in mock pass")
        return self.marks[key].get(sem, 0)

    def bar(self, eng, k):
        self.counts["bar"] = self.counts.get("bar", 0) + 1
        if self.mode == "real":
            eng.drain().then_inc(self.sems["bar"], 1)
            eng.wait_ge(self.sems["bar"], 5 * (k + 1))


class Chain:

    def __init__(self, ctx, en, eng, key):
        self.ctx, self.en, self.eng = ctx, en, eng
        self.sem = "ch_" + key
        self.dsem = {q: f"chd{q[0]}_{key}" for q in ("sync", "gpsimd", "scalar")}
        ctx.counts.setdefault(self.sem, 0)
        for s in self.dsem.values():
            ctx.counts.setdefault(s, 0)
        self.n = 0
        self.d = {q: 0 for q in self.dsem}
        self.after = []

    def join(self, others):
        for o in others:
            if o is self:
                continue
            if o.n:
                self.after.append((o.sem, o.n))
            for q, v in o.d.items():
                if v:
                    self.after.append((o.dsem[q], v))

    def _wait_cur(self):
        if self.ctx.mode == "real":
            if self.n:
                self.ctx.wait(self.eng, self.sem, self.n)
            for q, v in self.d.items():
                if v:
                    self.ctx.wait(self.eng, self.dsem[q], v)
            for (sm, v) in self.after:
                self.ctx.wait(self.eng, sm, v)

    def step(self, owner, fn, dv=None):
        dma = dv is not None
        if self.en == owner:
            if self.ctx.mode == "real":
                self._wait_cur()
                ins = fn(self.eng)
                if dma:
                    ins.then_inc(self.ctx.sems[self.dsem[owner]], 16)
                else:
                    ins.then_inc(self.ctx.sems[self.sem], 1)
            self.ctx.counts[self.dsem[owner] if dma else self.sem] += 16 if dma else 1
        if dma:
            self.d[owner] += 16
        else:
            self.n += 1

    def dma_group(self, owner, fns):
        first = True
        for fn in fns:
            if self.en == owner:
                if self.ctx.mode == "real":
                    if first:
                        self._wait_cur()
                    ins = fn(self.eng)
                    ins.then_inc(self.ctx.sems[self.dsem[owner]], 16)
                self.ctx.counts[self.dsem[owner]] += 16
            self.d[owner] += 16
            first = False

    def wait_all(self):
        self._wait_cur()




def mk_ops(ctx, en, eng, key):
    sems = {"tensor": key + "pe", "vector": key + "ve", "scalar": key + "sc",
            "sync": key + "ld", "gpsimd": key + "gp", "sync2": key + "ld2",
            "gpsimd2": key + "gp2", "gpsimdw": key + "gpw", "gpsimdc": key + "gc"}
    engine_of = {"sync2": "sync", "gpsimd2": "gpsimd", "gpsimdw": "gpsimd",
                 "gpsimdc": "gpsimd"}
    for sm in sems.values():
        ctx.counts.setdefault(sm, 0)

    def op(owner, fn, mark=None, deps=()):
        act = engine_of.get(owner, owner)
        if en == act:
            if ctx.mode == "real":
                for (mk, sm) in deps:
                    ctx.wait(eng, sm, ctx.at(mk, sm))
            dv = 1 if act in ("tensor", "vector", "scalar") or owner == "gpsimdc" else 16
            if ctx.mode == "real":
                ins = fn(eng)
                ins.then_inc(ctx.sems[sems[owner]], dv)
            ctx.counts[sems[owner]] += dv
            if mark:
                ctx.mark(mark)
        return None

    def waitall(extra=()):
        if ctx.mode == "real":
            for sm in list(sems.values()) + list(extra):
                v = ctx.count(sm)
                if v:
                    ctx.wait(eng, sm, v)
    return op, sems, waitall


def host_consts():
    c = {}
    c["ident_b"] = np.eye(128, dtype=BF)
    c["ident_f"] = np.eye(128, dtype=np.float32)
    r = np.arange(128)
    c["mask_causal"] = np.where(r[None, :] <= r[:, None], 0.0, NEG).astype(np.float32)
    c["mask_far"] = np.where(r[None, :] >= r[:, None] + 1, 0.0, NEG).astype(np.float32)
    c["ones_row_b"] = np.ones((1, 512), dtype=BF)
    c["ones_row_f"] = np.ones((1, 128), dtype=np.float32)
    oh = np.zeros((8, 8 * 128), dtype=BF)
    for e in range(8):
        oh[e, e * 128:(e + 1) * 128] = 1
    c["onehot8"] = oh
    c["onehot32"] = None
    del c["onehot32"]
    return c


def build_program(phase_list, io_kinds):
    nc = bass.Bass("TRN2", target_bir_lowering=False)
    ctx = Ctx(nc)
    for name, spec in io_kinds.items():
        kind, shape, dtype = spec[:3]
        kw = {"addr_space": spec[3]} if len(spec) > 3 else {}
        ctx.tensors[name] = nc.dram_tensor(name, list(shape), dtype, kind=kind, **kw)
    for cname, arr in host_consts().items():
        ctx.consts[cname] = nc.inline_tensor(arr, cname).ap()

    with ExitStack() as stack:
        en_ = stack.enter_context
        sb = ctx.sb
        sb["A"] = en_(nc.sbuf_tensor("bufA", [128, 32768], BF16))
        sb["B"] = en_(nc.sbuf_tensor("bufB", [128, 32768], BF16))
        sb["C"] = en_(nc.sbuf_tensor("bufC", [128, 16384], BF16))
        sb["D"] = en_(nc.sbuf_tensor("bufD", [128, 3072], F32))
        sb["E"] = en_(nc.sbuf_tensor("bufE", [128, 2048], F32))
        sb["F"] = en_(nc.sbuf_tensor("bufF", [128, 4096], BF16))
        sb["G"] = en_(nc.sbuf_tensor("bufG", [128, 512], F32))
        sb["H"] = en_(nc.sbuf_tensor("bufH", [128, 1024], BF16))
        sb["I"] = en_(nc.sbuf_tensor("bufI", [128, 128], F32))
        sb["X"] = en_(nc.sbuf_tensor("bufX", [128, 4096], BF16))
        ps = ctx.ps
        for i in range(6):
            ps[f"p{i}"] = en_(nc.psum_tensor(f"p{i}", [128, 512], F32))
        ps["tb0"] = en_(nc.psum_tensor("tb0", [128, 512], BF16))
        ps["tb1"] = en_(nc.psum_tensor("tb1", [128, 512], BF16))

        ctx.mode = "mock"
        for en in ENGINES:
            eng = MockEng(en)
            for k, ph in enumerate(phase_list):
                ph(ctx, en, eng)
                ctx.bar(eng, k)

        for name in sorted(ctx.counts):
            ctx.sems[name] = en_(nc.semaphore(name))
        ctx.mode = "real"
        ctx.counts = {}
        block = en_(nc.Block())

        def mk(en):
            def f(eng):
                for k, ph in enumerate(phase_list):
                    ph(ctx, en, eng)
                    ctx.bar(eng, k)
            return f
        block.sync(mk("sync"))
        block.gpsimd(mk("gpsimd"))
        block.tensor(mk("tensor"))
        block.vector(mk("vector"))
        block.scalar(mk("scalar"))
    return nc



def ln_stats_norm(ch, xt, xsq, st, xnf):
    s1, s2, mu, var, rec, rstd, nmr = (st[:, k:k + 1] for k in range(7))
    ch.step("scalar", lambda e: e.activation(xsq, xt, AF.Square))
    ch.step("vector", lambda e: e.reduce_sum(s1, xt, AX.X))
    ch.step("vector", lambda e: e.reduce_sum(s2, xsq, AX.X))
    ch.step("vector", lambda e: e.tensor_scalar(mu, s1, 1.0 / D, None, ALU.mult))
    ch.step("vector", lambda e: e.tensor_mul(var, mu, mu))
    ch.step("vector", lambda e: e.tensor_scalar(var, s2, 1.0 / D, var, ALU.mult, ALU.subtract))
    ch.step("vector", lambda e: e.tensor_scalar_add(var, var, EPS))
    ch.step("vector", lambda e: e.reciprocal(rec, var))
    ch.step("scalar", lambda e: e.activation(rstd, rec, AF.Sqrt))
    ch.step("scalar", lambda e: e.activation(nmr, mu, AF.Copy, scale=rstd))
    ch.step("vector", lambda e: e.tensor_scalar(xnf, xt, rstd, nmr, ALU.mult, ALU.subtract))


def ln_tiles(ctx, en, eng, key, src_ap, n_tiles, outs):
    sb = ctx.sb
    Dd, Ef, F, G = (sb[k].ap() for k in ("D", "E", "F", "G"))
    X = sb["X"].ap()
    Xf32 = X.bitcast(F32)
    chs = [Chain(ctx, en, eng, key + "A"), Chain(ctx, en, eng, key + "B")]
    for i in range(n_tiles):
        sl = i % 2
        ch = chs[sl]
        xt = Dd[:, 0:1024] if sl == 0 else Dd[:, 1024:2048]
        xsq = Ef[:, 0:1024] if sl == 0 else Ef[:, 1024:2048]
        xnf = Dd[:, 2048:3072] if sl == 0 else Xf32[:, 0:1024]
        xnb = F[:, 0:1024] if sl == 0 else F[:, 2048:3072]
        xnl = F[:, 1024:2048] if sl == 0 else F[:, 3072:4096]
        st = G[:, sl * 16:sl * 16 + 16]
        r0 = i * 128
        ch.step("sync", lambda e, xt=xt, r0=r0: e.dma_start(out=xt, in_=src_ap[r0:r0 + 128, :]), 16)
        ln_stats_norm(ch, xt, xsq, st, xnf)
        fns = []
        if "bf" in outs or "lo" in outs:
            ch.step("vector", lambda e, xnb=xnb, xnf=xnf: e.tensor_copy(xnb, xnf))
            fns.append(lambda e, r0=r0, xnb=xnb: e.dma_start(out=outs["bf"][r0:r0 + 128, :], in_=xnb))
        if "lo" in outs:
            ch.step("vector", lambda e, xnl=xnl, xnf=xnf, xnb=xnb: e.tensor_sub(xnl, xnf, xnb))
            fns.append(lambda e, r0=r0, xnl=xnl: e.dma_start(out=outs["lo"][r0:r0 + 128, :], in_=xnl))
        if "f32" in outs:
            fns.append(lambda e, r0=r0, xnf=xnf: e.dma_start(out=outs["f32"][r0:r0 + 128, :], in_=xnf))
        if "x32" in outs:
            fns.append(lambda e, r0=r0, xt=xt: e.dma_start(out=outs["x32"][r0:r0 + 128, :], in_=xt))
        ch.dma_group("gpsimd", fns)
    for c_ in chs:
        c_.wait_all()


def p0_ln0(ctx, en, eng):
    x = ctx.t("x_in").rearrange("b s d -> (b s) d")
    ln_tiles(ctx, en, eng, "p0", x, NT,
             {"bf": ctx.t("xn0"), "lo": ctx.t("xn0lo")})


def p2_gate(ctx, en, eng):
    sb, ps, cs = ctx.sb, ctx.ps, ctx.consts
    ch = Chain(ctx, en, eng, "p2")
    xn0, xn0lo, comb = ctx.t("xn0"), ctx.t("xn0lo"), ctx.t("comb")
    gw = ctx.t("w_gate")
    E = sb["E"].ap()
    gw_f = E[0:8, 0:1024]
    gwT_f = E[:, 1024:1024 + 64]
    gwT_hi = sb["H"].ap()[:, 0:64]
    gwT_lo = sb["H"].ap()[:, 64:128]
    idf = sb["I"].ap()[:, 0:128]
    idb = sb["H"].ap()[:, 1024 - 128:1024]

    ch.step("sync", lambda e: e.dma_start(out=gw_f, in_=gw), 16)
    ch.step("sync", lambda e: e.dma_start(out=idf, in_=cs["ident_f"]), 16)
    ch.step("sync", lambda e: e.dma_start(out=idb, in_=cs["ident_b"]), 16)
    for c in range(DCH):
        cc = c
        ch.step("tensor", lambda e: e.transpose(
            ps["p0"].ap()[:, 0:8], gw_f[0:8, cc * 128:(cc + 1) * 128], idf[0:8, 0:8]))
        ch.step("vector", lambda e: e.tensor_copy(gwT_f[:, cc * 8:(cc + 1) * 8], ps["p0"].ap()[:, 0:8]))
    ch.step("vector", lambda e: e.tensor_copy(gwT_hi, gwT_f))
    ch.step("vector", lambda e: e.tensor_sub(gwT_lo, gwT_f, gwT_hi))

    A = sb["A"].ap()
    ch2 = [ch, Chain(ctx, en, eng, "p2B")]
    ch2[1].join([ch])
    for i in range(T // 512):
        sl = i % 2
        cc2 = ch2[sl]
        t0 = i * 512
        base = sl * 8192
        xhi = A[:, base:base + 8 * 512].rearrange("p (c t) -> p c t", c=8)
        xlo = A[:, base + 8 * 512:base + 16 * 512].rearrange("p (c t) -> p c t", c=8)
        fns = []
        for c in range(DCH):
            fns.append(lambda e, cc=c, xhi=xhi, t0=t0: e.dma_start_transpose(
                out=xhi[:, cc, :], in_=xn0[t0:t0 + 512, cc * 128:(cc + 1) * 128]))
            fns.append(lambda e, cc=c, xlo=xlo, t0=t0: e.dma_start_transpose(
                out=xlo[:, cc, :], in_=xn0lo[t0:t0 + 512, cc * 128:(cc + 1) * 128]))
        cc2.dma_group("sync", fns)
        for q in range(4):
            qq = q
            L = (ps["p0"] if sl == 0 else ps["p1"]).ap()[:, 0:8]

            def mm_logits(e, q=qq, xhi=xhi, xlo=xlo, L=L):
                first = True
                for c in range(DCH):
                    for lhs, rhs in ((xhi, gwT_hi), (xhi, gwT_lo), (xlo, gwT_hi)):
                        ins = e.matmul(L, lhs[:, c, q * 128:(q + 1) * 128],
                                       rhs[:, c * 8:(c + 1) * 8],
                                       start=first, stop=(c == DCH - 1 and rhs is gwT_hi and lhs is xlo))
                        first = False
                return ins
            cc2.step("tensor", mm_logits)
            G = sb["G"].ap()
            gof = 32 + sl * 48
            Lf = G[:, gof:gof + 8]
            nm1 = G[:, gof + 8:gof + 9]
            nm2 = G[:, gof + 9:gof + 10]
            eq = G[:, gof + 10:gof + 18]
            l2 = G[:, gof + 18:gof + 26]
            w = G[:, gof + 26:gof + 34]
            ssum = G[:, gof + 34:gof + 35]
            rs = G[:, gof + 35:gof + 36]
            combf = G[:, gof + 36:gof + 44]
            combb = sb["F"].ap()[0:128, 2048 + sl * 256:2056 + sl * 256]
            cstg = sb["F"].ap()[0:8, 2056 + sl * 256:2184 + sl * 256]
            tbp = (ps["tb0"] if sl == 0 else ps["tb1"]).ap()
            cc2.step("vector", lambda e, Lf=Lf, L=L: e.tensor_copy(Lf, L))
            cc2.step("vector", lambda e, Lf=Lf, nm1=nm1: e.reduce_max(nm1, Lf, AX.X, negate=True))
            cc2.step("vector", lambda e, eq=eq, Lf=Lf, nm1=nm1: e.tensor_scalar(eq, Lf, nm1, 0.0, ALU.add, ALU.is_equal))
            cc2.step("vector", lambda e, l2=l2, eq=eq, Lf=Lf: e.scalar_tensor_tensor(l2, eq, NEG, Lf, ALU.mult, ALU.add))
            cc2.step("vector", lambda e, nm2=nm2, l2=l2: e.reduce_max(nm2, l2, AX.X, negate=True))
            cc2.step("vector", lambda e, eq=eq, Lf=Lf, nm2=nm2: e.tensor_scalar(eq, Lf, nm2, 0.0, ALU.add, ALU.is_ge))
            cc2.step("scalar", lambda e, l2=l2, Lf=Lf, nm1=nm1: e.activation(l2, Lf, AF.Exp, bias=nm1))
            cc2.step("vector", lambda e, w=w, l2=l2, eq=eq: e.tensor_mul(w, l2, eq))
            cc2.step("vector", lambda e, ssum=ssum, w=w: e.reduce_sum(ssum, w, AX.X))
            cc2.step("vector", lambda e, rs=rs, ssum=ssum: e.reciprocal(rs, ssum))
            cc2.step("vector", lambda e, combf=combf, w=w, rs=rs: e.tensor_scalar(combf, w, rs, None, ALU.mult))
            cc2.step("vector", lambda e, combb=combb, combf=combf: e.tensor_copy(combb, combf))
            cc2.step("tensor", lambda e, tbp=tbp, combb=combb: e.transpose(tbp[0:8, 0:128], combb, idb))
            cc2.step("vector", lambda e, cstg=cstg, tbp=tbp: e.tensor_copy(cstg, tbp[0:8, 0:128]))
            cc2.step("gpsimd", lambda e, t0=t0, qq=qq, cstg=cstg: e.dma_start(
                out=comb[0:8, t0 + qq * 128:t0 + (qq + 1) * 128], in_=cstg), 16)
    for c_ in ch2:
        c_.wait_all()


def load_wT(ch, ctx, w_dram, dest, stage, idb, tbp):
    ch.step("gpsimd", lambda e: e.dma_start(
        out=stage, in_=w_dram.rearrange("(oc p) d -> p oc d", p=128)), 16)
    for oc in range(8):
        for dc in range(8):
            o_, d_ = oc, dc
            ch.step("tensor", lambda e: e.transpose(
                tbp[:, 0:128], stage[:, o_, d_ * 128:(d_ + 1) * 128], idb))
            ch.step("vector", lambda e: e.tensor_copy(
                dest[:, d_, o_ * 128:(o_ + 1) * 128], tbp[:, 0:128]))


def attn_phase(ctx, en, eng, key, xn_dram, wq, wk, wv, wo, bq, bk, bv, bo,
               out_dram, window, n_b=B_LOC, n_h=NH):
    sb, ps, cs = ctx.sb, ctx.ps, ctx.consts
    A, B, C, Dd, Ef, F, G, H, I, X = (sb[k].ap() for k in
                                      ("A", "B", "C", "D", "E", "F", "G", "H", "I", "X"))
    op, sems, waitall = mk_ops(ctx, en, eng, key)
    PE, VE, SC, LD, GP = (sems[k] for k in ("tensor", "vector", "scalar", "sync", "gpsimd"))
    Qf = A[:, 0:16384].rearrange("p (c t) -> p c t", c=8)
    Kf = A[:, 16384:32768].rearrange("p (c t) -> p c t", c=8)
    Vt = B[:, 0:16384].rearrange("p (kt o) -> p kt o", kt=16)
    Cx = B[:, 16384:32768].rearrange("p (c t) -> p c t", c=8)
    Xt = X[:, 0:4096].rearrange("p (c t) -> p c t", c=8)
    wT0 = C[:, 0:8192].rearrange("p (c o) -> p c o", c=8)
    wT1 = C[:, 8192:16384].rearrange("p (c o) -> p c o", c=8)
    stage0 = B[:, 0:8192].rearrange("p (oc d) -> p oc d", oc=8)
    stageV = B[:, 16384:24576].rearrange("p (oc d) -> p oc d", oc=8)
    stageO = A[:, 0:8192].rearrange("p (oc d) -> p oc d", oc=8)
    idb = H[:, 896:1024]
    maskc = I[:, 0:128]
    maskf = Dd[:, 2560:2688]
    bcols = G[:, 456:488].rearrange("p (w c) -> p w c", w=4)
    qcol = G[:, 448:456]
    tb = [ps["tb0"].ap(), ps["tb1"].ap()]
    has_bias = bq is not None
    K_ = key
    nW = window // 128 if window is not None else None

    def kt_lo_of(qi):
        return 0 if nW is None else max(0, qi - nW)

    def nkt_of(gi):
        qi = gi % 16
        return qi + 1 - kt_lo_of(qi)

    def pt_eng(j):
        return VE if j % 2 == 0 else SC

    def last_pt_deps(b, gi):
        nk = nkt_of(gi)
        d = [(f"{K_}pt{b}_{gi}_{nk-1}", pt_eng(nk - 1))]
        if nk >= 2:
            d.append((f"{K_}pt{b}_{gi}_{nk-2}", pt_eng(nk - 2)))
        return d

    tbdep = [[], []]

    LD2, GP2, GPW = sems["sync2"], sems["gpsimd2"], sems["gpsimdw"]
    op("sync2", lambda e: e.dma_start(out=idb, in_=cs["ident_b"]))
    op("sync2", lambda e: e.dma_start(out=maskc, in_=cs["mask_causal"]))
    if window is not None:
        op("sync2", lambda e: e.dma_start(out=maskf, in_=cs["mask_far"]))
    if has_bias:
        brows = ctx.t("b5rows")
        for w_i, bt in enumerate((bq, bk, bv, bo)):
            wi, btt = w_i, bt
            op("gpsimd", lambda e, wi=wi, btt=btt: e.dma_start(
                out=brows[wi:wi + 1, :], in_=btt[None, :]), mark=f"{K_}brow{wi}")
        op("sync2", lambda e: e.dma_start_transpose(
            out=H[:, 512:544], in_=brows.rearrange("w (c p) -> (w c) p", p=128)),
            deps=[(f"{K_}brow3", GP)])
        op("vector", lambda e: e.tensor_copy(G[:, 456:488], H[:, 512:544]),
           deps=[(f"{K_}cst", sems["sync2"])], mark=f"{K_}bcf")
    op("sync2", lambda e: e.nop(), mark=f"{K_}cst") if False else None
    if en == "sync":
        ctx.mark(f"{K_}cst")
    if has_bias:
        op("vector", lambda e: e.tensor_scalar(qcol, bcols[:, 0, :], 0.125, None, ALU.mult),
           deps=[(f"{K_}bcf", sems["vector"])], mark=f"{K_}qcol")

    last_stage = [None]

    def load_w(b, wname, w_dram, dest, stage, stage_dep):
        mk = f"{K_}w{b}{wname}"
        sdep = list(stage_dep)
        if last_stage[0]:
            sdep.append((last_stage[0] + "st", GPW))
            sdep.append((last_stage[0] + "tr63", PE))
        op("gpsimdw", lambda e: e.dma_start(
            out=stage, in_=w_dram.rearrange("(oc p) d -> p oc d", p=128)),
            deps=sdep, mark=mk + "st")
        last_stage[0] = mk
        for i in range(64):
            o_, d_ = i // 8, i % 8
            ii = i
            if i >= 2:
                deps = [(mk + f"cp{ii-2}", VE)]
            else:
                deps = [(mk + "st", GPW), (f"{K_}cst", LD2)] + tbdep[i % 2]
            op("tensor", lambda e, o_=o_, d_=d_, ii=ii: e.transpose(
                tb[ii % 2][:, 0:128], stage[:, o_, d_ * 128:(d_ + 1) * 128], idb),
                deps=deps, mark=mk + f"tr{ii}")
            op("vector", lambda e, o_=o_, d_=d_, ii=ii: e.tensor_copy(
                dest[:, d_, o_ * 128:(o_ + 1) * 128], tb[ii % 2][:, 0:128]),
                deps=[(mk + f"tr{ii}", PE)], mark=mk + f"cp{ii}")
        tbdep[0] = [(mk + "cp62", VE)]
        tbdep[1] = [(mk + "cp63", VE)]
        return mk + "cp63"

    for b in range(n_b):
        tbse = b * S
        dep_prevav = [(f"{K_}av{b-1}_{n_h-1}_15", PE)] if b > 0 else []
        wq_done = load_w(b, "q", wq, wT0, stage0, dep_prevav)
        wk_done = load_w(b, "k", wk, wT1, stage0, [])

        for tt in range(4):
            t0 = tt * 512
            if tt > 0:
                xdeps = [(f"{K_}qkm{b}_{tt-1}_15", PE)]
            elif b > 0:
                xdeps = [(f"{K_}av{b-1}_{n_h-1}_15", PE)]
            else:
                xdeps = []
            for c in range(DCH):
                cc = c
                op("sync", lambda e, cc=cc, t0=t0: e.dma_start_transpose(
                    out=Xt[:, cc, :], in_=xn_dram[tbse + t0:tbse + t0 + 512,
                                                  cc * 128:(cc + 1) * 128]),
                    deps=xdeps if cc == 0 else [],
                    mark=f"{K_}qkx{b}_{tt}" if cc == DCH - 1 else None)
            for g in range(16):
                wh, o_ = (0, g) if g < 8 else (1, g - 8)
                wT = wT0 if wh == 0 else wT1
                pp = ps["p0"].ap() if g % 2 == 0 else ps["p1"].ap()
                gg = g
                deps = [(f"{K_}qkx{b}_{tt}", LD)]
                if g == 0 and tt == 0:
                    deps.append((wk_done, VE))
                    if b > 0:
                        deps.append((f"{K_}cx{b-1}_{n_h*16-1}", SC))
                if g >= 2:
                    deps.append((f"{K_}qke{b}_{tt}_{gg-2}", SC))
                elif tt > 0:
                    deps.append((f"{K_}qke{b}_{tt-1}_{gg+14}", SC))

                def mmqk(e, wT=wT, o_=o_, pp=pp):
                    for dc in range(DCH):
                        ins = e.matmul(pp[:], wT[:, dc, o_ * 128:(o_ + 1) * 128],
                                       Xt[:, dc, :], start=(dc == 0), stop=(dc == DCH - 1))
                    return ins
                op("tensor", mmqk, deps=deps, mark=f"{K_}qkm{b}_{tt}_{gg}")
                dst = Qf if wh == 0 else Kf
                sc_ = 0.125 if wh == 0 else 1.0
                if has_bias:
                    bias_ap = qcol[:, o_:o_ + 1] if wh == 0 else bcols[:, 1, o_:o_ + 1]
                else:
                    bias_ap = 0.0
                edeps = [(f"{K_}qkm{b}_{tt}_{gg}", PE)]
                if has_bias and b == 0 and tt == 0 and gg == 0:
                    edeps.append((f"{K_}qcol", VE))
                    edeps.append((f"{K_}cst", LD2))
                op("scalar", lambda e, dst=dst, o_=o_, t0=t0, pp=pp, sc_=sc_, bias_ap=bias_ap:
                    e.activation(dst[:, o_, t0:t0 + 512], pp[:], AF.Identity, bias=bias_ap, scale=sc_),
                    deps=edeps, mark=f"{K_}qke{b}_{tt}_{gg}")

        wv_done = load_w(b, "v", wv, wT0, stageV, [(f"{K_}qkm{b}_3_7", PE)])
        for tt in range(4):
            t0 = tt * 512
            xdeps = [(f"{K_}qkm{b}_3_15" if tt == 0 else f"{K_}vm{b}_{tt-1}_7", PE)]
            for c in range(DCH):
                cc = c
                op("sync", lambda e, cc=cc, t0=t0: e.dma_start_transpose(
                    out=Xt[:, cc, :], in_=xn_dram[tbse + t0:tbse + t0 + 512,
                                                  cc * 128:(cc + 1) * 128]),
                    deps=xdeps if cc == 0 else [],
                    mark=f"{K_}vx{b}_{tt}" if cc == DCH - 1 else None)
            for g in range(8):
                qq, og = g // 2, g % 2
                pp = ps["p0"].ap() if g % 2 == 0 else ps["p1"].ap()
                gg = g
                deps = [(f"{K_}vx{b}_{tt}", LD)]
                if g == 0 and tt == 0:
                    deps.append((wv_done, VE))
                if g < 2 and tt == 0:
                    deps.append((f"{K_}qke{b}_3_{14 + gg}", SC))
                if g >= 2:
                    deps.append((f"{K_}ve{b}_{tt}_{gg-2}", VE))
                elif tt > 0:
                    deps.append((f"{K_}ve{b}_{tt-1}_{gg+6}", VE))

                def mmv(e, qq=qq, og=og, pp=pp):
                    for dc in range(DCH):
                        ins = e.matmul(pp[:], Xt[:, dc, qq * 128:(qq + 1) * 128],
                                       wT0[:, dc, og * 512:(og + 1) * 512],
                                       start=(dc == 0), stop=(dc == DCH - 1))
                    return ins
                op("tensor", mmv, deps=deps, mark=f"{K_}vm{b}_{tt}_{gg}")
                kt = tt * 4 + qq
                op("vector", lambda e, kt=kt, og=og, pp=pp: e.tensor_copy(
                    Vt[:, kt, og * 512:(og + 1) * 512], pp[:]),
                    deps=[(f"{K_}vm{b}_{tt}_{gg}", PE)], mark=f"{K_}ve{b}_{tt}_{gg}")

        rowbs = [Dd[:, 0:2048], Ef[:, 0:2048], C[:, 8192:16384].bitcast(F32)]
        Pbs = [F[:, 0:2048], X[:, 0:2048], C[:, 0:2048]]
        Pts = [F[:, 2048:4096], X[:, 2048:4096], C[:, 2048:4096]]

        def row_env(gi):
            h, qi = gi // 16, gi % 16
            sl = gi % 3
            psA = ps[f"p{2*sl}"].ap()
            psB = ps[f"p{2*sl+1}"].ap()
            st8 = G[:, sl * 8:sl * 8 + 8]
            return dict(h=h, qi=qi, sl=sl, hc=h // 2, hr=(h % 2) * 64,
                        rowb=rowbs[sl], Pb=Pbs[sl], Pt=Pts[sl], psA=psA, psB=psB,
                        tbA=psA.bitcast(BF16), tbB=psB.bitcast(BF16),
                        pav=psA[0:64, 0:128],
                        negm=st8[:, 0:1], rsum=st8[:, 1:2], rinv=st8[:, 2:3],
                        kt_lo=kt_lo_of(qi), L0=kt_lo_of(qi) * 128,
                        L=(qi + 1) * 128 - kt_lo_of(qi) * 128)

        def emit_scores(gi):
            v = row_env(gi)
            h, qi, sl = v["h"], v["qi"], v["sl"]
            qs = Qf[v["hr"]:v["hr"] + 64, v["hc"], :]
            ks = Kf[v["hr"]:v["hr"] + 64, v["hc"], :]
            far = nW is not None and qi - v["kt_lo"] == nW
            L0, L = v["L0"], v["L"]
            pgi = gi - 3
            ng = (L + 511) // 512
            for kg in range(ng):
                k0 = L0 + kg * 512
                n = min(512, (qi + 1) * 128 - k0)
                pp = v["psA"] if kg % 2 == 0 else v["psB"]
                deps = []
                if kg >= 2:
                    deps.append((f"{K_}sev{b}_{gi}_{kg-2}", VE))
                elif pgi >= 0:
                    deps += last_pt_deps(b, pgi)
                    deps.append((f"{K_}cx{b}_{pgi}", SC))
                if gi == 0:
                    deps.append((f"{K_}qke{b}_3_15", SC))
                    deps.append((f"{K_}ve{b}_3_7", VE))
                    if b == 0:
                        deps.append((f"{K_}cst", LD2))
                if gi == 1 and b > 0:
                    deps.append((f"{K_}ops{b-1}_31", GP2))
                op("tensor", lambda e, k0=k0, n=n, pp=pp, qs=qs, ks=ks, qi=qi:
                    e.matmul(pp[:, 0:n], qs[:, qi * 128:(qi + 1) * 128],
                             ks[:, k0:k0 + n], start=True, stop=True),
                    deps=deps, mark=f"{K_}ss{b}_{gi}_{kg}")
                off0 = k0 - L0
                segs = [[0, n, None]]
                if kg == ng - 1:
                    segs = ([[0, n - 128, None]] if n > 128 else []) + [[n - 128, 128, "c"]]
                if far and kg == 0:
                    if segs[0][2] is None:
                        hn = segs[0][1]
                        rest = ([[128, hn - 128, None]] if hn > 128 else []) + segs[1:]
                    else:
                        rest = []
                    segs = [[0, 128, "f"]] + rest
                segs = [sg for sg in segs if sg[1] > 0]
                use_sc = False
                for si, (o, nn, mk) in enumerate(segs):
                    oo, nn_, mk_ = o, nn, mk
                    deps = [(f"{K_}ss{b}_{gi}_{kg}", PE)]
                    if kg == 0 and si == 0 and pgi >= 0:
                        deps.append((f"{K_}xp{b}_{pgi}", SC))
                    mkname = f"{K_}sev{b}_{gi}_{kg}" if si == len(segs) - 1 else None
                    rowb = v["rowb"]
                    if mk_ is None and use_sc:
                        op("scalar", lambda e, pp=pp, oo=oo, nn_=nn_, off0=off0, rowb=rowb:
                            e.copy(rowb[:, off0 + oo:off0 + oo + nn_], pp[:, oo:oo + nn_]),
                            deps=deps, mark=mkname)
                    elif mk_ is None:
                        op("vector", lambda e, pp=pp, oo=oo, nn_=nn_, off0=off0, rowb=rowb:
                            e.tensor_copy(rowb[:, off0 + oo:off0 + oo + nn_],
                                          pp[:, oo:oo + nn_]),
                            deps=deps, mark=mkname)
                    else:
                        msk = maskc if mk_ == "c" else maskf
                        op("vector", lambda e, pp=pp, oo=oo, nn_=nn_, off0=off0, msk=msk, rowb=rowb:
                            e.tensor_add(rowb[:, off0 + oo:off0 + oo + nn_],
                                         pp[:, oo:oo + nn_], msk),
                            deps=deps, mark=mkname)
            return ng

        def emit_softmax(gi, ng):
            v = row_env(gi)
            pgi = gi - 3
            rowb, Pb, L = v["rowb"], v["Pb"], v["L"]
            negm, rsum, rinv = v["negm"], v["rsum"], v["rinv"]
            mdeps = [(f"{K_}sev{b}_{gi}_{ng-1}", VE)]
            op("vector", lambda e, rowb=rowb, L=L, negm=negm:
                e.reduce_max(negm, rowb[:, 0:L], AX.X, negate=True),
                deps=mdeps, mark=f"{K_}rw{b}_{gi}")
            deps = [(f"{K_}rw{b}_{gi}", VE)]
            if pgi >= 0:
                deps.append((f"{K_}tp{b}_{pgi}", PE))
            op("scalar", lambda e, Pb=Pb, rowb=rowb, L=L, negm=negm, rsum=rsum:
                e.activation(Pb[:, 0:L], rowb[:, 0:L], AF.Exp, bias=negm, accum_out=rsum),
                deps=deps, mark=f"{K_}xp{b}_{gi}")
            op("vector", lambda e, rinv=rinv, rsum=rsum: e.reciprocal(rinv, rsum),
                deps=[(f"{K_}xp{b}_{gi}", SC)], mark=f"{K_}rc{b}_{gi}")
            op("vector", lambda e, Pb=Pb, L=L, rinv=rinv: e.tensor_scalar(
                Pb[:, 0:L], Pb[:, 0:L], rinv, None, ALU.mult),
                deps=[(f"{K_}rc{b}_{gi}", VE)], mark=f"{K_}pn{b}_{gi}")

        def emit_transposes(gi):
            v = row_env(gi)
            Pb, Pt = v["Pb"], v["Pt"]
            nkt = nkt_of(gi)
            for j in range(nkt):
                jj = j
                tpp = (v["tbA"] if j % 2 == 0 else v["tbB"])[:, 0:128]
                if j >= 2:
                    deps = [(f"{K_}pt{b}_{gi}_{jj-2}", pt_eng(jj - 2))]
                else:
                    deps = [(f"{K_}pn{b}_{gi}", VE)]
                op("tensor", lambda e, jj=jj, tpp=tpp, Pb=Pb: e.transpose(
                    tpp, Pb[:, jj * 128:(jj + 1) * 128], idb),
                    deps=deps, mark=f"{K_}tr{b}_{gi}_{jj}")
                if en == "tensor" and jj == nkt - 1:
                    ctx.mark(f"{K_}tp{b}_{gi}")
                if jj % 2 == 0:
                    op("vector", lambda e, tpp=tpp, jj=jj, Pt=Pt: e.tensor_copy(
                        Pt[:, jj * 128:(jj + 1) * 128], tpp),
                        deps=[(f"{K_}tr{b}_{gi}_{jj}", PE)], mark=f"{K_}pt{b}_{gi}_{jj}")
                else:
                    op("scalar", lambda e, tpp=tpp, jj=jj, Pt=Pt: e.copy(
                        Pt[:, jj * 128:(jj + 1) * 128], tpp),
                        deps=[(f"{K_}tr{b}_{gi}_{jj}", PE)], mark=f"{K_}pt{b}_{gi}_{jj}")

        def emit_av(gi):
            v = row_env(gi)
            h, qi = v["h"], v["qi"]
            Pt, pav, kt_lo = v["Pt"], v["pav"], v["kt_lo"]
            nkt = nkt_of(gi)
            deps = last_pt_deps(b, gi)

            def mm_av(e, nkt=nkt, kt_lo=kt_lo, h=h, pav=pav, Pt=Pt):
                for j in range(nkt):
                    ins = e.matmul(pav, Vt[:, kt_lo + j, h * 64:(h + 1) * 64],
                                   Pt[:, j * 128:(j + 1) * 128],
                                   start=(j == 0), stop=(j == nkt - 1))
                return ins
            op("tensor", mm_av, deps=deps, mark=f"{K_}av{b}_{h}_{qi}")

        def emit_cx(gi):
            v = row_env(gi)
            h, qi = v["h"], v["qi"]
            pav = v["pav"]
            bias_ap = (bcols[(h % 2) * 64:(h % 2) * 64 + 64, 2, v["hc"]:v["hc"] + 1]
                       if has_bias else 0.0)
            op("scalar", lambda e, hc=v["hc"], hr=v["hr"], qi=qi, pav=pav, bias_ap=bias_ap:
                e.activation(Cx[hr:hr + 64, hc, qi * 128:(qi + 1) * 128], pav,
                             AF.Identity, bias=bias_ap),
                deps=[(f"{K_}av{b}_{h}_{qi}", PE)], mark=f"{K_}cx{b}_{gi}")

        nrows = n_h * 16
        for i2 in range(nrows + 3):
            if i2 >= 3 and i2 - 3 < nrows:
                emit_cx(i2 - 3)
            if i2 < nrows:
                ng = emit_scores(i2)
            if i2 >= 2 and i2 - 2 < nrows:
                emit_transposes(i2 - 2)
            if i2 < nrows:
                emit_softmax(i2, ng)
            if i2 >= 2 and i2 - 2 < nrows:
                emit_av(i2 - 2)

        gl = n_h * 16 - 1
        wo_done = load_w(b, "o", wo, wT0, stageO, [(f"{K_}av{b}_{n_h-1}_15", PE)])
        for g in range(32):
            tt, o_ = g // 8, g % 8
            t0 = tt * 512
            pp = ps["p0"].ap() if g % 2 == 0 else ps["p1"].ap()
            gg = g
            deps = []
            if g == 0:
                deps.append((wo_done, VE))
                deps.append((f"{K_}cx{b}_{gl}", SC))
            if g >= 2:
                deps.append((f"{K_}ope{b}_{gg-2}", VE))

            def mmo(e, o_=o_, pp=pp, t0=t0):
                for dc in range(DCH):
                    ins = e.matmul(pp[:], wT0[:, dc, o_ * 128:(o_ + 1) * 128],
                                   Cx[:, dc, t0:t0 + 512], start=(dc == 0), stop=(dc == DCH - 1))
                return ins
            op("tensor", mmo, deps=deps, mark=f"{K_}opm{b}_{gg}")
            ev = Dd[:, 2048:2560] if g % 2 == 0 else Ef[:, 0:512]
            deps = [(f"{K_}opm{b}_{gg}", PE)]
            if g >= 2:
                deps.append((f"{K_}ops{b}_{gg-2}", GP if (gg - 2) % 2 == 0 else GP2))
            elif b > 0:
                deps.append((f"{K_}ops{b-1}_{30 + gg}", GP if gg % 2 == 0 else GP2))
            if has_bias:
                op("vector", lambda e, ev=ev, pp=pp, o_=o_: e.tensor_scalar(
                    ev, pp[:], bcols[:, 3, o_:o_ + 1], None, ALU.add),
                    deps=deps, mark=f"{K_}ope{b}_{gg}")
            else:
                op("vector", lambda e, ev=ev, pp=pp: e.tensor_copy(ev, pp[:]),
                    deps=deps, mark=f"{K_}ope{b}_{gg}")
            kwargs = {"accum_op": ALU.add} if window is None else {}
            op("gpsimd" if g % 2 == 0 else "gpsimd2",
               lambda e, ev=ev, o_=o_, t0=t0, kwargs=kwargs: e.dma_start(
                out=out_dram[o_ * 128:(o_ + 1) * 128, tbse + t0:tbse + t0 + 512],
                in_=ev, **kwargs),
                deps=[(f"{K_}ope{b}_{gg}", VE)], mark=f"{K_}ops{b}_{gg}")
    waitall()


def p3z_zero_acc(ctx, en, eng):
    sb = ctx.sb
    ch = Chain(ctx, en, eng, "p3z")
    accf = ctx.t("accf")
    z = sb["D"].ap()[:, 0:2048]
    ch.step("vector", lambda e: e.memset(z, 0.0))
    for dc in range(DCH):
        for half in range(T // 2048):
            d0, h0 = dc * 128, half * 2048
            ch.step("sync", lambda e: e.dma_start(
                out=accf[d0:d0 + 128, h0:h0 + 2048], in_=z), 16)
    ch.wait_all()


def p3_moe(ctx, en, eng, n_tt=NMT):
    sb, ps, cs = ctx.sb, ctx.ps, ctx.consts
    A, B, C, Dd, F, X = (sb[k].ap() for k in ("A", "B", "C", "D", "F", "X"))
    xn0, comb, accf = ctx.t("xn0"), ctx.t("comb"), ctx.t("accf")
    w1 = ctx.t("w_moe1")
    w2 = ctx.t("w_moe2")
    sw1 = ctx.t("w_sw1")
    sw2 = ctx.t("w_sw2")
    b2 = ctx.t("b_moe2")
    sb2 = ctx.t("b_ssb2")
    b1b = ctx.t("b1b")
    b1 = ctx.t("b_moe1")
    sb1 = ctx.t("b_ssb1")

    onehot = F[0:8, 0:1024]
    Be = F[:, 1024:1536]
    combst = F[0:8, 1536:2048]
    b2r = F[0:8, 2048:3072]
    sb2r = F[0:1, 3072:4096]
    onesr = sb["H"].ap()[0:1, 0:512]
    b1col = F[:, 3584:3616]

    ch = Chain(ctx, en, eng, "p3p")
    ch.step("gpsimd", lambda e: e.dma_start(out=b1b[0:8, :], in_=b1), 16)
    ch.step("gpsimd", lambda e: e.dma_start(out=b1b[8:9, :], in_=sb1[None, :]), 16)
    ch.step("gpsimd", lambda e: e.dma_start(out=b2r, in_=b2), 16)
    ch.step("gpsimd", lambda e: e.dma_start(out=sb2r, in_=sb2[None, :]), 16)
    ch.step("sync", lambda e: e.dma_start(out=onehot, in_=cs["onehot8"]), 16)
    ch.step("sync", lambda e: e.dma_start(out=onesr, in_=cs["ones_row_b"]), 16)
    ch.wait_all()

    Aw = A[:, 0:8 * 4096].rearrange("p (c f) -> p c f", c=8)
    Bw = B[:, 0:32 * 1024].rearrange("p (c o) -> p c o", c=32)
    Hb = C[:, 0:32 * 512].rearrange("p (c t) -> p c t", c=32)
    Xt = X[:, 0:8 * 512].rearrange("p (c t) -> p c t", c=8)

    for m in range(9):
        shared = (m == 0)
        e_id = m - 1
        if en == "gpsimd":
            if m > 0:
                ctx.wait(eng, "mpe", ctx.at(f"m_e{m-1}_end", "mpe"))
            src1 = sw1 if shared else w1[e_id]
            src2 = sw2 if shared else w2[e_id]
            ctx.inc(eng.dma_start(out=Aw, in_=src1.rearrange("(c p) f -> p c f", p=128)), "mgw", 16)
            ctx.inc(eng.dma_start(out=Bw, in_=src2.rearrange("(c p) o -> p c o", p=128)), "mgw", 16)
            ctx.mark(f"m_w_{m}")
        if en == "sync":
            if m > 0:
                ctx.wait(eng, "msc", ctx.at(f"m_gelu_end_{m-1}", "msc"))
            ctx.inc(eng.dma_start_transpose(
                out=b1col, in_=b1b[8 if shared else e_id].rearrange("(c p) -> c p", p=128)), "mlb", 16)
            ctx.mark(f"m_b1_{m}")

        for it in range(n_tt):
            t0 = it * MTT
            p1a = ps["p0"].ap()
            p1b = ps["p1"].ap()
            p2a = ps["p2"].ap()
            p2b = ps["p3"].ap()
            pbc = ps["p4"].ap()

            if en == "sync":
                if it > 0:
                    ctx.wait(eng, "mpe", ctx.at(f"m_s1_{m}_{it-1}_31", "mpe"))
                elif m > 0:
                    ctx.wait(eng, "mpe", ctx.at(f"m_e{m-1}_end", "mpe"))
                for c in range(DCH):
                    ctx.inc(eng.dma_start_transpose(
                        out=Xt[:, c, :], in_=xn0[t0:t0 + MTT, c * 128:(c + 1) * 128]), "mlx", 16)
                ctx.mark(f"m_x_{m}_{it}")

            if not shared:
                if en == "gpsimd":
                    if m == 8 and it > 0:
                        ctx.wait(eng, "mpe", ctx.at(f"m_s2_{8}_{it-1}_{7}", "mpe"))
                    elif not (m == 1 and it == 0):
                        ctx.wait(eng, "mpe", ctx.at(f"m_bc_prev_{m}_{it}", "mpe"))
                    ctx.inc(eng.dma_start(out=combst, in_=comb[0:8, t0:t0 + MTT]), "mgc", 16)
                    ctx.mark(f"m_compst_{m}_{it}")
                if en == "tensor":
                    ctx.wait(eng, "mgc", ctx.at(f"m_compst_{m}_{it}", "mgc"))
                    if not (m == 1 and it == 0):
                        ctx.wait(eng, "mev", ctx.at(f"m_bee_prev_{m}_{it}", "mev"))
                    ctx.inc(eng.matmul(pbc[:], onehot[:, e_id * 128:(e_id + 1) * 128],
                                       combst, start=True, stop=True), "mpe", 1)
                    ctx.mark(f"m_bc_{m}_{it}")
                    ctx.mark(f"m_bc_prev_{m}_{it + 1}" if it + 1 < n_tt else f"m_bc_prev_{m + 1}_0")
                if en == "vector":
                    ctx.wait(eng, "mpe", ctx.at(f"m_bc_{m}_{it}", "mpe"))
                    ctx.inc(eng.tensor_copy(Be, pbc[:]), "mev", 1)
                    ctx.mark(f"m_bee_{m}_{it}")
                    ctx.mark(f"m_bee_prev_{m}_{it + 1}" if it + 1 < n_tt else f"m_bee_prev_{m + 1}_0")

            for fc in range(NFC):
                pp = p1a if fc % 2 == 0 else p1b
                if en == "tensor":
                    ctx.wait(eng, "mlx", ctx.at(f"m_x_{m}_{it}", "mlx"))
                    if fc == 0:
                        ctx.wait(eng, "mgw", ctx.at(f"m_w_{m}", "mgw"))
                    if fc >= 2:
                        ctx.wait(eng, "msc", ctx.at(f"m_g_{m}_{it}_{fc-2}", "msc"))
                    elif it > 0:
                        ctx.wait(eng, "msc", ctx.at(f"m_g_{m}_{it-1}_{31 if fc == 0 else 31}", "msc") - (1 - fc))
                    elif m > 0:
                        ctx.wait(eng, "msc", ctx.at(f"m_gelu_end_{m-1}", "msc"))
                    for dc in range(DCH):
                        ins = eng.matmul(pp[:], Aw[:, dc, fc * 128:(fc + 1) * 128],
                                         Xt[:, dc, :], start=(dc == 0), stop=(dc == DCH - 1))
                    ctx.inc(ins, "mpe", 1)
                    ctx.mark(f"m_s1_{m}_{it}_{fc}")
                if en == "scalar":
                    ctx.wait(eng, "mpe", ctx.at(f"m_s1_{m}_{it}_{fc}", "mpe"))
                    ctx.wait(eng, "mlb", ctx.at(f"m_b1_{m}", "mlb"))
                    ctx.inc(eng.activation(Hb[:, fc, :], (p1a if fc % 2 == 0 else p1b)[:],
                                           AF.Gelu, bias=b1col[:, fc:fc + 1]), "msc", 1)
                    ctx.mark(f"m_g_{m}_{it}_{fc}")
                if not shared and en == "vector":
                    ctx.wait(eng, "msc", ctx.at(f"m_g_{m}_{it}_{fc}", "msc"))
                    ctx.wait(eng, "mev", ctx.at(f"m_bee_{m}_{it}", "mev"))
                    ctx.inc(eng.tensor_mul(Hb[:, fc, :], Hb[:, fc, :], Be), "mev", 1)
                    ctx.mark(f"m_sc_{m}_{it}_{fc}")
            if en in ("scalar",) and it == n_tt - 1:
                ctx.mark(f"m_gelu_end_{m}")

            for oc in range(8):
                pp = p2a if oc % 2 == 0 else p2b
                if en == "tensor":
                    if not shared:
                        ctx.wait(eng, "mev", ctx.at(f"m_sc_{m}_{it}_{31}", "mev"))
                    else:
                        ctx.wait(eng, "msc", ctx.at(f"m_g_{m}_{it}_{31}", "msc"))
                    if oc >= 2:
                        ctx.wait(eng, "mev", ctx.at(f"m_ev_{m}_{it}_{oc-2}", "mev"))
                    elif it > 0:
                        ctx.wait(eng, "mev", ctx.at(f"m_ev_{m}_{it-1}_{6 + oc}", "mev"))
                    elif m > 0:
                        ctx.wait(eng, "mev", ctx.at(f"m_ev_{m-1}_{n_tt-1}_{6 + oc}", "mev"))
                    bias_here = (m == 8)
                    for fc in range(NFC):
                        last = (fc == NFC - 1) and not bias_here
                        ins = eng.matmul(pp[:], Bw[:, fc, oc * 128:(oc + 1) * 128],
                                         Hb[:, fc, :], start=(fc == 0), stop=last)
                    if bias_here:
                        ins = eng.matmul(pp[:], b2r[:, oc * 128:(oc + 1) * 128], combst,
                                         start=False, stop=False)
                        ins = eng.matmul(pp[:], sb2r[:, oc * 128:(oc + 1) * 128], onesr,
                                         start=False, stop=True)
                    ctx.inc(ins, "mpe", 1)
                    ctx.mark(f"m_s2_{m}_{it}_{oc}")
                if en == "vector":
                    ctx.wait(eng, "mpe", ctx.at(f"m_s2_{m}_{it}_{oc}", "mpe"))
                    dsl = Dd[:, (oc % 2) * 512:(oc % 2) * 512 + 512]
                    par = f"mst{oc % 2}"
                    if not (m == 0 and it == 0 and oc < 2):
                        key = (f"m_st_{m}_{it}_{oc-2}" if oc >= 2 else
                               (f"m_st_{m}_{it-1}_{6+oc}" if it > 0 else
                                f"m_st_{m-1}_{n_tt-1}_{6+oc}"))
                        ctx.wait(eng, par, ctx.at(key, par))
                    ctx.inc(eng.tensor_copy(dsl, pp[:]), "mev", 1)
                    ctx.mark(f"m_ev_{m}_{it}_{oc}")
                if en == "gpsimd":
                    ctx.wait(eng, "mev", ctx.at(f"m_ev_{m}_{it}_{oc}", "mev"))
                    dsl = Dd[:, (oc % 2) * 512:(oc % 2) * 512 + 512]
                    ctx.inc(eng.dma_start(out=accf[oc * 128:(oc + 1) * 128, t0:t0 + MTT],
                                          in_=dsl, accum_op=ALU.add), f"mst{oc % 2}", 16)
                    ctx.mark(f"m_st_{m}_{it}_{oc}")
        if en == "tensor":
            ctx.mark(f"m_e{m}_end")
    ctx.wait(eng, "mst0", ctx.at(f"m_st_{8}_{n_tt-1}_{6}", "mst0"))
    ctx.wait(eng, "mst1", ctx.at(f"m_st_{8}_{n_tt-1}_{7}", "mst1"))


def p4_ln1(ctx, en, eng):
    sb, ps, cs = ctx.sb, ctx.ps, ctx.consts
    C, Dd, Ef, F, G, I, X = (sb[k].ap() for k in ("C", "D", "E", "F", "G", "I", "X"))
    Xf32 = X.bitcast(F32)
    Cf32 = C.bitcast(F32)
    ch0 = Chain(ctx, en, eng, "p4A")
    ch1 = Chain(ctx, en, eng, "p4B")
    chs = [ch0, ch1]
    x = ctx.t("x_in").rearrange("b s d -> (b s) d")
    accf, x1ag, xn1 = ctx.t("accf"), ctx.t("x1ag"), ctx.t("xn1")
    idf = I[:, 0:128]
    ch0.step("sync", lambda e: e.dma_start(out=idf, in_=cs["ident_f"]), 16)
    ch1.join([ch0])
    for i in range(NT):
        sl = i % 2
        cc_ = chs[sl]
        r0 = i * 128
        xt = Dd[:, 0:1024] if sl == 0 else Dd[:, 1024:2048]
        chunks = Ef[:, 0:1024] if sl == 0 else Ef[:, 1024:2048]
        xsq = Xf32[:, 0:1024] if sl == 0 else Xf32[:, 1024:2048]
        x1t = Dd[:, 2048:3072] if sl == 0 else Cf32[:, 0:1024]
        xnb = F[:, 0:1024] if sl == 0 else F[:, 1024:2048]
        st = G[:, sl * 16:sl * 16 + 16]
        cc_.dma_group("sync", [lambda e, xt=xt, r0=r0: e.dma_start(out=xt, in_=x[r0:r0 + 128, :])] + [
            (lambda e, d_=dc, r0=r0, chunks=chunks: e.dma_start(
                out=chunks[:, d_ * 128:(d_ + 1) * 128],
                in_=accf[d_ * 128:(d_ + 1) * 128, r0:r0 + 128]))
            for dc in range(DCH)])
        for dc in range(DCH):
            d_ = dc
            pb2 = (ps[f"p{2 + sl*2}"] if dc % 2 == 0 else ps[f"p{3 + sl*2}"]).ap()
            cc_.step("tensor", lambda e, d_=d_, pb2=pb2, chunks=chunks: e.transpose(
                pb2[:, 0:128], chunks[:, d_ * 128:(d_ + 1) * 128], idf))
            cc_.step("vector", lambda e, d_=d_, pb2=pb2, x1t=x1t, xt=xt: e.tensor_add(
                x1t[:, d_ * 128:(d_ + 1) * 128], pb2[:, 0:128],
                xt[:, d_ * 128:(d_ + 1) * 128]))
        cc_.step("gpsimd", lambda e, r0=r0, x1t=x1t: e.dma_start(out=x1ag[r0:r0 + 128, :], in_=x1t), 16)
        ln_stats_norm(cc_, x1t, xsq, st, xsq)
        cc_.step("vector", lambda e, xnb=xnb, xsq=xsq: e.tensor_copy(xnb, xsq))
        cc_.step("gpsimd", lambda e, r0=r0, xnb=xnb: e.dma_start(out=xn1[r0:r0 + 128, :], in_=xnb), 16)
    for c_ in chs:
        c_.wait_all()


def p6_ag1(ctx, en, eng, cc_group):
    if en == "gpsimd":
        ctx.inc(eng.collective_compute(
            "AllGather", ALU.bypass, replica_groups=[cc_group],
            ins=[ctx.t("x1ag").opt()], outs=[ctx.t("x1full").opt()]), "cc1", 1)


def rank0(ctx, eng):
    return 0 if ctx.mode == "mock" else eng.partition_id()


def dyn(ap, off_expr):
    return bass.AP(tensor=ap.tensor, offset=ap.offset + off_expr, ap=ap.ap)


def p7_ffn1(ctx, en, eng, ncc, cc_group):
    sb, ps, cs = ctx.sb, ctx.ps, ctx.consts
    A, B, C, Dd, Ef, F, G, H, I, X = (sb[k].ap() for k in
                                      ("A", "B", "C", "D", "E", "F", "G", "H", "I", "X"))
    jl = S // ncc
    njt = jl // 128
    cells = 32 * jl
    nct = cells // 128
    ch = Chain(ctx, en, eng, "p7")
    x1full = ctx.t("x1full")
    xn1s, xn1slo, t32 = ctx.t("xn1s"), ctx.t("xn1slo"), ctx.t("t32")
    probs_d = ctx.t("probs")
    qall_d, kvall_d, su_d = ctx.t("qall"), ctx.t("kvall"), ctx.t("su")
    v1d, v2d, m1d, m2d, cnt_d = (ctx.t(k) for k in ("v1d", "v2d", "m1d", "m2d", "cntd"))
    agsm_in, agsm_out = ctx.t("agsm_in"), ctx.t("agsm_out")
    ffn1ag = ctx.t("ffn1ag")
    wg, wqh, wkvh, woh = ctx.t("w1_wg"), ctx.t("w1_wq"), ctx.t("w1_wkv"), ctx.t("w1_wo")
    bqh, bkvh, boh = ctx.t("b1_bq"), ctx.t("b1_bkv"), ctx.t("b1_bo")
    lng, lnb = ctx.t("b1_lng"), ctx.t("b1_lnb")

    idf = I[:, 0:128]
    onesb = H[0:1, 0:512]
    r = [None]

    def R(e):
        if r[0] is None:
            r[0] = rank0(ctx, e)
        return r[0]

    wgHI = H[:, 512:640].rearrange("p (c h) -> p c h", c=8)
    wgLO = H[:, 640:768].rearrange("p (c h) -> p c h", c=8)
    wgF = Ef[:, 0:128].rearrange("p (c h) -> p c h", c=8)
    ch.step("sync", lambda e: e.dma_start(out=idf, in_=cs["ident_f"]), 16)
    ch.step("sync", lambda e: e.dma_start(out=onesb, in_=cs["ones_row_b"]), 16)
    ch.step("sync", lambda e: e.dma_start(
        out=wgF, in_=wg.rearrange("(c p) h -> p c h", p=128)), 16)
    ch.step("vector", lambda e: e.tensor_copy(wgHI, wgF))
    ch.step("vector", lambda e: e.tensor_sub(wgLO, wgF, wgHI))
    WqA = C[:, 0:8192].rearrange("p (c n) -> p c n", c=8)
    WoS = C[:, 8192:16384].rearrange("p (c n) -> p c n", c=8)
    WkvA = B[:, 0:16384].rearrange("p (c n) -> p c n", c=8)
    for c_ in range(DCH):
        cc = c_
        ch.step("gpsimd", lambda e, cc=cc: e.dma_start(
            out=C[:, cc * 1024:(cc + 1) * 1024].rearrange("p (h n) -> p h n", h=16),
            in_=wqh.rearrange("h (c p) n -> c p h n", p=128)[cc]), 16)
        ch.step("gpsimd", lambda e, cc=cc: e.dma_start(
            out=B[:, cc * 2048:(cc + 1) * 2048].rearrange("p (h n) -> p h n", h=16),
            in_=wkvh.rearrange("h (c p) n -> c p h n", p=128)[cc]), 16)
    ch.step("gpsimd", lambda e: e.dma_start(
        out=WoS, in_=woh.rearrange("h e o -> (h e) o").rearrange("(c p) o -> p c o", p=128)), 16)
    bqA = F[0:1, 0:1024]
    bkvA = F[0:1, 1024:3072]
    ch.step("gpsimd", lambda e: e.dma_start(out=bqA, in_=bqh.rearrange("h n -> (h n)")[None, :]), 16)
    ch.step("gpsimd", lambda e: e.dma_start(out=bkvA, in_=bkvh.rearrange("h n -> (h n)")[None, :]), 16)

    x1seq = ctx.t("x1seq")
    if en == "gpsimd" and ctx.count("cc1"):
        ctx.wait(eng, "cc1", ctx.count("cc1"))
    ch.step("gpsimd", lambda e: e.dma_start(
        out=x1seq.rearrange("(b j) d -> b j d", b=32),
        in_=dyn(x1full.rearrange("(b s) d -> b s d", b=32)[:, 0:jl, :], R(e) * (jl * D))), 16)

    ch7a = [ch, Chain(ctx, en, eng, "p7a2")]
    Xf32 = X.bitcast(F32)
    for ct in range(nct):
        sl = ct % 2
        cc_ = ch7a[sl]
        xt = Dd[:, 0:1024] if sl == 0 else Dd[:, 1024:2048]
        xsq = Ef[:, 0:1024] if sl == 0 else Ef[:, 1024:2048]
        xnf = Dd[:, 2048:3072] if sl == 0 else Xf32[:, 0:1024]
        xnb = F[:, sl * 2048:sl * 2048 + 1024]
        xnl = F[:, sl * 2048 + 1024:sl * 2048 + 2048]
        st = G[:, sl * 16:sl * 16 + 16]
        r0 = ct * 128
        cc_.step("sync", lambda e, r0=r0: e.dma_start(
            out=xt, in_=x1seq[r0:r0 + 128, :]), 16)
        ln_stats_norm(cc_, xt, xsq, st, xnf)
        cc_.step("vector", lambda e: e.tensor_copy(xnb, xnf))
        cc_.step("vector", lambda e: e.tensor_sub(xnl, xnf, xnb))
        cc_.dma_group("gpsimd", [
            lambda e, r0=r0: e.dma_start(out=xn1s[r0:r0 + 128, :], in_=xnb),
            lambda e, r0=r0: e.dma_start(out=xn1slo[r0:r0 + 128, :], in_=xnl),
            lambda e, r0=r0: e.dma_start(out=t32[r0:r0 + 128, :], in_=xnf)])


    ch7b = [Chain(ctx, en, eng, "p7bA"), Chain(ctx, en, eng, "p7bB")]
    for c_ in ch7b:
        c_.join(ch7a)
    for ct in range(nct):
        sl = ct % 2
        cc_ = ch7b[sl]
        r0 = ct * 128
        xhi = X[:, sl * 2048:sl * 2048 + 1024].rearrange("p (c t) -> p c t", c=8)
        xlo = X[:, sl * 2048 + 1024:sl * 2048 + 2048].rearrange("p (c t) -> p c t", c=8)
        fns = []
        for c in range(DCH):
            fns.append(lambda e, cc=c, r0=r0, xhi=xhi: e.dma_start_transpose(
                out=xhi[:, cc, :], in_=xn1s[r0:r0 + 128, cc * 128:(cc + 1) * 128]))
            fns.append(lambda e, cc=c, r0=r0, xlo=xlo: e.dma_start_transpose(
                out=xlo[:, cc, :], in_=xn1slo[r0:r0 + 128, cc * 128:(cc + 1) * 128]))
        cc_.dma_group("sync", fns)
        L = (ps["p0"] if sl == 0 else ps["p1"]).ap()[:, 0:16]

        def mmg(e, xhi=xhi, xlo=xlo, L=L):
            first = True
            for c in range(DCH):
                for lhs, rhs in ((xhi, wgHI), (xhi, wgLO), (xlo, wgHI)):
                    ins = e.matmul(L, lhs[:, c, :], rhs[:, c, :], start=first,
                                   stop=(c == DCH - 1 and lhs is xlo))
                    first = False
            return ins
        cc_.step("tensor", mmg)
        gof = 32 + sl * 40
        Lf = G[:, gof:gof + 16]
        nm1 = G[:, gof + 16:gof + 17]
        ssum = G[:, gof + 17:gof + 18]
        rs = G[:, gof + 18:gof + 19]
        pr = G[:, gof + 20:gof + 36]
        cc_.step("vector", lambda e, Lf=Lf, L=L: e.tensor_copy(Lf, L))
        cc_.step("vector", lambda e, Lf=Lf, nm1=nm1: e.reduce_max(nm1, Lf, AX.X, negate=True))
        cc_.step("scalar", lambda e, pr=pr, Lf=Lf, nm1=nm1, ssum=ssum: e.activation(
            pr, Lf, AF.Exp, bias=nm1, accum_out=ssum))
        cc_.step("vector", lambda e, rs=rs, ssum=ssum: e.reciprocal(rs, ssum))
        cc_.step("vector", lambda e, pr=pr, rs=rs: e.tensor_scalar(pr, pr, rs, None, ALU.mult))
        cc_.step("gpsimd", lambda e, r0=r0, pr=pr: e.dma_start(out=probs_d[r0:r0 + 128, :], in_=pr), 16)


    ch7d = [Chain(ctx, en, eng, "p7dA"), Chain(ctx, en, eng, "p7dB")]
    for c_ in ch7d:
        c_.join(ch7a)
        c_.join(ch7b)
    for ct in range(nct):
        sl = ct % 2
        cc_ = ch7d[sl]
        r0 = ct * 128
        xhi = X[:, sl * 1024:sl * 1024 + 1024].rearrange("p (c t) -> p c t", c=8)
        cc_.dma_group("sync", [
            (lambda e, cc=c, r0=r0, xhi=xhi: e.dma_start_transpose(
                out=xhi[:, cc, :], in_=xn1s[r0:r0 + 128, cc * 128:(cc + 1) * 128]))
            for c in range(DCH)])
        for ng in range(6):
            is_q = ng < 2
            W = WqA if is_q else WkvA
            brow = bqA if is_q else bkvA
            o0 = (ng if is_q else ng - 2) * 512
            pp = (ps[f"p{sl*2}"] if ng % 2 == 0 else ps[f"p{sl*2+1}"]).ap()

            def mmq(e, W=W, brow=brow, o0=o0, pp=pp, xhi=xhi):
                for dc in range(DCH):
                    ins = e.matmul(pp[:], xhi[:, dc, :], W[:, dc, o0:o0 + 512],
                                   start=(dc == 0), stop=False)
                ins = e.matmul(pp[:], onesb[0:1, 0:128], brow[:, o0:o0 + 512],
                               start=False, stop=True)
                return ins
            cc_.step("tensor", mmq)
            dst = (X[:, 2048 + (ng % 2) * 512: 2048 + (ng % 2) * 512 + 512] if sl == 0
                   else X[:, 3072 + (ng % 2) * 512: 3072 + (ng % 2) * 512 + 512])
            cc_.step("vector", lambda e, pp=pp, dst=dst: e.tensor_copy(dst, pp[:]))
            dd = qall_d if is_q else kvall_d
            cc_.step("gpsimd", lambda e, dd=dd, r0=r0, o0=o0, dst=dst: e.dma_start(
                out=dd[r0:r0 + 128, o0:o0 + 512], in_=dst), 16)

    ch.join(ch7a)
    ch.join(ch7b)
    ch.join(ch7d)
    for jt in range(njt):
        j0 = jt * 128
        pj = G[:, 0:512]
        ch.step("sync", lambda e, j0=j0: e.dma_start(
            out=pj.rearrange("p (b h) -> p b h", b=32),
            in_=probs_d.rearrange("(b j) h -> j b h", b=32)[j0:j0 + 128],
            allow_slow_non_contiguous=True), 16)
        m1v = Dd[:, 0:16]
        m2v = Dd[:, 16:32]
        eqv = Ef[:, 0:512]
        l2v = Ef[:, 1024:1536]
        tmpv = Dd[:, 512:1024]
        c0 = Ef[:, 0:512]
        c1 = Ef[:, 512:1024]
        M1 = X[:, 0:512]
        M2 = X[:, 512:1024]

        def treemax(dst, src0):
            ch.step("vector", lambda e: e.tensor_max(tmpv[:, 0:256], src0[:, 0:256], src0[:, 256:512]))
            ch.step("vector", lambda e: e.tensor_max(tmpv[:, 0:128], tmpv[:, 0:128], tmpv[:, 128:256]))
            ch.step("vector", lambda e: e.tensor_max(tmpv[:, 0:64], tmpv[:, 0:64], tmpv[:, 64:128]))
            ch.step("vector", lambda e: e.tensor_max(tmpv[:, 0:32], tmpv[:, 0:32], tmpv[:, 32:64]))
            ch.step("vector", lambda e: e.tensor_max(dst, tmpv[:, 0:16], tmpv[:, 16:32]))
        treemax(m1v, pj)
        pj3 = pj.rearrange("p (b h) -> p b h", b=32)
        ch.step("vector", lambda e: e.tensor_tensor(
            eqv.rearrange("p (b h) -> p b h", b=32), pj3,
            m1v[:, None, :].broadcast_to([128, 32, 16]), ALU.is_ge))
        ch.step("vector", lambda e: e.scalar_tensor_tensor(l2v, eqv, NEG, pj, ALU.mult, ALU.add))
        treemax(m2v, l2v)
        ch.step("vector", lambda e: e.tensor_tensor(
            c0.rearrange("p (b h) -> p b h", b=32), pj3,
            m2v[:, None, :].broadcast_to([128, 32, 16]), ALU.is_ge))
        ch.step("vector", lambda e: e.tensor_copy(M2, c0))
        cur, nxt = c0, c1
        for sh in (1, 2, 4, 8, 16):
            s_ = sh * 16
            ch.step("vector", lambda e, cur=cur, nxt=nxt, s_=s_: e.tensor_copy(nxt[:, 0:s_], cur[:, 0:s_]))
            ch.step("vector", lambda e, cur=cur, nxt=nxt, s_=s_: e.tensor_add(
                nxt[:, s_:512], cur[:, s_:512], cur[:, 0:512 - s_]))
            cur, nxt = nxt, cur
        ch.step("vector", lambda e, cur=cur: e.tensor_single_scalar(tmpv, cur, 1.0, ALU.is_equal))
        ch.step("vector", lambda e: e.tensor_mul(M1, tmpv, M2))
        ch.step("vector", lambda e, cur=cur: e.tensor_single_scalar(tmpv, cur, 2.0, ALU.is_equal))
        ch.step("vector", lambda e: e.tensor_mul(M2, tmpv, M2))
        ch.step("gpsimd", lambda e, j0=j0: e.dma_start(out=m1d[j0:j0 + 128, :], in_=M1), 16)
        ch.step("gpsimd", lambda e, j0=j0: e.dma_start(out=m2d[j0:j0 + 128, :], in_=M2), 16)
        ch.step("vector", lambda e: e.tensor_add(tmpv, M1, M2))
        cnt = Dd[:, 32:64]
        ch.step("vector", lambda e: e.tensor_reduce(
            cnt, tmpv.rearrange("p (b h) -> p b h", b=32), AX.X, ALU.add))
        ch.step("gpsimd", lambda e, j0=j0: e.dma_start(out=cnt_d[j0:j0 + 128, :], in_=cnt), 16)

        Q1, Q2 = F[:, 0:1024], F[:, 1024:2048]
        V1, V2 = F[:, 2048:3072], F[:, 3072:4096]
        K1, K2 = H[:, 0:1024], X[:, 1024:2048]
        for bufz in (Q1, Q2, V1, V2, K1, K2):
            bz = bufz
            ch.step("vector", lambda e, bz=bz: e.memset(bz, 0.0))
        qb = C[:, 0:1024]
        kvb = C[:, 1024:3072]
        tq = C[:, 3072:4096]
        tkv = C[:, 4096:6144]
        for b in range(32):
            bb = b
            ch.dma_group("sync", [
                lambda e, bb=bb, j0=j0: e.dma_start(
                    out=qb, in_=qall_d[bb * jl + j0: bb * jl + j0 + 128, :]),
                lambda e, bb=bb, j0=j0: e.dma_start(
                    out=kvb, in_=kvall_d[bb * jl + j0: bb * jl + j0 + 128, :])])
            m1s = M1[:, bb * 16:(bb + 1) * 16]
            m2s = M2[:, bb * 16:(bb + 1) * 16]
            qb3 = qb.rearrange("p (h e) -> p h e", h=16)
            kv4 = kvb.rearrange("p (h w e) -> p h w e", h=16, w=2)
            for (msk, qd, kd, vd) in ((m1s, Q1, K1, V1), (m2s, Q2, K2, V2)):
                mq, qd_, kd_, vd_ = msk, qd, kd, vd
                ch.step("vector", lambda e, mq=mq: e.tensor_tensor(
                    tq.rearrange("p (h e) -> p h e", h=16), qb3,
                    mq[:, :, None].broadcast_to([128, 16, 64]), ALU.mult))
                ch.step("vector", lambda e, qd_=qd_: e.tensor_add(qd_, qd_, tq))
                ch.step("vector", lambda e, mq=mq: e.tensor_tensor(
                    tkv.rearrange("p (h w e) -> p h w e", h=16, w=2), kv4,
                    mq[:, :, None, None].broadcast_to([128, 16, 2, 64]), ALU.mult))
                tkv4 = tkv.rearrange("p (h w e) -> p h w e", h=16, w=2)
                ch.step("vector", lambda e, kd_=kd_, tkv4=tkv4: e.tensor_add(
                    kd_.rearrange("p (h e) -> p h e", h=16), kd_.rearrange("p (h e) -> p h e", h=16),
                    tkv4[:, :, 0, :]))
                ch.step("vector", lambda e, vd_=vd_, tkv4=tkv4: e.tensor_add(
                    vd_.rearrange("p (h e) -> p h e", h=16), vd_.rearrange("p (h e) -> p h e", h=16),
                    tkv4[:, :, 1, :]))
        ch.step("gpsimd", lambda e, j0=j0: e.dma_start(out=v1d[j0:j0 + 128, :], in_=V1), 16)
        ch.step("gpsimd", lambda e, j0=j0: e.dma_start(out=v2d[j0:j0 + 128, :], in_=V2), 16)
        attj = G[:, 32 + jt * 64: 96 + jt * 64]
        tmpf = Dd[:, 0:1024]
        for ci, (qq, kk) in enumerate(((Q1, K1), (Q1, K2), (Q2, K1), (Q2, K2))):
            cc_, qq_, kk_ = ci, qq, kk
            ch.step("vector", lambda e, qq_=qq_, kk_=kk_: e.tensor_mul(tmpf, qq_, kk_))
            ch.step("vector", lambda e, cc_=cc_: e.tensor_reduce(
                attj[:, cc_ * 16:(cc_ + 1) * 16],
                tmpf.rearrange("p (h e) -> p h e", h=16), AX.X, ALU.add))
        ch.step("vector", lambda e, attj=attj: e.tensor_scalar(attj, attj, 0.125, None, ALU.mult))
        ch.step("tensor", lambda e, attj=attj: e.transpose(ps["p2"].ap()[0:64, 0:128], attj, idf))
        ch.step("vector", lambda e, jt=jt: e.tensor_copy(
            Ef[0:64, 1536 + jt * 128:1536 + jt * 128 + 128], ps["p2"].ap()[0:64, 0:128]))

    attT = Ef[0:64, 1536:1536 + 256]
    mloc = G[0:64, 480:481]
    ch.step("vector", lambda e: e.reduce_max(mloc, attT[:, 0:njt * 128], AX.X))
    ch.step("tensor", lambda e: e.transpose(ps["p3"].ap()[0:1, 0:64], mloc, idf[0:64, 0:64]))
    mrow = Ef[0:1, 1800:1864]
    ch.step("vector", lambda e: e.tensor_copy(mrow, ps["p3"].ap()[0:1, 0:64]))
    onesf = Dd[0:1, 0:128]
    ch.step("sync", lambda e: e.dma_start(out=onesf, in_=cs["ones_row_f"]), 16)
    ch.step("tensor", lambda e: e.matmul(ps["p4"].ap()[:, 0:64], onesf, mrow, start=True, stop=True))
    mb = Dd[:, 1024:1088]
    ch.step("vector", lambda e: e.tensor_scalar(mb, ps["p4"].ap()[:, 0:64], -1.0, None, ALU.mult))
    onescol = Dd[0:128, 1088:1089]
    ch.step("vector", lambda e: e.memset(onescol, 1.0))
    for jt in range(njt):
        attj = G[:, 32 + jt * 64: 96 + jt * 64]
        elj = Dd[:, 1152 + jt * 64: 1216 + jt * 64]
        ch.step("vector", lambda e, attj=attj, elj=elj: e.tensor_add(elj, attj, mb))
        ch.step("scalar", lambda e, elj=elj: e.activation(elj, elj, AF.Exp))
        ch.step("tensor", lambda e, elj=elj, jt=jt: e.matmul(
            ps["p5"].ap()[0:1, 0:64], onescol, elj, start=(jt == 0), stop=(jt == njt - 1)))
    srow = Ef[0:1, 1864:1928]
    ch.step("vector", lambda e: e.tensor_copy(srow, ps["p5"].ap()[0:1, 0:64]))
    ch.step("gpsimd", lambda e: e.dma_start(out=agsm_in[0:1, 0:64], in_=mrow), 16)
    ch.step("gpsimd", lambda e: e.dma_start(out=agsm_in[0:1, 64:128], in_=srow), 16)
    nrk = len(cc_group)
    if en == "gpsimd":
        if ctx.mode == "real":
            ch._wait_cur()
        ctx.inc(eng.collective_compute(
            "AllGather", ALU.bypass, replica_groups=[cc_group],
            ins=[agsm_in.opt()], outs=[agsm_out.opt()]), "cc2", 1)
        ctx.wait(eng, "cc2", ctx.count("cc2"))
    ch.step("gpsimd", lambda e: e.nop())
    ch.step("sync", lambda e, nrk=nrk: e.dma_start(out=Ef[0:nrk, 0:128], in_=agsm_out[0:nrk, :]), 16)
    GtT = Dd[:, 1280:1280 + 8]
    ch.step("tensor", lambda e, nrk=nrk: e.transpose(ps["p2"].ap()[:, 0:nrk], Ef[0:nrk, 0:128], idf[0:nrk, 0:nrk]))
    ch.step("vector", lambda e, nrk=nrk: e.tensor_copy(GtT[:, 0:nrk], ps["p2"].ap()[:, 0:nrk]))
    negM = G[0:64, 481:482]
    Sv = G[0:64, 482:483]
    rS = G[0:64, 483:484]
    Fv = G[0:64, 484:485]
    ch.step("vector", lambda e, nrk=nrk: e.reduce_max(negM, GtT[0:64, 0:nrk], AX.X, negate=True))
    wv_ = Dd[0:64, 1300:1300 + 8]
    ch.step("scalar", lambda e, nrk=nrk: e.activation(wv_[:, 0:nrk], GtT[0:64, 0:nrk], AF.Exp, bias=negM))
    pv_ = Dd[0:64, 1320:1320 + 8]
    scop = Dd[0:64, 1340:1340 + 8]
    ch.step("vector", lambda e, nrk=nrk: e.tensor_copy(scop[:, 0:nrk], GtT[64:128, 0:nrk]))
    ch.step("vector", lambda e, nrk=nrk: e.tensor_mul(pv_[:, 0:nrk], wv_[:, 0:nrk], scop[:, 0:nrk]))
    ch.step("vector", lambda e, nrk=nrk: e.reduce_sum(Sv, pv_[:, 0:nrk], AX.X))
    ch.step("vector", lambda e: e.reciprocal(rS, Sv))
    ch.step("scalar", lambda e: e.activation(Fv, mloc, AF.Exp, bias=negM))
    ch.step("vector", lambda e: e.tensor_mul(Fv, Fv, rS))
    ch.step("tensor", lambda e: e.transpose(ps["p3"].ap()[0:1, 0:64], Fv, idf[0:64, 0:64]))
    frow = Ef[0:1, 1800:1864]
    ch.step("vector", lambda e: e.tensor_copy(frow, ps["p3"].ap()[0:1, 0:64]))
    ch.step("tensor", lambda e: e.matmul(ps["p4"].ap()[:, 0:64], onesf, frow, start=True, stop=True))
    Fb = Dd[:, 2112:2176]
    ch.step("vector", lambda e: e.tensor_copy(Fb, ps["p4"].ap()[:, 0:64]))

    for jt in range(njt):
        j0 = jt * 128
        elj = Dd[:, 1152 + jt * 64: 1216 + jt * 64]
        pfin = Dd[:, 1152 + jt * 64: 1216 + jt * 64]
        ch.step("vector", lambda e, elj=elj: e.tensor_mul(pfin, elj, Fb))
        V1, V2 = F[:, 2048:3072], F[:, 3072:4096]
        M1, M2 = X[:, 0:512], X[:, 512:1024]
        ch.dma_group("sync", [
            lambda e, j0=j0: e.dma_start(out=V1, in_=v1d[j0:j0 + 128, :]),
            lambda e, j0=j0: e.dma_start(out=V2, in_=v2d[j0:j0 + 128, :]),
            lambda e, j0=j0: e.dma_start(out=M1, in_=m1d[j0:j0 + 128, :]),
            lambda e, j0=j0: e.dma_start(out=M2, in_=m2d[j0:j0 + 128, :])])
        av1, av2 = C[:, 0:1024], C[:, 1024:2048]
        tmpb = C[:, 2048:3072]
        for i_, avd in ((0, av1), (1, av2)):
            ii, avd_ = i_, avd
            p1s = pfin[:, (ii * 2 + 0) * 16:(ii * 2 + 0) * 16 + 16]
            p2s = pfin[:, (ii * 2 + 1) * 16:(ii * 2 + 1) * 16 + 16]
            ch.step("vector", lambda e, avd_=avd_, p1s=p1s: e.tensor_tensor(
                avd_.rearrange("p (h e) -> p h e", h=16),
                V1.rearrange("p (h e) -> p h e", h=16),
                p1s[:, :, None].broadcast_to([128, 16, 64]), ALU.mult))
            ch.step("vector", lambda e, p2s=p2s: e.tensor_tensor(
                tmpb.rearrange("p (h e) -> p h e", h=16),
                V2.rearrange("p (h e) -> p h e", h=16),
                p2s[:, :, None].broadcast_to([128, 16, 64]), ALU.mult))
            ch.step("vector", lambda e, avd_=avd_: e.tensor_add(avd_, avd_, tmpb))
        sut = C[:, 3072:4096]
        for b in range(32):
            bb = b
            m1s = M1[:, bb * 16:(bb + 1) * 16]
            m2s = M2[:, bb * 16:(bb + 1) * 16]
            ch.step("vector", lambda e, m1s=m1s: e.tensor_tensor(
                sut.rearrange("p (h e) -> p h e", h=16),
                av1.rearrange("p (h e) -> p h e", h=16),
                m1s[:, :, None].broadcast_to([128, 16, 64]), ALU.mult))
            ch.step("vector", lambda e, m2s=m2s: e.tensor_tensor(
                tmpb.rearrange("p (h e) -> p h e", h=16),
                av2.rearrange("p (h e) -> p h e", h=16),
                m2s[:, :, None].broadcast_to([128, 16, 64]), ALU.mult))
            ch.step("vector", lambda e: e.tensor_add(sut, sut, tmpb))
            ch.step("gpsimd", lambda e, bb=bb, j0=j0: e.dma_start(
                out=su_d[bb * jl + j0: bb * jl + j0 + 128, :], in_=sut), 16)

    gb = Ef[:, 0:1024]
    bb_ = Ef[:, 1024:2048]
    bob = Dd[:, 2048:3072]
    for half in range(2):
        hh = half
        for src_row, dstb in ((lng, gb), (lnb, bb_), (boh, bob)):
            sr, db = src_row, dstb
            ch.step("sync", lambda e, sr=sr, hh=hh: e.dma_start(
                out=Ef[0:1, 2048 - 512:2048].bitcast(F32) if False else G[0:1, 0:512],
                in_=sr[None, hh * 512:(hh + 1) * 512]), 16)
            ch.step("tensor", lambda e: e.matmul(ps["p2"].ap()[:, 0:512], onesf,
                                                 G[0:1, 0:512], start=True, stop=True))
            ch.step("vector", lambda e, db=db, hh=hh: e.tensor_copy(
                db[:, hh * 512:(hh + 1) * 512], ps["p2"].ap()[:, 0:512]))
    ch7g = [Chain(ctx, en, eng, "p7gA"), Chain(ctx, en, eng, "p7gB")]
    for c_ in ch7g:
        c_.join([ch])
    Cf32 = C.bitcast(F32)
    Xf32b = X.bitcast(F32)
    for ct in range(nct):
        b, jt = ct // njt, ct % njt
        sl = ct % 2
        cc_ = ch7g[sl]
        j0 = jt * 128
        r0 = ct * 128
        cntt = G[:, sl * 32:sl * 32 + 32]
        suf = X[:, sl * 1024:sl * 1024 + 1024].rearrange("p (c t) -> p c t", c=8)
        cc_.dma_group("sync", [lambda e, j0=j0, cntt=cntt: e.dma_start(out=cntt, in_=cnt_d[j0:j0 + 128, :])] + [
            (lambda e, cc=c, r0=r0, suf=suf: e.dma_start_transpose(
                out=suf[:, cc, :], in_=su_d[r0:r0 + 128, cc * 128:(cc + 1) * 128]))
            for c in range(DCH)])
        y = Dd[:, 0:1024] if sl == 0 else Xf32b[:, 1024:2048]
        scr = Dd[:, 1024:2048] if sl == 0 else Cf32[:, 0:1024]
        for og in range(2):
            oo = og
            pp = (ps[f"p{sl*2}"] if og == 0 else ps[f"p{sl*2+1}"]).ap()

            def mmo(e, oo=oo, pp=pp, suf=suf):
                for dc in range(DCH):
                    ins = e.matmul(pp[:], suf[:, dc, :], WoS[:, dc, oo * 512:(oo + 1) * 512],
                                   start=(dc == 0), stop=(dc == DCH - 1))
                return ins
            cc_.step("tensor", mmo)
            ccol = cntt[:, b:b + 1]
            cc_.step("vector", lambda e, oo=oo, pp=pp, ccol=ccol, y=y: e.scalar_tensor_tensor(
                y[:, oo * 512:(oo + 1) * 512], bob[:, oo * 512:(oo + 1) * 512],
                ccol, pp[:], ALU.mult, ALU.add))
        cc_.step("sync", lambda e, r0=r0, scr=scr: e.dma_start(out=scr, in_=t32[r0:r0 + 128, :]), 16)
        cc_.step("vector", lambda e, y=y, scr=scr: e.tensor_add(y, y, scr))
        st = G[:, 100 + sl * 16:116 + sl * 16]
        cc_.step("scalar", lambda e, scr=scr, y=y: e.activation(scr, y, AF.Square))
        s1, s2, mu, var, rec, rstd, nmr = (st[:, k:k + 1] for k in range(7))
        cc_.step("vector", lambda e, s1=s1, y=y: e.reduce_sum(s1, y, AX.X))
        cc_.step("vector", lambda e, s2=s2, scr=scr: e.reduce_sum(s2, scr, AX.X))
        cc_.step("vector", lambda e, mu=mu, s1=s1: e.tensor_scalar(mu, s1, 1.0 / D, None, ALU.mult))
        cc_.step("vector", lambda e, var=var, mu=mu: e.tensor_mul(var, mu, mu))
        cc_.step("vector", lambda e, var=var, s2=s2: e.tensor_scalar(var, s2, 1.0 / D, var, ALU.mult, ALU.subtract))
        cc_.step("vector", lambda e, var=var: e.tensor_scalar_add(var, var, EPS))
        cc_.step("vector", lambda e, rec=rec, var=var: e.reciprocal(rec, var))
        cc_.step("scalar", lambda e, rstd=rstd, rec=rec: e.activation(rstd, rec, AF.Sqrt))
        cc_.step("scalar", lambda e, nmr=nmr, mu=mu, rstd=rstd: e.activation(nmr, mu, AF.Copy, scale=rstd))
        cc_.step("vector", lambda e, scr=scr, y=y, rstd=rstd, nmr=nmr: e.tensor_scalar(
            scr, y, rstd, nmr, ALU.mult, ALU.subtract))
        cc_.step("vector", lambda e, scr=scr: e.tensor_mul(scr, scr, gb))
        cc_.step("vector", lambda e, scr=scr: e.tensor_add(scr, scr, bb_))
        cc_.step("gpsimd", lambda e, r0=r0, scr=scr: e.dma_start(out=ffn1ag[r0:r0 + 128, :], in_=scr), 16)
    ch.wait_all()
    for c_ in ch7a + ch7b + ch7d + ch7g:
        c_.wait_all()


def p7h_ag2(ctx, en, eng, cc_group):
    if en == "gpsimd":
        ctx.inc(eng.collective_compute(
            "AllGather", ALU.bypass, replica_groups=[cc_group],
            ins=[ctx.t("ffn1ag").opt()], outs=[ctx.t("ffn1full").opt()]), "cc3", 1)
        ctx.wait(eng, "cc3", ctx.count("cc3"))


def p8_final(ctx, en, eng, ncc):
    sb, ps, cs = ctx.sb, ctx.ps, ctx.consts
    C, Dd, Ef, I, X = (sb[k].ap() for k in ("C", "D", "E", "I", "X"))
    Xf32 = X.bitcast(F32)
    Cf32 = C.bitcast(F32)
    ch0 = Chain(ctx, en, eng, "p8A")
    ch1 = Chain(ctx, en, eng, "p8B")
    chs = [ch0, ch1]
    x1ag, attn1f, ffn1full = ctx.t("x1ag"), ctx.t("accf"), ctx.t("ffn1full")
    ffn1loc = ctx.t("ffn1loc")
    out = ctx.t("out").rearrange("b s d -> (b s) d")
    jl = S // ncc
    cells_ = 32 * jl
    idf = I[:, 0:128]
    r = [None]

    def R(e):
        if r[0] is None:
            r[0] = rank0(ctx, e)
        return r[0]
    ch0.step("sync", lambda e: e.dma_start(out=idf, in_=cs["ident_f"]), 16)
    for b_ in range(B_LOC):
        bb = b_
        src3 = ffn1full.rearrange("(r2 c) d -> r2 c d", c=cells_)[:, bb * jl:(bb + 1) * jl, :]
        ch0.step("gpsimd", lambda e, bb=bb, src3=src3: e.dma_start(
            out=ffn1loc.rearrange("(b s) d -> b s d", b=B_LOC)[bb].rearrange(
                "(r2 j) d -> r2 j d", j=jl),
            in_=dyn(src3, R(e) * (4 * jl * D))), 16)
    ch1.join([ch0])
    for i in range(NT):
        sl = i % 2
        cc_ = chs[sl]
        r0 = i * 128
        x1t = Dd[:, 0:1024] if sl == 0 else Dd[:, 1024:2048]
        ft = Dd[:, 2048:3072] if sl == 0 else Cf32[:, 0:1024]
        chunks = Ef[:, 0:1024] if sl == 0 else Ef[:, 1024:2048]
        cc_.dma_group("sync", [
            lambda e, x1t=x1t, r0=r0: e.dma_start(out=x1t, in_=x1ag[r0:r0 + 128, :]),
            lambda e, ft=ft, r0=r0: e.dma_start(out=ft, in_=ffn1loc[r0:r0 + 128, :])] + [
            (lambda e, d_=dc, r0=r0, chunks=chunks: e.dma_start(
                out=chunks[:, d_ * 128:(d_ + 1) * 128],
                in_=attn1f[d_ * 128:(d_ + 1) * 128, r0:r0 + 128]))
            for dc in range(DCH)])
        cc_.step("vector", lambda e, x1t=x1t, ft=ft: e.tensor_add(x1t, x1t, ft))
        for dc in range(DCH):
            d_ = dc
            pb2 = (ps[f"p{2 + sl*2}"] if dc % 2 == 0 else ps[f"p{3 + sl*2}"]).ap()
            cc_.step("tensor", lambda e, d_=d_, pb2=pb2, chunks=chunks: e.transpose(
                pb2[:, 0:128], chunks[:, d_ * 128:(d_ + 1) * 128], idf))
            cc_.step("vector", lambda e, d_=d_, pb2=pb2, x1t=x1t: e.tensor_add(
                x1t[:, d_ * 128:(d_ + 1) * 128], x1t[:, d_ * 128:(d_ + 1) * 128],
                pb2[:, 0:128]))
        cc_.step("gpsimd", lambda e, r0=r0, x1t=x1t: e.dma_start(out=out[r0:r0 + 128, :], in_=x1t), 16)
    for c_ in chs:
        c_.wait_all()



CELLS8 = 32 * JL

SHAPES = {
    "x_in": ([B_LOC, S, D], F32),
    "w0q": ([D, D], F32), "w0k": ([D, D], F32), "w0v": ([D, D], F32), "w0o": ([D, D], F32),
    "w_gate": ([8, D], F32),
    "w_moe1": ([8, D, FF], F32), "w_moe2": ([8, FF, D], F32),
    "w_sw1": ([D, FF], F32), "w_sw2": ([FF, D], F32),
    "b_moe1": ([8, FF], F32), "b_moe2": ([8, D], F32),
    "b_ssb1": ([FF], F32), "b_ssb2": ([D], F32),
    "w5q": ([D, D], F32), "w5k": ([D, D], F32), "w5v": ([D, D], F32), "w5o": ([D, D], F32),
    "b5q": ([D], F32), "b5k": ([D], F32), "b5v": ([D], F32), "b5o": ([D], F32),
    "w1_wg": ([D, 16], F32), "w1_wq": ([16, D, 64], F32), "w1_wkv": ([16, D, 128], F32),
    "w1_wo": ([16, 64, D], F32),
    "b1_bq": ([16, 64], F32), "b1_bkv": ([16, 128], F32), "b1_bo": ([D], F32),
    "b1_lng": ([D], F32), "b1_lnb": ([D], F32),
    "xn0": ([T, D], BF16), "xn0lo": ([T, D], BF16),
    "comb": ([8, T], BF16), "b1b": ([9, FF], BF16),
    "accf": ([D, T], F32),
    "b5rows": ([4, D], BF16),
    "x1ag": ([T, D], F32), "xn1": ([T, D], BF16),
    "x1full": ([NCORES * T, D], F32),
    "x1seq": ([CELLS8, D], F32),
    "xn1s": ([CELLS8, D], BF16), "xn1slo": ([CELLS8, D], BF16), "t32": ([CELLS8, D], F32),
    "probs": ([CELLS8, 16], F32),
    "qall": ([CELLS8, 1024], BF16), "kvall": ([CELLS8, 2048], BF16), "su": ([CELLS8, 1024], BF16),
    "v1d": ([JL, 1024], BF16), "v2d": ([JL, 1024], BF16),
    "m1d": ([JL, 512], BF16), "m2d": ([JL, 512], BF16), "cntd": ([JL, 32], F32),
    "agsm_in": ([1, 128], F32), "agsm_out": ([NCORES, 128], F32),
    "ffn1ag": ([CELLS8, D], F32), "ffn1full": ([NCORES * CELLS8, D], F32),
    "ffn1loc": ([T, D], F32),
    "out": ([B_LOC, S, D], F32),
}

WEIGHT_MAP = {
    "w0q": "L0_attn_Wq", "w0k": "L0_attn_Wk", "w0v": "L0_attn_Wv", "w0o": "L0_attn_Wo",
    "w_gate": "L0_ffn_gate_w",
    "w_moe1": "L0_ffn_w1", "w_moe2": "L0_ffn_w2",
    "w_sw1": "L0_ffn_sw1", "w_sw2": "L0_ffn_sw2",
    "b_moe1": "L0_ffn_b1", "b_moe2": "L0_ffn_b2",
    "b_ssb1": "L0_ffn_sb1", "b_ssb2": "L0_ffn_sb2",
    "w5q": "L1_attn_Wq", "w5k": "L1_attn_Wk", "w5v": "L1_attn_Wv", "w5o": "L1_attn_Wo",
    "b5q": "L1_attn_bq", "b5k": "L1_attn_bk", "b5v": "L1_attn_bv", "b5o": "L1_attn_bo",
    "w1_wg": "L1_ffn_Wg", "w1_wq": "L1_ffn_Wq", "w1_wkv": "L1_ffn_Wkv", "w1_wo": "L1_ffn_Wo",
    "b1_bq": "L1_ffn_bq", "b1_bkv": "L1_ffn_bkv", "b1_bo": "L1_ffn_bo",
    "b1_lng": "L1_ffn_ln_g", "b1_lnb": "L1_ffn_ln_b",
}


def ph_p1(ctx, en, eng):
    attn_phase(ctx, en, eng, "p1", ctx.t("xn0"), ctx.t("w0q"), ctx.t("w0k"),
               ctx.t("w0v"), ctx.t("w0o"), None, None, None, None,
               ctx.t("accf"), None)


def ph_p5(ctx, en, eng):
    attn_phase(ctx, en, eng, "p5", ctx.t("xn1"), ctx.t("w5q"), ctx.t("w5k"),
               ctx.t("w5v"), ctx.t("w5o"), ctx.t("b5q"), ctx.t("b5k"),
               ctx.t("b5v"), ctx.t("b5o"), ctx.t("accf"), WIN)


def ph_p7(ctx, en, eng):
    p7_ffn1(ctx, en, eng, NCORES, list(range(NCORES)))


def ph_p8(ctx, en, eng):
    p8_final(ctx, en, eng, NCORES)


def mk_io(ins, outs, shared=()):
    io = {}
    for n in ins:
        io[n] = ("ExternalInput", *SHAPES[n])
    for n in outs:
        io[n] = ("ExternalOutput", *SHAPES[n])
    for n, (shape, dt) in SHAPES.items():
        if n not in io:
            if n in shared:
                io[n] = ("Internal", shape, dt, "Shared")
            else:
                io[n] = ("Internal", shape, dt)
    return io


L0_W = ["w0q", "w0k", "w0v", "w0o", "w_gate", "w_moe1", "w_moe2", "w_sw1",
        "w_sw2", "b_moe1", "b_moe2", "b_ssb1", "b_ssb2"]
L5_W = ["w5q", "w5k", "w5v", "w5o", "b5q", "b5k", "b5v", "b5o"]
L7_W = ["w1_wg", "w1_wq", "w1_wkv", "w1_wo", "b1_bq", "b1_bkv", "b1_bo",
        "b1_lng", "b1_lnb"]

FUSED = True
_nc_cache = {}


def _get_nc(key, phases, io):
    if key not in _nc_cache:
        _nc_cache[key] = build_program(phases, io)
    return _nc_cache[key]


def kernel(**inputs):
    x = np.ascontiguousarray(np.asarray(inputs["L0_attn_x"], dtype=np.float32))
    w = {k: np.ascontiguousarray(np.asarray(inputs[v], dtype=np.float32))
         for k, v in WEIGHT_MAP.items()}
    xs = [x[4 * c:4 * c + 4] for c in range(NCORES)]

    if FUSED:
        phases = [p0_ln0, p2_gate, p3z_zero_acc, ph_p1, p3_moe, p4_ln1,
                  lambda c, en, e: p6_ag1(c, en, e, list(range(NCORES))),
                  ph_p5, ph_p7,
                  lambda c, en, e: p7h_ag2(c, en, e, list(range(NCORES))),
                  ph_p8]
        io = mk_io(["x_in"] + L0_W + L5_W + L7_W, ["out"],
                   shared=("x1full", "ffn1full", "agsm_out"))
        nc = _get_nc("fused", phases, io)
        in_maps = [{"x_in": xs[c], **w} for c in range(NCORES)]
        res = run_bass_kernel_spmd(nc, in_maps, core_ids=list(range(NCORES)))
        return np.concatenate([r["out"] for r in res.results], axis=0)

    ioA = mk_io(["x_in"] + L0_W, ["x1ag", "xn1"])
    ncA = _get_nc("A", [p0_ln0, p2_gate, p3z_zero_acc, ph_p1, p3_moe, p4_ln1], ioA)
    resA = run_bass_kernel_spmd(ncA, [{"x_in": xs[c], **{k: w[k] for k in L0_W}}
                                      for c in range(NCORES)],
                                core_ids=list(range(NCORES)))
    x1s = [r["x1ag"] for r in resA.results]
    xn1s_ = [r["xn1"] for r in resA.results]

    ioB = mk_io(["xn1"] + L5_W, ["accf"])
    ncB = _get_nc("B", [ph_p5], ioB)
    resB = run_bass_kernel_spmd(ncB, [{"xn1": xn1s_[c], **{k: w[k] for k in L5_W}}
                                      for c in range(NCORES)],
                                core_ids=list(range(NCORES)))
    attn1s = [r["accf"] for r in resB.results]

    x1full = np.concatenate(x1s, axis=0)
    ioC = mk_io(["x1full"] + L7_W, ["ffn1ag"], shared=("agsm_out",))
    ncC = _get_nc("C", [ph_p7], ioC)
    resC = run_bass_kernel_spmd(ncC, [{"x1full": x1full, **{k: w[k] for k in L7_W}}
                                      for c in range(NCORES)],
                                core_ids=list(range(NCORES)))
    ffn1full = np.concatenate([r["ffn1ag"] for r in resC.results], axis=0)

    ioD = mk_io(["x1ag", "accf", "ffn1full"], ["out"])
    ncD = _get_nc("D", [ph_p8], ioD)
    resD = run_bass_kernel_spmd(ncD, [{"x1ag": x1s[c], "accf": attn1s[c],
                                       "ffn1full": ffn1full}
                                      for c in range(NCORES)],
                                core_ids=list(range(NCORES)))
    return np.concatenate([r["out"] for r in resD.results], axis=0)


# version 15 (final)
# speedup vs baseline: 1.2551x; 1.0127x over previous
import numpy as np
import ml_dtypes
from contextlib import ExitStack
import concourse.bass as bass
import concourse.mybir as mybir
from concourse.bass_utils import run_bass_kernel_spmd

F32 = mybir.dt.float32
BF16 = mybir.dt.bfloat16
AF = mybir.ActivationFunctionType
ALU = mybir.AluOpType
AX = mybir.AxisListType

NCORES = 8
B_LOC = 4
S = 2048
D = 1024
T = B_LOC * S
NT = T // 128
DCH = D // 128
NH = 16
DH = 64
WIN = 512
NEXP = 8
FF = 4096
NFC = FF // 128
MTT = 512
NMT = T // MTT
JL = S // NCORES
NJT = JL // 128
EPS = 1e-5
NEG = -1e9

ENGINES = ["sync", "gpsimd", "tensor", "vector", "scalar"]
BF = ml_dtypes.bfloat16



class MockIns:
    def then_inc(self, *a, **k):
        raise RuntimeError("use ctx.inc / Chain.step, not .then_inc")


class MockEng:
    def __init__(self, name):
        self.name = name

    def __getattr__(self, op):
        def f(*a, **k):
            return MockIns()
        return f


class Ctx:
    def __init__(self, nc):
        self.nc = nc
        self.mode = "mock"
        self.counts = {}
        self.marks = {}
        self.sems = {}
        self.tensors = {}
        self.consts = {}
        self.sb = {}
        self.ps = {}

    def t(self, name):
        return self.tensors[name].ap()

    def inc(self, ins, name, v=1):
        self.counts[name] = self.counts.get(name, 0) + v
        if self.mode == "real":
            ins.then_inc(self.sems[name], v)

    def wait(self, eng, name, val):
        if self.mode == "real":
            eng.wait_ge(self.sems[name], val)

    def count(self, name):
        return self.counts.get(name, 0)

    def mark(self, key):
        snap = dict(self.counts)
        if self.mode == "mock":
            if key not in self.marks:
                self.marks[key] = snap
        return self.marks[key]

    def at(self, key, sem):
        if key not in self.marks:
            if self.mode == "mock":
                return 0
            raise KeyError(f"mark {key} never recorded in mock pass")
        return self.marks[key].get(sem, 0)

    def bar(self, eng, k):
        self.counts["bar"] = self.counts.get("bar", 0) + 1
        if self.mode == "real":
            eng.drain().then_inc(self.sems["bar"], 1)
            eng.wait_ge(self.sems["bar"], 5 * (k + 1))


class Chain:

    def __init__(self, ctx, en, eng, key):
        self.ctx, self.en, self.eng = ctx, en, eng
        self.sem = "ch_" + key
        self.dsem = {q: f"chd{q[0]}_{key}" for q in ("sync", "gpsimd", "scalar")}
        ctx.counts.setdefault(self.sem, 0)
        for s in self.dsem.values():
            ctx.counts.setdefault(s, 0)
        self.n = 0
        self.d = {q: 0 for q in self.dsem}
        self.after = []

    def join(self, others):
        for o in others:
            if o is self:
                continue
            if o.n:
                self.after.append((o.sem, o.n))
            for q, v in o.d.items():
                if v:
                    self.after.append((o.dsem[q], v))

    def _wait_cur(self):
        if self.ctx.mode == "real":
            if self.n:
                self.ctx.wait(self.eng, self.sem, self.n)
            for q, v in self.d.items():
                if v:
                    self.ctx.wait(self.eng, self.dsem[q], v)
            for (sm, v) in self.after:
                self.ctx.wait(self.eng, sm, v)

    def step(self, owner, fn, dv=None):
        dma = dv is not None
        if self.en == owner:
            if self.ctx.mode == "real":
                self._wait_cur()
                ins = fn(self.eng)
                if dma:
                    ins.then_inc(self.ctx.sems[self.dsem[owner]], 16)
                else:
                    ins.then_inc(self.ctx.sems[self.sem], 1)
            self.ctx.counts[self.dsem[owner] if dma else self.sem] += 16 if dma else 1
        if dma:
            self.d[owner] += 16
        else:
            self.n += 1

    def dma_group(self, owner, fns):
        first = True
        for fn in fns:
            if self.en == owner:
                if self.ctx.mode == "real":
                    if first:
                        self._wait_cur()
                    ins = fn(self.eng)
                    ins.then_inc(self.ctx.sems[self.dsem[owner]], 16)
                self.ctx.counts[self.dsem[owner]] += 16
            self.d[owner] += 16
            first = False

    def wait_all(self):
        self._wait_cur()




def mk_ops(ctx, en, eng, key):
    sems = {"tensor": key + "pe", "vector": key + "ve", "scalar": key + "sc",
            "sync": key + "ld", "gpsimd": key + "gp", "sync2": key + "ld2",
            "gpsimd2": key + "gp2", "gpsimdw": key + "gpw", "gpsimdc": key + "gc"}
    engine_of = {"sync2": "sync", "gpsimd2": "gpsimd", "gpsimdw": "gpsimd",
                 "gpsimdc": "gpsimd"}
    for sm in sems.values():
        ctx.counts.setdefault(sm, 0)

    def op(owner, fn, mark=None, deps=()):
        act = engine_of.get(owner, owner)
        if en == act:
            if ctx.mode == "real":
                for (mk, sm) in deps:
                    ctx.wait(eng, sm, ctx.at(mk, sm))
            dv = 1 if act in ("tensor", "vector", "scalar") or owner == "gpsimdc" else 16
            if ctx.mode == "real":
                ins = fn(eng)
                ins.then_inc(ctx.sems[sems[owner]], dv)
            ctx.counts[sems[owner]] += dv
            if mark:
                ctx.mark(mark)
        return None

    def waitall(extra=()):
        if ctx.mode == "real":
            for sm in list(sems.values()) + list(extra):
                v = ctx.count(sm)
                if v:
                    ctx.wait(eng, sm, v)
    return op, sems, waitall


def host_consts():
    c = {}
    c["ident_b"] = np.eye(128, dtype=BF)
    c["ident_f"] = np.eye(128, dtype=np.float32)
    r = np.arange(128)
    c["mask_causal"] = np.where(r[None, :] <= r[:, None], 0.0, NEG).astype(np.float32)
    c["mask_far"] = np.where(r[None, :] >= r[:, None] + 1, 0.0, NEG).astype(np.float32)
    c["ones_row_b"] = np.ones((1, 512), dtype=BF)
    c["ones_row_f"] = np.ones((1, 128), dtype=np.float32)
    oh = np.zeros((8, 8 * 128), dtype=BF)
    for e in range(8):
        oh[e, e * 128:(e + 1) * 128] = 1
    c["onehot8"] = oh
    c["onehot32"] = None
    del c["onehot32"]
    return c


def build_program(phase_list, io_kinds):
    nc = bass.Bass("TRN2", target_bir_lowering=False)
    ctx = Ctx(nc)
    for name, spec in io_kinds.items():
        kind, shape, dtype = spec[:3]
        kw = {"addr_space": spec[3]} if len(spec) > 3 else {}
        ctx.tensors[name] = nc.dram_tensor(name, list(shape), dtype, kind=kind, **kw)
    for cname, arr in host_consts().items():
        ctx.consts[cname] = nc.inline_tensor(arr, cname).ap()

    with ExitStack() as stack:
        en_ = stack.enter_context
        sb = ctx.sb
        sb["A"] = en_(nc.sbuf_tensor("bufA", [128, 32768], BF16))
        sb["B"] = en_(nc.sbuf_tensor("bufB", [128, 32768], BF16))
        sb["C"] = en_(nc.sbuf_tensor("bufC", [128, 16384], BF16))
        sb["D"] = en_(nc.sbuf_tensor("bufD", [128, 3072], F32))
        sb["E"] = en_(nc.sbuf_tensor("bufE", [128, 2048], F32))
        sb["F"] = en_(nc.sbuf_tensor("bufF", [128, 4096], BF16))
        sb["G"] = en_(nc.sbuf_tensor("bufG", [128, 512], F32))
        sb["H"] = en_(nc.sbuf_tensor("bufH", [128, 1024], BF16))
        sb["I"] = en_(nc.sbuf_tensor("bufI", [128, 128], F32))
        sb["X"] = en_(nc.sbuf_tensor("bufX", [128, 4096], BF16))
        ps = ctx.ps
        for i in range(6):
            ps[f"p{i}"] = en_(nc.psum_tensor(f"p{i}", [128, 512], F32))
        ps["tb0"] = en_(nc.psum_tensor("tb0", [128, 512], BF16))
        ps["tb1"] = en_(nc.psum_tensor("tb1", [128, 512], BF16))

        ctx.mode = "mock"
        for en in ENGINES:
            eng = MockEng(en)
            for k, ph in enumerate(phase_list):
                ph(ctx, en, eng)
                ctx.bar(eng, k)

        for name in sorted(ctx.counts):
            ctx.sems[name] = en_(nc.semaphore(name))
        ctx.mode = "real"
        ctx.counts = {}
        block = en_(nc.Block())

        def mk(en):
            def f(eng):
                for k, ph in enumerate(phase_list):
                    ph(ctx, en, eng)
                    ctx.bar(eng, k)
            return f
        block.sync(mk("sync"))
        block.gpsimd(mk("gpsimd"))
        block.tensor(mk("tensor"))
        block.vector(mk("vector"))
        block.scalar(mk("scalar"))
    return nc



def ln_stats_norm(ch, xt, xsq, st, xnf):
    s1, s2, mu, var, rec, rstd, nmr = (st[:, k:k + 1] for k in range(7))
    ch.step("scalar", lambda e: e.activation(xsq, xt, AF.Square))
    ch.step("vector", lambda e: e.reduce_sum(s1, xt, AX.X))
    ch.step("vector", lambda e: e.reduce_sum(s2, xsq, AX.X))
    ch.step("vector", lambda e: e.tensor_scalar(mu, s1, 1.0 / D, None, ALU.mult))
    ch.step("vector", lambda e: e.tensor_mul(var, mu, mu))
    ch.step("vector", lambda e: e.tensor_scalar(var, s2, 1.0 / D, var, ALU.mult, ALU.subtract))
    ch.step("vector", lambda e: e.tensor_scalar_add(var, var, EPS))
    ch.step("vector", lambda e: e.reciprocal(rec, var))
    ch.step("scalar", lambda e: e.activation(rstd, rec, AF.Sqrt))
    ch.step("scalar", lambda e: e.activation(nmr, mu, AF.Copy, scale=rstd))
    ch.step("vector", lambda e: e.tensor_scalar(xnf, xt, rstd, nmr, ALU.mult, ALU.subtract))


def ln_tiles(ctx, en, eng, key, src_ap, n_tiles, outs):
    sb = ctx.sb
    Dd, Ef, F, G = (sb[k].ap() for k in ("D", "E", "F", "G"))
    X = sb["X"].ap()
    Xf32 = X.bitcast(F32)
    chs = [Chain(ctx, en, eng, key + "A"), Chain(ctx, en, eng, key + "B")]
    for i in range(n_tiles):
        sl = i % 2
        ch = chs[sl]
        xt = Dd[:, 0:1024] if sl == 0 else Dd[:, 1024:2048]
        xsq = Ef[:, 0:1024] if sl == 0 else Ef[:, 1024:2048]
        xnf = Dd[:, 2048:3072] if sl == 0 else Xf32[:, 0:1024]
        xnb = F[:, 0:1024] if sl == 0 else F[:, 2048:3072]
        xnl = F[:, 1024:2048] if sl == 0 else F[:, 3072:4096]
        st = G[:, sl * 16:sl * 16 + 16]
        r0 = i * 128
        ch.step("sync", lambda e, xt=xt, r0=r0: e.dma_start(out=xt, in_=src_ap[r0:r0 + 128, :]), 16)
        ln_stats_norm(ch, xt, xsq, st, xnf)
        fns = []
        if "bf" in outs or "lo" in outs:
            ch.step("vector", lambda e, xnb=xnb, xnf=xnf: e.tensor_copy(xnb, xnf))
            fns.append(lambda e, r0=r0, xnb=xnb: e.dma_start(out=outs["bf"][r0:r0 + 128, :], in_=xnb))
        if "lo" in outs:
            ch.step("vector", lambda e, xnl=xnl, xnf=xnf, xnb=xnb: e.tensor_sub(xnl, xnf, xnb))
            fns.append(lambda e, r0=r0, xnl=xnl: e.dma_start(out=outs["lo"][r0:r0 + 128, :], in_=xnl))
        if "f32" in outs:
            fns.append(lambda e, r0=r0, xnf=xnf: e.dma_start(out=outs["f32"][r0:r0 + 128, :], in_=xnf))
        if "x32" in outs:
            fns.append(lambda e, r0=r0, xt=xt: e.dma_start(out=outs["x32"][r0:r0 + 128, :], in_=xt))
        ch.dma_group("gpsimd", fns)
    for c_ in chs:
        c_.wait_all()


def p0_ln0(ctx, en, eng):
    x = ctx.t("x_in").rearrange("b s d -> (b s) d")
    ln_tiles(ctx, en, eng, "p0", x, NT,
             {"bf": ctx.t("xn0"), "lo": ctx.t("xn0lo")})


def p2_gate(ctx, en, eng):
    sb, ps, cs = ctx.sb, ctx.ps, ctx.consts
    ch = Chain(ctx, en, eng, "p2")
    xn0, xn0lo, comb = ctx.t("xn0"), ctx.t("xn0lo"), ctx.t("comb")
    gw = ctx.t("w_gate")
    E = sb["E"].ap()
    gw_f = E[0:8, 0:1024]
    gwT_f = E[:, 1024:1024 + 64]
    gwT_hi = sb["H"].ap()[:, 0:64]
    gwT_lo = sb["H"].ap()[:, 64:128]
    idf = sb["I"].ap()[:, 0:128]
    idb = sb["H"].ap()[:, 1024 - 128:1024]

    ch.step("sync", lambda e: e.dma_start(out=gw_f, in_=gw), 16)
    ch.step("sync", lambda e: e.dma_start(out=idf, in_=cs["ident_f"]), 16)
    ch.step("sync", lambda e: e.dma_start(out=idb, in_=cs["ident_b"]), 16)
    for c in range(DCH):
        cc = c
        ch.step("tensor", lambda e: e.transpose(
            ps["p0"].ap()[:, 0:8], gw_f[0:8, cc * 128:(cc + 1) * 128], idf[0:8, 0:8]))
        ch.step("vector", lambda e: e.tensor_copy(gwT_f[:, cc * 8:(cc + 1) * 8], ps["p0"].ap()[:, 0:8]))
    ch.step("vector", lambda e: e.tensor_copy(gwT_hi, gwT_f))
    ch.step("vector", lambda e: e.tensor_sub(gwT_lo, gwT_f, gwT_hi))

    A = sb["A"].ap()
    ch2 = [ch, Chain(ctx, en, eng, "p2B")]
    ch2[1].join([ch])
    for i in range(T // 512):
        sl = i % 2
        cc2 = ch2[sl]
        t0 = i * 512
        base = sl * 8192
        xhi = A[:, base:base + 8 * 512].rearrange("p (c t) -> p c t", c=8)
        xlo = A[:, base + 8 * 512:base + 16 * 512].rearrange("p (c t) -> p c t", c=8)
        fns = []
        for c in range(DCH):
            fns.append(lambda e, cc=c, xhi=xhi, t0=t0: e.dma_start_transpose(
                out=xhi[:, cc, :], in_=xn0[t0:t0 + 512, cc * 128:(cc + 1) * 128]))
            fns.append(lambda e, cc=c, xlo=xlo, t0=t0: e.dma_start_transpose(
                out=xlo[:, cc, :], in_=xn0lo[t0:t0 + 512, cc * 128:(cc + 1) * 128]))
        cc2.dma_group("sync", fns)
        for q in range(4):
            qq = q
            L = (ps["p0"] if sl == 0 else ps["p1"]).ap()[:, 0:8]

            def mm_logits(e, q=qq, xhi=xhi, xlo=xlo, L=L):
                first = True
                for c in range(DCH):
                    for lhs, rhs in ((xhi, gwT_hi), (xhi, gwT_lo), (xlo, gwT_hi)):
                        ins = e.matmul(L, lhs[:, c, q * 128:(q + 1) * 128],
                                       rhs[:, c * 8:(c + 1) * 8],
                                       start=first, stop=(c == DCH - 1 and rhs is gwT_hi and lhs is xlo))
                        first = False
                return ins
            cc2.step("tensor", mm_logits)
            G = sb["G"].ap()
            gof = 32 + sl * 48
            Lf = G[:, gof:gof + 8]
            nm1 = G[:, gof + 8:gof + 9]
            nm2 = G[:, gof + 9:gof + 10]
            eq = G[:, gof + 10:gof + 18]
            l2 = G[:, gof + 18:gof + 26]
            w = G[:, gof + 26:gof + 34]
            ssum = G[:, gof + 34:gof + 35]
            rs = G[:, gof + 35:gof + 36]
            combf = G[:, gof + 36:gof + 44]
            combb = sb["F"].ap()[0:128, 2048 + sl * 256:2056 + sl * 256]
            cstg = sb["F"].ap()[0:8, 2056 + sl * 256:2184 + sl * 256]
            tbp = (ps["tb0"] if sl == 0 else ps["tb1"]).ap()
            cc2.step("vector", lambda e, Lf=Lf, L=L: e.tensor_copy(Lf, L))
            cc2.step("vector", lambda e, Lf=Lf, nm1=nm1: e.reduce_max(nm1, Lf, AX.X, negate=True))
            cc2.step("vector", lambda e, eq=eq, Lf=Lf, nm1=nm1: e.tensor_scalar(eq, Lf, nm1, 0.0, ALU.add, ALU.is_equal))
            cc2.step("vector", lambda e, l2=l2, eq=eq, Lf=Lf: e.scalar_tensor_tensor(l2, eq, NEG, Lf, ALU.mult, ALU.add))
            cc2.step("vector", lambda e, nm2=nm2, l2=l2: e.reduce_max(nm2, l2, AX.X, negate=True))
            cc2.step("vector", lambda e, eq=eq, Lf=Lf, nm2=nm2: e.tensor_scalar(eq, Lf, nm2, 0.0, ALU.add, ALU.is_ge))
            cc2.step("scalar", lambda e, l2=l2, Lf=Lf, nm1=nm1: e.activation(l2, Lf, AF.Exp, bias=nm1))
            cc2.step("vector", lambda e, w=w, l2=l2, eq=eq: e.tensor_mul(w, l2, eq))
            cc2.step("vector", lambda e, ssum=ssum, w=w: e.reduce_sum(ssum, w, AX.X))
            cc2.step("vector", lambda e, rs=rs, ssum=ssum: e.reciprocal(rs, ssum))
            cc2.step("vector", lambda e, combf=combf, w=w, rs=rs: e.tensor_scalar(combf, w, rs, None, ALU.mult))
            cc2.step("vector", lambda e, combb=combb, combf=combf: e.tensor_copy(combb, combf))
            cc2.step("tensor", lambda e, tbp=tbp, combb=combb: e.transpose(tbp[0:8, 0:128], combb, idb))
            cc2.step("vector", lambda e, cstg=cstg, tbp=tbp: e.tensor_copy(cstg, tbp[0:8, 0:128]))
            cc2.step("gpsimd", lambda e, t0=t0, qq=qq, cstg=cstg: e.dma_start(
                out=comb[0:8, t0 + qq * 128:t0 + (qq + 1) * 128], in_=cstg), 16)
    for c_ in ch2:
        c_.wait_all()


def load_wT(ch, ctx, w_dram, dest, stage, idb, tbp):
    ch.step("gpsimd", lambda e: e.dma_start(
        out=stage, in_=w_dram.rearrange("(oc p) d -> p oc d", p=128)), 16)
    for oc in range(8):
        for dc in range(8):
            o_, d_ = oc, dc
            ch.step("tensor", lambda e: e.transpose(
                tbp[:, 0:128], stage[:, o_, d_ * 128:(d_ + 1) * 128], idb))
            ch.step("vector", lambda e: e.tensor_copy(
                dest[:, d_, o_ * 128:(o_ + 1) * 128], tbp[:, 0:128]))


def attn_phase(ctx, en, eng, key, xn_dram, wq, wk, wv, wo, bq, bk, bv, bo,
               out_dram, window, n_b=B_LOC, n_h=NH):
    sb, ps, cs = ctx.sb, ctx.ps, ctx.consts
    A, B, C, Dd, Ef, F, G, H, I, X = (sb[k].ap() for k in
                                      ("A", "B", "C", "D", "E", "F", "G", "H", "I", "X"))
    op, sems, waitall = mk_ops(ctx, en, eng, key)
    PE, VE, SC, LD, GP = (sems[k] for k in ("tensor", "vector", "scalar", "sync", "gpsimd"))
    Qf = A[:, 0:16384].rearrange("p (c t) -> p c t", c=8)
    Kf = A[:, 16384:32768].rearrange("p (c t) -> p c t", c=8)
    Vt = B[:, 0:16384].rearrange("p (kt o) -> p kt o", kt=16)
    Cx = B[:, 16384:32768].rearrange("p (c t) -> p c t", c=8)
    Xt = X[:, 0:4096].rearrange("p (c t) -> p c t", c=8)
    wT0 = C[:, 0:8192].rearrange("p (c o) -> p c o", c=8)
    wT1 = C[:, 8192:16384].rearrange("p (c o) -> p c o", c=8)
    stage0 = B[:, 0:8192].rearrange("p (oc d) -> p oc d", oc=8)
    stageV = B[:, 16384:24576].rearrange("p (oc d) -> p oc d", oc=8)
    stageO = A[:, 0:8192].rearrange("p (oc d) -> p oc d", oc=8)
    idb = H[:, 896:1024]
    maskc = I[:, 0:128]
    maskf = Dd[:, 2560:2688]
    bcols = G[:, 456:488].rearrange("p (w c) -> p w c", w=4)
    qcol = G[:, 448:456]
    tb = [ps["tb0"].ap(), ps["tb1"].ap()]
    has_bias = bq is not None
    K_ = key
    nW = window // 128 if window is not None else None

    def kt_lo_of(qi):
        return 0 if nW is None else max(0, qi - nW)

    def nkt_of(gi):
        qi = gi % 16
        return qi + 1 - kt_lo_of(qi)

    def pt_eng(j):
        return VE if j % 2 == 0 else SC

    def last_pt_deps(b, gi):
        nk = nkt_of(gi)
        d = [(f"{K_}pt{b}_{gi}_{nk-1}", pt_eng(nk - 1))]
        if nk >= 2:
            d.append((f"{K_}pt{b}_{gi}_{nk-2}", pt_eng(nk - 2)))
        return d

    tbdep = [[], []]

    LD2, GP2, GPW = sems["sync2"], sems["gpsimd2"], sems["gpsimdw"]
    op("sync2", lambda e: e.dma_start(out=idb, in_=cs["ident_b"]))
    op("sync2", lambda e: e.dma_start(out=maskc, in_=cs["mask_causal"]))
    if window is not None:
        op("sync2", lambda e: e.dma_start(out=maskf, in_=cs["mask_far"]))
    if has_bias:
        brows = ctx.t("b5rows")
        for w_i, bt in enumerate((bq, bk, bv, bo)):
            wi, btt = w_i, bt
            op("gpsimd", lambda e, wi=wi, btt=btt: e.dma_start(
                out=brows[wi:wi + 1, :], in_=btt[None, :]), mark=f"{K_}brow{wi}")
        op("sync2", lambda e: e.dma_start_transpose(
            out=H[:, 512:544], in_=brows.rearrange("w (c p) -> (w c) p", p=128)),
            deps=[(f"{K_}brow3", GP)])
        op("vector", lambda e: e.tensor_copy(G[:, 456:488], H[:, 512:544]),
           deps=[(f"{K_}cst", sems["sync2"])], mark=f"{K_}bcf")
    op("sync2", lambda e: e.nop(), mark=f"{K_}cst") if False else None
    if en == "sync":
        ctx.mark(f"{K_}cst")
    if has_bias:
        op("vector", lambda e: e.tensor_scalar(qcol, bcols[:, 0, :], 0.125, None, ALU.mult),
           deps=[(f"{K_}bcf", sems["vector"])], mark=f"{K_}qcol")

    last_stage = [None]

    def load_w(b, wname, w_dram, dest, stage, stage_dep):
        mk = f"{K_}w{b}{wname}"
        sdep = list(stage_dep)
        if last_stage[0]:
            sdep.append((last_stage[0] + "st", GPW))
            sdep.append((last_stage[0] + "tr63", PE))
        op("gpsimdw", lambda e: e.dma_start(
            out=stage, in_=w_dram.rearrange("(oc p) d -> p oc d", p=128)),
            deps=sdep, mark=mk + "st")
        last_stage[0] = mk
        for i in range(64):
            o_, d_ = i // 8, i % 8
            ii = i
            if i >= 2:
                deps = [(mk + f"cp{ii-2}", VE)]
            else:
                deps = [(mk + "st", GPW), (f"{K_}cst", LD2)] + tbdep[i % 2]
            op("tensor", lambda e, o_=o_, d_=d_, ii=ii: e.transpose(
                tb[ii % 2][:, 0:128], stage[:, o_, d_ * 128:(d_ + 1) * 128], idb),
                deps=deps, mark=mk + f"tr{ii}")
            op("vector", lambda e, o_=o_, d_=d_, ii=ii: e.tensor_copy(
                dest[:, d_, o_ * 128:(o_ + 1) * 128], tb[ii % 2][:, 0:128]),
                deps=[(mk + f"tr{ii}", PE)], mark=mk + f"cp{ii}")
        tbdep[0] = [(mk + "cp62", VE)]
        tbdep[1] = [(mk + "cp63", VE)]
        return mk + "cp63"

    for b in range(n_b):
        tbse = b * S
        dep_prevav = [(f"{K_}av{b-1}_{n_h-1}_15", PE)] if b > 0 else []
        wq_done = load_w(b, "q", wq, wT0, stage0, dep_prevav)
        wk_done = load_w(b, "k", wk, wT1, stage0, [])

        for tt in range(4):
            t0 = tt * 512
            if tt > 0:
                xdeps = [(f"{K_}qkm{b}_{tt-1}_15", PE)]
            elif b > 0:
                xdeps = [(f"{K_}av{b-1}_{n_h-1}_15", PE)]
            else:
                xdeps = []
            for c in range(DCH):
                cc = c
                op("sync", lambda e, cc=cc, t0=t0: e.dma_start_transpose(
                    out=Xt[:, cc, :], in_=xn_dram[tbse + t0:tbse + t0 + 512,
                                                  cc * 128:(cc + 1) * 128]),
                    deps=xdeps if cc == 0 else [],
                    mark=f"{K_}qkx{b}_{tt}" if cc == DCH - 1 else None)
            for g in range(16):
                wh, o_ = (0, g) if g < 8 else (1, g - 8)
                wT = wT0 if wh == 0 else wT1
                pp = ps["p0"].ap() if g % 2 == 0 else ps["p1"].ap()
                gg = g
                deps = [(f"{K_}qkx{b}_{tt}", LD)]
                if g == 0 and tt == 0:
                    deps.append((wk_done, VE))
                    if b > 0:
                        deps.append((f"{K_}cx{b-1}_{n_h*16-1}", SC))
                if g >= 2:
                    deps.append((f"{K_}qke{b}_{tt}_{gg-2}", SC))
                elif tt > 0:
                    deps.append((f"{K_}qke{b}_{tt-1}_{gg+14}", SC))

                def mmqk(e, wT=wT, o_=o_, pp=pp):
                    for dc in range(DCH):
                        ins = e.matmul(pp[:], wT[:, dc, o_ * 128:(o_ + 1) * 128],
                                       Xt[:, dc, :], start=(dc == 0), stop=(dc == DCH - 1))
                    return ins
                op("tensor", mmqk, deps=deps, mark=f"{K_}qkm{b}_{tt}_{gg}")
                dst = Qf if wh == 0 else Kf
                sc_ = 0.125 if wh == 0 else 1.0
                if has_bias:
                    bias_ap = qcol[:, o_:o_ + 1] if wh == 0 else bcols[:, 1, o_:o_ + 1]
                else:
                    bias_ap = 0.0
                edeps = [(f"{K_}qkm{b}_{tt}_{gg}", PE)]
                if has_bias and b == 0 and tt == 0 and gg == 0:
                    edeps.append((f"{K_}qcol", VE))
                    edeps.append((f"{K_}cst", LD2))
                op("scalar", lambda e, dst=dst, o_=o_, t0=t0, pp=pp, sc_=sc_, bias_ap=bias_ap:
                    e.activation(dst[:, o_, t0:t0 + 512], pp[:], AF.Identity, bias=bias_ap, scale=sc_),
                    deps=edeps, mark=f"{K_}qke{b}_{tt}_{gg}")

        wv_done = load_w(b, "v", wv, wT0, stageV, [(f"{K_}qkm{b}_3_7", PE)])
        for tt in range(4):
            t0 = tt * 512
            xdeps = [(f"{K_}qkm{b}_3_15" if tt == 0 else f"{K_}vm{b}_{tt-1}_7", PE)]
            for c in range(DCH):
                cc = c
                op("sync", lambda e, cc=cc, t0=t0: e.dma_start_transpose(
                    out=Xt[:, cc, :], in_=xn_dram[tbse + t0:tbse + t0 + 512,
                                                  cc * 128:(cc + 1) * 128]),
                    deps=xdeps if cc == 0 else [],
                    mark=f"{K_}vx{b}_{tt}" if cc == DCH - 1 else None)
            for g in range(8):
                qq, og = g // 2, g % 2
                pp = ps["p0"].ap() if g % 2 == 0 else ps["p1"].ap()
                gg = g
                deps = [(f"{K_}vx{b}_{tt}", LD)]
                if g == 0 and tt == 0:
                    deps.append((wv_done, VE))
                if g < 2 and tt == 0:
                    deps.append((f"{K_}qke{b}_3_{14 + gg}", SC))
                if g >= 2:
                    deps.append((f"{K_}ve{b}_{tt}_{gg-2}", VE))
                elif tt > 0:
                    deps.append((f"{K_}ve{b}_{tt-1}_{gg+6}", VE))

                def mmv(e, qq=qq, og=og, pp=pp):
                    for dc in range(DCH):
                        ins = e.matmul(pp[:], Xt[:, dc, qq * 128:(qq + 1) * 128],
                                       wT0[:, dc, og * 512:(og + 1) * 512],
                                       start=(dc == 0), stop=(dc == DCH - 1))
                    return ins
                op("tensor", mmv, deps=deps, mark=f"{K_}vm{b}_{tt}_{gg}")
                kt = tt * 4 + qq
                op("vector", lambda e, kt=kt, og=og, pp=pp: e.tensor_copy(
                    Vt[:, kt, og * 512:(og + 1) * 512], pp[:]),
                    deps=[(f"{K_}vm{b}_{tt}_{gg}", PE)], mark=f"{K_}ve{b}_{tt}_{gg}")

        rowbs = [Dd[:, 0:2048], Ef[:, 0:2048], C[:, 8192:16384].bitcast(F32)]
        Pbs = [F[:, 0:2048], X[:, 0:2048], C[:, 0:2048]]
        Pts = [F[:, 2048:4096], X[:, 2048:4096], C[:, 2048:4096]]

        def row_env(gi):
            h, qi = gi // 16, gi % 16
            sl = gi % 3
            psA = ps[f"p{2*sl}"].ap()
            psB = ps[f"p{2*sl+1}"].ap()
            st8 = G[:, sl * 8:sl * 8 + 8]
            return dict(h=h, qi=qi, sl=sl, hc=h // 2, hr=(h % 2) * 64,
                        rowb=rowbs[sl], Pb=Pbs[sl], Pt=Pts[sl], psA=psA, psB=psB,
                        tbA=psA.bitcast(BF16), tbB=psB.bitcast(BF16),
                        pav=psA[0:64, 0:128],
                        negm=st8[:, 0:1], rsum=st8[:, 1:2], rinv=st8[:, 2:3],
                        kt_lo=kt_lo_of(qi), L0=kt_lo_of(qi) * 128,
                        L=(qi + 1) * 128 - kt_lo_of(qi) * 128)

        def emit_scores(gi):
            v = row_env(gi)
            h, qi, sl = v["h"], v["qi"], v["sl"]
            qs = Qf[v["hr"]:v["hr"] + 64, v["hc"], :]
            ks = Kf[v["hr"]:v["hr"] + 64, v["hc"], :]
            far = nW is not None and qi - v["kt_lo"] == nW
            L0, L = v["L0"], v["L"]
            pgi = gi - 3
            ng = (L + 511) // 512
            for kg in range(ng):
                k0 = L0 + kg * 512
                n = min(512, (qi + 1) * 128 - k0)
                pp = v["psA"] if kg % 2 == 0 else v["psB"]
                deps = []
                if kg >= 2:
                    deps.append((f"{K_}sev{b}_{gi}_{kg-2}", VE))
                elif pgi >= 0:
                    deps += last_pt_deps(b, pgi)
                    deps.append((f"{K_}cx{b}_{pgi}", SC))
                if gi == 0:
                    deps.append((f"{K_}qke{b}_3_15", SC))
                    deps.append((f"{K_}ve{b}_3_7", VE))
                    if b == 0:
                        deps.append((f"{K_}cst", LD2))
                if gi == 1 and b > 0:
                    deps.append((f"{K_}ops{b-1}_31", GP2))
                op("tensor", lambda e, k0=k0, n=n, pp=pp, qs=qs, ks=ks, qi=qi:
                    e.matmul(pp[:, 0:n], qs[:, qi * 128:(qi + 1) * 128],
                             ks[:, k0:k0 + n], start=True, stop=True),
                    deps=deps, mark=f"{K_}ss{b}_{gi}_{kg}")
                off0 = k0 - L0
                segs = [[0, n, None]]
                if kg == ng - 1:
                    segs = ([[0, n - 128, None]] if n > 128 else []) + [[n - 128, 128, "c"]]
                if far and kg == 0:
                    if segs[0][2] is None:
                        hn = segs[0][1]
                        rest = ([[128, hn - 128, None]] if hn > 128 else []) + segs[1:]
                    else:
                        rest = []
                    segs = [[0, 128, "f"]] + rest
                segs = [sg for sg in segs if sg[1] > 0]
                use_sc = False
                for si, (o, nn, mk) in enumerate(segs):
                    oo, nn_, mk_ = o, nn, mk
                    deps = [(f"{K_}ss{b}_{gi}_{kg}", PE)]
                    if kg == 0 and si == 0 and pgi >= 0:
                        deps.append((f"{K_}xp{b}_{pgi}", SC))
                    mkname = f"{K_}sev{b}_{gi}_{kg}" if si == len(segs) - 1 else None
                    rowb = v["rowb"]
                    if mk_ is None and use_sc:
                        op("scalar", lambda e, pp=pp, oo=oo, nn_=nn_, off0=off0, rowb=rowb:
                            e.copy(rowb[:, off0 + oo:off0 + oo + nn_], pp[:, oo:oo + nn_]),
                            deps=deps, mark=mkname)
                    elif mk_ is None:
                        op("vector", lambda e, pp=pp, oo=oo, nn_=nn_, off0=off0, rowb=rowb:
                            e.tensor_copy(rowb[:, off0 + oo:off0 + oo + nn_],
                                          pp[:, oo:oo + nn_]),
                            deps=deps, mark=mkname)
                    else:
                        msk = maskc if mk_ == "c" else maskf
                        op("vector", lambda e, pp=pp, oo=oo, nn_=nn_, off0=off0, msk=msk, rowb=rowb:
                            e.tensor_add(rowb[:, off0 + oo:off0 + oo + nn_],
                                         pp[:, oo:oo + nn_], msk),
                            deps=deps, mark=mkname)
            return ng

        def emit_softmax(gi, ng):
            v = row_env(gi)
            pgi = gi - 3
            rowb, Pb, L = v["rowb"], v["Pb"], v["L"]
            negm, rsum, rinv = v["negm"], v["rsum"], v["rinv"]
            mdeps = [(f"{K_}sev{b}_{gi}_{ng-1}", VE)]
            op("vector", lambda e, rowb=rowb, L=L, negm=negm:
                e.reduce_max(negm, rowb[:, 0:L], AX.X, negate=True),
                deps=mdeps, mark=f"{K_}rw{b}_{gi}")
            deps = [(f"{K_}rw{b}_{gi}", VE)]
            if pgi >= 0:
                deps.append((f"{K_}tp{b}_{pgi}", PE))
            op("scalar", lambda e, Pb=Pb, rowb=rowb, L=L, negm=negm, rsum=rsum:
                e.activation(Pb[:, 0:L], rowb[:, 0:L], AF.Exp, bias=negm, accum_out=rsum),
                deps=deps, mark=f"{K_}xp{b}_{gi}")
            op("vector", lambda e, rinv=rinv, rsum=rsum: e.reciprocal(rinv, rsum),
                deps=[(f"{K_}xp{b}_{gi}", SC)], mark=f"{K_}rc{b}_{gi}")
            op("vector", lambda e, Pb=Pb, L=L, rinv=rinv: e.tensor_scalar(
                Pb[:, 0:L], Pb[:, 0:L], rinv, None, ALU.mult),
                deps=[(f"{K_}rc{b}_{gi}", VE)], mark=f"{K_}pn{b}_{gi}")

        def emit_transposes(gi):
            v = row_env(gi)
            Pb, Pt = v["Pb"], v["Pt"]
            nkt = nkt_of(gi)
            for j in range(nkt):
                jj = j
                tpp = (v["tbA"] if j % 2 == 0 else v["tbB"])[:, 0:128]
                if j >= 2:
                    deps = [(f"{K_}pt{b}_{gi}_{jj-2}", pt_eng(jj - 2))]
                else:
                    deps = [(f"{K_}pn{b}_{gi}", VE)]
                op("tensor", lambda e, jj=jj, tpp=tpp, Pb=Pb: e.transpose(
                    tpp, Pb[:, jj * 128:(jj + 1) * 128], idb),
                    deps=deps, mark=f"{K_}tr{b}_{gi}_{jj}")
                if en == "tensor" and jj == nkt - 1:
                    ctx.mark(f"{K_}tp{b}_{gi}")
                if jj % 2 == 0:
                    op("vector", lambda e, tpp=tpp, jj=jj, Pt=Pt: e.tensor_copy(
                        Pt[:, jj * 128:(jj + 1) * 128], tpp),
                        deps=[(f"{K_}tr{b}_{gi}_{jj}", PE)], mark=f"{K_}pt{b}_{gi}_{jj}")
                else:
                    op("scalar", lambda e, tpp=tpp, jj=jj, Pt=Pt: e.copy(
                        Pt[:, jj * 128:(jj + 1) * 128], tpp),
                        deps=[(f"{K_}tr{b}_{gi}_{jj}", PE)], mark=f"{K_}pt{b}_{gi}_{jj}")

        def emit_av(gi):
            v = row_env(gi)
            h, qi = v["h"], v["qi"]
            Pt, pav, kt_lo = v["Pt"], v["pav"], v["kt_lo"]
            nkt = nkt_of(gi)
            deps = last_pt_deps(b, gi)

            def mm_av(e, nkt=nkt, kt_lo=kt_lo, h=h, pav=pav, Pt=Pt):
                for j in range(nkt):
                    ins = e.matmul(pav, Vt[:, kt_lo + j, h * 64:(h + 1) * 64],
                                   Pt[:, j * 128:(j + 1) * 128],
                                   start=(j == 0), stop=(j == nkt - 1))
                return ins
            op("tensor", mm_av, deps=deps, mark=f"{K_}av{b}_{h}_{qi}")

        def emit_cx(gi):
            v = row_env(gi)
            h, qi = v["h"], v["qi"]
            pav = v["pav"]
            bias_ap = (bcols[(h % 2) * 64:(h % 2) * 64 + 64, 2, v["hc"]:v["hc"] + 1]
                       if has_bias else 0.0)
            op("scalar", lambda e, hc=v["hc"], hr=v["hr"], qi=qi, pav=pav, bias_ap=bias_ap:
                e.activation(Cx[hr:hr + 64, hc, qi * 128:(qi + 1) * 128], pav,
                             AF.Identity, bias=bias_ap),
                deps=[(f"{K_}av{b}_{h}_{qi}", PE)], mark=f"{K_}cx{b}_{gi}")

        nrows = n_h * 16
        for i2 in range(nrows + 3):
            if i2 >= 3 and i2 - 3 < nrows:
                emit_cx(i2 - 3)
            if i2 < nrows:
                ng = emit_scores(i2)
            if i2 >= 2 and i2 - 2 < nrows:
                emit_transposes(i2 - 2)
            if i2 < nrows:
                emit_softmax(i2, ng)
            if i2 >= 2 and i2 - 2 < nrows:
                emit_av(i2 - 2)

        gl = n_h * 16 - 1
        wo_done = load_w(b, "o", wo, wT0, stageO, [(f"{K_}av{b}_{n_h-1}_15", PE)])
        for g in range(32):
            tt, o_ = g // 8, g % 8
            t0 = tt * 512
            pp = ps["p0"].ap() if g % 2 == 0 else ps["p1"].ap()
            gg = g
            deps = []
            if g == 0:
                deps.append((wo_done, VE))
                deps.append((f"{K_}cx{b}_{gl}", SC))
            if g >= 2:
                deps.append((f"{K_}ope{b}_{gg-2}", VE))

            def mmo(e, o_=o_, pp=pp, t0=t0):
                for dc in range(DCH):
                    ins = e.matmul(pp[:], wT0[:, dc, o_ * 128:(o_ + 1) * 128],
                                   Cx[:, dc, t0:t0 + 512], start=(dc == 0), stop=(dc == DCH - 1))
                return ins
            op("tensor", mmo, deps=deps, mark=f"{K_}opm{b}_{gg}")
            ev = Dd[:, 2048:2560] if g % 2 == 0 else Ef[:, 0:512]
            deps = [(f"{K_}opm{b}_{gg}", PE)]
            if g >= 2:
                deps.append((f"{K_}ops{b}_{gg-2}", GP if (gg - 2) % 2 == 0 else GP2))
            elif b > 0:
                deps.append((f"{K_}ops{b-1}_{30 + gg}", GP if gg % 2 == 0 else GP2))
            if has_bias:
                op("vector", lambda e, ev=ev, pp=pp, o_=o_: e.tensor_scalar(
                    ev, pp[:], bcols[:, 3, o_:o_ + 1], None, ALU.add),
                    deps=deps, mark=f"{K_}ope{b}_{gg}")
            else:
                op("vector", lambda e, ev=ev, pp=pp: e.tensor_copy(ev, pp[:]),
                    deps=deps, mark=f"{K_}ope{b}_{gg}")
            kwargs = {"accum_op": ALU.add} if window is None else {}
            op("gpsimd" if g % 2 == 0 else "gpsimd2",
               lambda e, ev=ev, o_=o_, t0=t0, kwargs=kwargs: e.dma_start(
                out=out_dram[o_ * 128:(o_ + 1) * 128, tbse + t0:tbse + t0 + 512],
                in_=ev, **kwargs),
                deps=[(f"{K_}ope{b}_{gg}", VE)], mark=f"{K_}ops{b}_{gg}")
    waitall()


def p3z_zero_acc(ctx, en, eng):
    sb = ctx.sb
    ch = Chain(ctx, en, eng, "p3z")
    accf = ctx.t("accf")
    z = sb["D"].ap()[:, 0:2048]
    ch.step("vector", lambda e: e.memset(z, 0.0))
    for dc in range(DCH):
        for half in range(T // 2048):
            d0, h0 = dc * 128, half * 2048
            ch.step("sync", lambda e: e.dma_start(
                out=accf[d0:d0 + 128, h0:h0 + 2048], in_=z), 16)
    ch.wait_all()


def p3_moe(ctx, en, eng, n_tt=NMT):
    sb, ps, cs = ctx.sb, ctx.ps, ctx.consts
    A, B, C, Dd, F, X = (sb[k].ap() for k in ("A", "B", "C", "D", "F", "X"))
    xn0, comb, accf = ctx.t("xn0"), ctx.t("comb"), ctx.t("accf")
    w1 = ctx.t("w_moe1")
    w2 = ctx.t("w_moe2")
    sw1 = ctx.t("w_sw1")
    sw2 = ctx.t("w_sw2")
    b2 = ctx.t("b_moe2")
    sb2 = ctx.t("b_ssb2")
    b1b = ctx.t("b1b")
    b1 = ctx.t("b_moe1")
    sb1 = ctx.t("b_ssb1")

    onehot = F[0:8, 0:1024]
    Be = F[:, 1024:1536]
    combst = F[0:8, 1536:2048]
    b2r = F[0:8, 2048:3072]
    sb2r = F[0:1, 3072:4096]
    onesr = sb["H"].ap()[0:1, 0:512]
    b1col = F[:, 3584:3616]

    ch = Chain(ctx, en, eng, "p3p")
    ch.step("gpsimd", lambda e: e.dma_start(out=b1b[0:8, :], in_=b1), 16)
    ch.step("gpsimd", lambda e: e.dma_start(out=b1b[8:9, :], in_=sb1[None, :]), 16)
    ch.step("gpsimd", lambda e: e.dma_start(out=b2r, in_=b2), 16)
    ch.step("gpsimd", lambda e: e.dma_start(out=sb2r, in_=sb2[None, :]), 16)
    ch.step("sync", lambda e: e.dma_start(out=onehot, in_=cs["onehot8"]), 16)
    ch.step("sync", lambda e: e.dma_start(out=onesr, in_=cs["ones_row_b"]), 16)
    ch.wait_all()

    Aw = A[:, 0:8 * 4096].rearrange("p (c f) -> p c f", c=8)
    Bw = B[:, 0:32 * 1024].rearrange("p (c o) -> p c o", c=32)
    Hb = C[:, 0:32 * 512].rearrange("p (c t) -> p c t", c=32)
    Xt = X[:, 0:8 * 512].rearrange("p (c t) -> p c t", c=8)

    for m in range(9):
        shared = (m == 0)
        e_id = m - 1
        if en == "gpsimd":
            if m > 0:
                ctx.wait(eng, "mpe", ctx.at(f"m_e{m-1}_end", "mpe"))
            src1 = sw1 if shared else w1[e_id]
            src2 = sw2 if shared else w2[e_id]
            ctx.inc(eng.dma_start(out=Aw, in_=src1.rearrange("(c p) f -> p c f", p=128)), "mgw", 16)
            ctx.inc(eng.dma_start(out=Bw, in_=src2.rearrange("(c p) o -> p c o", p=128)), "mgw", 16)
            ctx.mark(f"m_w_{m}")
        if en == "sync":
            if m > 0:
                ctx.wait(eng, "msc", ctx.at(f"m_gelu_end_{m-1}", "msc"))
            ctx.inc(eng.dma_start_transpose(
                out=b1col, in_=b1b[8 if shared else e_id].rearrange("(c p) -> c p", p=128)), "mlb", 16)
            ctx.mark(f"m_b1_{m}")

        for it in range(n_tt):
            t0 = it * MTT
            p1a = ps["p0"].ap()
            p1b = ps["p1"].ap()
            p2a = ps["p2"].ap()
            p2b = ps["p3"].ap()
            pbc = ps["p4"].ap()

            if en == "sync":
                if it > 0:
                    ctx.wait(eng, "mpe", ctx.at(f"m_s1_{m}_{it-1}_31", "mpe"))
                elif m > 0:
                    ctx.wait(eng, "mpe", ctx.at(f"m_e{m-1}_end", "mpe"))
                for c in range(DCH):
                    ctx.inc(eng.dma_start_transpose(
                        out=Xt[:, c, :], in_=xn0[t0:t0 + MTT, c * 128:(c + 1) * 128]), "mlx", 16)
                ctx.mark(f"m_x_{m}_{it}")

            if not shared:
                if en == "gpsimd":
                    if m == 8 and it > 0:
                        ctx.wait(eng, "mpe", ctx.at(f"m_s2_{8}_{it-1}_{7}", "mpe"))
                    elif not (m == 1 and it == 0):
                        ctx.wait(eng, "mpe", ctx.at(f"m_bc_prev_{m}_{it}", "mpe"))
                    ctx.inc(eng.dma_start(out=combst, in_=comb[0:8, t0:t0 + MTT]), "mgc", 16)
                    ctx.mark(f"m_compst_{m}_{it}")
                if en == "tensor":
                    ctx.wait(eng, "mgc", ctx.at(f"m_compst_{m}_{it}", "mgc"))
                    if not (m == 1 and it == 0):
                        ctx.wait(eng, "mev", ctx.at(f"m_bee_prev_{m}_{it}", "mev"))
                    ctx.inc(eng.matmul(pbc[:], onehot[:, e_id * 128:(e_id + 1) * 128],
                                       combst, start=True, stop=True), "mpe", 1)
                    ctx.mark(f"m_bc_{m}_{it}")
                    ctx.mark(f"m_bc_prev_{m}_{it + 1}" if it + 1 < n_tt else f"m_bc_prev_{m + 1}_0")
                if en == "vector":
                    ctx.wait(eng, "mpe", ctx.at(f"m_bc_{m}_{it}", "mpe"))
                    ctx.inc(eng.tensor_copy(Be, pbc[:]), "mev", 1)
                    ctx.mark(f"m_bee_{m}_{it}")
                    ctx.mark(f"m_bee_prev_{m}_{it + 1}" if it + 1 < n_tt else f"m_bee_prev_{m + 1}_0")

            for fc in range(NFC):
                pp = p1a if fc % 2 == 0 else p1b
                if en == "tensor":
                    ctx.wait(eng, "mlx", ctx.at(f"m_x_{m}_{it}", "mlx"))
                    if fc == 0:
                        ctx.wait(eng, "mgw", ctx.at(f"m_w_{m}", "mgw"))
                    if fc >= 2:
                        ctx.wait(eng, "msc", ctx.at(f"m_g_{m}_{it}_{fc-2}", "msc"))
                    elif it > 0:
                        ctx.wait(eng, "msc", ctx.at(f"m_g_{m}_{it-1}_{31 if fc == 0 else 31}", "msc") - (1 - fc))
                    elif m > 0:
                        ctx.wait(eng, "msc", ctx.at(f"m_gelu_end_{m-1}", "msc"))
                    for dc in range(DCH):
                        ins = eng.matmul(pp[:], Aw[:, dc, fc * 128:(fc + 1) * 128],
                                         Xt[:, dc, :], start=(dc == 0), stop=(dc == DCH - 1))
                    ctx.inc(ins, "mpe", 1)
                    ctx.mark(f"m_s1_{m}_{it}_{fc}")
                if en == "scalar":
                    ctx.wait(eng, "mpe", ctx.at(f"m_s1_{m}_{it}_{fc}", "mpe"))
                    ctx.wait(eng, "mlb", ctx.at(f"m_b1_{m}", "mlb"))
                    ctx.inc(eng.activation(Hb[:, fc, :], (p1a if fc % 2 == 0 else p1b)[:],
                                           AF.Gelu, bias=b1col[:, fc:fc + 1]), "msc", 1)
                    ctx.mark(f"m_g_{m}_{it}_{fc}")
                if not shared and en == "vector":
                    ctx.wait(eng, "msc", ctx.at(f"m_g_{m}_{it}_{fc}", "msc"))
                    ctx.wait(eng, "mev", ctx.at(f"m_bee_{m}_{it}", "mev"))
                    ctx.inc(eng.tensor_mul(Hb[:, fc, :], Hb[:, fc, :], Be), "mev", 1)
                    ctx.mark(f"m_sc_{m}_{it}_{fc}")
            if en in ("scalar",) and it == n_tt - 1:
                ctx.mark(f"m_gelu_end_{m}")

            for oc in range(8):
                pp = p2a if oc % 2 == 0 else p2b
                if en == "tensor":
                    if not shared:
                        ctx.wait(eng, "mev", ctx.at(f"m_sc_{m}_{it}_{31}", "mev"))
                    else:
                        ctx.wait(eng, "msc", ctx.at(f"m_g_{m}_{it}_{31}", "msc"))
                    if oc >= 2:
                        ctx.wait(eng, "mev", ctx.at(f"m_ev_{m}_{it}_{oc-2}", "mev"))
                    elif it > 0:
                        ctx.wait(eng, "mev", ctx.at(f"m_ev_{m}_{it-1}_{6 + oc}", "mev"))
                    elif m > 0:
                        ctx.wait(eng, "mev", ctx.at(f"m_ev_{m-1}_{n_tt-1}_{6 + oc}", "mev"))
                    bias_here = (m == 8)
                    for fc in range(NFC):
                        last = (fc == NFC - 1) and not bias_here
                        ins = eng.matmul(pp[:], Bw[:, fc, oc * 128:(oc + 1) * 128],
                                         Hb[:, fc, :], start=(fc == 0), stop=last)
                    if bias_here:
                        ins = eng.matmul(pp[:], b2r[:, oc * 128:(oc + 1) * 128], combst,
                                         start=False, stop=False)
                        ins = eng.matmul(pp[:], sb2r[:, oc * 128:(oc + 1) * 128], onesr,
                                         start=False, stop=True)
                    ctx.inc(ins, "mpe", 1)
                    ctx.mark(f"m_s2_{m}_{it}_{oc}")
                if en == "vector":
                    ctx.wait(eng, "mpe", ctx.at(f"m_s2_{m}_{it}_{oc}", "mpe"))
                    dsl = Dd[:, (oc % 2) * 512:(oc % 2) * 512 + 512]
                    par = f"mst{oc % 2}"
                    if not (m == 0 and it == 0 and oc < 2):
                        key = (f"m_st_{m}_{it}_{oc-2}" if oc >= 2 else
                               (f"m_st_{m}_{it-1}_{6+oc}" if it > 0 else
                                f"m_st_{m-1}_{n_tt-1}_{6+oc}"))
                        ctx.wait(eng, par, ctx.at(key, par))
                    ctx.inc(eng.tensor_copy(dsl, pp[:]), "mev", 1)
                    ctx.mark(f"m_ev_{m}_{it}_{oc}")
                if en == "gpsimd":
                    ctx.wait(eng, "mev", ctx.at(f"m_ev_{m}_{it}_{oc}", "mev"))
                    dsl = Dd[:, (oc % 2) * 512:(oc % 2) * 512 + 512]
                    ctx.inc(eng.dma_start(out=accf[oc * 128:(oc + 1) * 128, t0:t0 + MTT],
                                          in_=dsl, accum_op=ALU.add), f"mst{oc % 2}", 16)
                    ctx.mark(f"m_st_{m}_{it}_{oc}")
        if en == "tensor":
            ctx.mark(f"m_e{m}_end")
    ctx.wait(eng, "mst0", ctx.at(f"m_st_{8}_{n_tt-1}_{6}", "mst0"))
    ctx.wait(eng, "mst1", ctx.at(f"m_st_{8}_{n_tt-1}_{7}", "mst1"))


def p4_ln1(ctx, en, eng):
    sb, ps, cs = ctx.sb, ctx.ps, ctx.consts
    C, Dd, Ef, F, G, I, X = (sb[k].ap() for k in ("C", "D", "E", "F", "G", "I", "X"))
    Xf32 = X.bitcast(F32)
    Cf32 = C.bitcast(F32)
    ch0 = Chain(ctx, en, eng, "p4A")
    ch1 = Chain(ctx, en, eng, "p4B")
    chs = [ch0, ch1]
    x = ctx.t("x_in").rearrange("b s d -> (b s) d")
    accf, x1ag, xn1 = ctx.t("accf"), ctx.t("x1ag"), ctx.t("xn1")
    idf = I[:, 0:128]
    ch0.step("sync", lambda e: e.dma_start(out=idf, in_=cs["ident_f"]), 16)
    ch1.join([ch0])
    for i in range(NT):
        sl = i % 2
        cc_ = chs[sl]
        r0 = i * 128
        xt = Dd[:, 0:1024] if sl == 0 else Dd[:, 1024:2048]
        chunks = Ef[:, 0:1024] if sl == 0 else Ef[:, 1024:2048]
        xsq = Xf32[:, 0:1024] if sl == 0 else Xf32[:, 1024:2048]
        x1t = Dd[:, 2048:3072] if sl == 0 else Cf32[:, 0:1024]
        xnb = F[:, 0:1024] if sl == 0 else F[:, 1024:2048]
        st = G[:, sl * 16:sl * 16 + 16]
        cc_.dma_group("sync", [lambda e, xt=xt, r0=r0: e.dma_start(out=xt, in_=x[r0:r0 + 128, :])] + [
            (lambda e, d_=dc, r0=r0, chunks=chunks: e.dma_start(
                out=chunks[:, d_ * 128:(d_ + 1) * 128],
                in_=accf[d_ * 128:(d_ + 1) * 128, r0:r0 + 128]))
            for dc in range(DCH)])
        for dc in range(DCH):
            d_ = dc
            pb2 = (ps[f"p{2 + sl*2}"] if dc % 2 == 0 else ps[f"p{3 + sl*2}"]).ap()
            cc_.step("tensor", lambda e, d_=d_, pb2=pb2, chunks=chunks: e.transpose(
                pb2[:, 0:128], chunks[:, d_ * 128:(d_ + 1) * 128], idf))
            cc_.step("vector", lambda e, d_=d_, pb2=pb2, x1t=x1t, xt=xt: e.tensor_add(
                x1t[:, d_ * 128:(d_ + 1) * 128], pb2[:, 0:128],
                xt[:, d_ * 128:(d_ + 1) * 128]))
        cc_.step("gpsimd", lambda e, r0=r0, x1t=x1t: e.dma_start(out=x1ag[r0:r0 + 128, :], in_=x1t), 16)
        ln_stats_norm(cc_, x1t, xsq, st, xsq)
        cc_.step("vector", lambda e, xnb=xnb, xsq=xsq: e.tensor_copy(xnb, xsq))
        cc_.step("gpsimd", lambda e, r0=r0, xnb=xnb: e.dma_start(out=xn1[r0:r0 + 128, :], in_=xnb), 16)
    for c_ in chs:
        c_.wait_all()


def p6_ag1(ctx, en, eng, cc_group):
    if en == "gpsimd":
        ctx.inc(eng.collective_compute(
            "AllGather", ALU.bypass, replica_groups=[cc_group],
            ins=[ctx.t("x1ag").opt()], outs=[ctx.t("x1full").opt()]), "cc1", 1)


def rank0(ctx, eng):
    return 0 if ctx.mode == "mock" else eng.partition_id()


def dyn(ap, off_expr):
    return bass.AP(tensor=ap.tensor, offset=ap.offset + off_expr, ap=ap.ap)


def p7_ffn1(ctx, en, eng, ncc, cc_group):
    sb, ps, cs = ctx.sb, ctx.ps, ctx.consts
    A, B, C, Dd, Ef, F, G, H, I, X = (sb[k].ap() for k in
                                      ("A", "B", "C", "D", "E", "F", "G", "H", "I", "X"))
    jl = S // ncc
    njt = jl // 128
    cells = 32 * jl
    nct = cells // 128
    ch = Chain(ctx, en, eng, "p7")
    x1full = ctx.t("x1full")
    xn1s, xn1slo, t32 = ctx.t("xn1s"), ctx.t("xn1slo"), ctx.t("t32")
    probs_d = ctx.t("probs")
    qall_d, kvall_d, su_d = ctx.t("qall"), ctx.t("kvall"), ctx.t("su")
    v1d, v2d, m1d, m2d, cnt_d = (ctx.t(k) for k in ("v1d", "v2d", "m1d", "m2d", "cntd"))
    agsm_in, agsm_out = ctx.t("agsm_in"), ctx.t("agsm_out")
    ffn1ag = ctx.t("ffn1ag")
    wg, wqh, wkvh, woh = ctx.t("w1_wg"), ctx.t("w1_wq"), ctx.t("w1_wkv"), ctx.t("w1_wo")
    bqh, bkvh, boh = ctx.t("b1_bq"), ctx.t("b1_bkv"), ctx.t("b1_bo")
    lng, lnb = ctx.t("b1_lng"), ctx.t("b1_lnb")

    idf = I[:, 0:128]
    onesb = H[0:1, 0:512]
    r = [None]

    def R(e):
        if r[0] is None:
            r[0] = rank0(ctx, e)
        return r[0]

    wgHI = H[:, 512:640].rearrange("p (c h) -> p c h", c=8)
    wgLO = H[:, 640:768].rearrange("p (c h) -> p c h", c=8)
    wgF = Ef[:, 0:128].rearrange("p (c h) -> p c h", c=8)
    ch.step("sync", lambda e: e.dma_start(out=idf, in_=cs["ident_f"]), 16)
    ch.step("sync", lambda e: e.dma_start(out=onesb, in_=cs["ones_row_b"]), 16)
    ch.step("sync", lambda e: e.dma_start(
        out=wgF, in_=wg.rearrange("(c p) h -> p c h", p=128)), 16)
    ch.step("vector", lambda e: e.tensor_copy(wgHI, wgF))
    ch.step("vector", lambda e: e.tensor_sub(wgLO, wgF, wgHI))
    WqA = C[:, 0:8192].rearrange("p (c n) -> p c n", c=8)
    WoS = C[:, 8192:16384].rearrange("p (c n) -> p c n", c=8)
    WkvA = B[:, 0:16384].rearrange("p (c n) -> p c n", c=8)
    for c_ in range(DCH):
        cc = c_
        ch.step("gpsimd", lambda e, cc=cc: e.dma_start(
            out=C[:, cc * 1024:(cc + 1) * 1024].rearrange("p (h n) -> p h n", h=16),
            in_=wqh.rearrange("h (c p) n -> c p h n", p=128)[cc]), 16)
        ch.step("gpsimd", lambda e, cc=cc: e.dma_start(
            out=B[:, cc * 2048:(cc + 1) * 2048].rearrange("p (h n) -> p h n", h=16),
            in_=wkvh.rearrange("h (c p) n -> c p h n", p=128)[cc]), 16)
    ch.step("gpsimd", lambda e: e.dma_start(
        out=WoS, in_=woh.rearrange("h e o -> (h e) o").rearrange("(c p) o -> p c o", p=128)), 16)
    bqA = F[0:1, 0:1024]
    bkvA = F[0:1, 1024:3072]
    ch.step("gpsimd", lambda e: e.dma_start(out=bqA, in_=bqh.rearrange("h n -> (h n)")[None, :]), 16)
    ch.step("gpsimd", lambda e: e.dma_start(out=bkvA, in_=bkvh.rearrange("h n -> (h n)")[None, :]), 16)

    x1seq = ctx.t("x1seq")
    if en == "gpsimd" and ctx.count("cc1"):
        ctx.wait(eng, "cc1", ctx.count("cc1"))
    ch.step("gpsimd", lambda e: e.dma_start(
        out=x1seq.rearrange("(b j) d -> b j d", b=32),
        in_=dyn(x1full.rearrange("(b s) d -> b s d", b=32)[:, 0:jl, :], R(e) * (jl * D))), 16)

    ch7a = [ch, Chain(ctx, en, eng, "p7a2")]
    Xf32 = X.bitcast(F32)
    for ct in range(nct):
        sl = ct % 2
        cc_ = ch7a[sl]
        xt = Dd[:, 0:1024] if sl == 0 else Dd[:, 1024:2048]
        xsq = Ef[:, 0:1024] if sl == 0 else Ef[:, 1024:2048]
        xnf = Dd[:, 2048:3072] if sl == 0 else Xf32[:, 0:1024]
        xnb = F[:, sl * 2048:sl * 2048 + 1024]
        xnl = F[:, sl * 2048 + 1024:sl * 2048 + 2048]
        st = G[:, sl * 16:sl * 16 + 16]
        r0 = ct * 128
        cc_.step("sync", lambda e, r0=r0: e.dma_start(
            out=xt, in_=x1seq[r0:r0 + 128, :]), 16)
        ln_stats_norm(cc_, xt, xsq, st, xnf)
        cc_.step("vector", lambda e: e.tensor_copy(xnb, xnf))
        cc_.step("vector", lambda e: e.tensor_sub(xnl, xnf, xnb))
        cc_.dma_group("gpsimd", [
            lambda e, r0=r0: e.dma_start(out=xn1s[r0:r0 + 128, :], in_=xnb),
            lambda e, r0=r0: e.dma_start(out=xn1slo[r0:r0 + 128, :], in_=xnl),
            lambda e, r0=r0: e.dma_start(out=t32[r0:r0 + 128, :], in_=xnf)])


    ch7b = [Chain(ctx, en, eng, "p7bA"), Chain(ctx, en, eng, "p7bB")]
    for c_ in ch7b:
        c_.join(ch7a)
    for ct in range(nct):
        sl = ct % 2
        cc_ = ch7b[sl]
        r0 = ct * 128
        xhi = X[:, sl * 2048:sl * 2048 + 1024].rearrange("p (c t) -> p c t", c=8)
        xlo = X[:, sl * 2048 + 1024:sl * 2048 + 2048].rearrange("p (c t) -> p c t", c=8)
        fns = []
        for c in range(DCH):
            fns.append(lambda e, cc=c, r0=r0, xhi=xhi: e.dma_start_transpose(
                out=xhi[:, cc, :], in_=xn1s[r0:r0 + 128, cc * 128:(cc + 1) * 128]))
            fns.append(lambda e, cc=c, r0=r0, xlo=xlo: e.dma_start_transpose(
                out=xlo[:, cc, :], in_=xn1slo[r0:r0 + 128, cc * 128:(cc + 1) * 128]))
        cc_.dma_group("sync", fns)
        L = (ps["p0"] if sl == 0 else ps["p1"]).ap()[:, 0:16]

        def mmg(e, xhi=xhi, xlo=xlo, L=L):
            first = True
            for c in range(DCH):
                for lhs, rhs in ((xhi, wgHI), (xhi, wgLO), (xlo, wgHI)):
                    ins = e.matmul(L, lhs[:, c, :], rhs[:, c, :], start=first,
                                   stop=(c == DCH - 1 and lhs is xlo))
                    first = False
            return ins
        cc_.step("tensor", mmg)
        gof = 32 + sl * 40
        Lf = G[:, gof:gof + 16]
        nm1 = G[:, gof + 16:gof + 17]
        ssum = G[:, gof + 17:gof + 18]
        rs = G[:, gof + 18:gof + 19]
        pr = G[:, gof + 20:gof + 36]
        cc_.step("vector", lambda e, Lf=Lf, L=L: e.tensor_copy(Lf, L))
        cc_.step("vector", lambda e, Lf=Lf, nm1=nm1: e.reduce_max(nm1, Lf, AX.X, negate=True))
        cc_.step("scalar", lambda e, pr=pr, Lf=Lf, nm1=nm1, ssum=ssum: e.activation(
            pr, Lf, AF.Exp, bias=nm1, accum_out=ssum))
        cc_.step("vector", lambda e, rs=rs, ssum=ssum: e.reciprocal(rs, ssum))
        cc_.step("vector", lambda e, pr=pr, rs=rs: e.tensor_scalar(pr, pr, rs, None, ALU.mult))
        cc_.step("gpsimd", lambda e, r0=r0, pr=pr: e.dma_start(out=probs_d[r0:r0 + 128, :], in_=pr), 16)


    ch7d = [Chain(ctx, en, eng, "p7dA"), Chain(ctx, en, eng, "p7dB")]
    for c_ in ch7d:
        c_.join(ch7a)
        c_.join(ch7b)
    for ct in range(nct):
        sl = ct % 2
        cc_ = ch7d[sl]
        r0 = ct * 128
        xhi = X[:, sl * 1024:sl * 1024 + 1024].rearrange("p (c t) -> p c t", c=8)
        cc_.dma_group("sync", [
            (lambda e, cc=c, r0=r0, xhi=xhi: e.dma_start_transpose(
                out=xhi[:, cc, :], in_=xn1s[r0:r0 + 128, cc * 128:(cc + 1) * 128]))
            for c in range(DCH)])
        for ng in range(6):
            is_q = ng < 2
            W = WqA if is_q else WkvA
            brow = bqA if is_q else bkvA
            o0 = (ng if is_q else ng - 2) * 512
            pp = (ps[f"p{sl*2}"] if ng % 2 == 0 else ps[f"p{sl*2+1}"]).ap()

            def mmq(e, W=W, brow=brow, o0=o0, pp=pp, xhi=xhi):
                for dc in range(DCH):
                    ins = e.matmul(pp[:], xhi[:, dc, :], W[:, dc, o0:o0 + 512],
                                   start=(dc == 0), stop=False)
                ins = e.matmul(pp[:], onesb[0:1, 0:128], brow[:, o0:o0 + 512],
                               start=False, stop=True)
                return ins
            cc_.step("tensor", mmq)
            dst = (X[:, 2048 + (ng % 2) * 512: 2048 + (ng % 2) * 512 + 512] if sl == 0
                   else X[:, 3072 + (ng % 2) * 512: 3072 + (ng % 2) * 512 + 512])
            cc_.step("vector", lambda e, pp=pp, dst=dst: e.tensor_copy(dst, pp[:]))
            dd = qall_d if is_q else kvall_d
            cc_.step("gpsimd", lambda e, dd=dd, r0=r0, o0=o0, dst=dst: e.dma_start(
                out=dd[r0:r0 + 128, o0:o0 + 512], in_=dst), 16)

    assert njt == 2
    ch.join(ch7a)
    ch.join(ch7b)
    ch.join(ch7d)
    chB = Chain(ctx, en, eng, "p7jB")
    chB.join(ch7a + ch7b + ch7d)

    class Rec:
        def __init__(self):
            self.items = []

        def step(self, owner, fn, dv=None):
            self.items.append(("s", owner, fn, dv))

        def dma_group(self, owner, fns):
            self.items.append(("g", owner, fns, None))

    Bf32 = B.bitcast(F32)
    PA = dict(pj=G[:, 0:512], m1v=Dd[:, 0:16], m2v=Dd[:, 16:32],
              eqv=Ef[:, 0:512], l2v=Ef[:, 1024:1536], tmpv=Dd[:, 512:1024],
              c1=Ef[:, 512:1024], M1=X[:, 0:512], M2=X[:, 512:1024],
              cnt=Dd[:, 32:64], Q1=F[:, 0:1024], Q2=F[:, 1024:2048],
              V1=F[:, 2048:3072], V2=F[:, 3072:4096], K1=H[:, 0:1024],
              K2=X[:, 1024:2048], qb=C[:, 0:1024], kvb=C[:, 1024:3072],
              tq=C[:, 3072:4096], tkv=C[:, 4096:6144], tmpf=Dd[:, 0:1024],
              psT=ps["p2"].ap(), ld="sync", st="gpsimd")
    PB = dict(pj=Bf32[:, 0:512], m1v=Bf32[:, 3584:3600], m2v=Bf32[:, 3600:3616],
              eqv=Bf32[:, 1024:1536], l2v=Bf32[:, 2048:2560], tmpv=Bf32[:, 512:1024],
              c1=Bf32[:, 1536:2048], M1=B[:, 19712:20224], M2=B[:, 20224:20736],
              cnt=Bf32[:, 3616:3648], Q1=B[:, 7424:8448], Q2=B[:, 8448:9472],
              V1=B[:, 9472:10496], V2=B[:, 10496:11520], K1=B[:, 11520:12544],
              K2=B[:, 12544:13568], qb=B[:, 13568:14592], kvb=B[:, 14592:16640],
              tq=B[:, 16640:17664], tkv=B[:, 17664:19712], tmpf=Bf32[:, 2560:3584],
              psT=ps["p3"].ap(), ld="gpsimd", st="sync")

    def jt_body(rec, jt, P):
        j0 = jt * 128
        pj, m1v, m2v, eqv, l2v, tmpv = (P[k] for k in ("pj", "m1v", "m2v", "eqv", "l2v", "tmpv"))
        c0, c1, M1, M2, cnt = eqv, P["c1"], P["M1"], P["M2"], P["cnt"]
        Q1, Q2, V1, V2, K1, K2 = (P[k] for k in ("Q1", "Q2", "V1", "V2", "K1", "K2"))
        qb, kvb, tq, tkv, tmpf, psT = (P[k] for k in ("qb", "kvb", "tq", "tkv", "tmpf", "psT"))
        ldq, stq = P["ld"], P["st"]
        rec.step(ldq, lambda e, j0=j0, pj=pj: e.dma_start(
            out=pj.rearrange("p (b h) -> p b h", b=32),
            in_=probs_d.rearrange("(b j) h -> j b h", b=32)[j0:j0 + 128],
            allow_slow_non_contiguous=True), 16)

        def treemax(dst, src0):
            rec.step("vector", lambda e, src0=src0: e.tensor_max(tmpv[:, 0:256], src0[:, 0:256], src0[:, 256:512]))
            rec.step("vector", lambda e: e.tensor_max(tmpv[:, 0:128], tmpv[:, 0:128], tmpv[:, 128:256]))
            rec.step("vector", lambda e: e.tensor_max(tmpv[:, 0:64], tmpv[:, 0:64], tmpv[:, 64:128]))
            rec.step("vector", lambda e: e.tensor_max(tmpv[:, 0:32], tmpv[:, 0:32], tmpv[:, 32:64]))
            rec.step("vector", lambda e, dst=dst: e.tensor_max(dst, tmpv[:, 0:16], tmpv[:, 16:32]))
        treemax(m1v, pj)
        pj3 = pj.rearrange("p (b h) -> p b h", b=32)
        rec.step("vector", lambda e: e.tensor_tensor(
            eqv.rearrange("p (b h) -> p b h", b=32), pj3,
            m1v[:, None, :].broadcast_to([128, 32, 16]), ALU.is_ge))
        rec.step("vector", lambda e: e.scalar_tensor_tensor(l2v, eqv, NEG, pj, ALU.mult, ALU.add))
        treemax(m2v, l2v)
        rec.step("vector", lambda e: e.tensor_tensor(
            c0.rearrange("p (b h) -> p b h", b=32), pj3,
            m2v[:, None, :].broadcast_to([128, 32, 16]), ALU.is_ge))
        rec.step("vector", lambda e: e.tensor_copy(M2, c0))
        cur, nxt = c0, c1
        for sh in (1, 2, 4, 8, 16):
            s_ = sh * 16
            rec.step("vector", lambda e, cur=cur, nxt=nxt, s_=s_: e.tensor_copy(nxt[:, 0:s_], cur[:, 0:s_]))
            rec.step("vector", lambda e, cur=cur, nxt=nxt, s_=s_: e.tensor_add(
                nxt[:, s_:512], cur[:, s_:512], cur[:, 0:512 - s_]))
            cur, nxt = nxt, cur
        rec.step("vector", lambda e, cur=cur: e.tensor_single_scalar(tmpv, cur, 1.0, ALU.is_equal))
        rec.step("vector", lambda e: e.tensor_mul(M1, tmpv, M2))
        rec.step("vector", lambda e, cur=cur: e.tensor_single_scalar(tmpv, cur, 2.0, ALU.is_equal))
        rec.step("vector", lambda e: e.tensor_mul(M2, tmpv, M2))
        rec.step(stq, lambda e, j0=j0, M1=M1: e.dma_start(out=m1d[j0:j0 + 128, :], in_=M1), 16)
        rec.step(stq, lambda e, j0=j0, M2=M2: e.dma_start(out=m2d[j0:j0 + 128, :], in_=M2), 16)
        rec.step("vector", lambda e: e.tensor_add(tmpv, M1, M2))
        rec.step("vector", lambda e: e.tensor_reduce(
            cnt, tmpv.rearrange("p (b h) -> p b h", b=32), AX.X, ALU.add))
        rec.step(stq, lambda e, j0=j0, cnt=cnt: e.dma_start(out=cnt_d[j0:j0 + 128, :], in_=cnt), 16)

        for bufz in (Q1, Q2, V1, V2, K1, K2):
            bz = bufz
            rec.step("vector", lambda e, bz=bz: e.memset(bz, 0.0))
        for b in range(32):
            bb = b
            rec.dma_group(ldq, [
                lambda e, bb=bb, j0=j0, qb=qb: e.dma_start(
                    out=qb, in_=qall_d[bb * jl + j0: bb * jl + j0 + 128, :]),
                lambda e, bb=bb, j0=j0, kvb=kvb: e.dma_start(
                    out=kvb, in_=kvall_d[bb * jl + j0: bb * jl + j0 + 128, :])])
            m1s = M1[:, bb * 16:(bb + 1) * 16]
            m2s = M2[:, bb * 16:(bb + 1) * 16]
            qb3 = qb.rearrange("p (h e) -> p h e", h=16)
            kv4 = kvb.rearrange("p (h w e) -> p h w e", h=16, w=2)
            for (msk, qd, kd, vd) in ((m1s, Q1, K1, V1), (m2s, Q2, K2, V2)):
                mq, qd_, kd_, vd_ = msk, qd, kd, vd
                rec.step("vector", lambda e, mq=mq, qb3=qb3: e.tensor_tensor(
                    tq.rearrange("p (h e) -> p h e", h=16), qb3,
                    mq[:, :, None].broadcast_to([128, 16, 64]), ALU.mult))
                rec.step("vector", lambda e, qd_=qd_: e.tensor_add(qd_, qd_, tq))
                rec.step("vector", lambda e, mq=mq, kv4=kv4: e.tensor_tensor(
                    tkv.rearrange("p (h w e) -> p h w e", h=16, w=2), kv4,
                    mq[:, :, None, None].broadcast_to([128, 16, 2, 64]), ALU.mult))
                tkv4 = tkv.rearrange("p (h w e) -> p h w e", h=16, w=2)
                rec.step("vector", lambda e, kd_=kd_, tkv4=tkv4: e.tensor_add(
                    kd_.rearrange("p (h e) -> p h e", h=16), kd_.rearrange("p (h e) -> p h e", h=16),
                    tkv4[:, :, 0, :]))
                rec.step("vector", lambda e, vd_=vd_, tkv4=tkv4: e.tensor_add(
                    vd_.rearrange("p (h e) -> p h e", h=16), vd_.rearrange("p (h e) -> p h e", h=16),
                    tkv4[:, :, 1, :]))
        rec.step(stq, lambda e, j0=j0, V1=V1: e.dma_start(out=v1d[j0:j0 + 128, :], in_=V1), 16)
        rec.step(stq, lambda e, j0=j0, V2=V2: e.dma_start(out=v2d[j0:j0 + 128, :], in_=V2), 16)
        attj = G[:, 32 + jt * 64: 96 + jt * 64]
        for ci, (qq, kk) in enumerate(((Q1, K1), (Q1, K2), (Q2, K1), (Q2, K2))):
            ci_, qq_, kk_ = ci, qq, kk
            rec.step("vector", lambda e, qq_=qq_, kk_=kk_: e.tensor_mul(tmpf, qq_, kk_))
            rec.step("vector", lambda e, ci_=ci_, attj=attj: e.tensor_reduce(
                attj[:, ci_ * 16:(ci_ + 1) * 16],
                tmpf.rearrange("p (h e) -> p h e", h=16), AX.X, ALU.add))
        rec.step("vector", lambda e, attj=attj: e.tensor_scalar(attj, attj, 0.125, None, ALU.mult))
        rec.step("tensor", lambda e, attj=attj, psT=psT: e.transpose(psT[0:64, 0:128], attj, idf))
        rec.step("vector", lambda e, jt=jt, psT=psT: e.tensor_copy(
            Ef[0:64, 1536 + jt * 128:1536 + jt * 128 + 128], psT[0:64, 0:128]))

    recs = [Rec(), Rec()]
    jt_body(recs[0], 0, PA)
    jt_body(recs[1], 1, PB)
    chpair = [ch, chB]
    for i_ in range(max(len(r_.items) for r_ in recs)):
        for sl in range(2):
            if i_ < len(recs[sl].items):
                kind, owner, payload, dv = recs[sl].items[i_]
                if kind == "s":
                    chpair[sl].step(owner, payload, dv)
                else:
                    chpair[sl].dma_group(owner, payload)
    ch.join([chB])

    attT = Ef[0:64, 1536:1536 + 256]
    mloc = G[0:64, 480:481]
    ch.step("vector", lambda e: e.reduce_max(mloc, attT[:, 0:njt * 128], AX.X))
    ch.step("tensor", lambda e: e.transpose(ps["p3"].ap()[0:1, 0:64], mloc, idf[0:64, 0:64]))
    mrow = Ef[0:1, 1800:1864]
    ch.step("vector", lambda e: e.tensor_copy(mrow, ps["p3"].ap()[0:1, 0:64]))
    onesf = Dd[0:1, 0:128]
    ch.step("sync", lambda e: e.dma_start(out=onesf, in_=cs["ones_row_f"]), 16)
    ch.step("tensor", lambda e: e.matmul(ps["p4"].ap()[:, 0:64], onesf, mrow, start=True, stop=True))
    mb = Dd[:, 1024:1088]
    ch.step("vector", lambda e: e.tensor_scalar(mb, ps["p4"].ap()[:, 0:64], -1.0, None, ALU.mult))
    onescol = Dd[0:128, 1088:1089]
    ch.step("vector", lambda e: e.memset(onescol, 1.0))
    for jt in range(njt):
        attj = G[:, 32 + jt * 64: 96 + jt * 64]
        elj = Dd[:, 1152 + jt * 64: 1216 + jt * 64]
        ch.step("vector", lambda e, attj=attj, elj=elj: e.tensor_add(elj, attj, mb))
        ch.step("scalar", lambda e, elj=elj: e.activation(elj, elj, AF.Exp))
        ch.step("tensor", lambda e, elj=elj, jt=jt: e.matmul(
            ps["p5"].ap()[0:1, 0:64], onescol, elj, start=(jt == 0), stop=(jt == njt - 1)))
    srow = Ef[0:1, 1864:1928]
    ch.step("vector", lambda e: e.tensor_copy(srow, ps["p5"].ap()[0:1, 0:64]))
    ch.step("gpsimd", lambda e: e.dma_start(out=agsm_in[0:1, 0:64], in_=mrow), 16)
    ch.step("gpsimd", lambda e: e.dma_start(out=agsm_in[0:1, 64:128], in_=srow), 16)
    nrk = len(cc_group)
    if en == "gpsimd":
        if ctx.mode == "real":
            ch._wait_cur()
        ctx.inc(eng.collective_compute(
            "AllGather", ALU.bypass, replica_groups=[cc_group],
            ins=[agsm_in.opt()], outs=[agsm_out.opt()]), "cc2", 1)
        ctx.wait(eng, "cc2", ctx.count("cc2"))
    ch.step("gpsimd", lambda e: e.nop())
    ch.step("sync", lambda e, nrk=nrk: e.dma_start(out=Ef[0:nrk, 0:128], in_=agsm_out[0:nrk, :]), 16)
    GtT = Dd[:, 1280:1280 + 8]
    ch.step("tensor", lambda e, nrk=nrk: e.transpose(ps["p2"].ap()[:, 0:nrk], Ef[0:nrk, 0:128], idf[0:nrk, 0:nrk]))
    ch.step("vector", lambda e, nrk=nrk: e.tensor_copy(GtT[:, 0:nrk], ps["p2"].ap()[:, 0:nrk]))
    negM = G[0:64, 481:482]
    Sv = G[0:64, 482:483]
    rS = G[0:64, 483:484]
    Fv = G[0:64, 484:485]
    ch.step("vector", lambda e, nrk=nrk: e.reduce_max(negM, GtT[0:64, 0:nrk], AX.X, negate=True))
    wv_ = Dd[0:64, 1300:1300 + 8]
    ch.step("scalar", lambda e, nrk=nrk: e.activation(wv_[:, 0:nrk], GtT[0:64, 0:nrk], AF.Exp, bias=negM))
    pv_ = Dd[0:64, 1320:1320 + 8]
    scop = Dd[0:64, 1340:1340 + 8]
    ch.step("vector", lambda e, nrk=nrk: e.tensor_copy(scop[:, 0:nrk], GtT[64:128, 0:nrk]))
    ch.step("vector", lambda e, nrk=nrk: e.tensor_mul(pv_[:, 0:nrk], wv_[:, 0:nrk], scop[:, 0:nrk]))
    ch.step("vector", lambda e, nrk=nrk: e.reduce_sum(Sv, pv_[:, 0:nrk], AX.X))
    ch.step("vector", lambda e: e.reciprocal(rS, Sv))
    ch.step("scalar", lambda e: e.activation(Fv, mloc, AF.Exp, bias=negM))
    ch.step("vector", lambda e: e.tensor_mul(Fv, Fv, rS))
    ch.step("tensor", lambda e: e.transpose(ps["p3"].ap()[0:1, 0:64], Fv, idf[0:64, 0:64]))
    frow = Ef[0:1, 1800:1864]
    ch.step("vector", lambda e: e.tensor_copy(frow, ps["p3"].ap()[0:1, 0:64]))
    ch.step("tensor", lambda e: e.matmul(ps["p4"].ap()[:, 0:64], onesf, frow, start=True, stop=True))
    Fb = Dd[:, 2112:2176]
    ch.step("vector", lambda e: e.tensor_copy(Fb, ps["p4"].ap()[:, 0:64]))

    ch7fB = Chain(ctx, en, eng, "p7fB")
    ch7fB.join([ch, chB])
    PF_A = dict(V1=F[:, 2048:3072], V2=F[:, 3072:4096], M1=X[:, 0:512], M2=X[:, 512:1024],
                av1=C[:, 0:1024], av2=C[:, 1024:2048], tmpb=C[:, 2048:3072], sut=C[:, 3072:4096],
                ld="sync", st="gpsimd")
    PF_B = dict(V1=B[:, 9472:10496], V2=B[:, 10496:11520], M1=B[:, 19712:20224], M2=B[:, 20224:20736],
                av1=B[:, 12544:13568], av2=B[:, 13568:14592], tmpb=B[:, 14592:15616], sut=B[:, 15616:16640],
                ld="gpsimd", st="sync")

    def ffav_body(rec, jt, P):
        j0 = jt * 128
        V1, V2, M1, M2 = P["V1"], P["V2"], P["M1"], P["M2"]
        av1, av2, tmpb, sut = P["av1"], P["av2"], P["tmpb"], P["sut"]
        ldq, stq = P["ld"], P["st"]
        elj = Dd[:, 1152 + jt * 64: 1216 + jt * 64]
        pfin = elj
        rec.step("vector", lambda e, elj=elj, pfin=pfin: e.tensor_mul(pfin, elj, Fb))
        rec.dma_group(ldq, [
            lambda e, j0=j0, V1=V1: e.dma_start(out=V1, in_=v1d[j0:j0 + 128, :]),
            lambda e, j0=j0, V2=V2: e.dma_start(out=V2, in_=v2d[j0:j0 + 128, :]),
            lambda e, j0=j0, M1=M1: e.dma_start(out=M1, in_=m1d[j0:j0 + 128, :]),
            lambda e, j0=j0, M2=M2: e.dma_start(out=M2, in_=m2d[j0:j0 + 128, :])])
        for i_, avd in ((0, av1), (1, av2)):
            ii, avd_ = i_, avd
            p1s = pfin[:, (ii * 2 + 0) * 16:(ii * 2 + 0) * 16 + 16]
            p2s = pfin[:, (ii * 2 + 1) * 16:(ii * 2 + 1) * 16 + 16]
            rec.step("vector", lambda e, avd_=avd_, p1s=p1s, V1=V1: e.tensor_tensor(
                avd_.rearrange("p (h e) -> p h e", h=16),
                V1.rearrange("p (h e) -> p h e", h=16),
                p1s[:, :, None].broadcast_to([128, 16, 64]), ALU.mult))
            rec.step("vector", lambda e, p2s=p2s, V2=V2: e.tensor_tensor(
                tmpb.rearrange("p (h e) -> p h e", h=16),
                V2.rearrange("p (h e) -> p h e", h=16),
                p2s[:, :, None].broadcast_to([128, 16, 64]), ALU.mult))
            rec.step("vector", lambda e, avd_=avd_: e.tensor_add(avd_, avd_, tmpb))
        for b in range(32):
            bb = b
            m1s = M1[:, bb * 16:(bb + 1) * 16]
            m2s = M2[:, bb * 16:(bb + 1) * 16]
            rec.step("vector", lambda e, m1s=m1s: e.tensor_tensor(
                sut.rearrange("p (h e) -> p h e", h=16),
                av1.rearrange("p (h e) -> p h e", h=16),
                m1s[:, :, None].broadcast_to([128, 16, 64]), ALU.mult))
            rec.step("vector", lambda e, m2s=m2s: e.tensor_tensor(
                tmpb.rearrange("p (h e) -> p h e", h=16),
                av2.rearrange("p (h e) -> p h e", h=16),
                m2s[:, :, None].broadcast_to([128, 16, 64]), ALU.mult))
            rec.step("vector", lambda e: e.tensor_add(sut, sut, tmpb))
            rec.step(stq, lambda e, bb=bb, j0=j0, sut=sut: e.dma_start(
                out=su_d[bb * jl + j0: bb * jl + j0 + 128, :], in_=sut), 16)

    recsF = [Rec(), Rec()]
    ffav_body(recsF[0], 0, PF_A)
    ffav_body(recsF[1], 1, PF_B)
    chpairF = [ch, ch7fB]
    for i_ in range(max(len(r_.items) for r_ in recsF)):
        for sl in range(2):
            if i_ < len(recsF[sl].items):
                kind, owner, payload, dv = recsF[sl].items[i_]
                if kind == "s":
                    chpairF[sl].step(owner, payload, dv)
                else:
                    chpairF[sl].dma_group(owner, payload)
    ch.join([ch7fB])

    gb = Ef[:, 0:1024]
    bb_ = Ef[:, 1024:2048]
    bob = Dd[:, 2048:3072]
    for half in range(2):
        hh = half
        for src_row, dstb in ((lng, gb), (lnb, bb_), (boh, bob)):
            sr, db = src_row, dstb
            ch.step("sync", lambda e, sr=sr, hh=hh: e.dma_start(
                out=Ef[0:1, 2048 - 512:2048].bitcast(F32) if False else G[0:1, 0:512],
                in_=sr[None, hh * 512:(hh + 1) * 512]), 16)
            ch.step("tensor", lambda e: e.matmul(ps["p2"].ap()[:, 0:512], onesf,
                                                 G[0:1, 0:512], start=True, stop=True))
            ch.step("vector", lambda e, db=db, hh=hh: e.tensor_copy(
                db[:, hh * 512:(hh + 1) * 512], ps["p2"].ap()[:, 0:512]))
    ch7g = [Chain(ctx, en, eng, "p7gA"), Chain(ctx, en, eng, "p7gB")]
    for c_ in ch7g:
        c_.join([ch])
    Cf32 = C.bitcast(F32)
    Xf32b = X.bitcast(F32)
    for ct in range(nct):
        b, jt = ct // njt, ct % njt
        sl = ct % 2
        cc_ = ch7g[sl]
        j0 = jt * 128
        r0 = ct * 128
        cntt = G[:, sl * 32:sl * 32 + 32]
        suf = X[:, sl * 1024:sl * 1024 + 1024].rearrange("p (c t) -> p c t", c=8)
        cc_.dma_group("sync", [lambda e, j0=j0, cntt=cntt: e.dma_start(out=cntt, in_=cnt_d[j0:j0 + 128, :])] + [
            (lambda e, cc=c, r0=r0, suf=suf: e.dma_start_transpose(
                out=suf[:, cc, :], in_=su_d[r0:r0 + 128, cc * 128:(cc + 1) * 128]))
            for c in range(DCH)])
        y = Dd[:, 0:1024] if sl == 0 else Xf32b[:, 1024:2048]
        scr = Dd[:, 1024:2048] if sl == 0 else Cf32[:, 0:1024]
        for og in range(2):
            oo = og
            pp = (ps[f"p{sl*2}"] if og == 0 else ps[f"p{sl*2+1}"]).ap()

            def mmo(e, oo=oo, pp=pp, suf=suf):
                for dc in range(DCH):
                    ins = e.matmul(pp[:], suf[:, dc, :], WoS[:, dc, oo * 512:(oo + 1) * 512],
                                   start=(dc == 0), stop=(dc == DCH - 1))
                return ins
            cc_.step("tensor", mmo)
            ccol = cntt[:, b:b + 1]
            cc_.step("vector", lambda e, oo=oo, pp=pp, ccol=ccol, y=y: e.scalar_tensor_tensor(
                y[:, oo * 512:(oo + 1) * 512], bob[:, oo * 512:(oo + 1) * 512],
                ccol, pp[:], ALU.mult, ALU.add))
        cc_.step("sync", lambda e, r0=r0, scr=scr: e.dma_start(out=scr, in_=t32[r0:r0 + 128, :]), 16)
        cc_.step("vector", lambda e, y=y, scr=scr: e.tensor_add(y, y, scr))
        st = G[:, 100 + sl * 16:116 + sl * 16]
        cc_.step("scalar", lambda e, scr=scr, y=y: e.activation(scr, y, AF.Square))
        s1, s2, mu, var, rec, rstd, nmr = (st[:, k:k + 1] for k in range(7))
        cc_.step("vector", lambda e, s1=s1, y=y: e.reduce_sum(s1, y, AX.X))
        cc_.step("vector", lambda e, s2=s2, scr=scr: e.reduce_sum(s2, scr, AX.X))
        cc_.step("vector", lambda e, mu=mu, s1=s1: e.tensor_scalar(mu, s1, 1.0 / D, None, ALU.mult))
        cc_.step("vector", lambda e, var=var, mu=mu: e.tensor_mul(var, mu, mu))
        cc_.step("vector", lambda e, var=var, s2=s2: e.tensor_scalar(var, s2, 1.0 / D, var, ALU.mult, ALU.subtract))
        cc_.step("vector", lambda e, var=var: e.tensor_scalar_add(var, var, EPS))
        cc_.step("vector", lambda e, rec=rec, var=var: e.reciprocal(rec, var))
        cc_.step("scalar", lambda e, rstd=rstd, rec=rec: e.activation(rstd, rec, AF.Sqrt))
        cc_.step("scalar", lambda e, nmr=nmr, mu=mu, rstd=rstd: e.activation(nmr, mu, AF.Copy, scale=rstd))
        cc_.step("vector", lambda e, scr=scr, y=y, rstd=rstd, nmr=nmr: e.tensor_scalar(
            scr, y, rstd, nmr, ALU.mult, ALU.subtract))
        cc_.step("vector", lambda e, scr=scr: e.tensor_mul(scr, scr, gb))
        cc_.step("vector", lambda e, scr=scr: e.tensor_add(scr, scr, bb_))
        cc_.step("gpsimd", lambda e, r0=r0, scr=scr: e.dma_start(out=ffn1ag[r0:r0 + 128, :], in_=scr), 16)
    ch.wait_all()
    for c_ in ch7a + ch7b + ch7d + ch7g + [chB, ch7fB]:
        c_.wait_all()


def p7h_ag2(ctx, en, eng, cc_group):
    if en == "gpsimd":
        ctx.inc(eng.collective_compute(
            "AllGather", ALU.bypass, replica_groups=[cc_group],
            ins=[ctx.t("ffn1ag").opt()], outs=[ctx.t("ffn1full").opt()]), "cc3", 1)
        ctx.wait(eng, "cc3", ctx.count("cc3"))


def p8_final(ctx, en, eng, ncc):
    sb, ps, cs = ctx.sb, ctx.ps, ctx.consts
    C, Dd, Ef, I, X = (sb[k].ap() for k in ("C", "D", "E", "I", "X"))
    Xf32 = X.bitcast(F32)
    Cf32 = C.bitcast(F32)
    ch0 = Chain(ctx, en, eng, "p8A")
    ch1 = Chain(ctx, en, eng, "p8B")
    chs = [ch0, ch1]
    x1ag, attn1f, ffn1full = ctx.t("x1ag"), ctx.t("accf"), ctx.t("ffn1full")
    ffn1loc = ctx.t("ffn1loc")
    out = ctx.t("out").rearrange("b s d -> (b s) d")
    jl = S // ncc
    cells_ = 32 * jl
    idf = I[:, 0:128]
    r = [None]

    def R(e):
        if r[0] is None:
            r[0] = rank0(ctx, e)
        return r[0]
    ch0.step("sync", lambda e: e.dma_start(out=idf, in_=cs["ident_f"]), 16)
    for b_ in range(B_LOC):
        bb = b_
        src3 = ffn1full.rearrange("(r2 c) d -> r2 c d", c=cells_)[:, bb * jl:(bb + 1) * jl, :]
        ch0.step("gpsimd", lambda e, bb=bb, src3=src3: e.dma_start(
            out=ffn1loc.rearrange("(b s) d -> b s d", b=B_LOC)[bb].rearrange(
                "(r2 j) d -> r2 j d", j=jl),
            in_=dyn(src3, R(e) * (4 * jl * D))), 16)
    ch1.join([ch0])
    for i in range(NT):
        sl = i % 2
        cc_ = chs[sl]
        r0 = i * 128
        x1t = Dd[:, 0:1024] if sl == 0 else Dd[:, 1024:2048]
        ft = Dd[:, 2048:3072] if sl == 0 else Cf32[:, 0:1024]
        chunks = Ef[:, 0:1024] if sl == 0 else Ef[:, 1024:2048]
        cc_.dma_group("sync", [
            lambda e, x1t=x1t, r0=r0: e.dma_start(out=x1t, in_=x1ag[r0:r0 + 128, :]),
            lambda e, ft=ft, r0=r0: e.dma_start(out=ft, in_=ffn1loc[r0:r0 + 128, :])] + [
            (lambda e, d_=dc, r0=r0, chunks=chunks: e.dma_start(
                out=chunks[:, d_ * 128:(d_ + 1) * 128],
                in_=attn1f[d_ * 128:(d_ + 1) * 128, r0:r0 + 128]))
            for dc in range(DCH)])
        cc_.step("vector", lambda e, x1t=x1t, ft=ft: e.tensor_add(x1t, x1t, ft))
        for dc in range(DCH):
            d_ = dc
            pb2 = (ps[f"p{2 + sl*2}"] if dc % 2 == 0 else ps[f"p{3 + sl*2}"]).ap()
            cc_.step("tensor", lambda e, d_=d_, pb2=pb2, chunks=chunks: e.transpose(
                pb2[:, 0:128], chunks[:, d_ * 128:(d_ + 1) * 128], idf))
            cc_.step("vector", lambda e, d_=d_, pb2=pb2, x1t=x1t: e.tensor_add(
                x1t[:, d_ * 128:(d_ + 1) * 128], x1t[:, d_ * 128:(d_ + 1) * 128],
                pb2[:, 0:128]))
        cc_.step("gpsimd", lambda e, r0=r0, x1t=x1t: e.dma_start(out=out[r0:r0 + 128, :], in_=x1t), 16)
    for c_ in chs:
        c_.wait_all()



CELLS8 = 32 * JL

SHAPES = {
    "x_in": ([B_LOC, S, D], F32),
    "w0q": ([D, D], F32), "w0k": ([D, D], F32), "w0v": ([D, D], F32), "w0o": ([D, D], F32),
    "w_gate": ([8, D], F32),
    "w_moe1": ([8, D, FF], F32), "w_moe2": ([8, FF, D], F32),
    "w_sw1": ([D, FF], F32), "w_sw2": ([FF, D], F32),
    "b_moe1": ([8, FF], F32), "b_moe2": ([8, D], F32),
    "b_ssb1": ([FF], F32), "b_ssb2": ([D], F32),
    "w5q": ([D, D], F32), "w5k": ([D, D], F32), "w5v": ([D, D], F32), "w5o": ([D, D], F32),
    "b5q": ([D], F32), "b5k": ([D], F32), "b5v": ([D], F32), "b5o": ([D], F32),
    "w1_wg": ([D, 16], F32), "w1_wq": ([16, D, 64], F32), "w1_wkv": ([16, D, 128], F32),
    "w1_wo": ([16, 64, D], F32),
    "b1_bq": ([16, 64], F32), "b1_bkv": ([16, 128], F32), "b1_bo": ([D], F32),
    "b1_lng": ([D], F32), "b1_lnb": ([D], F32),
    "xn0": ([T, D], BF16), "xn0lo": ([T, D], BF16),
    "comb": ([8, T], BF16), "b1b": ([9, FF], BF16),
    "accf": ([D, T], F32),
    "b5rows": ([4, D], BF16),
    "x1ag": ([T, D], F32), "xn1": ([T, D], BF16),
    "x1full": ([NCORES * T, D], F32),
    "x1seq": ([CELLS8, D], F32),
    "xn1s": ([CELLS8, D], BF16), "xn1slo": ([CELLS8, D], BF16), "t32": ([CELLS8, D], F32),
    "probs": ([CELLS8, 16], F32),
    "qall": ([CELLS8, 1024], BF16), "kvall": ([CELLS8, 2048], BF16), "su": ([CELLS8, 1024], BF16),
    "v1d": ([JL, 1024], BF16), "v2d": ([JL, 1024], BF16),
    "m1d": ([JL, 512], BF16), "m2d": ([JL, 512], BF16), "cntd": ([JL, 32], F32),
    "agsm_in": ([1, 128], F32), "agsm_out": ([NCORES, 128], F32),
    "ffn1ag": ([CELLS8, D], F32), "ffn1full": ([NCORES * CELLS8, D], F32),
    "ffn1loc": ([T, D], F32),
    "out": ([B_LOC, S, D], F32),
}

WEIGHT_MAP = {
    "w0q": "L0_attn_Wq", "w0k": "L0_attn_Wk", "w0v": "L0_attn_Wv", "w0o": "L0_attn_Wo",
    "w_gate": "L0_ffn_gate_w",
    "w_moe1": "L0_ffn_w1", "w_moe2": "L0_ffn_w2",
    "w_sw1": "L0_ffn_sw1", "w_sw2": "L0_ffn_sw2",
    "b_moe1": "L0_ffn_b1", "b_moe2": "L0_ffn_b2",
    "b_ssb1": "L0_ffn_sb1", "b_ssb2": "L0_ffn_sb2",
    "w5q": "L1_attn_Wq", "w5k": "L1_attn_Wk", "w5v": "L1_attn_Wv", "w5o": "L1_attn_Wo",
    "b5q": "L1_attn_bq", "b5k": "L1_attn_bk", "b5v": "L1_attn_bv", "b5o": "L1_attn_bo",
    "w1_wg": "L1_ffn_Wg", "w1_wq": "L1_ffn_Wq", "w1_wkv": "L1_ffn_Wkv", "w1_wo": "L1_ffn_Wo",
    "b1_bq": "L1_ffn_bq", "b1_bkv": "L1_ffn_bkv", "b1_bo": "L1_ffn_bo",
    "b1_lng": "L1_ffn_ln_g", "b1_lnb": "L1_ffn_ln_b",
}


def ph_p1(ctx, en, eng):
    attn_phase(ctx, en, eng, "p1", ctx.t("xn0"), ctx.t("w0q"), ctx.t("w0k"),
               ctx.t("w0v"), ctx.t("w0o"), None, None, None, None,
               ctx.t("accf"), None)


def ph_p5(ctx, en, eng):
    attn_phase(ctx, en, eng, "p5", ctx.t("xn1"), ctx.t("w5q"), ctx.t("w5k"),
               ctx.t("w5v"), ctx.t("w5o"), ctx.t("b5q"), ctx.t("b5k"),
               ctx.t("b5v"), ctx.t("b5o"), ctx.t("accf"), WIN)


def ph_p7(ctx, en, eng):
    p7_ffn1(ctx, en, eng, NCORES, list(range(NCORES)))


def ph_p8(ctx, en, eng):
    p8_final(ctx, en, eng, NCORES)


def mk_io(ins, outs, shared=()):
    io = {}
    for n in ins:
        io[n] = ("ExternalInput", *SHAPES[n])
    for n in outs:
        io[n] = ("ExternalOutput", *SHAPES[n])
    for n, (shape, dt) in SHAPES.items():
        if n not in io:
            if n in shared:
                io[n] = ("Internal", shape, dt, "Shared")
            else:
                io[n] = ("Internal", shape, dt)
    return io


L0_W = ["w0q", "w0k", "w0v", "w0o", "w_gate", "w_moe1", "w_moe2", "w_sw1",
        "w_sw2", "b_moe1", "b_moe2", "b_ssb1", "b_ssb2"]
L5_W = ["w5q", "w5k", "w5v", "w5o", "b5q", "b5k", "b5v", "b5o"]
L7_W = ["w1_wg", "w1_wq", "w1_wkv", "w1_wo", "b1_bq", "b1_bkv", "b1_bo",
        "b1_lng", "b1_lnb"]

FUSED = True
_nc_cache = {}


def _get_nc(key, phases, io):
    if key not in _nc_cache:
        _nc_cache[key] = build_program(phases, io)
    return _nc_cache[key]


def kernel(**inputs):
    x = np.ascontiguousarray(np.asarray(inputs["L0_attn_x"], dtype=np.float32))
    w = {k: np.ascontiguousarray(np.asarray(inputs[v], dtype=np.float32))
         for k, v in WEIGHT_MAP.items()}
    xs = [x[4 * c:4 * c + 4] for c in range(NCORES)]

    if FUSED:
        phases = [p0_ln0, p2_gate, p3z_zero_acc, ph_p1, p3_moe, p4_ln1,
                  lambda c, en, e: p6_ag1(c, en, e, list(range(NCORES))),
                  ph_p5, ph_p7,
                  lambda c, en, e: p7h_ag2(c, en, e, list(range(NCORES))),
                  ph_p8]
        io = mk_io(["x_in"] + L0_W + L5_W + L7_W, ["out"],
                   shared=("x1full", "ffn1full", "agsm_out"))
        nc = _get_nc("fused", phases, io)
        in_maps = [{"x_in": xs[c], **w} for c in range(NCORES)]
        res = run_bass_kernel_spmd(nc, in_maps, core_ids=list(range(NCORES)))
        return np.concatenate([r["out"] for r in res.results], axis=0)

    ioA = mk_io(["x_in"] + L0_W, ["x1ag", "xn1"])
    ncA = _get_nc("A", [p0_ln0, p2_gate, p3z_zero_acc, ph_p1, p3_moe, p4_ln1], ioA)
    resA = run_bass_kernel_spmd(ncA, [{"x_in": xs[c], **{k: w[k] for k in L0_W}}
                                      for c in range(NCORES)],
                                core_ids=list(range(NCORES)))
    x1s = [r["x1ag"] for r in resA.results]
    xn1s_ = [r["xn1"] for r in resA.results]

    ioB = mk_io(["xn1"] + L5_W, ["accf"])
    ncB = _get_nc("B", [ph_p5], ioB)
    resB = run_bass_kernel_spmd(ncB, [{"xn1": xn1s_[c], **{k: w[k] for k in L5_W}}
                                      for c in range(NCORES)],
                                core_ids=list(range(NCORES)))
    attn1s = [r["accf"] for r in resB.results]

    x1full = np.concatenate(x1s, axis=0)
    ioC = mk_io(["x1full"] + L7_W, ["ffn1ag"], shared=("agsm_out",))
    ncC = _get_nc("C", [ph_p7], ioC)
    resC = run_bass_kernel_spmd(ncC, [{"x1full": x1full, **{k: w[k] for k in L7_W}}
                                      for c in range(NCORES)],
                                core_ids=list(range(NCORES)))
    ffn1full = np.concatenate([r["ffn1ag"] for r in resC.results], axis=0)

    ioD = mk_io(["x1ag", "accf", "ffn1full"], ["out"])
    ncD = _get_nc("D", [ph_p8], ioD)
    resD = run_bass_kernel_spmd(ncD, [{"x1ag": x1s[c], "accf": attn1s[c],
                                       "ffn1full": ffn1full}
                                      for c in range(NCORES)],
                                core_ids=list(range(NCORES)))
    return np.concatenate([r["out"] for r in resD.results], axis=0)
